# version 16
# baseline (speedup 1.0000x reference)
import math
from contextlib import ExitStack

import numpy as np

import bass_rust
import concourse.bass as bass
from concourse import library_config as _LC
import concourse.mybir as mybir
from concourse.bass_utils import run_bass_kernel_spmd

F32 = mybir.dt.float32
BF16 = mybir.dt.bfloat16
AF = mybir.ActivationFunctionType
ALU = mybir.AluOpType
AXX = mybir.AxisListType.X

TWO_PI = float(2.0 * math.pi)
PI = float(math.pi)
C_L0 = 12.0
C_L1 = 16.0
INV_SQRT_H0 = 1.0 / math.sqrt(128.0)
L1_SCALE = 0.125


ATT1_STOP = None
PAIR_STAGE = None
DEBUG_DUMP = False


class Cfg:
    def __init__(self, Tc=2048):
        self.Tc = Tc
        self.D = 1024
        self.E = 8
        self.M = 1408
        self.H1 = 16
        self.DH = 64
        self.BS = 64
        self.SEQC = 4
        self.NCH = Tc // 128
        self.DC = self.D // 128
        self.MC = self.M // 128
        self.S = Tc * self.SEQC
        self.NBLK = Tc // self.BS
        self.NPAIR = Tc // 128
        self.TH = Tc // 2
        self.QT = min(512, self.TH)


class Sem:
    __slots__ = ("h", "n", "name")
    def __init__(self, h, name):
        self.h = h; self.n = 0; self.name = name
    def bump(self, k=1):
        self.n += k
        return (self, k)
    @property
    def now(self):
        return (self, self.n)


class Prog:
    ENGINES = ("sync", "tensor", "scalar", "vector", "gpsimd")

    def __init__(self, nc, es):
        self.nc = nc
        self.es = es
        self.ops = {e: [] for e in self.ENGINES}

    def sem(self, name):
        return Sem(self.es.enter_context(self.nc.semaphore(name)), name)

    def op(self, engine, fn, waits=(), inc=None, incs=None):
        ii = list(incs) if incs else []
        if inc is not None:
            ii.append(inc)
        self.ops[engine].append((list(waits), fn, ii))

    def emit(self):
        with self.nc.Block("main") as b:
            def mk(engine):
                def run(eng):
                    seen = {}
                    for waits, fn, incs in self.ops[engine]:
                        for (s, v) in waits:
                            if seen.get(id(s), -1) >= v:
                                continue
                            eng.wait_ge(s.h, v)
                            seen[id(s)] = v
                        ins = fn(eng)
                        for (s, d) in incs:
                            ins = ins.then_inc(s.h, d)
                return run
            b.sync(mk("sync"))
            b.tensor(mk("tensor"))
            b.scalar(mk("scalar"))
            b.vector(mk("vector"))
            b.gpsimd(mk("gpsimd"))


class Builder:
    def __init__(self, cfg: Cfg):
        self.cfg = cfg
        self.nc = bass.Bass("TRN2", target_bir_lowering=False)
        self.es = ExitStack()
        self.P = Prog(self.nc, self.es)
        self._declare_io()
        self._alloc()
        self._sems()

    def _declare_io(self):
        nc, c = self.nc, self.cfg
        D, E, M = c.D, c.E, c.M
        dram = lambda n, sh, dt=F32: nc.dram_tensor(n, sh, dt, kind="ExternalInput").ap()
        self.dx = dram("x", [c.Tc, D])
        self.dout = nc.dram_tensor("out", [c.Tc, D], F32, kind="ExternalOutput").ap()
        self.dident_bf = dram("ident_bf", [128, 128], BF16)
        self.dcpool = dram("cpool", [128, 8])
        self.dones_bf = dram("ones_bf", [128, 1], BF16)
        self.dwgate = dram("L0_w_gate", [D, E])
        self.dwg = dram("L0_w_g", [E, D, M])
        self.dwu = dram("L0_w_u", [E, D, M])
        self.dwd = dram("L0_w_d", [E, M, D])
        self.da0_wqT = dram("A0_wqT", [D, 128])
        self.da0_wkT = dram("A0_wkT", [D, 128])
        self.da0_wvT = dram("A0_wvT", [D, 128])
        self.da0_woT = dram("A0_woT", [128, D])
        self.da0_bo = dram("A0_bo", [1, D])
        self.da0_vecs = dram("A0_vecs", [128, 5])
        self.dcc0_in = nc.dram_tensor("cc0_in", [384, c.Tc], BF16).ap()
        self.dcc0_out = nc.dram_tensor("cc0_out", [4 * 384, c.Tc], BF16).ap()
        self.da1_wqT = dram("A1_wqT", [D, D])
        self.da1_wkT = dram("A1_wkT", [D, D])
        self.da1_wvT = dram("A1_wvT", [D, D])
        self.da1_woT = dram("A1_woT", [D, D])
        self.da1_bqkv = dram("A1_bqkv", [128, 24])
        self.da1_bo = dram("A1_bo", [1, D])
        self.da1_rolesel = dram("A1_rolesel", [128, 10])
        self.dq1 = nc.dram_tensor("q1_spill", [128, 8 * c.Tc], BF16).ap()
        self.dv1 = nc.dram_tensor("v1_spill", [c.Tc, D], BF16).ap()
        self.dcc1_in = nc.dram_tensor("cc1_in", [128, 3072], BF16).ap()
        self.dcc1_out = nc.dram_tensor("cc1_out", [512, 3072], BF16).ap()
        self.dcc1r_in = nc.dram_tensor("cc1r_in", [130, 1024], F32).ap()
        self.dcc1r_out = nc.dram_tensor("cc1r_out", [130, 1024], F32).ap()
        self.dl1_wT = dram("L1f_wT", [D, D])
        self.dl1_bias = dram("L1f_bias", [1, D])
        self.dones_row_bf = dram("ones_row_bf", [1, 128], BF16)
        self.dones_row_f32 = dram("ones_row_f32", [1, 128])
        self.dmx_in = nc.dram_tensor("mx_in", [1, 1], F32).ap()
        self.dmx_out = nc.dram_tensor("mx_out", [1, 1], F32).ap()
        self.dqw = nc.dram_tensor("qw_spill", [128, 8 * D], BF16).ap()
        self.dws = nc.dram_tensor("ws_spill", [128, 1], F32).ap()
        if DEBUG_DUMP:
            self.ddbg = nc.dram_tensor("dbg", [128, 24576], BF16,
                                       kind="ExternalOutput").ap()

    def _alloc(self):
        nc, c = self.nc, self.cfg
        sb = lambda n, sh, dt=F32: self.es.enter_context(nc.sbuf_tensor(n, sh, dt))
        psa = lambda n, sh, dt=F32: self.es.enter_context(nc.psum_tensor(n, sh, dt))
        self.R = sb("R", [128, c.NCH * 1024])
        self.XT = sb("XT", [128, c.DC * c.Tc], BF16)
        self.ABF = sb("ABF", [128, 46592], BF16)
        self.AFP = sb("AFP", [128, 4096], F32)
        self.ident_bf = sb("identbf", [128, 128], BF16)
        self.cpool = sb("cpools", [128, 8], F32)
        self.ones_bf = sb("onesbf", [128, 1], BF16)
        self.ln_s6 = sb("ln_s6", [128, 12], F32)
        self.ln_mv = sb("ln_mv", [128, 2], F32)
        self.ln_rs = sb("ln_rs", [128, 1], F32)
        self.ln_b = sb("ln_b", [128, 1], F32)
        self.xstage = None
        self.Gh = sb("Gh", [128, c.NCH * 8], F32)
        self.gsc = sb("gsc", [128, 24], F32)
        self.geq1 = sb("geq1", [128, 8], F32)
        self.geq2 = sb("geq2", [128, 8], F32)
        self.wgate_bf = sb("wgatebf", [128, c.DC * 8], BF16)
        self.a1_bqkv = sb("a1bqkv", [128, 24], F32)
        self.a1_dl = sb("a1dl", [128, 8], F32)
        self.a1_dg = sb("a1dg", [128, 8], F32)
        self.a1_msk = sb("a1msk", [1, 256], BF16)
        self.a1_rsel = sb("a1rsel", [128, 10], F32)
        self.ones_row_bf = sb("onesrowbf", [1, 128], BF16)
        self.ones_row_f32 = sb("onesrowf32", [1, 128], F32)
        self.ibuf = sb("ibuf", [128, 512], mybir.dt.int32)
        self.l1v = sb("l1v", [128, 8], F32)
        self.l1s = sb("l1s", [1, 8], F32)
        self.a0v = sb("a0v", [128, 16], F32)
        self.P1 = psa("P1", [128, 1536])
        self.pg = [self.P1[:, 0:512]]
        self.pu = [self.P1[:, 512:1024]]
        self.po = [psa(f"po{i}", [128, 1024]) for i in range(2)]
        self.pt = psa("pt_bf", [128, 1024], BF16)
        self.xstage = self.ABF[:, 45568:46592]

        A = 0
        def carve(n):
            nonlocal A
            r = A; A += n; return r
        self.o_H = carve(c.MC * c.TH)
        self.o_wg = carve(c.MC * 128 * c.DC)
        self.o_wu = carve(c.MC * 128 * c.DC)
        self.o_wd = carve(c.MC * 1024)
        self.o_sg = carve(2 * c.QT)
        self.o_su = carve(c.QT)
        self.ffn_bf_end = A
        assert A <= 46592, A
        self.of_stage = 0

    def _sems(self):
        P = self.P
        self.s_in = P.sem("s_in")
        self.s_const = P.sem("s_const")
        self.s_stg = [P.sem("s_stg0"), P.sem("s_stg1")]
        self.s_stgu = P.sem("s_stgu")
        self.s_wcast = P.sem("s_wcast")
        self.s_io = P.sem("s_io")
        self.s_cc = P.sem("s_cc")
        self.stage_free = [None, None]
        self.s_sl = [P.sem(f"s_sl{i}") for i in range(4)]
        self.s_msk = P.sem("s_msk")
        self.s_tp = P.sem("s_tp")
        self.tp_free = None
        self.s_l1b = P.sem("s_l1b")
        self.s_cast = P.sem("s_cast")
        self.s_mm = P.sem("s_mm")
        self.s_ev = P.sem("s_ev")
        self.s_ve = P.sem("s_ve")
        self.s_ln = P.sem("s_ln")
        self.s_xt = P.sem("s_xt")
        self.s_g = P.sem("s_g")
        self.s_h = P.sem("s_h")
        self.s_r = P.sem("s_r")
        self.s_st = P.sem("s_st")

    def p_load(self):
        P, c = self.P, self.cfg
        for k in range(c.NCH):
            P.op("sync", lambda e, k=k: e.dma_start(
                out=self.R[:, 1024 * k:1024 * (k + 1)],
                in_=self.dx[128 * k:128 * (k + 1), :]), inc=self.s_in.bump(16))
        P.op("sync", lambda e: e.dma_start(out=self.ident_bf[:], in_=self.dident_bf),
             inc=self.s_const.bump(16))
        P.op("sync", lambda e: e.dma_start(out=self.cpool[:], in_=self.dcpool),
             inc=self.s_const.bump(16))
        P.op("sync", lambda e: e.dma_start(out=self.ones_bf[:], in_=self.dones_bf),
             inc=self.s_const.bump(16))
        P.op("sync", lambda e: e.dma_start(out=self.ones_row_bf[:], in_=self.dones_row_bf),
             inc=self.s_const.bump(16))
        P.op("sync", lambda e: e.dma_start(out=self.ones_row_f32[:], in_=self.dones_row_f32),
             inc=self.s_const.bump(16))
        self.consts_ready = self.s_const.now
        self.x_ready = self.s_in.now

    def emit_ln(self, r_waits):
        P, c = self.P, self.cfg
        for k in range(c.NCH):
            w = list(r_waits) + [self.consts_ready, self.s_mm.now]
            P.op("scalar", lambda e, k=k: e.activation(
                self.xstage[:], self.R[:, 1024 * k:1024 * (k + 1)], AF.Copy,
                accum_out=self.ln_s6[:, 0:1]),
                waits=w, inc=self.s_ln.bump())
            P.op("scalar", lambda e, k=k: e.activation(
                self.xstage[:], self.R[:, 1024 * k:1024 * (k + 1)], AF.Square,
                accum_out=self.ln_s6[:, 1:2]),
                waits=[self.s_ln.now], inc=self.s_ln.bump())
            P.op("vector", lambda e: e.tensor_scalar(
                self.ln_mv[:, 0:1], self.ln_s6[:, 0:1], 1.0 / 1024.0, None, ALU.mult),
                waits=[self.s_ln.now], inc=self.s_ln.bump())
            P.op("vector", lambda e: e.scalar_tensor_tensor(
                self.ln_s6[:, 2:3], self.ln_mv[:, 0:1], -1.0, self.ln_mv[:, 0:1],
                ALU.mult, ALU.mult),
                waits=[self.s_ln.now], inc=self.s_ln.bump())
            P.op("vector", lambda e: e.scalar_tensor_tensor(
                self.ln_mv[:, 1:2], self.ln_s6[:, 1:2], 1.0 / 1024.0, self.ln_s6[:, 2:3],
                ALU.mult, ALU.add),
                waits=[self.s_ln.now], inc=self.s_ln.bump())
            P.op("scalar", lambda e: e.activation(self.ln_rs[:], self.ln_mv[:, 1:2],
                                                  AF.Sqrt, bias=self.cpool[:, 1:2]),
                 waits=[self.s_ln.now], inc=self.s_ln.bump())
            P.op("vector", lambda e: e.reciprocal(self.ln_rs[:], self.ln_rs[:]),
                 waits=[self.s_ln.now], inc=self.s_ln.bump())
            P.op("vector", lambda e: e.scalar_tensor_tensor(
                self.ln_b[:], self.ln_mv[:, 0:1], -1.0, self.ln_rs[:], ALU.mult, ALU.mult),
                waits=[self.s_ln.now], inc=self.s_ln.bump())
            P.op("vector", lambda e, k=k: e.tensor_scalar(
                self.xstage[:], self.R[:, 1024 * k:1024 * (k + 1)],
                self.ln_rs[:], self.ln_b[:], ALU.mult, ALU.add),
                waits=[self.s_ln.now, self.s_mm.now], inc=self.s_ln.bump())
            vx = self.s_ln.n
            for j in range(c.DC):
                P.op("tensor", lambda e, j=j: e.transpose(
                    self.pt[:, 128 * j:128 * (j + 1)],
                    self.xstage[:, 128 * j:128 * (j + 1)], self.ident_bf[:]),
                    waits=[(self.s_ln, vx), self.s_xt.now], inc=self.s_mm.bump())
            vt = self.s_mm.n

            def evac_xt(e, k=k):
                dst = self.XT[:].rearrange("p (j t) -> p j t", t=c.Tc)[:, :, 128 * k:128 * (k + 1)]
                src = self.pt[:].rearrange("p (j t) -> p j t", t=128)
                return e.tensor_copy(dst, src)
            P.op("vector", evac_xt, waits=[(self.s_mm, vt)], inc=self.s_xt.bump())
        return self.s_xt.now

    def p_ffn0(self, r_dep):
        P, c = self.P, self.cfg
        A = self.ABF
        E, MC, DC, TH, QT = c.E, c.MC, c.DC, c.TH, c.QT
        NQ = TH // QT
        xt_ok = self.emit_ln([r_dep])

        P.op("sync", lambda e: e.dma_start(
            out=self.AFP[:, 0:8 * DC].rearrange("p (j e) -> p j e", e=8),
            in_=self.dwgate[:].rearrange("(j p) e -> p j e", p=128)),
             waits=[w for w in [self.stage_free[0]] if w], inc=self.s_stg[0].bump(16))
        P.op("vector", lambda e: e.tensor_copy(self.wgate_bf[:], self.AFP[:, 0:8 * DC]),
             waits=[self.s_stg[0].now, self.s_mm.now], inc=self.s_cast.bump())
        wgate_ok = self.s_cast.now
        self.stage_free[0] = self.s_cast.now
        sg = self.s_g
        gate_bank_free = [None, None]
        for k in range(c.NCH):
            bank = self.pg[0]
            bw = [xt_ok, wgate_ok]
            if gate_bank_free[0] is not None:
                bw.append(gate_bank_free[0])
            else:
                bw += [self.s_ev.now, self.s_ve.now]
            for j in range(DC):
                P.op("tensor", lambda e, k=k, j=j, bank=bank: e.matmul(
                    bank[:, 0:8],
                    self.XT[:, c.Tc * j + 128 * k:c.Tc * j + 128 * (k + 1)],
                    self.wgate_bf[:, 8 * j:8 * (j + 1)],
                    start=(j == 0), stop=(j == DC - 1)),
                    waits=(bw if j == 0 else []),
                    inc=(self.s_mm.bump() if j == DC - 1 else None))
            vmm = self.s_mm.n
            P.op("vector", lambda e, bank=bank: e.reduce_max(self.gsc[:, 0:1], bank[:, 0:8], axis=AXX),
                 waits=[(self.s_mm, vmm)], inc=sg.bump())
            P.op("vector", lambda e, bank=bank: e.tensor_scalar(self.geq1[:], bank[:, 0:8],
                                                                self.gsc[:, 0:1], None, ALU.is_ge),
                 waits=[sg.now], inc=sg.bump())
            P.op("vector", lambda e, bank=bank: e.scalar_tensor_tensor(
                self.gsc[:, 8:16], self.geq1[:], -1e30, bank[:, 0:8], ALU.mult, ALU.add),
                waits=[sg.now], inc=sg.bump())
            gate_bank_free[0] = sg.now
            P.op("vector", lambda e: e.reduce_max(self.gsc[:, 1:2], self.gsc[:, 8:16], axis=AXX),
                 waits=[sg.now], inc=sg.bump())
            P.op("vector", lambda e: e.tensor_scalar(self.geq2[:], self.gsc[:, 8:16],
                                                     self.gsc[:, 1:2], None, ALU.is_ge),
                 waits=[sg.now], inc=sg.bump())
            P.op("scalar", lambda e: e.activation(self.gsc[:, 2:3], self.gsc[:, 1:2],
                                                  AF.Sigmoid, bias=self.gsc[:, 0:1], scale=-1.0),
                 waits=[sg.now], inc=sg.bump())
            P.op("vector", lambda e: e.tensor_scalar(self.gsc[:, 3:4], self.gsc[:, 2:3],
                                                     -0.5, 0.5, ALU.mult, ALU.add),
                 waits=[sg.now], inc=sg.bump())
            P.op("vector", lambda e: e.tensor_scalar(self.gsc[:, 2:3], self.gsc[:, 2:3],
                                                     0.5, None, ALU.mult),
                 waits=[sg.now], inc=sg.bump())
            P.op("vector", lambda e: e.tensor_scalar(self.geq1[:], self.geq1[:],
                                                     self.gsc[:, 2:3], None, ALU.mult),
                 waits=[sg.now], inc=sg.bump())
            P.op("vector", lambda e, k=k: e.scalar_tensor_tensor(
                self.Gh[:, 8 * k:8 * (k + 1)], self.geq2[:], self.gsc[:, 3:4],
                self.geq1[:], ALU.mult, ALU.add),
                waits=[sg.now], inc=sg.bump())
        g_all = self.s_g.now

        o_H, o_wg, o_wu, o_wd = self.o_H, self.o_wg, self.o_wu, self.o_wd
        o_sg, o_su = self.o_sg, self.o_su
        stg = [self.AFP[:, 0:1408], self.AFP[:, 1408:2816]]
        wgwu_free = [self.s_mm.now, self.s_io.now]
        wd_free = [self.s_mm.now, self.s_io.now]
        po_free = [self.s_r.now, self.s_r.now]
        pgpu_free = [[g_all, self.s_ve.now]]
        sgsu_free = [self.s_ve.now, self.s_ve.now]

        wd_cast_ok = [self.s_cast.now]
        wgwu_cast_ok = [self.s_cast.now]
        h_write_guard = self.s_mm.now

        def dma_weight_dchunk(src3, e_, j_, st_i, q_eng):
            sem = self.s_stgu if q_eng == "gpsimd" else self.s_stg[st_i]
            P.op(q_eng, lambda e, e_=e_, j_=j_, st_i=st_i: e.dma_start(
                out=stg[st_i], in_=src3[e_, 128 * j_:128 * (j_ + 1), :]),
                waits=[w for w in [self.stage_free[st_i]] if w], inc=sem.bump(16))
            return sem.now

        def dma_wd_mchunk(e_, m_, st_i, q_eng):
            sem = self.s_stgu if q_eng == "gpsimd" else self.s_stg[st_i]
            P.op(q_eng, lambda e, e_=e_, m_=m_, st_i=st_i: e.dma_start(
                out=stg[st_i][:, 0:1024], in_=self.dwd[e_, 128 * m_:128 * (m_ + 1), :]),
                waits=[w for w in [self.stage_free[st_i]] if w], inc=sem.bump(16))
            return sem.now

        for ei in range(E):
            for j in range(DC):
                for (src3, dsto, st_i) in ((self.dwg, o_wg, 0), (self.dwu, o_wu, 1)):
                    wdma = dma_weight_dchunk(src3, ei, j, st_i, "sync")
                    def cast(e, dsto=dsto, j=j, si=st_i):
                        d3 = A[:, dsto:dsto + MC * 1024].rearrange("p (mc q) -> p mc q", q=1024)
                        dd = d3[:, :, 128 * j:128 * (j + 1)]
                        ss = stg[si][:].rearrange("p (mc q) -> p mc q", q=128)
                        return e.activation(dd, ss, AF.Copy)
                    P.op("scalar", cast, waits=[wdma] + wgwu_free, inc=self.s_wcast.bump())
                    self.stage_free[st_i] = self.s_wcast.now
            wgwu_cast_ok = [self.s_wcast.now]
            wd_cast_val = [None] * MC

            for half in range(2):
                t0 = half * TH
                for m in range(MC):
                    for q in range(NQ):
                        pb = 0
                        for j in range(DC):
                            P.op("tensor", lambda e, m=m, q=q, j=j, pb=pb, t0=t0: e.matmul(
                                self.pg[pb][:, 0:QT],
                                A[:, o_wg + 1024 * m + 128 * j:o_wg + 1024 * m + 128 * (j + 1)],
                                self.XT[:, c.Tc * j + t0 + QT * q:c.Tc * j + t0 + QT * (q + 1)],
                                start=(j == 0), stop=(j == DC - 1)),
                                waits=(wgwu_cast_ok + [xt_ok] + ([pgpu_free[pb][0]] if j == 0 else [])),
                                inc=(self.s_mm.bump() if j == DC - 1 else None))
                        vg = self.s_mm.n
                        for j in range(DC):
                            P.op("tensor", lambda e, m=m, q=q, j=j, pb=pb, t0=t0: e.matmul(
                                self.pu[pb][:, 0:QT],
                                A[:, o_wu + 1024 * m + 128 * j:o_wu + 1024 * m + 128 * (j + 1)],
                                self.XT[:, c.Tc * j + t0 + QT * q:c.Tc * j + t0 + QT * (q + 1)],
                                start=(j == 0), stop=(j == DC - 1)),
                                waits=([pgpu_free[pb][1]] if j == 0 else []),
                                inc=(self.s_mm.bump() if j == DC - 1 else None))
                        vu = self.s_mm.n
                        sbuf_i = (m * NQ + q) % 2
                        P.op("scalar", lambda e, pb=pb, sbuf_i=sbuf_i: e.activation(
                            A[:, o_sg + QT * sbuf_i:o_sg + QT * (sbuf_i + 1)],
                            self.pg[pb][:, 0:QT], AF.Silu),
                            waits=[(self.s_mm, vg), sgsu_free[sbuf_i]], inc=self.s_ev.bump())
                        pgpu_free[pb][0] = self.s_ev.now
                        vsilu = self.s_ev.n
                        P.op("vector", lambda e, pb=pb: e.tensor_copy(
                            A[:, o_su:o_su + QT], self.pu[pb][:, 0:QT]),
                            waits=[(self.s_mm, vu), self.s_h.now], inc=self.s_ve.bump())
                        pgpu_free[pb][1] = self.s_ve.now
                        vsu = self.s_ve.n
                        P.op("vector", lambda e, m=m, q=q, sbuf_i=sbuf_i, t0=t0: e.tensor_tensor(
                            A[:, o_H + TH * m + QT * q:o_H + TH * m + QT * (q + 1)],
                            A[:, o_sg + QT * sbuf_i:o_sg + QT * (sbuf_i + 1)],
                            A[:, o_su:o_su + QT], ALU.mult),
                            waits=[(self.s_ev, vsilu), (self.s_ve, vsu), h_write_guard],
                            inc=self.s_h.bump())
                        sgsu_free[sbuf_i] = self.s_h.now
                h_all = self.s_h.n
                if half == 1:
                    hid_end = self.s_mm.n
                if half == 0:
                    for m in range(MC):
                        st_i = m % 2
                        wdma = dma_wd_mchunk(ei, m, st_i, "sync")
                        def castd(e, m=m, si=st_i):
                            return e.tensor_copy(A[:, o_wd + 1024 * m:o_wd + 1024 * (m + 1)],
                                                 stg[si][:, 0:1024])
                        P.op("vector", castd, waits=[wdma] + wd_free, inc=self.s_cast.bump())
                        self.stage_free[st_i] = self.s_cast.now
                        wd_cast_val[m] = self.s_cast.n
                for kk in range(TH // 128):
                    k = half * (TH // 128) + kk
                    pb = kk % 2
                    for m in range(MC):
                        for hh in range(2):
                            P.op("tensor", lambda e, k=k, m=m, hh=hh, pb=pb, t0=t0: e.matmul(
                                self.po[pb][:, 512 * hh:512 * (hh + 1)],
                                A[:, o_H + TH * m + 128 * (k - (t0 // 128)):o_H + TH * m + 128 * (k - (t0 // 128) + 1)],
                                A[:, o_wd + 1024 * m + 512 * hh:o_wd + 1024 * m + 512 * (hh + 1)],
                                start=(m == 0), stop=(m == MC - 1)),
                                waits=([(self.s_h, h_all), (self.s_cast, wd_cast_val[m])] +
                                       ([po_free[pb]] if m == 0 and hh == 0 else [])),
                                inc=(self.s_mm.bump() if (m == MC - 1 and hh == 1) else None))
                    vwd = self.s_mm.n
                    P.op("vector", lambda e, k=k, ei=ei, pb=pb: e.scalar_tensor_tensor(
                        self.R[:, 1024 * k:1024 * (k + 1)], self.po[pb][:],
                        self.Gh[:, 8 * k + ei:8 * k + ei + 1],
                        self.R[:, 1024 * k:1024 * (k + 1)], ALU.mult, ALU.add),
                        waits=[(self.s_mm, vwd), g_all], inc=self.s_r.bump())
                    po_free[pb] = self.s_r.now
                h_write_guard = self.s_mm.now
            wgwu_free = [(self.s_mm, hid_end)]
            wd_free = [self.s_mm.now]
        return self.s_r.now

    def p_att0(self, r_dep):
        P, c = self.P, self.cfg
        A = self.ABF
        Tc, S, DC, NCH = c.Tc, c.S, c.DC, c.NCH
        NS = S // 128
        FQ = min(512, Tc)
        NFQ = Tc // FQ
        xt_ok = self.emit_ln([r_dep])
        guard_mm = self.s_mm.now
        guard_r = self.s_r.now
        guard_io = self.s_io.now

        a_cosQ, a_sinQ = 0, Tc
        a_cosK, a_sinK = 2 * Tc, 3 * Tc
        a_vtm = 4 * Tc
        a_kallc, a_kalls, a_vall = 5 * Tc, 5 * Tc + S, 5 * Tc + 2 * S
        a_PT = 5 * Tc + 3 * S
        a_OT = a_PT + 2 * 512
        a_vT = a_PT
        a_w = a_OT + Tc
        a_dbf = a_w + 4096
        a_bo0 = a_dbf + Tc
        assert a_bo0 + 1024 <= 46592
        self.a0_Dbf = A[0:1, a_dbf:a_dbf + Tc]
        self.a0_bobf = A[0:1, a_bo0:a_bo0 + 1024]
        self.a0_D = self.AFP[0:1, 0:Tc]
        self.a0_Dpp = self.AFP[:, 3968:3968 + NCH]

        stg = [self.AFP[:, 0:1408], self.AFP[:, 1408:2816]]
        wslots = []
        for i, dsrc in enumerate((self.da0_wqT, self.da0_wkT, self.da0_wvT, self.da0_woT)):
            si = i % 2
            w = [self.stage_free[si]] if self.stage_free[si] else []
            P.op("sync", lambda e, dsrc=dsrc, si=si: e.dma_start(
                out=stg[si][:, 0:1024].rearrange("p (j q) -> p j q", q=128),
                in_=dsrc[:].rearrange("(j p) q -> p j q", p=128)),
                waits=w + [guard_r], inc=self.s_stg[si].bump(16))
            wd = self.s_stg[si].now
            P.op("vector", lambda e, i=i, si=si: e.tensor_copy(
                A[:, a_w + 1024 * i:a_w + 1024 * (i + 1)], stg[si][:, 0:1024]),
                waits=[wd, guard_mm, guard_io], inc=self.s_cast.bump())
            self.stage_free[si] = self.s_cast.now
        w_ok = self.s_cast.now
        wq = lambda j: A[:, a_w + 128 * j:a_w + 128 * (j + 1)]
        wk = lambda j: A[:, a_w + 1024 + 128 * j:a_w + 1024 + 128 * (j + 1)]
        wv = lambda j: A[:, a_w + 2048 + 128 * j:a_w + 2048 + 128 * (j + 1)]
        woT = A[:, a_w + 3072:a_w + 4096]
        si = 0
        w = [self.stage_free[si]] if self.stage_free[si] else []
        P.op("sync", lambda e: e.dma_start(out=self.AFP[:, 0:5], in_=self.da0_vecs[:, 0:5]),
             waits=w, inc=self.s_stg[0].bump(16))
        vd = self.s_stg[0].now
        P.op("vector", lambda e: e.tensor_copy(self.a0v[:, 0:5], self.AFP[:, 0:5]),
             waits=[vd], inc=self.s_cast.bump())
        self.stage_free[0] = self.s_cast.now
        w = [self.stage_free[1]] if self.stage_free[1] else []
        P.op("sync", lambda e: e.dma_start(out=self.AFP[0:1, 1408:2432], in_=self.da0_bo),
             waits=w, inc=self.s_stg[1].bump(16))
        bod = self.s_stg[1].now
        P.op("vector", lambda e: e.tensor_copy(self.a0_bobf[:], self.AFP[0:1, 1408:2432]),
             waits=[bod], inc=self.s_cast.bump())
        self.stage_free[1] = self.s_cast.now
        sv = self.s_cast
        P.op("vector", lambda e: e.tensor_scalar(self.a0v[:, 5:6], self.a0v[:, 0:1],
                                                 1e-8, None, ALU.add),
             waits=[sv.now], inc=sv.bump())
        P.op("vector", lambda e: e.reciprocal(self.a0v[:, 5:6], self.a0v[:, 5:6]),
             waits=[sv.now], inc=sv.bump())
        P.op("vector", lambda e: e.tensor_scalar(self.a0v[:, 6:7], self.a0v[:, 5:6],
                                                 1.0 / TWO_PI, None, ALU.mult),
             waits=[sv.now], inc=sv.bump())
        P.op("vector", lambda e: e.scalar_tensor_tensor(self.a0v[:, 7:8], self.a0v[:, 2:3],
                                                        1.0, self.a0v[:, 5:6], ALU.mult, ALU.mult),
             waits=[sv.now], inc=sv.bump())
        P.op("vector", lambda e: e.tensor_tensor(self.a0v[:, 7:8], self.a0v[:, 7:8],
                                                 self.a0v[:, 1:2], ALU.add),
             waits=[sv.now], inc=sv.bump())
        P.op("vector", lambda e: e.tensor_scalar(self.a0v[:, 7:8], self.a0v[:, 7:8],
                                                 1.0 / TWO_PI, None, ALU.mult),
             waits=[sv.now], inc=sv.bump())
        P.op("vector", lambda e: e.scalar_tensor_tensor(self.a0v[:, 8:9], self.a0v[:, 3:4],
                                                        1.0, self.a0v[:, 5:6], ALU.mult, ALU.mult),
             waits=[sv.now], inc=sv.bump())
        P.op("vector", lambda e: e.tensor_tensor(self.a0v[:, 8:9], self.a0v[:, 8:9],
                                                 self.a0v[:, 1:2], ALU.add),
             waits=[sv.now], inc=sv.bump())
        P.op("vector", lambda e: e.tensor_scalar(self.a0v[:, 8:9], self.a0v[:, 8:9],
                                                 1.0 / TWO_PI, None, ALU.mult),
             waits=[sv.now], inc=sv.bump())
        vecs_ok = sv.now

        NT = Tc // 512 if Tc >= 512 else 1
        PW = min(512, Tc)
        pqk = [self.pg[0], self.pu[0]]
        pq_free = [None, None]
        ubuf = self.AFP[:, 2048:2048 + PW]
        vb = self.AFP[:, 2816:2816 + PW]
        rb = self.AFP[:, 3328:3328 + PW]

        def proj_one(wt, bias_col, kind):
            for tt in range(NT):
                pb = tt % 2
                bank = pqk[pb]
                for j in range(DC):
                    ww = [xt_ok, w_ok] if j == 0 else []
                    if j == 0 and pq_free[pb] is not None:
                        ww.append(pq_free[pb])
                    P.op("tensor", lambda e, wt=wt, j=j, tt=tt, bank=bank: e.matmul(
                        bank[:, 0:PW], wt(j),
                        self.XT[:, Tc * j + PW * tt:Tc * j + PW * (tt + 1)],
                        start=(j == 0), stop=(j == DC - 1)),
                        waits=ww, inc=(self.s_mm.bump() if j == DC - 1 else None))
                vmm = self.s_mm.n
                if kind != "v":
                    a_cos = a_cosQ if kind == "q" else a_cosK
                    a_sin = a_sinQ if kind == "q" else a_sinK
                    P.op("vector", lambda e, bank=bank, bc=bias_col: e.tensor_scalar(
                        ubuf, bank[:, 0:PW],
                        self.a0v[:, 6:7], self.a0v[:, bc:bc + 1], ALU.mult, ALU.add),
                        waits=[(self.s_mm, vmm), vecs_ok, self.s_cast.now,
                               self.s_ve.now, self.s_ev.now],
                        inc=self.s_ve.bump())
                    pq_free[pb] = self.s_ve.now
                    for (shift, dsto) in ((128.5, a_sin), (128.75, a_cos)):
                        P.op("vector", lambda e, shift=shift: e.tensor_scalar(
                            vb, ubuf, shift, None, ALU.add),
                            waits=[self.s_ve.now, self.s_ev.now], inc=self.s_ve.bump())
                        P.op("vector", lambda e: e.tensor_copy(self.ibuf[:, 0:PW], vb),
                             waits=[self.s_ve.now], inc=self.s_ve.bump())
                        P.op("vector", lambda e: e.tensor_copy(rb, self.ibuf[:, 0:PW]),
                             waits=[self.s_ve.now], inc=self.s_ve.bump())
                        P.op("vector", lambda e: e.tensor_sub(vb, vb, rb),
                             waits=[self.s_ve.now], inc=self.s_ve.bump())
                        P.op("vector", lambda e: e.tensor_single_scalar(
                            rb, vb, 0.0, ALU.is_lt),
                            waits=[self.s_ve.now], inc=self.s_ve.bump())
                        P.op("vector", lambda e: e.scalar_tensor_tensor(
                            vb, rb, -0.5, vb, ALU.add, ALU.add),
                            waits=[self.s_ve.now], inc=self.s_ve.bump())
                        P.op("scalar", lambda e, tt=tt, dsto=dsto: e.activation(
                            A[:, dsto + PW * tt:dsto + PW * (tt + 1)],
                            vb, AF.Sin, scale=TWO_PI),
                            waits=[self.s_ve.now], inc=self.s_ev.bump())
                else:
                    P.op("vector", lambda e, tt=tt, bank=bank: e.tensor_scalar(
                        A[:, a_vT + PW * tt:a_vT + PW * (tt + 1)], bank[:, 0:PW],
                        self.a0v[:, 4:5], None, ALU.add),
                        waits=[(self.s_mm, vmm), vecs_ok], inc=self.s_ve.bump())
                    pq_free[pb] = self.s_ve.now

        proj_one(wk, 8, "k")
        ktrig_ok = self.s_ev.now
        proj_one(wv, 4, "v")

        kdone = 0
        while kdone < NCH:
            grp = min(8, NCH - kdone)
            for i in range(grp):
                k = kdone + i
                P.op("tensor", lambda e, k=k, i=i: e.transpose(
                    self.pt[:, 128 * i:128 * (i + 1)],
                    A[:, a_vT + 128 * k:a_vT + 128 * (k + 1)], self.ident_bf[:]),
                    waits=([self.s_ve.now, self.s_xt.now] if i == 0 else []),
                    inc=(self.s_mm.bump() if i == grp - 1 else None))
            vt = self.s_mm.n
            P.op("vector", lambda e, kdone=kdone, grp=grp: e.tensor_copy(
                A[:, a_vtm + 128 * kdone:a_vtm + 128 * (kdone + grp)],
                self.pt[:, 0:128 * grp]),
                waits=[(self.s_mm, vt)], inc=self.s_xt.bump())
            kdone += grp
        vtm_ok = self.s_xt.now

        sio = self.s_io
        P.op("gpsimd", lambda e: e.dma_start(out=self.dcc0_in[0:128, :],
                                             in_=A[:, a_cosK:a_cosK + Tc]),
             waits=[ktrig_ok], inc=sio.bump(16))
        P.op("gpsimd", lambda e: e.dma_start(out=self.dcc0_in[128:256, :],
                                             in_=A[:, a_sinK:a_sinK + Tc]),
             inc=sio.bump(16))
        P.op("gpsimd", lambda e: e.dma_start(out=self.dcc0_in[256:384, :],
                                             in_=A[:, a_vtm:a_vtm + Tc]),
             waits=[vtm_ok], inc=sio.bump(16))
        P.op("gpsimd", lambda e: e.collective_compute(
            "AllGather", ALU.bypass, replica_groups=[[0, 1, 2, 3], [4, 5, 6, 7]],
            ins=[self.dcc0_in.opt()], outs=[self.dcc0_out.opt()]),
            waits=[sio.now], inc=self.s_cc.bump())
        cc_ok = self.s_cc.now

        proj_one(wq, 7, "q")
        trig_ok = self.s_ev.now
        for r in range(4):
            P.op("gpsimd", lambda e, r=r: e.dma_start(
                out=A[:, a_kallc + Tc * r:a_kallc + Tc * (r + 1)],
                in_=self.dcc0_out[384 * r:384 * r + 128, :]),
                waits=[cc_ok], inc=sio.bump(16))
            P.op("gpsimd", lambda e, r=r: e.dma_start(
                out=A[:, a_kalls + Tc * r:a_kalls + Tc * (r + 1)],
                in_=self.dcc0_out[384 * r + 128:384 * r + 256, :]),
                inc=sio.bump(16))
            P.op("gpsimd", lambda e, r=r: e.dma_start(
                out=A[:, a_vall + Tc * r:a_vall + Tc * (r + 1)],
                in_=self.dcc0_out[384 * r + 256:384 * r + 384, :]),
                inc=sio.bump(16))
        kv_ok = sio.now

        sbanks = [self.pg[0], self.pu[0]]
        sbank_free = [None, None]
        pt_free = [None, None]
        for qt in range(NFQ):
            pO = self.po[0][:, 512 * (qt % 2):512 * (qt % 2) + FQ]
            pD = self.po[1][0:1, 512 * (qt % 2):512 * (qt % 2) + FQ]
            for kc in range(NS):
                pb = kc % 2
                sb_ = sbanks[pb]
                ww = [kv_ok, trig_ok]
                if sbank_free[pb] is not None:
                    ww.append(sbank_free[pb])
                P.op("tensor", lambda e, kc=kc, qt=qt, sb_=sb_: e.matmul(
                    sb_[:, 0:FQ], A[:, a_kallc + 128 * kc:a_kallc + 128 * (kc + 1)],
                    A[:, a_cosQ + FQ * qt:a_cosQ + FQ * (qt + 1)],
                    start=True, stop=False), waits=ww)
                P.op("tensor", lambda e, kc=kc, qt=qt, sb_=sb_: e.matmul(
                    sb_[:, 0:FQ], A[:, a_kalls + 128 * kc:a_kalls + 128 * (kc + 1)],
                    A[:, a_sinQ + FQ * qt:a_sinQ + FQ * (qt + 1)],
                    start=False, stop=True), inc=self.s_mm.bump())
                vS = self.s_mm.n
                ptb = a_PT + 512 * (kc % 2)
                ww = [(self.s_mm, vS)]
                if pt_free[kc % 2] is not None:
                    ww.append(pt_free[kc % 2])
                P.op("scalar", lambda e, sb_=sb_, ptb=ptb: e.activation(
                    A[:, ptb:ptb + FQ], sb_[:, 0:FQ], AF.Exp,
                    bias=self.cpool[:, 2:3], scale=INV_SQRT_H0),
                    waits=ww + [self.consts_ready], inc=self.s_ev.bump())
                sbank_free[pb] = self.s_ev.now
                vP = self.s_ev.n
                P.op("tensor", lambda e, kc=kc, ptb=ptb, pO=pO: e.matmul(
                    pO, A[:, a_vall + 128 * kc:a_vall + 128 * (kc + 1)],
                    A[:, ptb:ptb + FQ], start=(kc == 0), stop=(kc == NS - 1)),
                    waits=[(self.s_ev, vP)])
                P.op("tensor", lambda e, kc=kc, ptb=ptb, pD=pD: e.matmul(
                    pD, self.ones_bf[:], A[:, ptb:ptb + FQ],
                    start=(kc == 0), stop=(kc == NS - 1)),
                    waits=([self.consts_ready] + ([self.s_ve.now] if kc == 0 else [])),
                    inc=self.s_mm.bump())
                pt_free[kc % 2] = self.s_mm.now
            vOD = self.s_mm.n
            P.op("scalar", lambda e, qt=qt, pO=pO: e.copy(
                A[:, a_OT + FQ * qt:a_OT + FQ * (qt + 1)], pO),
                waits=[(self.s_mm, vOD)], inc=self.s_ev.bump())
            P.op("vector", lambda e, qt=qt, pD=pD: e.tensor_copy(
                self.a0_D[0:1, FQ * qt:FQ * (qt + 1)], pD),
                waits=[(self.s_mm, vOD)], inc=self.s_ve.bump())
        ot_ok = self.s_ev.now
        d_ok = self.s_ve.now

        P.op("vector", lambda e: e.tensor_copy(self.a0_Dbf[:], self.a0_D[:]),
             waits=[d_ok], inc=self.s_ve.bump())
        P.op("vector", lambda e: e.reciprocal(self.a0_D[:], self.a0_D[:]),
             waits=[d_ok, self.s_ve.now], inc=self.s_ve.bump())
        drec_ok = self.s_ve.n
        for k in range(NCH):
            P.op("tensor", lambda e, k=k: e.matmul(
                self.po[1][:, k:k + 1], self.a0_D[0:1, 128 * k:128 * (k + 1)],
                self.cpool[0:1, 4:5], start=True, stop=True),
                waits=[(self.s_ve, drec_ok), self.consts_ready] if k == 0 else [],
                inc=(self.s_mm.bump() if k == NCH - 1 else None))
        vdt = self.s_mm.n
        P.op("vector", lambda e: e.tensor_copy(self.a0_Dpp[:], self.po[1][:, 0:NCH]),
             waits=[(self.s_mm, vdt)], inc=self.s_ve.bump())
        dpp_ok = self.s_ve.now

        po_free2 = [None, None]
        for k in range(NCH):
            pb = k % 2
            bank = self.po[pb]
            ww = [ot_ok, self.s_ve.now, self.s_cast.now]
            if po_free2[pb] is not None:
                ww.append(po_free2[pb])
            for hh in range(2):
                P.op("tensor", lambda e, k=k, hh=hh, bank=bank: e.matmul(
                    bank[:, 512 * hh:512 * (hh + 1)],
                    self.a0_Dbf[0:1, 128 * k:128 * (k + 1)],
                    self.a0_bobf[0:1, 512 * hh:512 * (hh + 1)],
                    start=True, stop=False), waits=(ww if hh == 0 else []))
                P.op("tensor", lambda e, k=k, hh=hh, bank=bank: e.matmul(
                    bank[:, 512 * hh:512 * (hh + 1)],
                    A[:, a_OT + 128 * k:a_OT + 128 * (k + 1)],
                    woT[:, 512 * hh:512 * (hh + 1)],
                    start=False, stop=True),
                    inc=(self.s_mm.bump() if hh == 1 else None))
            vmm = self.s_mm.n
            P.op("vector", lambda e, k=k, bank=bank: e.scalar_tensor_tensor(
                self.R[:, 1024 * k:1024 * (k + 1)], bank[:],
                self.a0_Dpp[:, k:k + 1], self.R[:, 1024 * k:1024 * (k + 1)],
                ALU.mult, ALU.add),
                waits=[(self.s_mm, vmm), dpp_ok], inc=self.s_r.bump())
            po_free2[pb] = self.s_r.now
        self.stage_free[0] = self.s_r.now
        self.stage_free[1] = self.s_r.now
        return self.s_r.now

    def p_att1(self, r_dep):
        P, c = self.P, self.cfg
        A = self.ABF
        Tc, DC, NCH, NP = c.Tc, c.DC, c.NCH, c.NPAIR
        NB = c.NBLK
        TCH = Tc + 128
        NT = max(1, Tc // 512)
        PW = min(512, Tc)
        SG = [[0, 1, 2, 3], [4, 5, 6, 7]]
        xt_ok = self.emit_ln([r_dep])
        guard_mm = self.s_mm.now
        guard_r = self.s_r.now
        guard_io = self.s_io.now
        guard_sp = self.s_stgu.now
        sv, se, sm = self.s_ve, self.s_ev, self.s_mm

        a_kh = 0
        a_vst = a_kh + 8 * TCH
        a_w = a_vst + 8192
        a_gq = a_w + 8192
        a_gk = a_gq + 1024
        a_gv = a_gk + 1024
        a_vedge = a_gv + 1024
        a_qp = a_vedge + 1024
        a_pl = a_qp + 1024
        a_pgl = a_pl + 1536
        a_pt = a_pgl + 1024
        a_ctxp = a_pt + 3072
        a_rcn = a_ctxp + 1024
        assert a_rcn + 1024 <= 46592, a_rcn + 1024
        a_rpt = a_pt
        a_qst = a_pt

        stg = [self.AFP[:, 0:1408], self.AFP[:, 1408:2816]]
        self.a1_ctxred = self.AFP[:, 2816:3840]
        self.a1_drow = self.AFP[0:1, 0:2048]
        wsb = lambda j: A[:, a_w + 1024 * j:a_w + 1024 * (j + 1)]

        si = 0
        w = [x for x in [self.stage_free[0]] if x]
        P.op("sync", lambda e: e.dma_start(out=self.AFP[:, 0:24], in_=self.da1_bqkv),
             waits=w, inc=self.s_stg[0].bump(16))
        P.op("vector", lambda e: e.tensor_copy(self.a1_bqkv[:], self.AFP[:, 0:24]),
             waits=[self.s_stg[0].now], inc=self.s_cast.bump())
        self.stage_free[0] = self.s_cast.now
        self.a1_bobf = A[0:1, a_gq:a_gq + 1024]
        P.op("sync", lambda e: e.dma_start(out=self.a1_rsel[:], in_=self.da1_rolesel),
             inc=self.s_msk.bump(16))
        small_ok = self.s_msk.now
        bias_ok = self.s_cast.now
        mrep_ok = small_ok

        def load_w(dsrc, ready_evt):
            for j in range(DC):
                sj = j % 2
                w = [x for x in [self.stage_free[sj]] if x]
                P.op("sync", lambda e, j=j, sj=sj, dsrc=dsrc: e.dma_start(
                    out=stg[sj][:, 0:1024], in_=dsrc[128 * j:128 * (j + 1), :]),
                    waits=w, inc=self.s_stg[sj].bump(16))
                wd = self.s_stg[sj].now
                P.op("vector", lambda e, j=j, sj=sj: e.tensor_copy(wsb(j), stg[sj][:, 0:1024]),
                     waits=[wd] + ready_evt, inc=self.s_cast.bump())
                self.stage_free[sj] = self.s_cast.now
            return self.s_cast.now

        pq_free = [None, None]
        pqk = [self.pg[0], self.pu[0]]

        def proj_tiles(cast_ok, bias_col, emit_out):
            for oc in range(DC):
                for tt in range(NT):
                    pb = (oc * NT + tt) % 2
                    bank = pqk[pb]
                    for j in range(DC):
                        ww = [xt_ok, cast_ok] if j == 0 else []
                        if j == 0 and pq_free[pb] is not None:
                            ww.append(pq_free[pb])
                        P.op("tensor", lambda e, oc=oc, tt=tt, j=j, bank=bank: e.matmul(
                            bank[:, 0:PW], wsb(j)[:, 128 * oc:128 * (oc + 1)],
                            self.XT[:, Tc * j + PW * tt:Tc * j + PW * (tt + 1)],
                            start=(j == 0), stop=(j == DC - 1)),
                            waits=ww, inc=(sm.bump() if j == DC - 1 else None))
                    emit_out(oc, tt, bank, sm.n)

        kcast = load_w(self.da1_wkT, [guard_mm, guard_io])
        def k_out(oc, tt, bank, vmm):
            P.op("vector", lambda e, oc=oc, tt=tt, bank=bank: e.tensor_scalar(
                A[:, a_kh + TCH * oc + PW * tt:a_kh + TCH * oc + PW * (tt + 1)],
                bank[:, 0:PW], self.a1_bqkv[:, 8 + oc:9 + oc], None, ALU.add),
                waits=[(sm, vmm), bias_ok, guard_io, guard_sp], inc=sv.bump())
            pq_free[(oc * NT + tt) % 2] = sv.now
        proj_tiles(kcast, 8, k_out)
        k_ok = sv.now

        vcast = load_w(self.da1_wvT, [self.s_mm.now])
        vT = lambda oc: A[:, a_vst + PW * oc:a_vst + PW * (oc + 1)]
        for tt in range(NT):
            for oc in range(DC):
                pb = (tt * DC + oc) % 2
                bank = pqk[pb]
                for j in range(DC):
                    ww = [xt_ok, vcast] if j == 0 else []
                    if j == 0 and pq_free[pb] is not None:
                        ww.append(pq_free[pb])
                    P.op("tensor", lambda e, oc=oc, tt=tt, j=j, bank=bank: e.matmul(
                        bank[:, 0:PW], wsb(j)[:, 128 * oc:128 * (oc + 1)],
                        self.XT[:, Tc * j + PW * tt:Tc * j + PW * (tt + 1)],
                        start=(j == 0), stop=(j == DC - 1)),
                        waits=ww, inc=(sm.bump() if j == DC - 1 else None))
                vmm = sm.n
                P.op("vector", lambda e, oc=oc, bank=bank: e.tensor_scalar(
                    vT(oc), bank[:, 0:PW], self.a1_bqkv[:, 16 + oc:17 + oc], None, ALU.add),
                    waits=[(sm, vmm), bias_ok, sm.now, guard_io], inc=sv.bump())
                pq_free[pb] = sv.now
            vT_ok = sv.n
            for i in range(PW // 128):
                for oc in range(DC):
                    P.op("tensor", lambda e, i=i, oc=oc: e.transpose(
                        self.pt[:, 128 * oc:128 * (oc + 1)],
                        vT(oc)[:, 128 * i:128 * (i + 1)], self.ident_bf[:]),
                        waits=([(sv, vT_ok), self.s_xt.now] if oc == 0 else []),
                        inc=(sm.bump() if oc == DC - 1 else None))
                vt2 = sm.n
                P.op("vector", lambda e: e.tensor_copy(A[:, a_qst:a_qst + 1024], self.pt[:]),
                     waits=[(sm, vt2), self.s_io.now], inc=self.s_xt.bump())
                kchunk = tt * (PW // 128) + i
                P.op("gpsimd", lambda e, kchunk=kchunk: e.dma_start(
                    out=self.dv1[128 * kchunk:128 * (kchunk + 1), :],
                    in_=A[:, a_qst:a_qst + 1024]),
                    waits=[self.s_xt.now], inc=self.s_io.bump(16))
        vspill_ok = self.s_io.now

        qcast = load_w(self.da1_wqT, [self.s_mm.now])
        def q_out(oc, tt, bank, vmm):
            P.op("vector", lambda e, oc=oc, tt=tt, bank=bank: e.tensor_scalar(
                A[:, a_qst + 1024:a_qst + 1024 + PW], bank[:, 0:PW],
                self.a1_bqkv[:, 0 + oc:1 + oc], None, ALU.add),
                waits=[(sm, vmm), bias_ok, self.s_io.now], inc=sv.bump())
            pq_free[(oc * NT + tt) % 2] = sv.now
            P.op("gpsimd", lambda e, oc=oc, tt=tt: e.dma_start(
                out=self.dq1[:, Tc * oc + PW * tt:Tc * oc + PW * (tt + 1)],
                in_=A[:, a_qst + 1024:a_qst + 1024 + PW]),
                waits=[sv.now], inc=self.s_io.bump(16))
        proj_tiles(qcast, 0, q_out)
        qspill_ok = self.s_io.now
        if ATT1_STOP == "proj":
            return self.s_r.now

        kh3 = lambda off: A[:, a_kh + off:a_kh + off + 8 * TCH].rearrange(
            "p (j t) -> p j t", t=TCH)[:, :, 0:64]
        P.op("gpsimd", lambda e: e.dma_start(
            out=self.dcc1_in[:, 0:512].rearrange("p (j t) -> p j t", t=64),
            in_=A[:, a_kh:a_kh + 8 * TCH].rearrange("p (j t) -> p j t", t=TCH)[:, :, 0:64]),
            waits=[k_ok, vspill_ok, qspill_ok], inc=self.s_io.bump(16))
        P.op("gpsimd", lambda e: e.dma_start(
            out=self.dcc1_in[:, 512:1024].rearrange("p (j t) -> p j t", t=64),
            in_=A[:, a_kh:a_kh + 8 * TCH].rearrange("p (j t) -> p j t", t=TCH)[:, :, Tc - 64:Tc]),
            inc=self.s_io.bump(16))
        P.op("gpsimd", lambda e: e.dma_start(
            out=self.dcc1_in[:, 1024:1536].rearrange("p (j t) -> p j t", t=64),
            in_=self.dq1[:].rearrange("p (j t) -> p j t", t=Tc)[:, :, 0:64]),
            waits=[qspill_ok], inc=self.s_io.bump(16))
        P.op("gpsimd", lambda e: e.dma_start(
            out=self.dcc1_in[:, 1536:2048].rearrange("p (j t) -> p j t", t=64),
            in_=self.dq1[:].rearrange("p (j t) -> p j t", t=Tc)[:, :, Tc - 64:Tc]),
            inc=self.s_io.bump(16))
        P.op("gpsimd", lambda e: e.dma_start(
            out=self.dcc1_in[0:64, 2048:3072], in_=self.dv1[0:64, :]),
            waits=[vspill_ok], inc=self.s_io.bump(16))
        P.op("gpsimd", lambda e: e.dma_start(
            out=self.dcc1_in[64:128, 2048:3072], in_=self.dv1[Tc - 64:Tc, :]),
            inc=self.s_io.bump(16))
        P.op("gpsimd", lambda e: e.collective_compute(
            "AllGather", ALU.bypass, replica_groups=SG,
            ins=[self.dcc1_in.opt()], outs=[self.dcc1_out.opt()]),
            waits=[self.s_io.now], inc=self.s_cc.bump())
        ag_ok = self.s_cc.now

        P.op("gpsimd", lambda e: e.dma_start(
            out=A[:, a_gq:a_gq + 1024].rearrange("p (j t) -> p j t", t=128)[:, :, 0:64],
            in_=self.dcc1_out[0:128, 1024:1536].rearrange("p (j t) -> p j t", t=64)),
            waits=[ag_ok], inc=self.s_io.bump(16))
        P.op("gpsimd", lambda e: e.dma_start(
            out=A[:, a_gq:a_gq + 1024].rearrange("p (j t) -> p j t", t=128)[:, :, 64:128],
            in_=self.dcc1_out[128:256, 1536:2048].rearrange("p (j t) -> p j t", t=64)),
            inc=self.s_io.bump(16))
        P.op("gpsimd", lambda e: e.dma_start(
            out=A[:, a_gk:a_gk + 1024].rearrange("p (j t) -> p j t", t=128)[:, :, 0:64],
            in_=self.dcc1_out[0:128, 0:512].rearrange("p (j t) -> p j t", t=64)),
            inc=self.s_io.bump(16))
        P.op("gpsimd", lambda e: e.dma_start(
            out=A[:, a_gk:a_gk + 1024].rearrange("p (j t) -> p j t", t=128)[:, :, 64:128],
            in_=self.dcc1_out[128:256, 512:1024].rearrange("p (j t) -> p j t", t=64)),
            inc=self.s_io.bump(16))
        P.op("gpsimd", lambda e: e.dma_start(
            out=A[0:64, a_gv:a_gv + 1024], in_=self.dcc1_out[0:64, 2048:3072]),
            inc=self.s_io.bump(16))
        P.op("gpsimd", lambda e: e.dma_start(
            out=A[64:128, a_gv:a_gv + 1024], in_=self.dcc1_out[128 + 64:256, 2048:3072]),
            inc=self.s_io.bump(16))
        gfix_ok = self.s_io.now
        if ATT1_STOP == "ag":
            return self.s_r.now

        scr = A[:, a_pt:a_pt + 512]
        scr2 = A[:, a_pt + 512:a_pt + 1024]
        kh_slot = lambda off: A[:, a_kh:a_kh + 8 * TCH].rearrange(
            "p (j t) -> p j t", t=TCH)[:, :, Tc + off:Tc + off + 64]
        def masked_fill(dst3, piece_col, sel_base, name):
            P.op("vector", lambda e, dst3=dst3: e.memset(dst3, 0.0),
                 waits=[k_ok, sv.now], inc=sv.bump())
            for r in range(4):
                P.op("gpsimd", lambda e, r=r, piece_col=piece_col: e.dma_start(
                    out=scr.rearrange("p (j t) -> p j t", t=64),
                    in_=self.dcc1_out[128 * r:128 * (r + 1),
                                      piece_col:piece_col + 512].rearrange(
                                          "p (j t) -> p j t", t=64)),
                    waits=[ag_ok, sv.now, self.s_io.now], inc=self.s_io.bump(16))
                P.op("vector", lambda e, r=r, sel_base=sel_base: e.tensor_scalar(
                    scr2, scr, self.a1_rsel[:, sel_base + r:sel_base + r + 1],
                    None, ALU.mult),
                    waits=[self.s_io.now, small_ok], inc=sv.bump())
                P.op("vector", lambda e, dst3=dst3: e.tensor_tensor(
                    dst3, dst3, scr2.rearrange("p (j t) -> p j t", t=64), ALU.add),
                    waits=[sv.now], inc=sv.bump())
        masked_fill(kh_slot(0), 512, 0, "kprev")
        masked_fill(kh_slot(64), 0, 4, "knext")
        def masked_fill_v(p0, rowr, sel_base):
            dst = A[p0:p0 + 64, a_vedge:a_vedge + 1024]
            P.op("vector", lambda e, dst=dst: e.memset(dst, 0.0),
                 waits=[sv.now], inc=sv.bump())
            for r in range(4):
                P.op("gpsimd", lambda e, r=r, rowr=rowr, p0=p0: e.dma_start(
                    out=scr[p0:p0 + 64, 0:512], in_=self.dcc1_out[
                        128 * r + rowr:128 * r + rowr + 64, 2048:2560]),
                    waits=[ag_ok, sv.now, self.s_io.now], inc=self.s_io.bump(16))
                P.op("gpsimd", lambda e, r=r, rowr=rowr, p0=p0: e.dma_start(
                    out=scr2[p0:p0 + 64, 0:512], in_=self.dcc1_out[
                        128 * r + rowr:128 * r + rowr + 64, 2560:3072]),
                    inc=self.s_io.bump(16))
                P.op("vector", lambda e, r=r, sel_base=sel_base, dst=dst, p0=p0: e.scalar_tensor_tensor(
                    dst[:, 0:512], scr[p0:p0 + 64, 0:512],
                    self.a1_rsel[p0:p0 + 64, sel_base + r:sel_base + r + 1],
                    dst[:, 0:512], ALU.mult, ALU.add),
                    waits=[self.s_io.now, small_ok], inc=sv.bump())
                P.op("vector", lambda e, r=r, sel_base=sel_base, dst=dst, p0=p0: e.scalar_tensor_tensor(
                    dst[:, 512:1024], scr2[p0:p0 + 64, 0:512],
                    self.a1_rsel[p0:p0 + 64, sel_base + r:sel_base + r + 1],
                    dst[:, 512:1024], ALU.mult, ALU.add),
                    waits=[sv.now], inc=sv.bump())
        masked_fill_v(64, 64, 0)
        masked_fill_v(0, 0, 4)
        halo_ok = sv.now

        wocast = load_w(self.da1_woT, [self.s_mm.now])

        P.op("vector", lambda e: e.memset(self.a1_drow[:], 0.0),
             waits=[sv.now, self.s_cast.now], inc=sv.bump())
        vgrp = [A[:, a_vst:a_vst + 4096], A[:, a_vst + 4096:a_vst + 8192]]
        vgrp_free = [None, None]
        rpt_free = [None, None]
        pS_free = [None, None]
        NKG = max(1, NCH // 4)
        KGW = min(4, NCH)
        row_ctx_started = False
        for kcg in range(NKG):
            gi = kcg % 2
            ww = [vspill_ok, self.s_cast.now]
            if vgrp_free[gi] is not None:
                ww.append(vgrp_free[gi])
            for i in range(KGW):
                P.op("sync", lambda e, kcg=kcg, i=i, gi=gi: e.dma_start(
                    out=vgrp[gi][:, 1024 * i:1024 * (i + 1)],
                    in_=self.dv1[128 * (KGW * kcg + i):128 * (KGW * kcg + i + 1), :]),
                    waits=(ww if i == 0 else []), inc=self.s_sl[gi].bump(16))
            vg_ok = self.s_sl[gi].now
            for h in range(16):
                hp, hc = h % 2, h // 2
                pb = (kcg * 16 + h) % 2
                bank = pqk[pb]
                for i in range(KGW):
                    ww = [gfix_ok, vg_ok, k_ok]
                    if i == 0 and pS_free[pb] is not None:
                        ww.append(pS_free[pb])
                    P.op("tensor", lambda e, kcg=kcg, h=h, i=i, hp=hp, hc=hc, bank=bank: e.matmul(
                        bank[:, 128 * i:128 * (i + 1)],
                        A[64 * hp:64 * hp + 64,
                          a_kh + TCH * hc + 128 * (KGW * kcg + i):
                          a_kh + TCH * hc + 128 * (KGW * kcg + i + 1)],
                        A[64 * hp:64 * hp + 64, a_gq + 128 * hc:a_gq + 128 * (hc + 1)],
                        start=True, stop=True),
                        waits=(ww if i == 0 else []),
                        inc=(sm.bump() if i == KGW - 1 else None))
                vS = sm.n
                rp = (kcg * 16 + h) % 2
                ww2 = [(sm, vS), self.consts_ready]
                if rpt_free[rp] is not None:
                    ww2.append(rpt_free[rp])
                P.op("scalar", lambda e, rp=rp, bank=bank: e.activation(
                    A[:, a_rpt + 512 * rp:a_rpt + 512 * rp + 128 * KGW],
                    bank[:, 0:128 * KGW], AF.Exp,
                    bias=self.cpool[:, 3:4], scale=L1_SCALE),
                    waits=ww2, inc=se.bump())
                pS_free[pb] = se.now
                vP = se.n
                for i in range(KGW):
                    P.op("tensor", lambda e, kcg=kcg, h=h, i=i, hp=hp, hc=hc, rp=rp, gi=gi: e.matmul(
                        self.po[0][64 * hp:64 * hp + 64, 128 * hc:128 * (hc + 1)],
                        vgrp[gi][:, 1024 * i + 64 * h:1024 * i + 64 * h + 64],
                        A[:, a_rpt + 512 * rp + 128 * i:a_rpt + 512 * rp + 128 * (i + 1)],
                        start=(kcg == 0 and i == 0), stop=(kcg == NKG - 1 and i == KGW - 1)),
                        waits=[(se, vP)])
                    P.op("tensor", lambda e, i=i, rp=rp: e.matmul(
                        self.P1[0:1, 1024 + 128 * 0:1024 + 128 * 1],
                        self.ones_bf[:],
                        A[:, a_rpt + 512 * rp + 128 * i:a_rpt + 512 * rp + 128 * (i + 1)],
                        start=(i == 0), stop=(i == KGW - 1)),
                        waits=[self.consts_ready] + ([sv.now] if i == 0 else []),
                        inc=(sm.bump() if i == KGW - 1 else None))
                vD = sm.n
                rpt_free[rp] = sm.now
                vgrp_free[gi] = sm.now
                P.op("vector", lambda e, h=h: e.tensor_tensor(
                    self.a1_drow[0:1, 128 * h:128 * (h + 1)],
                    self.a1_drow[0:1, 128 * h:128 * (h + 1)],
                    self.P1[0:1, 1024:1152], ALU.add),
                    waits=[(sm, vD)], inc=sv.bump())
        vrowmm = sm.n
        drow_ok = sv.now
        if ATT1_STOP == "row":
            return self.s_r.now
        P.op("vector", lambda e: e.tensor_copy(self.a1_ctxred[:], self.po[0][:]),
             waits=[(sm, vrowmm)], inc=sv.bump())
        ctxp_ok = sv.now
        P.op("gpsimd", lambda e: e.dma_start(out=self.dcc1r_in[0:128, :],
                                             in_=self.a1_ctxred[:]),
             waits=[ctxp_ok, self.s_io.now], inc=self.s_io.bump(16))
        P.op("gpsimd", lambda e: e.dma_start(
            out=self.dcc1r_in[128:130, :],
            in_=self.a1_drow[0:1, :].rearrange("p (a b) -> p a b", b=1024)),
            waits=[drow_ok], inc=self.s_io.bump(16))
        P.op("gpsimd", lambda e: e.collective_compute(
            "AllReduce", ALU.add, replica_groups=SG,
            ins=[self.dcc1r_in.opt()], outs=[self.dcc1r_out.opt()]),
            waits=[self.s_io.now], inc=self.s_cc.bump())
        rr_ok = self.s_cc.now
        P.op("gpsimd", lambda e: e.dma_start(out=self.a1_ctxred[:],
                                             in_=self.dcc1r_out[0:128, :]),
             waits=[rr_ok], inc=self.s_io.bump(16))
        P.op("gpsimd", lambda e: e.dma_start(
            out=self.a1_drow[0:1, :].rearrange("p (a b) -> p a b", b=1024),
            in_=self.dcc1r_out[128:130, :]),
            inc=self.s_io.bump(16))
        rred_ok = self.s_io.now
        P.op("gpsimd", lambda e: e.dma_start(out=self.a1_bobf, in_=self.da1_bo),
             waits=[(sm, vrowmm)], inc=self.s_stgu.bump(16))
        bias_ok = self.s_stgu.now

        def emit_normalize(po0_guard):
            P.op("vector", lambda e: e.reciprocal(self.a1_drow[:], self.a1_drow[:]),
                 waits=[rred_ok, sv.now], inc=sv.bump())
            for j in range(DC):
                for par in range(2):
                    h = 2 * j + par
                    P.op("tensor", lambda e, h=h: e.matmul(
                        self.po[0][:, 0:128], self.ones_row_f32[:],
                        self.a1_drow[0:1, 128 * h:128 * (h + 1)], start=True, stop=True),
                        waits=([sv.now, self.consts_ready, ctxp_ok, po0_guard]
                               if (j == 0 and par == 0) else [sv.now]),
                        inc=sm.bump())
                    P.op("vector", lambda e, j=j, par=par: e.tensor_tensor(
                        A[64 * par:64 * par + 64, a_rcn + 128 * j:a_rcn + 128 * (j + 1)],
                        self.a1_ctxred[64 * par:64 * par + 64, 128 * j:128 * (j + 1)],
                        self.po[0][64 * par:64 * par + 64, 0:128], ALU.mult),
                        waits=[(sm, sm.n)], inc=sv.bump())
            return sv.now

        row_done_se = se.now
        row_done_sv = sv.now
        row_ctxevac = ctxp_ok

        qp_free = [None, None]
        vw_free = [(sm, vrowmm), (sm, vrowmm)]
        ctx_free = None
        sg_free = None
        sl_free = None
        pt_free = None
        po1_r_free = None
        NPrun = NP if not (isinstance(ATT1_STOP, str) and ATT1_STOP.startswith("p")
                           and ATT1_STOP[1:].isdigit()) else int(ATT1_STOP[1:])
        row_sl0 = self.s_sl[0].now
        pair_order = (list(range(1, NP - 1)) + [0, NP - 1]) if NP > 2 else list(range(NP))
        rcn_ok = None
        for p_idx, p in enumerate(pair_order[:NPrun]):
            pi = p_idx % 2
            if p in (0, NP - 1) and rcn_ok is None:
                rcn_ok = emit_normalize(sg_free if sg_free is not None else ctxp_ok)
                sg_free = rcn_ok
            ww = [qspill_ok]
            ww.append(qp_free[0] if qp_free[0] is not None else row_sl0)
            P.op("sync", lambda e, p=p: e.dma_start(
                out=A[:, a_qp:a_qp + 1024].rearrange("p (j t) -> p j t", t=128),
                in_=self.dq1[:].rearrange("p (j t) -> p j t", t=Tc)[:, :, 128 * p:128 * (p + 1)]),
                waits=ww, inc=self.s_sl[0].bump(16))
            qp_ok = self.s_sl[0].now
            qp_ap = A[:, a_qp:a_qp + 1024]
            vwb = a_vst + 4096 * pi
            ww = [vspill_ok, halo_ok]
            if vw_free[pi] is not None:
                ww.append(vw_free[pi])
            def vdma(dst_lo, dst_n, src_row, first):
                P.op("sync", lambda e, dst_lo=dst_lo, dst_n=dst_n, src_row=src_row: e.dma_start(
                    out=A[dst_lo[0]:dst_lo[0] + dst_n, dst_lo[1]:dst_lo[1] + 1024],
                    in_=self.dv1[src_row:src_row + dst_n, :]),
                    waits=(ww if first else []), inc=self.s_sl[2 + pi].bump(16))
            first = True
            if p == 0:
                P.op("sync", lambda e, vwb=vwb: e.dma_start(
                    out=A[0:64, vwb:vwb + 1024], in_=A[64:128, a_vedge:a_vedge + 1024]),
                    waits=ww, inc=self.s_sl[2 + pi].bump(16))
                first = False
                vdma((64, vwb), 64, 0, first)
            else:
                vdma((0, vwb), 128, 128 * p - 64, first)
                first = False
            vdma((0, vwb + 1024), 64, 128 * p + 64, False)
            vdma((0, vwb + 2048), 128, 128 * p, False)
            if p != NP - 1:
                vdma((0, vwb + 3072), 64, 128 * p + 128, False)
            vw_ok = self.s_sl[2 + pi].now
            tA0 = A[:, vwb:vwb + 1024]
            tA1 = A[0:64, vwb + 1024:vwb + 2048]
            tB0 = A[:, vwb + 2048:vwb + 3072]
            tB1 = (A[0:64, a_vedge:a_vedge + 1024] if p == NP - 1
                   else A[0:64, vwb + 3072:vwb + 3072 + 1024])
            for hg in range(2):
                swl = [qp_ok, k_ok, halo_ok]
                if sl_free is not None:
                    swl.append(sl_free)
                else:
                    swl += [row_done_se, row_done_sv]
                first_sl = True
                for m in range(DC):
                    for bp in range(2):
                        b = 2 * p + bp
                        lhs = qp_ap[64 * hg:64 * hg + 64, 128 * m + 64 * bp:128 * m + 64 * bp + 64]
                        if b == 0:
                            pieces = [(Tc, 64, 0), (0, 64, 64), (64, 64, 128)]
                        elif b == NB - 1:
                            pieces = [(64 * (b - 1), 128, 0), (Tc + 64, 64, 128)]
                        else:
                            pieces = [(64 * (b - 1), 128, 0), (64 * (b - 1) + 128, 64, 128)]
                        for (src, n, dsto) in pieces:
                            if dsto < 128:
                                outap = self.P1[64 * bp:64 * bp + 64,
                                                128 * m + dsto:128 * m + dsto + n]
                            else:
                                outap = self.P1[64 * bp:64 * bp + 64,
                                                1024 + 64 * m:1024 + 64 * m + 64]
                            P.op("tensor", lambda e, src=src, n=n, lhs=lhs, hg=hg,
                                 m=m, outap=outap: e.matmul(
                                     outap, lhs,
                                     A[64 * hg:64 * hg + 64,
                                       a_kh + TCH * m + src:a_kh + TCH * m + src + n],
                                     start=True, stop=True),
                                 waits=(swl if first_sl else []))
                            first_sl = False
                        P.op("tensor", lambda e, m=m, bp=bp, lhs=lhs, hg=hg: e.matmul(
                            self.po[0][64 * bp:64 * bp + 64, 128 * m:128 * (m + 1)],
                            lhs,
                            A[64 * hg:64 * hg + 64, a_gk + 128 * m:a_gk + 128 * (m + 1)],
                            start=True, stop=True),
                            waits=(([gfix_ok, sg_free] if sg_free else
                                    [gfix_ok, ctxp_ok])
                                   if (m == 0 and bp == 0) else []),
                            inc=(sm.bump() if (m == DC - 1 and bp == 1) else None))
                vSg = sm.n
                vSl = sm.n
                if PAIR_STAGE == "sl":
                    sl_free = (sm, vSl); sg_free = (sm, vSg); continue
                ww = [(sm, vSl), self.consts_ready]
                if pt_free is not None:
                    ww.append(pt_free)
                P.op("scalar", lambda e: e.activation(
                    A[:, a_pl:a_pl + 1536], self.P1[:, 0:1536], AF.Exp,
                    bias=self.cpool[:, 3:4], scale=L1_SCALE),
                    waits=ww, inc=se.bump())
                sl_free = se.now
                P.op("scalar", lambda e: e.activation(
                    A[:, a_pgl:a_pgl + 1024], self.po[0][:], AF.Exp,
                    bias=self.cpool[:, 3:4], scale=L1_SCALE),
                    waits=[(sm, vSg)], inc=se.bump())
                sg_free = se.now
                vexp = se.n
                if PAIR_STAGE == "exp":
                    continue
                P.op("vector", lambda e: e.tensor_reduce(
                    self.a1_dl[:], A[:, a_pl:a_pl + 1024].rearrange("p (m t) -> p m t", t=128),
                    AXX, ALU.add),
                    waits=[(se, vexp)], inc=sv.bump())
                P.op("vector", lambda e: e.tensor_reduce(
                    self.gsc[:, 0:8],
                    A[:, a_pl + 1024:a_pl + 1536].rearrange("p (m t) -> p m t", t=64),
                    AXX, ALU.add),
                    waits=[(se, vexp)], inc=sv.bump())
                P.op("vector", lambda e: e.tensor_add(self.a1_dl[:], self.a1_dl[:],
                                                      self.gsc[:, 0:8]),
                     waits=[sv.now], inc=sv.bump())
                P.op("vector", lambda e: e.tensor_reduce(
                    self.a1_dg[:], A[:, a_pgl:a_pgl + 1024].rearrange("p (m t) -> p m t", t=128),
                    AXX, ALU.add),
                    waits=[(se, vexp)], inc=sv.bump())
                P.op("vector", lambda e: e.reciprocal(self.a1_dl[:], self.a1_dl[:]),
                     waits=[sv.now], inc=sv.bump())
                P.op("vector", lambda e: e.reciprocal(self.a1_dg[:], self.a1_dg[:]),
                     waits=[sv.now], inc=sv.bump())
                vrec = sv.n
                for m in range(DC):
                    P.op("vector", lambda e, m=m: e.tensor_scalar(
                        A[:, a_pl + 128 * m:a_pl + 128 * (m + 1)],
                        A[:, a_pl + 128 * m:a_pl + 128 * (m + 1)],
                        self.a1_dl[:, m:m + 1], None, ALU.mult),
                        waits=[(sv, vrec)], inc=sv.bump())
                    P.op("vector", lambda e, m=m: e.tensor_scalar(
                        A[:, a_pl + 1024 + 64 * m:a_pl + 1024 + 64 * (m + 1)],
                        A[:, a_pl + 1024 + 64 * m:a_pl + 1024 + 64 * (m + 1)],
                        self.a1_dl[:, m:m + 1], None, ALU.mult),
                        waits=[(sv, vrec)], inc=sv.bump())
                    P.op("vector", lambda e, m=m: e.tensor_scalar(
                        A[:, a_pgl + 128 * m:a_pgl + 128 * (m + 1)],
                        A[:, a_pgl + 128 * m:a_pgl + 128 * (m + 1)],
                        self.a1_dg[:, m:m + 1], None, ALU.mult),
                        waits=[(sv, vrec)], inc=sv.bump())
                vnorm = sv.n
                if PAIR_STAGE == "norm":
                    continue
                ptw = [(sv, vnorm), self.s_xt.now]
                for rnd, (co, width) in enumerate((((0), 128), ((128), 64), (("g"), 128))):
                    for m in range(DC):
                        if co == "g":
                            src = A[:, a_pgl + 128 * m:a_pgl + 128 * (m + 1)]
                        elif co == 0:
                            src = A[:, a_pl + 128 * m:a_pl + 128 * (m + 1)]
                        else:
                            src = A[:, a_pl + 1024 + 64 * m:a_pl + 1024 + 64 * (m + 1)]
                        P.op("tensor", lambda e, src=src, m=m, width=width: e.transpose(
                            self.pt[0:width, 128 * m:128 * (m + 1)], src,
                            self.ident_bf[:]),
                            waits=(ptw if m == 0 else []),
                            inc=(sm.bump() if m == DC - 1 else None))
                    vtr = sm.n
                    P.op("vector", lambda e, rnd=rnd: e.tensor_copy(
                        A[:, a_pt + 1024 * rnd:a_pt + 1024 * (rnd + 1)], self.pt[:]),
                        waits=[(sm, vtr)], inc=self.s_xt.bump())
                    ptw = [self.s_xt.now]
                pt_free = self.s_xt.now
                ptall = self.s_xt.now
                if PAIR_STAGE == "tr":
                    continue
                cw = [ptall, vw_ok, gfix_ok]
                if hg == 0:
                    cw.append(ctx_free if ctx_free is not None else guard_r)
                    if po1_r_free is not None:
                        cw.append(po1_r_free)
                first_ctx = True
                for m in range(DC):
                    h = 2 * m + hg
                    for bp in range(2):
                        idx = 2 * m + bp
                        t0 = tA0 if bp == 0 else tB0
                        t1 = tA1 if bp == 0 else tB1
                        outap = self.po[1][64 * hg:64 * hg + 64,
                                           128 * m + 64 * bp:128 * m + 64 * bp + 64]
                        P.op("tensor", lambda e, t0=t0, h=h, idx=idx, outap=outap: e.matmul(
                            outap, t0[:, 64 * h:64 * h + 64],
                            A[:, a_pt + 64 * idx:a_pt + 64 * (idx + 1)],
                            start=True, stop=False),
                            waits=(cw if first_ctx else []))
                        first_ctx = False
                        P.op("tensor", lambda e, t1=t1, h=h, idx=idx, outap=outap: e.matmul(
                            outap, t1[0:64, 64 * h:64 * h + 64],
                            A[0:64, a_pt + 1024 + 64 * idx:a_pt + 1024 + 64 * (idx + 1)],
                            start=False, stop=False))
                        P.op("tensor", lambda e, h=h, idx=idx, outap=outap: e.matmul(
                            outap, A[:, a_gv + 64 * h:a_gv + 64 * h + 64],
                            A[:, a_pt + 2048 + 64 * idx:a_pt + 2048 + 64 * (idx + 1)],
                            start=False, stop=True),
                            inc=(sm.bump() if (m == DC - 1 and bp == 1) else None))
                vctx = sm.n
                vw_free[pi] = sm.now
                qp_free[0] = sm.now
            if PAIR_STAGE in ("sl", "exp", "norm", "tr"):
                continue
            P.op("vector", lambda e: e.tensor_copy(
                A[:, a_ctxp:a_ctxp + 1024], self.po[1][:]),
                waits=[(sm, vctx)], inc=sv.bump())
            ctx_free = sv.now
            ctxp_done = sv.n
            if PAIR_STAGE == "ctx":
                continue
            if p == 0 or p == NP - 1:
                qoff = 0 if p == 0 else 64
                mcol = 8 if p == 0 else 9
                mscr = A[:, a_pgl:a_pgl + 64]
                for j in range(DC):
                    P.op("vector", lambda e, j=j, qoff=qoff, mscr=mscr: e.tensor_sub(
                        mscr,
                        A[:, a_rcn + 128 * j + qoff:a_rcn + 128 * j + qoff + 64],
                        A[:, a_ctxp + 128 * j + qoff:a_ctxp + 128 * j + qoff + 64]),
                        waits=[(sv, ctxp_done), rcn_ok, mrep_ok, sv.now, sl_free],
                        inc=sv.bump())
                    P.op("vector", lambda e, j=j, qoff=qoff, mcol=mcol, mscr=mscr: e.scalar_tensor_tensor(
                        A[:, a_ctxp + 128 * j + qoff:a_ctxp + 128 * j + qoff + 64],
                        mscr, self.a1_rsel[:, mcol:mcol + 1],
                        A[:, a_ctxp + 128 * j + qoff:a_ctxp + 128 * j + qoff + 64],
                        ALU.mult, ALU.add),
                        waits=[sv.now, small_ok], inc=sv.bump())
            merge_done = sv.n
            ww = [(sv, merge_done), wocast, bias_ok, self.consts_ready,
                  (sv, ctxp_done)]
            if po1_r_free is not None:
                ww.append(po1_r_free)
            for hh in range(2):
                P.op("tensor", lambda e, hh=hh: e.matmul(
                    self.po[1][:, 512 * hh:512 * (hh + 1)], self.ones_row_bf[:],
                    self.a1_bobf[0:1, 512 * hh:512 * (hh + 1)],
                    start=True, stop=False), waits=(ww if hh == 0 else []))
                for j in range(DC):
                    P.op("tensor", lambda e, j=j, hh=hh: e.matmul(
                        self.po[1][:, 512 * hh:512 * (hh + 1)],
                        A[:, a_ctxp + 128 * j:a_ctxp + 128 * (j + 1)],
                        wsb(j)[:, 512 * hh:512 * (hh + 1)],
                        start=False, stop=(j == DC - 1)),
                        inc=(sm.bump() if (j == DC - 1 and hh == 1) else None))
            vwo = sm.n
            P.op("vector", lambda e, p=p: e.tensor_tensor(
                self.R[:, 1024 * p:1024 * (p + 1)], self.R[:, 1024 * p:1024 * (p + 1)],
                self.po[1][:], ALU.add),
                waits=[(sm, vwo)], inc=self.s_r.bump())
            po1_r_free = self.s_r.now
        self.stage_free[0] = rcn_ok
        self.stage_free[1] = rcn_ok
        return self.s_r.now

    def p_ffn1(self, r_dep, first):
        P, c = self.P, self.cfg
        A = self.ABF
        Tc, DC, NCH = c.Tc, c.DC, c.NCH
        xt_ok = self.emit_ln([r_dep])
        guard_mm = self.s_mm.now
        guard_io = self.s_io.now
        a_qw = 0
        a_qT = 8 * 1024
        a_sgn = a_qT + DC * Tc
        a_cmp = a_sgn + 1024
        a_l1b = a_cmp + 1024
        assert a_l1b + 2048 <= 46592
        l1b = A[0:1, a_l1b:a_l1b + 1024]
        l1bpre = A[0:1, a_l1b + 1024:a_l1b + 2048]
        stg = [self.AFP[:, 0:1408], self.AFP[:, 1408:2816]]
        sv, se = self.s_ve, self.s_ev
        L = self.l1v

        P.op("vector", lambda e: e.reduce_max(L[:, 0:1], self.XT[:, 0:c.Tc], axis=AXX,
                                              apply_absolute_value=True),
             waits=[xt_ok], inc=sv.bump())
        for jj in range(1, DC):
            P.op("vector", lambda e, jj=jj: e.reduce_max(L[:, 1:2], self.XT[:, Tc * jj:Tc * (jj + 1)],
                                                         axis=AXX, apply_absolute_value=True),
                 waits=[sv.now], inc=sv.bump())
            P.op("vector", lambda e: e.tensor_max(L[:, 0:1], L[:, 0:1], L[:, 1:2]),
                 waits=[sv.now], inc=sv.bump())
        mrow = self.AFP[0:1, 2432:2560]
        w = [sv.now] + ([self.tp_free] if self.tp_free else [])
        P.op("sync", lambda e, mrow=mrow: e.dma_start(out=mrow, in_=L[:, 0:1]),
             waits=w, inc=self.s_tp.bump(16))
        P.op("vector", lambda e, mrow=mrow: e.reduce_max(self.l1s[0:1, 3:4], mrow, axis=AXX),
             waits=[self.s_tp.now], inc=sv.bump())
        self.tp_free = sv.now
        P.op("gpsimd", lambda e: e.dma_start(out=self.dmx_in, in_=self.l1s[0:1, 3:4]),
             waits=[sv.now, self.s_io.now], inc=self.s_io.bump(16))
        P.op("gpsimd", lambda e: e.collective_compute(
            "AllReduce", ALU.max, replica_groups=[list(range(8))],
            ins=[self.dmx_in.opt()], outs=[self.dmx_out.opt()]),
            waits=[self.s_io.now], inc=self.s_cc.bump())
        P.op("gpsimd", lambda e: e.dma_start(out=self.l1s[0:1, 0:1], in_=self.dmx_out),
             waits=[self.s_cc.now], inc=self.s_io.bump(16))
        gmax_ok = self.s_io.now

        if first:
            awcols = self.gsc
            passA_done = []
            for j in range(DC):
                si = j % 2
                w = [w for w in [self.stage_free[si]] if w] + [guard_mm]
                P.op("sync", lambda e, j=j, si=si: e.dma_start(
                    out=stg[si][:, 0:1024], in_=self.dl1_wT[128 * j:128 * (j + 1), :]),
                    waits=w, inc=self.s_stg[si].bump(16))
                wd = self.s_stg[si].now
                P.op("vector", lambda e, j=j, si=si: e.reduce_sum(
                    awcols[:, j:j + 1], stg[si][:, 0:1024], axis=AXX,
                    apply_absolute_value=True),
                    waits=[wd, sv.now], inc=sv.bump())
                self.stage_free[si] = sv.now
            P.op("vector", lambda e: e.reduce_sum(L[:, 1:2], awcols[:, 0:DC], axis=AXX),
                 waits=[sv.now], inc=sv.bump())
            vws = sv.n
            P.op("tensor", lambda e: e.matmul(self.po[1][0:1, 512:513], L[:, 1:2],
                                              self.cpool[:, 4:5], start=True, stop=True),
                 waits=[(sv, vws), self.consts_ready], inc=self.s_mm.bump())
            vsum = self.s_mm.n
            P.op("vector", lambda e: e.tensor_scalar(self.l1s[0:1, 4:5], self.po[1][0:1, 512:513],
                                                     1.0 / (1024.0 * 1024.0), None, ALU.mult),
                 waits=[(self.s_mm, vsum)], inc=sv.bump())
            P.op("tensor", lambda e: e.matmul(self.po[1][:, 513:514], self.ones_row_f32[:],
                                              self.l1s[0:1, 4:5], start=True, stop=True),
                 waits=[sv.now], inc=self.s_mm.bump())
            vwb = self.s_mm.n
            P.op("vector", lambda e: e.tensor_copy(L[:, 2:3], self.po[1][:, 513:514]),
                 waits=[(self.s_mm, vwb)], inc=sv.bump())
            P.op("vector", lambda e: e.tensor_scalar(L[:, 3:4], L[:, 2:3],
                                                     0.5, None, ALU.mult),
                 waits=[sv.now], inc=sv.bump())
            ws_ok = sv.now
            absbuf = self.AFP[:, 2816:3840]
            for j in range(DC):
                si = j % 2
                w = [w for w in [self.stage_free[si]] if w]
                P.op("sync", lambda e, j=j, si=si: e.dma_start(
                    out=stg[si][:, 0:1024], in_=self.dl1_wT[128 * j:128 * (j + 1), :]),
                    waits=w, inc=self.s_stg[si].bump(16))
                wd = self.s_stg[si].now
                P.op("scalar", lambda e, si=si: e.activation(
                    A[:, a_sgn:a_sgn + 1024], stg[si][:, 0:1024], AF.Sign, scale=-1.0),
                    waits=[wd, se.now, sv.now, guard_mm, guard_io], inc=se.bump())
                P.op("scalar", lambda e, si=si: e.activation(
                    absbuf, stg[si][:, 0:1024], AF.Abs),
                    waits=[wd, sv.now], inc=se.bump())
                self.stage_free[si] = se.now
                P.op("vector", lambda e: e.tensor_scalar(
                    A[:, a_cmp:a_cmp + 1024], absbuf, L[:, 3:4], None, ALU.is_gt),
                    waits=[se.now, ws_ok], inc=sv.bump())
                P.op("vector", lambda e, j=j: e.tensor_tensor(
                    A[:, a_qw + 1024 * j:a_qw + 1024 * (j + 1)],
                    A[:, a_sgn:a_sgn + 1024], A[:, a_cmp:a_cmp + 1024], ALU.mult),
                    waits=[sv.now, se.now], inc=sv.bump())
            qw_ok = sv.now
            P.op("gpsimd", lambda e: e.dma_start(out=self.dqw, in_=A[:, a_qw:a_qw + 8 * 1024]),
                 waits=[qw_ok], inc=self.s_stgu.bump(16))
            P.op("gpsimd", lambda e: e.dma_start(out=self.dws, in_=L[:, 2:3]),
                 waits=[qw_ok], inc=self.s_stgu.bump(16))
            self.qw_spill_done = self.s_stgu.now
        else:
            P.op("sync", lambda e: e.dma_start(out=A[:, a_qw:a_qw + 8 * 1024], in_=self.dqw),
                 waits=[guard_mm, self.qw_spill_done], inc=self.s_stg[0].bump(16))
            P.op("sync", lambda e: e.dma_start(out=L[:, 2:3], in_=self.dws),
                 inc=self.s_stg[0].bump(16))
            qw_ok = self.s_stg[0].now

        P.op("vector", lambda e: e.reciprocal(self.l1s[0:1, 1:2], self.l1s[0:1, 0:1]),
             waits=[gmax_ok, sv.now], inc=sv.bump())
        P.op("vector", lambda e: e.tensor_scalar(self.l1s[0:1, 1:2], self.l1s[0:1, 1:2],
                                                 127.0, None, ALU.mult),
             waits=[sv.now], inc=sv.bump())
        P.op("vector", lambda e: e.tensor_scalar(self.l1s[0:1, 2:3], self.l1s[0:1, 0:1],
                                                 1.0 / 127.0, None, ALU.mult),
             waits=[sv.now], inc=sv.bump())
        vsc = sv.n
        P.op("tensor", lambda e: e.matmul(self.po[1][:, 0:1], self.ones_row_f32[:],
                                          self.l1s[0:1, 1:2], start=True, stop=True),
             waits=[(sv, vsc), self.consts_ready], inc=self.s_mm.bump())
        P.op("tensor", lambda e: e.matmul(self.po[1][:, 1:2], self.ones_row_f32[:],
                                          self.l1s[0:1, 2:3], start=True, stop=True),
             inc=self.s_mm.bump())
        vbc = self.s_mm.n
        P.op("vector", lambda e: e.tensor_copy(L[:, 4:5], self.po[1][:, 0:1]),
             waits=[(self.s_mm, vbc)], inc=sv.bump())
        P.op("vector", lambda e: e.tensor_copy(L[:, 5:6], self.po[1][:, 1:2]),
             waits=[(self.s_mm, vbc)], inc=sv.bump())
        iscale_ok = sv.now

        w = [x for x in [self.stage_free[1]] if x]
        P.op("sync", lambda e: e.dma_start(out=self.AFP[0:1, 1408:2432], in_=self.dl1_bias),
             waits=w, inc=self.s_stg[1].bump(16))
        P.op("vector", lambda e: e.tensor_copy(l1b, self.AFP[0:1, 1408:2432]),
             waits=[self.s_stg[1].now, guard_mm], inc=self.s_cast.bump())
        self.stage_free[1] = self.s_cast.now
        bias_ok = self.s_cast.now
        P.op("vector", lambda e: e.scalar_tensor_tensor(
            self.l1s[0:1, 3:4], self.l1s[0:1, 2:3], 1.0, L[0:1, 2:3], ALU.mult, ALU.mult),
            waits=[iscale_ok, qw_ok], inc=sv.bump())
        P.op("vector", lambda e: e.reciprocal(self.l1s[0:1, 3:4], self.l1s[0:1, 3:4]),
             waits=[sv.now], inc=sv.bump())
        P.op("vector", lambda e: e.tensor_scalar(l1bpre, l1b,
                                                 self.l1s[0:1, 3:4], None, ALU.mult),
             waits=[sv.now, bias_ok], inc=sv.bump())
        P.op("vector", lambda e: e.scalar_tensor_tensor(
            L[:, 6:7], L[:, 2:3], 0.5, L[:, 5:6], ALU.mult, ALU.mult),
            waits=[sv.now], inc=sv.bump())
        prep_ok = sv.now

        QW = min(512, Tc)
        vb2 = self.AFP[:, 0:QW]
        rb2 = self.AFP[:, 512:512 + QW]
        NCC = Tc // QW
        qx_val = [None] * NCC
        for cc in range(NCC):
            for jj in range(DC):
                P.op("vector", lambda e, jj=jj, cc=cc: e.tensor_scalar(
                    vb2, self.XT[:, Tc * jj + QW * cc:Tc * jj + QW * (cc + 1)],
                    L[:, 4:5], 256.5, ALU.mult, ALU.add),
                    waits=[xt_ok, iscale_ok, sv.now, self.stage_free[0] or sv.now,
                           self.stage_free[1] or sv.now],
                    inc=sv.bump())
                P.op("vector", lambda e: e.tensor_copy(self.ibuf[:, 0:QW], vb2),
                     waits=[sv.now], inc=sv.bump())
                P.op("vector", lambda e: e.tensor_copy(rb2, self.ibuf[:, 0:QW]),
                     waits=[sv.now], inc=sv.bump())
                P.op("vector", lambda e: e.tensor_sub(vb2, vb2, rb2),
                     waits=[sv.now], inc=sv.bump())
                P.op("vector", lambda e: e.tensor_single_scalar(
                    vb2, vb2, 0.0, ALU.is_lt),
                    waits=[sv.now], inc=sv.bump())
                P.op("vector", lambda e: e.tensor_scalar(
                    rb2, rb2, -1.0, 256.0, ALU.mult, ALU.add),
                    waits=[sv.now], inc=sv.bump())
                P.op("vector", lambda e, jj=jj, cc=cc: e.tensor_tensor(
                    A[:, a_qT + Tc * jj + QW * cc:a_qT + Tc * jj + QW * (cc + 1)],
                    rb2, vb2, ALU.add),
                    waits=[sv.now, guard_mm, guard_io], inc=sv.bump())
            qx_val[cc] = sv.n
        qx_ok = sv.now
        self.stage_free[0] = qx_ok
        self.stage_free[1] = qx_ok

        po_free = [None, None]
        for k in range(NCH):
            pb = k % 2
            bank = self.po[pb]
            ww = [(sv, qx_val[k // (NCH // NCC)]), qw_ok, prep_ok, self.consts_ready]
            if po_free[pb] is not None:
                ww.append(po_free[pb])
            for hh in range(2):
                P.op("tensor", lambda e, hh=hh, bank=bank: e.matmul(
                    bank[:, 512 * hh:512 * (hh + 1)], self.ones_row_bf[:],
                    l1bpre[0:1, 512 * hh:512 * (hh + 1)],
                    start=True, stop=False), waits=(ww if hh == 0 else []))
                for j in range(DC):
                    P.op("tensor", lambda e, k=k, j=j, hh=hh, bank=bank: e.matmul(
                        bank[:, 512 * hh:512 * (hh + 1)],
                        A[:, a_qT + Tc * j + 128 * k:a_qT + Tc * j + 128 * (k + 1)],
                        A[:, a_qw + 1024 * j + 512 * hh:a_qw + 1024 * j + 512 * (hh + 1)],
                        start=False, stop=(j == DC - 1)),
                        inc=(self.s_mm.bump() if (j == DC - 1 and hh == 1) else None))
            vmm = self.s_mm.n
            P.op("vector", lambda e, k=k, bank=bank: e.scalar_tensor_tensor(
                self.R[:, 1024 * k:1024 * (k + 1)], bank[:], L[:, 6:7],
                self.R[:, 1024 * k:1024 * (k + 1)], ALU.mult, ALU.add),
                waits=[(self.s_mm, vmm)], inc=self.s_r.bump())
            po_free[pb] = self.s_r.now
        return self.s_r.now

    def p_store(self, dep):
        P, c = self.P, self.cfg
        if DEBUG_DUMP:
            P.op("gpsimd", lambda e: e.dma_start(
                out=self.ddbg, in_=self.ABF[:, 0:24576]),
                waits=[dep], inc=self.s_st.bump(16))
        for k in range(c.NCH):
            P.op("gpsimd", lambda e, k=k: e.dma_start(
                out=self.dout[128 * k:128 * (k + 1), :],
                in_=self.R[:, 1024 * k:1024 * (k + 1)]),
                waits=[dep], inc=self.s_st.bump(16))
        P.op("gpsimd", lambda e: e.engine_nop(), waits=[self.s_st.now])

    def finish(self):
        self.P.emit()
        return self.nc


ALL_PHASES = ("ffn0a", "att0", "ffn0b", "ffn1a", "att1", "ffn1b")


def build(cfg: Cfg, phases=ALL_PHASES):
    b = Builder(cfg)
    b.p_load()
    done = b.x_ready
    if "ffn0a" in phases:
        done = b.p_ffn0(done)
    if "att0" in phases:
        done = b.p_att0(done)
    if "ffn0b" in phases:
        done = b.p_ffn0(done)
    if "ffn1a" in phases:
        done = b.p_ffn1(done, first=True)
    if "att1" in phases:
        done = b.p_att1(done)
    if "ffn1b" in phases:
        done = b.p_ffn1(done, first="ffn1a" not in phases)
    b.p_store(done)
    nc = b.finish()
    return nc, b


def make_const_inputs(cfg: Cfg):
    ident = np.eye(128, dtype=np.float32)
    import ml_dtypes
    cpool = np.zeros((128, 8), np.float32)
    cpool[:, 0] = -PI
    cpool[:, 1] = 1e-5
    cpool[:, 2] = -C_L0
    cpool[:, 3] = -C_L1
    cpool[:, 4] = 1.0
    cpool[:, 5] = 0.5
    cpool[:, 6] = -1.0
    return {
        "ident_bf": ident.astype(ml_dtypes.bfloat16),
        "cpool": cpool,
        "ones_bf": np.ones((128, 1), ml_dtypes.bfloat16),
        "ones_row_bf": np.ones((1, 128), ml_dtypes.bfloat16),
        "ones_row_f32": np.ones((1, 128), np.float32),
    }


def make_shared_inputs(inputs):
    f = lambda a: np.ascontiguousarray(np.asarray(a, np.float32))
    m = {
        "L0_w_gate": f(inputs["L0_ffn_w_gate"]),
        "L0_w_g": f(inputs["L0_ffn_w_g"]),
        "L0_w_u": f(inputs["L0_ffn_w_u"]),
        "L0_w_d": f(inputs["L0_ffn_w_d"]),
        "A0_wqT": f(np.asarray(inputs["L0_attn_Wq"], np.float32).T),
        "A0_wkT": f(np.asarray(inputs["L0_attn_Wk"], np.float32).T),
        "A0_wvT": f(np.asarray(inputs["L0_attn_Wv"], np.float32).T),
        "A0_woT": f(np.asarray(inputs["L0_attn_Wo"], np.float32).T),
        "A0_bo": f(inputs["L0_attn_bo"]).reshape(1, 1024),
        "A0_vecs": np.stack([f(inputs["L0_attn_wavelengths"]),
                             f(inputs["L0_attn_phase_bias"]),
                             f(inputs["L0_attn_bq"]),
                             f(inputs["L0_attn_bk"]),
                             f(inputs["L0_attn_bv"])], axis=1).copy(),
        "A1_wqT": f(np.asarray(inputs["L1_attn_Wq"], np.float32).T),
        "A1_wkT": f(np.asarray(inputs["L1_attn_Wk"], np.float32).T),
        "A1_wvT": f(np.asarray(inputs["L1_attn_Wv"], np.float32).T),
        "A1_woT": f(np.asarray(inputs["L1_attn_Wo"], np.float32).T),
        "A1_bqkv": np.concatenate([
            f(inputs["L1_attn_bq"]).reshape(8, 128).T,
            f(inputs["L1_attn_bk"]).reshape(8, 128).T,
            f(inputs["L1_attn_bv"]).reshape(8, 128).T], axis=1).copy(),
        "A1_bo": f(inputs["L1_attn_bo"]).reshape(1, 1024),
        "L1f_wT": f(np.asarray(inputs["L1_ffn_weight"], np.float32).T),
        "L1f_bias": f(inputs["L1_ffn_bias"]).reshape(1, 1024),
    }
    return m


def make_core_inputs(cfg, core):
    r = core % 4
    rolesel = np.zeros((128, 10), np.float32)
    rolesel[:, (r - 1) % 4] = 1.0
    rolesel[:, 4 + (r + 1) % 4] = 1.0
    if r == 0:
        rolesel[:, 8] = 1.0
    if r == 1:
        rolesel[:, 9] = 1.0
    return {"A1_rolesel": rolesel}


def kernel(**inputs):
    cfg = Cfg(Tc=2048)
    gi = np.asarray(inputs["L1_attn_global_indices"]).ravel()
    assert int(gi[0]) == 0 and int(gi[1]) == 2 * cfg.NBLK - 1, gi
    nc, _ = build(cfg)
    consts = make_const_inputs(cfg)
    shared = make_shared_inputs(inputs)
    x = np.ascontiguousarray(np.asarray(inputs["L0_attn_x"], np.float32).reshape(16384, 1024))
    in_maps = []
    for c in range(8):
        m = dict(consts)
        m.update(shared)
        m.update(make_core_inputs(cfg, c))
        m["x"] = np.ascontiguousarray(x[2048 * c:2048 * (c + 1)])
        in_maps.append(m)
    res = run_bass_kernel_spmd(nc, in_maps, core_ids=list(range(8)))
    out = np.concatenate([res.results[c]["out"] for c in range(8)], axis=0)
    return out.reshape(2, 8192, 1024)


# version 17
# speedup vs baseline: 1.0035x; 1.0035x over previous
import math
from contextlib import ExitStack

import numpy as np

import bass_rust
import concourse.bass as bass
from concourse import library_config as _LC
import concourse.mybir as mybir
from concourse.bass_utils import run_bass_kernel_spmd

F32 = mybir.dt.float32
BF16 = mybir.dt.bfloat16
AF = mybir.ActivationFunctionType
ALU = mybir.AluOpType
AXX = mybir.AxisListType.X

TWO_PI = float(2.0 * math.pi)
PI = float(math.pi)
C_L0 = 12.0
C_L1 = 16.0
INV_SQRT_H0 = 1.0 / math.sqrt(128.0)
L1_SCALE = 0.125


ATT1_STOP = None
PAIR_STAGE = None
DEBUG_DUMP = False


class Cfg:
    def __init__(self, Tc=2048):
        self.Tc = Tc
        self.D = 1024
        self.E = 8
        self.M = 1408
        self.H1 = 16
        self.DH = 64
        self.BS = 64
        self.SEQC = 4
        self.NCH = Tc // 128
        self.DC = self.D // 128
        self.MC = self.M // 128
        self.S = Tc * self.SEQC
        self.NBLK = Tc // self.BS
        self.NPAIR = Tc // 128
        self.TH = Tc // 2
        self.QT = min(512, self.TH)


class Sem:
    __slots__ = ("h", "n", "name")
    def __init__(self, h, name):
        self.h = h; self.n = 0; self.name = name
    def bump(self, k=1):
        self.n += k
        return (self, k)
    @property
    def now(self):
        return (self, self.n)


class Prog:
    ENGINES = ("sync", "tensor", "scalar", "vector", "gpsimd")

    def __init__(self, nc, es):
        self.nc = nc
        self.es = es
        self.ops = {e: [] for e in self.ENGINES}

    def sem(self, name):
        return Sem(self.es.enter_context(self.nc.semaphore(name)), name)

    def op(self, engine, fn, waits=(), inc=None, incs=None):
        ii = list(incs) if incs else []
        if inc is not None:
            ii.append(inc)
        self.ops[engine].append((list(waits), fn, ii))

    def emit(self):
        with self.nc.Block("main") as b:
            def mk(engine):
                def run(eng):
                    seen = {}
                    for waits, fn, incs in self.ops[engine]:
                        for (s, v) in waits:
                            if seen.get(id(s), -1) >= v:
                                continue
                            eng.wait_ge(s.h, v)
                            seen[id(s)] = v
                        ins = fn(eng)
                        for (s, d) in incs:
                            ins = ins.then_inc(s.h, d)
                return run
            b.sync(mk("sync"))
            b.tensor(mk("tensor"))
            b.scalar(mk("scalar"))
            b.vector(mk("vector"))
            b.gpsimd(mk("gpsimd"))


class Builder:
    def __init__(self, cfg: Cfg):
        self.cfg = cfg
        self.nc = bass.Bass("TRN2", target_bir_lowering=False)
        self.es = ExitStack()
        self.P = Prog(self.nc, self.es)
        self._declare_io()
        self._alloc()
        self._sems()

    def _declare_io(self):
        nc, c = self.nc, self.cfg
        D, E, M = c.D, c.E, c.M
        dram = lambda n, sh, dt=F32: nc.dram_tensor(n, sh, dt, kind="ExternalInput").ap()
        self.dx = dram("x", [c.Tc, D])
        self.dout = nc.dram_tensor("out", [c.Tc, D], F32, kind="ExternalOutput").ap()
        self.dident_bf = dram("ident_bf", [128, 128], BF16)
        self.dcpool = dram("cpool", [128, 8])
        self.dones_bf = dram("ones_bf", [128, 1], BF16)
        self.dwgate = dram("L0_w_gate", [D, E])
        self.dwg = dram("L0_w_g", [E, D, M])
        self.dwu = dram("L0_w_u", [E, D, M])
        self.dwd = dram("L0_w_d", [E, M, D])
        self.da0_wqT = dram("A0_wqT", [D, 128])
        self.da0_wkT = dram("A0_wkT", [D, 128])
        self.da0_wvT = dram("A0_wvT", [D, 128])
        self.da0_woT = dram("A0_woT", [128, D])
        self.da0_bo = dram("A0_bo", [1, D])
        self.da0_vecs = dram("A0_vecs", [128, 5])
        self.dcc0_in = nc.dram_tensor("cc0_in", [384, c.Tc], BF16).ap()
        self.dcc0_out = nc.dram_tensor("cc0_out", [4 * 384, c.Tc], BF16).ap()
        self.da1_wqT = dram("A1_wqT", [D, D])
        self.da1_wkT = dram("A1_wkT", [D, D])
        self.da1_wvT = dram("A1_wvT", [D, D])
        self.da1_woT = dram("A1_woT", [D, D])
        self.da1_bqkv = dram("A1_bqkv", [128, 24])
        self.da1_bo = dram("A1_bo", [1, D])
        self.da1_rolesel = dram("A1_rolesel", [128, 10])
        self.dq1 = nc.dram_tensor("q1_spill", [128, 8 * c.Tc], BF16).ap()
        self.dv1 = nc.dram_tensor("v1_spill", [c.Tc, D], BF16).ap()
        self.dcc1_in = nc.dram_tensor("cc1_in", [128, 3072], BF16).ap()
        self.dcc1_out = nc.dram_tensor("cc1_out", [512, 3072], BF16).ap()
        self.dcc1r_in = nc.dram_tensor("cc1r_in", [130, 1024], F32).ap()
        self.dcc1r_out = nc.dram_tensor("cc1r_out", [130, 1024], F32).ap()
        self.dl1_wT = dram("L1f_wT", [D, D])
        self.dl1_bias = dram("L1f_bias", [1, D])
        self.dones_row_bf = dram("ones_row_bf", [1, 128], BF16)
        self.dones_row_f32 = dram("ones_row_f32", [1, 128])
        self.dmx_in = nc.dram_tensor("mx_in", [1, 1], F32).ap()
        self.dmx_out = nc.dram_tensor("mx_out", [1, 1], F32).ap()
        self.dqw = nc.dram_tensor("qw_spill", [128, 8 * D], BF16).ap()
        self.dws = nc.dram_tensor("ws_spill", [128, 1], F32).ap()
        if DEBUG_DUMP:
            self.ddbg = nc.dram_tensor("dbg", [128, 24576], BF16,
                                       kind="ExternalOutput").ap()

    def _alloc(self):
        nc, c = self.nc, self.cfg
        sb = lambda n, sh, dt=F32: self.es.enter_context(nc.sbuf_tensor(n, sh, dt))
        psa = lambda n, sh, dt=F32: self.es.enter_context(nc.psum_tensor(n, sh, dt))
        self.R = sb("R", [128, c.NCH * 1024])
        self.XT = sb("XT", [128, c.DC * c.Tc], BF16)
        self.ABF = sb("ABF", [128, 46592], BF16)
        self.AFP = sb("AFP", [128, 4096], F32)
        self.ident_bf = sb("identbf", [128, 128], BF16)
        self.cpool = sb("cpools", [128, 8], F32)
        self.ones_bf = sb("onesbf", [128, 1], BF16)
        self.ln_s6 = sb("ln_s6", [128, 12], F32)
        self.ln_mv = sb("ln_mv", [128, 2], F32)
        self.ln_rs = sb("ln_rs", [128, 1], F32)
        self.ln_b = sb("ln_b", [128, 1], F32)
        self.xstage = None
        self.Gh = sb("Gh", [128, c.NCH * 8], F32)
        self.gsc = sb("gsc", [128, 24], F32)
        self.geq1 = sb("geq1", [128, 8], F32)
        self.geq2 = sb("geq2", [128, 8], F32)
        self.wgate_bf = sb("wgatebf", [128, c.DC * 8], BF16)
        self.a1_bqkv = sb("a1bqkv", [128, 24], F32)
        self.a1_dl = sb("a1dl", [128, 8], F32)
        self.a1_dg = sb("a1dg", [128, 8], F32)
        self.a1_msk = sb("a1msk", [1, 256], BF16)
        self.a1_rsel = sb("a1rsel", [128, 10], F32)
        self.ones_row_bf = sb("onesrowbf", [1, 128], BF16)
        self.ones_row_f32 = sb("onesrowf32", [1, 128], F32)
        self.ibuf = sb("ibuf", [128, 512], mybir.dt.int32)
        self.l1v = sb("l1v", [128, 8], F32)
        self.l1s = sb("l1s", [1, 8], F32)
        self.a0v = sb("a0v", [128, 16], F32)
        self.P1 = psa("P1", [128, 1536])
        self.pg = [self.P1[:, 0:512]]
        self.pu = [self.P1[:, 512:1024]]
        self.po = [psa(f"po{i}", [128, 1024]) for i in range(2)]
        self.pt = psa("pt_bf", [128, 1024], BF16)
        self.xstage = self.ABF[:, 45568:46592]

        A = 0
        def carve(n):
            nonlocal A
            r = A; A += n; return r
        self.o_H = carve(c.MC * c.TH)
        self.o_wg = carve(c.MC * 128 * c.DC)
        self.o_wu = carve(c.MC * 128 * c.DC)
        self.o_wd = carve(c.MC * 1024)
        self.o_sg = carve(2 * c.QT)
        self.o_su = carve(c.QT)
        self.ffn_bf_end = A
        assert A <= 46592, A
        self.of_stage = 0

    def _sems(self):
        P = self.P
        self.s_in = P.sem("s_in")
        self.s_const = P.sem("s_const")
        self.s_stg = [P.sem("s_stg0"), P.sem("s_stg1")]
        self.s_stgu = P.sem("s_stgu")
        self.s_wcast = P.sem("s_wcast")
        self.s_io = P.sem("s_io")
        self.s_cc = P.sem("s_cc")
        self.stage_free = [None, None]
        self.s_sl = [P.sem(f"s_sl{i}") for i in range(4)]
        self.s_msk = P.sem("s_msk")
        self.s_tp = P.sem("s_tp")
        self.tp_free = None
        self.s_l1b = P.sem("s_l1b")
        self.s_cast = P.sem("s_cast")
        self.s_mm = P.sem("s_mm")
        self.s_ev = P.sem("s_ev")
        self.s_ve = P.sem("s_ve")
        self.s_ln = P.sem("s_ln")
        self.s_xt = P.sem("s_xt")
        self.s_g = P.sem("s_g")
        self.s_h = P.sem("s_h")
        self.s_r = P.sem("s_r")
        self.s_st = P.sem("s_st")
        self.r_chunk_done = [None] * self.cfg.NCH

    def p_load(self):
        P, c = self.P, self.cfg
        for k in range(c.NCH):
            P.op("sync", lambda e, k=k: e.dma_start(
                out=self.R[:, 1024 * k:1024 * (k + 1)],
                in_=self.dx[128 * k:128 * (k + 1), :]), inc=self.s_in.bump(16))
        P.op("sync", lambda e: e.dma_start(out=self.ident_bf[:], in_=self.dident_bf),
             inc=self.s_const.bump(16))
        P.op("sync", lambda e: e.dma_start(out=self.cpool[:], in_=self.dcpool),
             inc=self.s_const.bump(16))
        P.op("sync", lambda e: e.dma_start(out=self.ones_bf[:], in_=self.dones_bf),
             inc=self.s_const.bump(16))
        P.op("sync", lambda e: e.dma_start(out=self.ones_row_bf[:], in_=self.dones_row_bf),
             inc=self.s_const.bump(16))
        P.op("sync", lambda e: e.dma_start(out=self.ones_row_f32[:], in_=self.dones_row_f32),
             inc=self.s_const.bump(16))
        self.consts_ready = self.s_const.now
        self.x_ready = self.s_in.now

    def emit_ln(self, r_waits):
        P, c = self.P, self.cfg
        for k in range(c.NCH):
            w = list(r_waits) + [self.consts_ready, self.s_mm.now]
            P.op("scalar", lambda e, k=k: e.activation(
                self.xstage[:], self.R[:, 1024 * k:1024 * (k + 1)], AF.Copy,
                accum_out=self.ln_s6[:, 0:1]),
                waits=w, inc=self.s_ln.bump())
            P.op("scalar", lambda e, k=k: e.activation(
                self.xstage[:], self.R[:, 1024 * k:1024 * (k + 1)], AF.Square,
                accum_out=self.ln_s6[:, 1:2]),
                waits=[self.s_ln.now], inc=self.s_ln.bump())
            P.op("vector", lambda e: e.tensor_scalar(
                self.ln_mv[:, 0:1], self.ln_s6[:, 0:1], 1.0 / 1024.0, None, ALU.mult),
                waits=[self.s_ln.now], inc=self.s_ln.bump())
            P.op("vector", lambda e: e.scalar_tensor_tensor(
                self.ln_s6[:, 2:3], self.ln_mv[:, 0:1], -1.0, self.ln_mv[:, 0:1],
                ALU.mult, ALU.mult),
                waits=[self.s_ln.now], inc=self.s_ln.bump())
            P.op("vector", lambda e: e.scalar_tensor_tensor(
                self.ln_mv[:, 1:2], self.ln_s6[:, 1:2], 1.0 / 1024.0, self.ln_s6[:, 2:3],
                ALU.mult, ALU.add),
                waits=[self.s_ln.now], inc=self.s_ln.bump())
            P.op("scalar", lambda e: e.activation(self.ln_rs[:], self.ln_mv[:, 1:2],
                                                  AF.Sqrt, bias=self.cpool[:, 1:2]),
                 waits=[self.s_ln.now], inc=self.s_ln.bump())
            P.op("vector", lambda e: e.reciprocal(self.ln_rs[:], self.ln_rs[:]),
                 waits=[self.s_ln.now], inc=self.s_ln.bump())
            P.op("vector", lambda e: e.scalar_tensor_tensor(
                self.ln_b[:], self.ln_mv[:, 0:1], -1.0, self.ln_rs[:], ALU.mult, ALU.mult),
                waits=[self.s_ln.now], inc=self.s_ln.bump())
            P.op("vector", lambda e, k=k: e.tensor_scalar(
                self.xstage[:], self.R[:, 1024 * k:1024 * (k + 1)],
                self.ln_rs[:], self.ln_b[:], ALU.mult, ALU.add),
                waits=[self.s_ln.now, self.s_mm.now], inc=self.s_ln.bump())
            vx = self.s_ln.n
            for j in range(c.DC):
                P.op("tensor", lambda e, j=j: e.transpose(
                    self.pt[:, 128 * j:128 * (j + 1)],
                    self.xstage[:, 128 * j:128 * (j + 1)], self.ident_bf[:]),
                    waits=[(self.s_ln, vx), self.s_xt.now], inc=self.s_mm.bump())
            vt = self.s_mm.n

            def evac_xt(e, k=k):
                dst = self.XT[:].rearrange("p (j t) -> p j t", t=c.Tc)[:, :, 128 * k:128 * (k + 1)]
                src = self.pt[:].rearrange("p (j t) -> p j t", t=128)
                return e.tensor_copy(dst, src)
            P.op("vector", evac_xt, waits=[(self.s_mm, vt)], inc=self.s_xt.bump())
        return self.s_xt.now

    def p_ffn0(self, r_dep):
        P, c = self.P, self.cfg
        self.r_chunk_done = [None] * c.NCH
        A = self.ABF
        E, MC, DC, TH, QT = c.E, c.MC, c.DC, c.TH, c.QT
        NQ = TH // QT
        xt_ok = self.emit_ln([r_dep])

        P.op("sync", lambda e: e.dma_start(
            out=self.AFP[:, 0:8 * DC].rearrange("p (j e) -> p j e", e=8),
            in_=self.dwgate[:].rearrange("(j p) e -> p j e", p=128)),
             waits=[w for w in [self.stage_free[0]] if w], inc=self.s_stg[0].bump(16))
        P.op("vector", lambda e: e.tensor_copy(self.wgate_bf[:], self.AFP[:, 0:8 * DC]),
             waits=[self.s_stg[0].now, self.s_mm.now], inc=self.s_cast.bump())
        wgate_ok = self.s_cast.now
        self.stage_free[0] = self.s_cast.now
        sg = self.s_g
        gate_bank_free = [None, None]
        for k in range(c.NCH):
            bank = self.pg[0]
            bw = [xt_ok, wgate_ok]
            if gate_bank_free[0] is not None:
                bw.append(gate_bank_free[0])
            else:
                bw += [self.s_ev.now, self.s_ve.now]
            for j in range(DC):
                P.op("tensor", lambda e, k=k, j=j, bank=bank: e.matmul(
                    bank[:, 0:8],
                    self.XT[:, c.Tc * j + 128 * k:c.Tc * j + 128 * (k + 1)],
                    self.wgate_bf[:, 8 * j:8 * (j + 1)],
                    start=(j == 0), stop=(j == DC - 1)),
                    waits=(bw if j == 0 else []),
                    inc=(self.s_mm.bump() if j == DC - 1 else None))
            vmm = self.s_mm.n
            P.op("vector", lambda e, bank=bank: e.reduce_max(self.gsc[:, 0:1], bank[:, 0:8], axis=AXX),
                 waits=[(self.s_mm, vmm)], inc=sg.bump())
            P.op("vector", lambda e, bank=bank: e.tensor_scalar(self.geq1[:], bank[:, 0:8],
                                                                self.gsc[:, 0:1], None, ALU.is_ge),
                 waits=[sg.now], inc=sg.bump())
            P.op("vector", lambda e, bank=bank: e.scalar_tensor_tensor(
                self.gsc[:, 8:16], self.geq1[:], -1e30, bank[:, 0:8], ALU.mult, ALU.add),
                waits=[sg.now], inc=sg.bump())
            gate_bank_free[0] = sg.now
            P.op("vector", lambda e: e.reduce_max(self.gsc[:, 1:2], self.gsc[:, 8:16], axis=AXX),
                 waits=[sg.now], inc=sg.bump())
            P.op("vector", lambda e: e.tensor_scalar(self.geq2[:], self.gsc[:, 8:16],
                                                     self.gsc[:, 1:2], None, ALU.is_ge),
                 waits=[sg.now], inc=sg.bump())
            P.op("scalar", lambda e: e.activation(self.gsc[:, 2:3], self.gsc[:, 1:2],
                                                  AF.Sigmoid, bias=self.gsc[:, 0:1], scale=-1.0),
                 waits=[sg.now], inc=sg.bump())
            P.op("vector", lambda e: e.tensor_scalar(self.gsc[:, 3:4], self.gsc[:, 2:3],
                                                     -0.5, 0.5, ALU.mult, ALU.add),
                 waits=[sg.now], inc=sg.bump())
            P.op("vector", lambda e: e.tensor_scalar(self.gsc[:, 2:3], self.gsc[:, 2:3],
                                                     0.5, None, ALU.mult),
                 waits=[sg.now], inc=sg.bump())
            P.op("vector", lambda e: e.tensor_scalar(self.geq1[:], self.geq1[:],
                                                     self.gsc[:, 2:3], None, ALU.mult),
                 waits=[sg.now], inc=sg.bump())
            P.op("vector", lambda e, k=k: e.scalar_tensor_tensor(
                self.Gh[:, 8 * k:8 * (k + 1)], self.geq2[:], self.gsc[:, 3:4],
                self.geq1[:], ALU.mult, ALU.add),
                waits=[sg.now], inc=sg.bump())
        g_all = self.s_g.now

        o_H, o_wg, o_wu, o_wd = self.o_H, self.o_wg, self.o_wu, self.o_wd
        o_sg, o_su = self.o_sg, self.o_su
        stg = [self.AFP[:, 0:1408], self.AFP[:, 1408:2816]]
        wgwu_free = [self.s_mm.now, self.s_io.now]
        wd_free = [self.s_mm.now, self.s_io.now]
        po_free = [self.s_r.now, self.s_r.now]
        pgpu_free = [[g_all, self.s_ve.now]]
        sgsu_free = [self.s_ve.now, self.s_ve.now]

        wd_cast_ok = [self.s_cast.now]
        wgwu_cast_ok = [self.s_cast.now]
        h_write_guard = self.s_mm.now

        def dma_weight_dchunk(src3, e_, j_, st_i, q_eng):
            sem = self.s_stgu if q_eng == "gpsimd" else self.s_stg[st_i]
            P.op(q_eng, lambda e, e_=e_, j_=j_, st_i=st_i: e.dma_start(
                out=stg[st_i], in_=src3[e_, 128 * j_:128 * (j_ + 1), :]),
                waits=[w for w in [self.stage_free[st_i]] if w], inc=sem.bump(16))
            return sem.now

        def dma_wd_mchunk(e_, m_, st_i, q_eng):
            sem = self.s_stgu if q_eng == "gpsimd" else self.s_stg[st_i]
            P.op(q_eng, lambda e, e_=e_, m_=m_, st_i=st_i: e.dma_start(
                out=stg[st_i][:, 0:1024], in_=self.dwd[e_, 128 * m_:128 * (m_ + 1), :]),
                waits=[w for w in [self.stage_free[st_i]] if w], inc=sem.bump(16))
            return sem.now

        for ei in range(E):
            for j in range(DC):
                for (src3, dsto, st_i) in ((self.dwg, o_wg, 0), (self.dwu, o_wu, 1)):
                    wdma = dma_weight_dchunk(src3, ei, j, st_i, "sync")
                    def cast(e, dsto=dsto, j=j, si=st_i):
                        d3 = A[:, dsto:dsto + MC * 1024].rearrange("p (mc q) -> p mc q", q=1024)
                        dd = d3[:, :, 128 * j:128 * (j + 1)]
                        ss = stg[si][:].rearrange("p (mc q) -> p mc q", q=128)
                        return e.activation(dd, ss, AF.Copy)
                    P.op("scalar", cast, waits=[wdma] + wgwu_free, inc=self.s_wcast.bump())
                    self.stage_free[st_i] = self.s_wcast.now
            wgwu_cast_ok = [self.s_wcast.now]
            wd_cast_val = [None] * MC

            for half in range(2):
                t0 = half * TH
                for m in range(MC):
                    for q in range(NQ):
                        pb = 0
                        for j in range(DC):
                            P.op("tensor", lambda e, m=m, q=q, j=j, pb=pb, t0=t0: e.matmul(
                                self.pg[pb][:, 0:QT],
                                A[:, o_wg + 1024 * m + 128 * j:o_wg + 1024 * m + 128 * (j + 1)],
                                self.XT[:, c.Tc * j + t0 + QT * q:c.Tc * j + t0 + QT * (q + 1)],
                                start=(j == 0), stop=(j == DC - 1)),
                                waits=(wgwu_cast_ok + [xt_ok] + ([pgpu_free[pb][0]] if j == 0 else [])),
                                inc=(self.s_mm.bump() if j == DC - 1 else None))
                        vg = self.s_mm.n
                        for j in range(DC):
                            P.op("tensor", lambda e, m=m, q=q, j=j, pb=pb, t0=t0: e.matmul(
                                self.pu[pb][:, 0:QT],
                                A[:, o_wu + 1024 * m + 128 * j:o_wu + 1024 * m + 128 * (j + 1)],
                                self.XT[:, c.Tc * j + t0 + QT * q:c.Tc * j + t0 + QT * (q + 1)],
                                start=(j == 0), stop=(j == DC - 1)),
                                waits=([pgpu_free[pb][1]] if j == 0 else []),
                                inc=(self.s_mm.bump() if j == DC - 1 else None))
                        vu = self.s_mm.n
                        sbuf_i = (m * NQ + q) % 2
                        P.op("scalar", lambda e, pb=pb, sbuf_i=sbuf_i: e.activation(
                            A[:, o_sg + QT * sbuf_i:o_sg + QT * (sbuf_i + 1)],
                            self.pg[pb][:, 0:QT], AF.Silu),
                            waits=[(self.s_mm, vg), sgsu_free[sbuf_i]], inc=self.s_ev.bump())
                        pgpu_free[pb][0] = self.s_ev.now
                        vsilu = self.s_ev.n
                        P.op("vector", lambda e, pb=pb: e.tensor_copy(
                            A[:, o_su:o_su + QT], self.pu[pb][:, 0:QT]),
                            waits=[(self.s_mm, vu), self.s_h.now], inc=self.s_ve.bump())
                        pgpu_free[pb][1] = self.s_ve.now
                        vsu = self.s_ve.n
                        P.op("vector", lambda e, m=m, q=q, sbuf_i=sbuf_i, t0=t0: e.tensor_tensor(
                            A[:, o_H + TH * m + QT * q:o_H + TH * m + QT * (q + 1)],
                            A[:, o_sg + QT * sbuf_i:o_sg + QT * (sbuf_i + 1)],
                            A[:, o_su:o_su + QT], ALU.mult),
                            waits=[(self.s_ev, vsilu), (self.s_ve, vsu), h_write_guard],
                            inc=self.s_h.bump())
                        sgsu_free[sbuf_i] = self.s_h.now
                h_all = self.s_h.n
                if half == 1:
                    hid_end = self.s_mm.n
                if half == 0:
                    for m in range(MC):
                        st_i = m % 2
                        wdma = dma_wd_mchunk(ei, m, st_i, "sync")
                        def castd(e, m=m, si=st_i):
                            return e.tensor_copy(A[:, o_wd + 1024 * m:o_wd + 1024 * (m + 1)],
                                                 stg[si][:, 0:1024])
                        P.op("vector", castd, waits=[wdma] + wd_free, inc=self.s_cast.bump())
                        self.stage_free[st_i] = self.s_cast.now
                        wd_cast_val[m] = self.s_cast.n
                for kk in range(TH // 128):
                    k = half * (TH // 128) + kk
                    pb = kk % 2
                    for m in range(MC):
                        for hh in range(2):
                            P.op("tensor", lambda e, k=k, m=m, hh=hh, pb=pb, t0=t0: e.matmul(
                                self.po[pb][:, 512 * hh:512 * (hh + 1)],
                                A[:, o_H + TH * m + 128 * (k - (t0 // 128)):o_H + TH * m + 128 * (k - (t0 // 128) + 1)],
                                A[:, o_wd + 1024 * m + 512 * hh:o_wd + 1024 * m + 512 * (hh + 1)],
                                start=(m == 0), stop=(m == MC - 1)),
                                waits=([(self.s_h, h_all), (self.s_cast, wd_cast_val[m])] +
                                       ([po_free[pb]] if m == 0 and hh == 0 else [])),
                                inc=(self.s_mm.bump() if (m == MC - 1 and hh == 1) else None))
                    vwd = self.s_mm.n
                    P.op("vector", lambda e, k=k, ei=ei, pb=pb: e.scalar_tensor_tensor(
                        self.R[:, 1024 * k:1024 * (k + 1)], self.po[pb][:],
                        self.Gh[:, 8 * k + ei:8 * k + ei + 1],
                        self.R[:, 1024 * k:1024 * (k + 1)], ALU.mult, ALU.add),
                        waits=[(self.s_mm, vwd), g_all], inc=self.s_r.bump())
                    po_free[pb] = self.s_r.now
                h_write_guard = self.s_mm.now
            wgwu_free = [(self.s_mm, hid_end)]
            wd_free = [self.s_mm.now]
        return self.s_r.now

    def p_att0(self, r_dep):
        P, c = self.P, self.cfg
        self.r_chunk_done = [None] * c.NCH
        A = self.ABF
        Tc, S, DC, NCH = c.Tc, c.S, c.DC, c.NCH
        NS = S // 128
        FQ = min(512, Tc)
        NFQ = Tc // FQ
        xt_ok = self.emit_ln([r_dep])
        guard_mm = self.s_mm.now
        guard_r = self.s_r.now
        guard_io = self.s_io.now

        a_cosQ, a_sinQ = 0, Tc
        a_cosK, a_sinK = 2 * Tc, 3 * Tc
        a_vtm = 4 * Tc
        a_kallc, a_kalls, a_vall = 5 * Tc, 5 * Tc + S, 5 * Tc + 2 * S
        a_PT = 5 * Tc + 3 * S
        a_OT = a_PT + 2 * 512
        a_vT = a_PT
        a_w = a_OT + Tc
        a_dbf = a_w + 4096
        a_bo0 = a_dbf + Tc
        assert a_bo0 + 1024 <= 46592
        self.a0_Dbf = A[0:1, a_dbf:a_dbf + Tc]
        self.a0_bobf = A[0:1, a_bo0:a_bo0 + 1024]
        self.a0_D = self.AFP[0:1, 0:Tc]
        self.a0_Dpp = self.AFP[:, 3968:3968 + NCH]

        stg = [self.AFP[:, 0:1408], self.AFP[:, 1408:2816]]
        wslots = []
        for i, dsrc in enumerate((self.da0_wqT, self.da0_wkT, self.da0_wvT, self.da0_woT)):
            si = i % 2
            w = [self.stage_free[si]] if self.stage_free[si] else []
            P.op("sync", lambda e, dsrc=dsrc, si=si: e.dma_start(
                out=stg[si][:, 0:1024].rearrange("p (j q) -> p j q", q=128),
                in_=dsrc[:].rearrange("(j p) q -> p j q", p=128)),
                waits=w + [guard_r], inc=self.s_stg[si].bump(16))
            wd = self.s_stg[si].now
            P.op("vector", lambda e, i=i, si=si: e.tensor_copy(
                A[:, a_w + 1024 * i:a_w + 1024 * (i + 1)], stg[si][:, 0:1024]),
                waits=[wd, guard_mm, guard_io], inc=self.s_cast.bump())
            self.stage_free[si] = self.s_cast.now
        w_ok = self.s_cast.now
        wq = lambda j: A[:, a_w + 128 * j:a_w + 128 * (j + 1)]
        wk = lambda j: A[:, a_w + 1024 + 128 * j:a_w + 1024 + 128 * (j + 1)]
        wv = lambda j: A[:, a_w + 2048 + 128 * j:a_w + 2048 + 128 * (j + 1)]
        woT = A[:, a_w + 3072:a_w + 4096]
        si = 0
        w = [self.stage_free[si]] if self.stage_free[si] else []
        P.op("sync", lambda e: e.dma_start(out=self.AFP[:, 0:5], in_=self.da0_vecs[:, 0:5]),
             waits=w, inc=self.s_stg[0].bump(16))
        vd = self.s_stg[0].now
        P.op("vector", lambda e: e.tensor_copy(self.a0v[:, 0:5], self.AFP[:, 0:5]),
             waits=[vd], inc=self.s_cast.bump())
        self.stage_free[0] = self.s_cast.now
        w = [self.stage_free[1]] if self.stage_free[1] else []
        P.op("sync", lambda e: e.dma_start(out=self.AFP[0:1, 1408:2432], in_=self.da0_bo),
             waits=w, inc=self.s_stg[1].bump(16))
        bod = self.s_stg[1].now
        P.op("vector", lambda e: e.tensor_copy(self.a0_bobf[:], self.AFP[0:1, 1408:2432]),
             waits=[bod], inc=self.s_cast.bump())
        self.stage_free[1] = self.s_cast.now
        sv = self.s_cast
        P.op("vector", lambda e: e.tensor_scalar(self.a0v[:, 5:6], self.a0v[:, 0:1],
                                                 1e-8, None, ALU.add),
             waits=[sv.now], inc=sv.bump())
        P.op("vector", lambda e: e.reciprocal(self.a0v[:, 5:6], self.a0v[:, 5:6]),
             waits=[sv.now], inc=sv.bump())
        P.op("vector", lambda e: e.tensor_scalar(self.a0v[:, 6:7], self.a0v[:, 5:6],
                                                 1.0 / TWO_PI, None, ALU.mult),
             waits=[sv.now], inc=sv.bump())
        P.op("vector", lambda e: e.scalar_tensor_tensor(self.a0v[:, 7:8], self.a0v[:, 2:3],
                                                        1.0, self.a0v[:, 5:6], ALU.mult, ALU.mult),
             waits=[sv.now], inc=sv.bump())
        P.op("vector", lambda e: e.tensor_tensor(self.a0v[:, 7:8], self.a0v[:, 7:8],
                                                 self.a0v[:, 1:2], ALU.add),
             waits=[sv.now], inc=sv.bump())
        P.op("vector", lambda e: e.tensor_scalar(self.a0v[:, 7:8], self.a0v[:, 7:8],
                                                 1.0 / TWO_PI, None, ALU.mult),
             waits=[sv.now], inc=sv.bump())
        P.op("vector", lambda e: e.scalar_tensor_tensor(self.a0v[:, 8:9], self.a0v[:, 3:4],
                                                        1.0, self.a0v[:, 5:6], ALU.mult, ALU.mult),
             waits=[sv.now], inc=sv.bump())
        P.op("vector", lambda e: e.tensor_tensor(self.a0v[:, 8:9], self.a0v[:, 8:9],
                                                 self.a0v[:, 1:2], ALU.add),
             waits=[sv.now], inc=sv.bump())
        P.op("vector", lambda e: e.tensor_scalar(self.a0v[:, 8:9], self.a0v[:, 8:9],
                                                 1.0 / TWO_PI, None, ALU.mult),
             waits=[sv.now], inc=sv.bump())
        vecs_ok = sv.now

        NT = Tc // 512 if Tc >= 512 else 1
        PW = min(512, Tc)
        pqk = [self.pg[0], self.pu[0]]
        pq_free = [None, None]
        ubuf = self.AFP[:, 2048:2048 + PW]
        vb = self.AFP[:, 2816:2816 + PW]
        rb = self.AFP[:, 3328:3328 + PW]

        def proj_one(wt, bias_col, kind):
            for tt in range(NT):
                pb = tt % 2
                bank = pqk[pb]
                for j in range(DC):
                    ww = [xt_ok, w_ok] if j == 0 else []
                    if j == 0 and pq_free[pb] is not None:
                        ww.append(pq_free[pb])
                    P.op("tensor", lambda e, wt=wt, j=j, tt=tt, bank=bank: e.matmul(
                        bank[:, 0:PW], wt(j),
                        self.XT[:, Tc * j + PW * tt:Tc * j + PW * (tt + 1)],
                        start=(j == 0), stop=(j == DC - 1)),
                        waits=ww, inc=(self.s_mm.bump() if j == DC - 1 else None))
                vmm = self.s_mm.n
                if kind != "v":
                    a_cos = a_cosQ if kind == "q" else a_cosK
                    a_sin = a_sinQ if kind == "q" else a_sinK
                    P.op("vector", lambda e, bank=bank, bc=bias_col: e.tensor_scalar(
                        ubuf, bank[:, 0:PW],
                        self.a0v[:, 6:7], self.a0v[:, bc:bc + 1], ALU.mult, ALU.add),
                        waits=[(self.s_mm, vmm), vecs_ok, self.s_cast.now,
                               self.s_ve.now, self.s_ev.now],
                        inc=self.s_ve.bump())
                    pq_free[pb] = self.s_ve.now
                    for (shift, dsto) in ((128.5, a_sin), (128.75, a_cos)):
                        P.op("vector", lambda e, shift=shift: e.tensor_scalar(
                            vb, ubuf, shift, None, ALU.add),
                            waits=[self.s_ve.now, self.s_ev.now], inc=self.s_ve.bump())
                        P.op("vector", lambda e: e.tensor_copy(self.ibuf[:, 0:PW], vb),
                             waits=[self.s_ve.now], inc=self.s_ve.bump())
                        P.op("vector", lambda e: e.tensor_copy(rb, self.ibuf[:, 0:PW]),
                             waits=[self.s_ve.now], inc=self.s_ve.bump())
                        P.op("vector", lambda e: e.tensor_sub(vb, vb, rb),
                             waits=[self.s_ve.now], inc=self.s_ve.bump())
                        P.op("vector", lambda e: e.tensor_single_scalar(
                            rb, vb, 0.0, ALU.is_lt),
                            waits=[self.s_ve.now], inc=self.s_ve.bump())
                        P.op("vector", lambda e: e.scalar_tensor_tensor(
                            vb, rb, -0.5, vb, ALU.add, ALU.add),
                            waits=[self.s_ve.now], inc=self.s_ve.bump())
                        P.op("scalar", lambda e, tt=tt, dsto=dsto: e.activation(
                            A[:, dsto + PW * tt:dsto + PW * (tt + 1)],
                            vb, AF.Sin, scale=TWO_PI),
                            waits=[self.s_ve.now], inc=self.s_ev.bump())
                else:
                    P.op("vector", lambda e, tt=tt, bank=bank: e.tensor_scalar(
                        A[:, a_vT + PW * tt:a_vT + PW * (tt + 1)], bank[:, 0:PW],
                        self.a0v[:, 4:5], None, ALU.add),
                        waits=[(self.s_mm, vmm), vecs_ok], inc=self.s_ve.bump())
                    pq_free[pb] = self.s_ve.now

        proj_one(wk, 8, "k")
        ktrig_ok = self.s_ev.now
        proj_one(wv, 4, "v")

        kdone = 0
        while kdone < NCH:
            grp = min(8, NCH - kdone)
            for i in range(grp):
                k = kdone + i
                P.op("tensor", lambda e, k=k, i=i: e.transpose(
                    self.pt[:, 128 * i:128 * (i + 1)],
                    A[:, a_vT + 128 * k:a_vT + 128 * (k + 1)], self.ident_bf[:]),
                    waits=([self.s_ve.now, self.s_xt.now] if i == 0 else []),
                    inc=(self.s_mm.bump() if i == grp - 1 else None))
            vt = self.s_mm.n
            P.op("vector", lambda e, kdone=kdone, grp=grp: e.tensor_copy(
                A[:, a_vtm + 128 * kdone:a_vtm + 128 * (kdone + grp)],
                self.pt[:, 0:128 * grp]),
                waits=[(self.s_mm, vt)], inc=self.s_xt.bump())
            kdone += grp
        vtm_ok = self.s_xt.now

        sio = self.s_io
        P.op("gpsimd", lambda e: e.dma_start(out=self.dcc0_in[0:128, :],
                                             in_=A[:, a_cosK:a_cosK + Tc]),
             waits=[ktrig_ok], inc=sio.bump(16))
        P.op("gpsimd", lambda e: e.dma_start(out=self.dcc0_in[128:256, :],
                                             in_=A[:, a_sinK:a_sinK + Tc]),
             inc=sio.bump(16))
        P.op("gpsimd", lambda e: e.dma_start(out=self.dcc0_in[256:384, :],
                                             in_=A[:, a_vtm:a_vtm + Tc]),
             waits=[vtm_ok], inc=sio.bump(16))
        P.op("gpsimd", lambda e: e.collective_compute(
            "AllGather", ALU.bypass, replica_groups=[[0, 1, 2, 3], [4, 5, 6, 7]],
            ins=[self.dcc0_in.opt()], outs=[self.dcc0_out.opt()]),
            waits=[sio.now], inc=self.s_cc.bump())
        cc_ok = self.s_cc.now

        proj_one(wq, 7, "q")
        trig_ok = self.s_ev.now
        for r in range(4):
            P.op("gpsimd", lambda e, r=r: e.dma_start(
                out=A[:, a_kallc + Tc * r:a_kallc + Tc * (r + 1)],
                in_=self.dcc0_out[384 * r:384 * r + 128, :]),
                waits=[cc_ok], inc=sio.bump(16))
            P.op("gpsimd", lambda e, r=r: e.dma_start(
                out=A[:, a_kalls + Tc * r:a_kalls + Tc * (r + 1)],
                in_=self.dcc0_out[384 * r + 128:384 * r + 256, :]),
                inc=sio.bump(16))
            P.op("gpsimd", lambda e, r=r: e.dma_start(
                out=A[:, a_vall + Tc * r:a_vall + Tc * (r + 1)],
                in_=self.dcc0_out[384 * r + 256:384 * r + 384, :]),
                inc=sio.bump(16))
        kv_ok = sio.now

        sbanks = [self.pg[0], self.pu[0]]
        sbank_free = [None, None]
        pt_free = [None, None]
        for qt in range(NFQ):
            pO = self.po[0][:, 512 * (qt % 2):512 * (qt % 2) + FQ]
            pD = self.po[1][0:1, 512 * (qt % 2):512 * (qt % 2) + FQ]
            for kc in range(NS):
                pb = kc % 2
                sb_ = sbanks[pb]
                ww = [kv_ok, trig_ok]
                if sbank_free[pb] is not None:
                    ww.append(sbank_free[pb])
                P.op("tensor", lambda e, kc=kc, qt=qt, sb_=sb_: e.matmul(
                    sb_[:, 0:FQ], A[:, a_kallc + 128 * kc:a_kallc + 128 * (kc + 1)],
                    A[:, a_cosQ + FQ * qt:a_cosQ + FQ * (qt + 1)],
                    start=True, stop=False), waits=ww)
                P.op("tensor", lambda e, kc=kc, qt=qt, sb_=sb_: e.matmul(
                    sb_[:, 0:FQ], A[:, a_kalls + 128 * kc:a_kalls + 128 * (kc + 1)],
                    A[:, a_sinQ + FQ * qt:a_sinQ + FQ * (qt + 1)],
                    start=False, stop=True), inc=self.s_mm.bump())
                vS = self.s_mm.n
                ptb = a_PT + 512 * (kc % 2)
                ww = [(self.s_mm, vS)]
                if pt_free[kc % 2] is not None:
                    ww.append(pt_free[kc % 2])
                P.op("scalar", lambda e, sb_=sb_, ptb=ptb: e.activation(
                    A[:, ptb:ptb + FQ], sb_[:, 0:FQ], AF.Exp,
                    bias=self.cpool[:, 2:3], scale=INV_SQRT_H0),
                    waits=ww + [self.consts_ready], inc=self.s_ev.bump())
                sbank_free[pb] = self.s_ev.now
                vP = self.s_ev.n
                P.op("tensor", lambda e, kc=kc, ptb=ptb, pO=pO: e.matmul(
                    pO, A[:, a_vall + 128 * kc:a_vall + 128 * (kc + 1)],
                    A[:, ptb:ptb + FQ], start=(kc == 0), stop=(kc == NS - 1)),
                    waits=[(self.s_ev, vP)])
                P.op("tensor", lambda e, kc=kc, ptb=ptb, pD=pD: e.matmul(
                    pD, self.ones_bf[:], A[:, ptb:ptb + FQ],
                    start=(kc == 0), stop=(kc == NS - 1)),
                    waits=([self.consts_ready] + ([self.s_ve.now] if kc == 0 else [])),
                    inc=self.s_mm.bump())
                pt_free[kc % 2] = self.s_mm.now
            vOD = self.s_mm.n
            P.op("scalar", lambda e, qt=qt, pO=pO: e.copy(
                A[:, a_OT + FQ * qt:a_OT + FQ * (qt + 1)], pO),
                waits=[(self.s_mm, vOD)], inc=self.s_ev.bump())
            P.op("vector", lambda e, qt=qt, pD=pD: e.tensor_copy(
                self.a0_D[0:1, FQ * qt:FQ * (qt + 1)], pD),
                waits=[(self.s_mm, vOD)], inc=self.s_ve.bump())
        ot_ok = self.s_ev.now
        d_ok = self.s_ve.now

        P.op("vector", lambda e: e.tensor_copy(self.a0_Dbf[:], self.a0_D[:]),
             waits=[d_ok], inc=self.s_ve.bump())
        P.op("vector", lambda e: e.reciprocal(self.a0_D[:], self.a0_D[:]),
             waits=[d_ok, self.s_ve.now], inc=self.s_ve.bump())
        drec_ok = self.s_ve.n
        for k in range(NCH):
            P.op("tensor", lambda e, k=k: e.matmul(
                self.po[1][:, k:k + 1], self.a0_D[0:1, 128 * k:128 * (k + 1)],
                self.cpool[0:1, 4:5], start=True, stop=True),
                waits=[(self.s_ve, drec_ok), self.consts_ready] if k == 0 else [],
                inc=(self.s_mm.bump() if k == NCH - 1 else None))
        vdt = self.s_mm.n
        P.op("vector", lambda e: e.tensor_copy(self.a0_Dpp[:], self.po[1][:, 0:NCH]),
             waits=[(self.s_mm, vdt)], inc=self.s_ve.bump())
        dpp_ok = self.s_ve.now

        po_free2 = [None, None]
        for k in range(NCH):
            pb = k % 2
            bank = self.po[pb]
            ww = [ot_ok, self.s_ve.now, self.s_cast.now]
            if po_free2[pb] is not None:
                ww.append(po_free2[pb])
            for hh in range(2):
                P.op("tensor", lambda e, k=k, hh=hh, bank=bank: e.matmul(
                    bank[:, 512 * hh:512 * (hh + 1)],
                    self.a0_Dbf[0:1, 128 * k:128 * (k + 1)],
                    self.a0_bobf[0:1, 512 * hh:512 * (hh + 1)],
                    start=True, stop=False), waits=(ww if hh == 0 else []))
                P.op("tensor", lambda e, k=k, hh=hh, bank=bank: e.matmul(
                    bank[:, 512 * hh:512 * (hh + 1)],
                    A[:, a_OT + 128 * k:a_OT + 128 * (k + 1)],
                    woT[:, 512 * hh:512 * (hh + 1)],
                    start=False, stop=True),
                    inc=(self.s_mm.bump() if hh == 1 else None))
            vmm = self.s_mm.n
            P.op("vector", lambda e, k=k, bank=bank: e.scalar_tensor_tensor(
                self.R[:, 1024 * k:1024 * (k + 1)], bank[:],
                self.a0_Dpp[:, k:k + 1], self.R[:, 1024 * k:1024 * (k + 1)],
                ALU.mult, ALU.add),
                waits=[(self.s_mm, vmm), dpp_ok], inc=self.s_r.bump())
            po_free2[pb] = self.s_r.now
        self.stage_free[0] = self.s_r.now
        self.stage_free[1] = self.s_r.now
        return self.s_r.now

    def p_att1(self, r_dep):
        P, c = self.P, self.cfg
        self.r_chunk_done = [None] * c.NCH
        A = self.ABF
        Tc, DC, NCH, NP = c.Tc, c.DC, c.NCH, c.NPAIR
        NB = c.NBLK
        TCH = Tc + 128
        NT = max(1, Tc // 512)
        PW = min(512, Tc)
        SG = [[0, 1, 2, 3], [4, 5, 6, 7]]
        xt_ok = self.emit_ln([r_dep])
        guard_mm = self.s_mm.now
        guard_r = self.s_r.now
        guard_io = self.s_io.now
        guard_sp = self.s_stgu.now
        sv, se, sm = self.s_ve, self.s_ev, self.s_mm

        a_kh = 0
        a_vst = a_kh + 8 * TCH
        a_w = a_vst + 8192
        a_gq = a_w + 8192
        a_gk = a_gq + 1024
        a_gv = a_gk + 1024
        a_vedge = a_gv + 1024
        a_qp = a_vedge + 1024
        a_pl = a_qp + 1024
        a_pgl = a_pl + 1536
        a_pt = a_pgl + 1024
        a_ctxp = a_pt + 3072
        a_rcn = a_ctxp + 1024
        assert a_rcn + 1024 <= 46592, a_rcn + 1024
        a_rpt = a_pt
        a_qst = a_pt

        stg = [self.AFP[:, 0:1408], self.AFP[:, 1408:2816]]
        self.a1_ctxred = self.AFP[:, 2816:3840]
        self.a1_drow = self.AFP[0:1, 0:2048]
        wsb = lambda j: A[:, a_w + 1024 * j:a_w + 1024 * (j + 1)]

        si = 0
        w = [x for x in [self.stage_free[0]] if x]
        P.op("sync", lambda e: e.dma_start(out=self.AFP[:, 0:24], in_=self.da1_bqkv),
             waits=w, inc=self.s_stg[0].bump(16))
        P.op("vector", lambda e: e.tensor_copy(self.a1_bqkv[:], self.AFP[:, 0:24]),
             waits=[self.s_stg[0].now], inc=self.s_cast.bump())
        self.stage_free[0] = self.s_cast.now
        self.a1_bobf = A[0:1, a_gq:a_gq + 1024]
        P.op("sync", lambda e: e.dma_start(out=self.a1_rsel[:], in_=self.da1_rolesel),
             inc=self.s_msk.bump(16))
        small_ok = self.s_msk.now
        bias_ok = self.s_cast.now
        mrep_ok = small_ok

        def load_w(dsrc, ready_evt):
            for j in range(DC):
                sj = j % 2
                w = [x for x in [self.stage_free[sj]] if x]
                P.op("sync", lambda e, j=j, sj=sj, dsrc=dsrc: e.dma_start(
                    out=stg[sj][:, 0:1024], in_=dsrc[128 * j:128 * (j + 1), :]),
                    waits=w, inc=self.s_stg[sj].bump(16))
                wd = self.s_stg[sj].now
                P.op("vector", lambda e, j=j, sj=sj: e.tensor_copy(wsb(j), stg[sj][:, 0:1024]),
                     waits=[wd] + ready_evt, inc=self.s_cast.bump())
                self.stage_free[sj] = self.s_cast.now
            return self.s_cast.now

        pq_free = [None, None]
        pqk = [self.pg[0], self.pu[0]]

        def proj_tiles(cast_ok, bias_col, emit_out):
            for oc in range(DC):
                for tt in range(NT):
                    pb = (oc * NT + tt) % 2
                    bank = pqk[pb]
                    for j in range(DC):
                        ww = [xt_ok, cast_ok] if j == 0 else []
                        if j == 0 and pq_free[pb] is not None:
                            ww.append(pq_free[pb])
                        P.op("tensor", lambda e, oc=oc, tt=tt, j=j, bank=bank: e.matmul(
                            bank[:, 0:PW], wsb(j)[:, 128 * oc:128 * (oc + 1)],
                            self.XT[:, Tc * j + PW * tt:Tc * j + PW * (tt + 1)],
                            start=(j == 0), stop=(j == DC - 1)),
                            waits=ww, inc=(sm.bump() if j == DC - 1 else None))
                    emit_out(oc, tt, bank, sm.n)

        kcast = load_w(self.da1_wkT, [guard_mm, guard_io])
        def k_out(oc, tt, bank, vmm):
            P.op("vector", lambda e, oc=oc, tt=tt, bank=bank: e.tensor_scalar(
                A[:, a_kh + TCH * oc + PW * tt:a_kh + TCH * oc + PW * (tt + 1)],
                bank[:, 0:PW], self.a1_bqkv[:, 8 + oc:9 + oc], None, ALU.add),
                waits=[(sm, vmm), bias_ok, guard_io, guard_sp], inc=sv.bump())
            pq_free[(oc * NT + tt) % 2] = sv.now
        proj_tiles(kcast, 8, k_out)
        k_ok = sv.now

        vcast = load_w(self.da1_wvT, [self.s_mm.now])
        vT = lambda oc: A[:, a_vst + PW * oc:a_vst + PW * (oc + 1)]
        for tt in range(NT):
            for oc in range(DC):
                pb = (tt * DC + oc) % 2
                bank = pqk[pb]
                for j in range(DC):
                    ww = [xt_ok, vcast] if j == 0 else []
                    if j == 0 and pq_free[pb] is not None:
                        ww.append(pq_free[pb])
                    P.op("tensor", lambda e, oc=oc, tt=tt, j=j, bank=bank: e.matmul(
                        bank[:, 0:PW], wsb(j)[:, 128 * oc:128 * (oc + 1)],
                        self.XT[:, Tc * j + PW * tt:Tc * j + PW * (tt + 1)],
                        start=(j == 0), stop=(j == DC - 1)),
                        waits=ww, inc=(sm.bump() if j == DC - 1 else None))
                vmm = sm.n
                P.op("vector", lambda e, oc=oc, bank=bank: e.tensor_scalar(
                    vT(oc), bank[:, 0:PW], self.a1_bqkv[:, 16 + oc:17 + oc], None, ALU.add),
                    waits=[(sm, vmm), bias_ok, sm.now, guard_io], inc=sv.bump())
                pq_free[pb] = sv.now
            vT_ok = sv.n
            for i in range(PW // 128):
                for oc in range(DC):
                    P.op("tensor", lambda e, i=i, oc=oc: e.transpose(
                        self.pt[:, 128 * oc:128 * (oc + 1)],
                        vT(oc)[:, 128 * i:128 * (i + 1)], self.ident_bf[:]),
                        waits=([(sv, vT_ok), self.s_xt.now] if oc == 0 else []),
                        inc=(sm.bump() if oc == DC - 1 else None))
                vt2 = sm.n
                P.op("vector", lambda e: e.tensor_copy(A[:, a_qst:a_qst + 1024], self.pt[:]),
                     waits=[(sm, vt2), self.s_io.now], inc=self.s_xt.bump())
                kchunk = tt * (PW // 128) + i
                P.op("gpsimd", lambda e, kchunk=kchunk: e.dma_start(
                    out=self.dv1[128 * kchunk:128 * (kchunk + 1), :],
                    in_=A[:, a_qst:a_qst + 1024]),
                    waits=[self.s_xt.now], inc=self.s_io.bump(16))
        vspill_ok = self.s_io.now

        qcast = load_w(self.da1_wqT, [self.s_mm.now])
        def q_out(oc, tt, bank, vmm):
            P.op("vector", lambda e, oc=oc, tt=tt, bank=bank: e.tensor_scalar(
                A[:, a_qst + 1024:a_qst + 1024 + PW], bank[:, 0:PW],
                self.a1_bqkv[:, 0 + oc:1 + oc], None, ALU.add),
                waits=[(sm, vmm), bias_ok, self.s_io.now], inc=sv.bump())
            pq_free[(oc * NT + tt) % 2] = sv.now
            P.op("gpsimd", lambda e, oc=oc, tt=tt: e.dma_start(
                out=self.dq1[:, Tc * oc + PW * tt:Tc * oc + PW * (tt + 1)],
                in_=A[:, a_qst + 1024:a_qst + 1024 + PW]),
                waits=[sv.now], inc=self.s_io.bump(16))
        proj_tiles(qcast, 0, q_out)
        qspill_ok = self.s_io.now
        if ATT1_STOP == "proj":
            return self.s_r.now

        kh3 = lambda off: A[:, a_kh + off:a_kh + off + 8 * TCH].rearrange(
            "p (j t) -> p j t", t=TCH)[:, :, 0:64]
        P.op("gpsimd", lambda e: e.dma_start(
            out=self.dcc1_in[:, 0:512].rearrange("p (j t) -> p j t", t=64),
            in_=A[:, a_kh:a_kh + 8 * TCH].rearrange("p (j t) -> p j t", t=TCH)[:, :, 0:64]),
            waits=[k_ok, vspill_ok, qspill_ok], inc=self.s_io.bump(16))
        P.op("gpsimd", lambda e: e.dma_start(
            out=self.dcc1_in[:, 512:1024].rearrange("p (j t) -> p j t", t=64),
            in_=A[:, a_kh:a_kh + 8 * TCH].rearrange("p (j t) -> p j t", t=TCH)[:, :, Tc - 64:Tc]),
            inc=self.s_io.bump(16))
        P.op("gpsimd", lambda e: e.dma_start(
            out=self.dcc1_in[:, 1024:1536].rearrange("p (j t) -> p j t", t=64),
            in_=self.dq1[:].rearrange("p (j t) -> p j t", t=Tc)[:, :, 0:64]),
            waits=[qspill_ok], inc=self.s_io.bump(16))
        P.op("gpsimd", lambda e: e.dma_start(
            out=self.dcc1_in[:, 1536:2048].rearrange("p (j t) -> p j t", t=64),
            in_=self.dq1[:].rearrange("p (j t) -> p j t", t=Tc)[:, :, Tc - 64:Tc]),
            inc=self.s_io.bump(16))
        P.op("gpsimd", lambda e: e.dma_start(
            out=self.dcc1_in[0:64, 2048:3072], in_=self.dv1[0:64, :]),
            waits=[vspill_ok], inc=self.s_io.bump(16))
        P.op("gpsimd", lambda e: e.dma_start(
            out=self.dcc1_in[64:128, 2048:3072], in_=self.dv1[Tc - 64:Tc, :]),
            inc=self.s_io.bump(16))
        P.op("gpsimd", lambda e: e.collective_compute(
            "AllGather", ALU.bypass, replica_groups=SG,
            ins=[self.dcc1_in.opt()], outs=[self.dcc1_out.opt()]),
            waits=[self.s_io.now], inc=self.s_cc.bump())
        ag_ok = self.s_cc.now

        P.op("gpsimd", lambda e: e.dma_start(
            out=A[:, a_gq:a_gq + 1024].rearrange("p (j t) -> p j t", t=128)[:, :, 0:64],
            in_=self.dcc1_out[0:128, 1024:1536].rearrange("p (j t) -> p j t", t=64)),
            waits=[ag_ok], inc=self.s_io.bump(16))
        P.op("gpsimd", lambda e: e.dma_start(
            out=A[:, a_gq:a_gq + 1024].rearrange("p (j t) -> p j t", t=128)[:, :, 64:128],
            in_=self.dcc1_out[128:256, 1536:2048].rearrange("p (j t) -> p j t", t=64)),
            inc=self.s_io.bump(16))
        P.op("gpsimd", lambda e: e.dma_start(
            out=A[:, a_gk:a_gk + 1024].rearrange("p (j t) -> p j t", t=128)[:, :, 0:64],
            in_=self.dcc1_out[0:128, 0:512].rearrange("p (j t) -> p j t", t=64)),
            inc=self.s_io.bump(16))
        P.op("gpsimd", lambda e: e.dma_start(
            out=A[:, a_gk:a_gk + 1024].rearrange("p (j t) -> p j t", t=128)[:, :, 64:128],
            in_=self.dcc1_out[128:256, 512:1024].rearrange("p (j t) -> p j t", t=64)),
            inc=self.s_io.bump(16))
        P.op("gpsimd", lambda e: e.dma_start(
            out=A[0:64, a_gv:a_gv + 1024], in_=self.dcc1_out[0:64, 2048:3072]),
            inc=self.s_io.bump(16))
        P.op("gpsimd", lambda e: e.dma_start(
            out=A[64:128, a_gv:a_gv + 1024], in_=self.dcc1_out[128 + 64:256, 2048:3072]),
            inc=self.s_io.bump(16))
        gfix_ok = self.s_io.now
        if ATT1_STOP == "ag":
            return self.s_r.now

        scr = A[:, a_pt:a_pt + 512]
        scr2 = A[:, a_pt + 512:a_pt + 1024]
        kh_slot = lambda off: A[:, a_kh:a_kh + 8 * TCH].rearrange(
            "p (j t) -> p j t", t=TCH)[:, :, Tc + off:Tc + off + 64]
        def masked_fill(dst3, piece_col, sel_base, name):
            P.op("vector", lambda e, dst3=dst3: e.memset(dst3, 0.0),
                 waits=[k_ok, sv.now], inc=sv.bump())
            for r in range(4):
                P.op("gpsimd", lambda e, r=r, piece_col=piece_col: e.dma_start(
                    out=scr.rearrange("p (j t) -> p j t", t=64),
                    in_=self.dcc1_out[128 * r:128 * (r + 1),
                                      piece_col:piece_col + 512].rearrange(
                                          "p (j t) -> p j t", t=64)),
                    waits=[ag_ok, sv.now, self.s_io.now], inc=self.s_io.bump(16))
                P.op("vector", lambda e, r=r, sel_base=sel_base: e.tensor_scalar(
                    scr2, scr, self.a1_rsel[:, sel_base + r:sel_base + r + 1],
                    None, ALU.mult),
                    waits=[self.s_io.now, small_ok], inc=sv.bump())
                P.op("vector", lambda e, dst3=dst3: e.tensor_tensor(
                    dst3, dst3, scr2.rearrange("p (j t) -> p j t", t=64), ALU.add),
                    waits=[sv.now], inc=sv.bump())
        masked_fill(kh_slot(0), 512, 0, "kprev")
        masked_fill(kh_slot(64), 0, 4, "knext")
        def masked_fill_v(p0, rowr, sel_base):
            dst = A[p0:p0 + 64, a_vedge:a_vedge + 1024]
            P.op("vector", lambda e, dst=dst: e.memset(dst, 0.0),
                 waits=[sv.now], inc=sv.bump())
            for r in range(4):
                P.op("gpsimd", lambda e, r=r, rowr=rowr, p0=p0: e.dma_start(
                    out=scr[p0:p0 + 64, 0:512], in_=self.dcc1_out[
                        128 * r + rowr:128 * r + rowr + 64, 2048:2560]),
                    waits=[ag_ok, sv.now, self.s_io.now], inc=self.s_io.bump(16))
                P.op("gpsimd", lambda e, r=r, rowr=rowr, p0=p0: e.dma_start(
                    out=scr2[p0:p0 + 64, 0:512], in_=self.dcc1_out[
                        128 * r + rowr:128 * r + rowr + 64, 2560:3072]),
                    inc=self.s_io.bump(16))
                P.op("vector", lambda e, r=r, sel_base=sel_base, dst=dst, p0=p0: e.scalar_tensor_tensor(
                    dst[:, 0:512], scr[p0:p0 + 64, 0:512],
                    self.a1_rsel[p0:p0 + 64, sel_base + r:sel_base + r + 1],
                    dst[:, 0:512], ALU.mult, ALU.add),
                    waits=[self.s_io.now, small_ok], inc=sv.bump())
                P.op("vector", lambda e, r=r, sel_base=sel_base, dst=dst, p0=p0: e.scalar_tensor_tensor(
                    dst[:, 512:1024], scr2[p0:p0 + 64, 0:512],
                    self.a1_rsel[p0:p0 + 64, sel_base + r:sel_base + r + 1],
                    dst[:, 512:1024], ALU.mult, ALU.add),
                    waits=[sv.now], inc=sv.bump())
        masked_fill_v(64, 64, 0)
        masked_fill_v(0, 0, 4)
        halo_ok = sv.now

        wocast = load_w(self.da1_woT, [self.s_mm.now])

        P.op("vector", lambda e: e.memset(self.a1_drow[:], 0.0),
             waits=[sv.now, self.s_cast.now], inc=sv.bump())
        vgrp = [A[:, a_vst:a_vst + 4096], A[:, a_vst + 4096:a_vst + 8192]]
        vgrp_free = [None, None]
        rpt_free = [None, None]
        pS_free = [None, None]
        NKG = max(1, NCH // 4)
        KGW = min(4, NCH)
        row_ctx_started = False
        for kcg in range(NKG):
            gi = kcg % 2
            ww = [vspill_ok, self.s_cast.now]
            if vgrp_free[gi] is not None:
                ww.append(vgrp_free[gi])
            for i in range(KGW):
                P.op("sync", lambda e, kcg=kcg, i=i, gi=gi: e.dma_start(
                    out=vgrp[gi][:, 1024 * i:1024 * (i + 1)],
                    in_=self.dv1[128 * (KGW * kcg + i):128 * (KGW * kcg + i + 1), :]),
                    waits=(ww if i == 0 else []), inc=self.s_sl[gi].bump(16))
            vg_ok = self.s_sl[gi].now
            for h in range(16):
                hp, hc = h % 2, h // 2
                pb = (kcg * 16 + h) % 2
                bank = pqk[pb]
                for i in range(KGW):
                    ww = [gfix_ok, vg_ok, k_ok]
                    if i == 0 and pS_free[pb] is not None:
                        ww.append(pS_free[pb])
                    P.op("tensor", lambda e, kcg=kcg, h=h, i=i, hp=hp, hc=hc, bank=bank: e.matmul(
                        bank[:, 128 * i:128 * (i + 1)],
                        A[64 * hp:64 * hp + 64,
                          a_kh + TCH * hc + 128 * (KGW * kcg + i):
                          a_kh + TCH * hc + 128 * (KGW * kcg + i + 1)],
                        A[64 * hp:64 * hp + 64, a_gq + 128 * hc:a_gq + 128 * (hc + 1)],
                        start=True, stop=True),
                        waits=(ww if i == 0 else []),
                        inc=(sm.bump() if i == KGW - 1 else None))
                vS = sm.n
                rp = (kcg * 16 + h) % 2
                ww2 = [(sm, vS), self.consts_ready]
                if rpt_free[rp] is not None:
                    ww2.append(rpt_free[rp])
                P.op("scalar", lambda e, rp=rp, bank=bank: e.activation(
                    A[:, a_rpt + 512 * rp:a_rpt + 512 * rp + 128 * KGW],
                    bank[:, 0:128 * KGW], AF.Exp,
                    bias=self.cpool[:, 3:4], scale=L1_SCALE),
                    waits=ww2, inc=se.bump())
                pS_free[pb] = se.now
                vP = se.n
                for i in range(KGW):
                    P.op("tensor", lambda e, kcg=kcg, h=h, i=i, hp=hp, hc=hc, rp=rp, gi=gi: e.matmul(
                        self.po[0][64 * hp:64 * hp + 64, 128 * hc:128 * (hc + 1)],
                        vgrp[gi][:, 1024 * i + 64 * h:1024 * i + 64 * h + 64],
                        A[:, a_rpt + 512 * rp + 128 * i:a_rpt + 512 * rp + 128 * (i + 1)],
                        start=(kcg == 0 and i == 0), stop=(kcg == NKG - 1 and i == KGW - 1)),
                        waits=[(se, vP)])
                    P.op("tensor", lambda e, i=i, rp=rp: e.matmul(
                        self.P1[0:1, 1024 + 128 * 0:1024 + 128 * 1],
                        self.ones_bf[:],
                        A[:, a_rpt + 512 * rp + 128 * i:a_rpt + 512 * rp + 128 * (i + 1)],
                        start=(i == 0), stop=(i == KGW - 1)),
                        waits=[self.consts_ready] + ([sv.now] if i == 0 else []),
                        inc=(sm.bump() if i == KGW - 1 else None))
                vD = sm.n
                rpt_free[rp] = sm.now
                vgrp_free[gi] = sm.now
                P.op("vector", lambda e, h=h: e.tensor_tensor(
                    self.a1_drow[0:1, 128 * h:128 * (h + 1)],
                    self.a1_drow[0:1, 128 * h:128 * (h + 1)],
                    self.P1[0:1, 1024:1152], ALU.add),
                    waits=[(sm, vD)], inc=sv.bump())
        vrowmm = sm.n
        drow_ok = sv.now
        if ATT1_STOP == "row":
            return self.s_r.now
        P.op("vector", lambda e: e.tensor_copy(self.a1_ctxred[:], self.po[0][:]),
             waits=[(sm, vrowmm)], inc=sv.bump())
        ctxp_ok = sv.now
        P.op("gpsimd", lambda e: e.dma_start(out=self.dcc1r_in[0:128, :],
                                             in_=self.a1_ctxred[:]),
             waits=[ctxp_ok, self.s_io.now], inc=self.s_io.bump(16))
        P.op("gpsimd", lambda e: e.dma_start(
            out=self.dcc1r_in[128:130, :],
            in_=self.a1_drow[0:1, :].rearrange("p (a b) -> p a b", b=1024)),
            waits=[drow_ok], inc=self.s_io.bump(16))
        P.op("gpsimd", lambda e: e.collective_compute(
            "AllReduce", ALU.add, replica_groups=SG,
            ins=[self.dcc1r_in.opt()], outs=[self.dcc1r_out.opt()]),
            waits=[self.s_io.now], inc=self.s_cc.bump())
        rr_ok = self.s_cc.now
        P.op("gpsimd", lambda e: e.dma_start(out=self.a1_ctxred[:],
                                             in_=self.dcc1r_out[0:128, :]),
             waits=[rr_ok], inc=self.s_io.bump(16))
        P.op("gpsimd", lambda e: e.dma_start(
            out=self.a1_drow[0:1, :].rearrange("p (a b) -> p a b", b=1024),
            in_=self.dcc1r_out[128:130, :]),
            inc=self.s_io.bump(16))
        rred_ok = self.s_io.now
        P.op("gpsimd", lambda e: e.dma_start(out=self.a1_bobf, in_=self.da1_bo),
             waits=[(sm, vrowmm)], inc=self.s_stgu.bump(16))
        bias_ok = self.s_stgu.now

        def emit_normalize(po0_guard):
            P.op("vector", lambda e: e.reciprocal(self.a1_drow[:], self.a1_drow[:]),
                 waits=[rred_ok, sv.now], inc=sv.bump())
            for j in range(DC):
                for par in range(2):
                    h = 2 * j + par
                    P.op("tensor", lambda e, h=h: e.matmul(
                        self.po[0][:, 0:128], self.ones_row_f32[:],
                        self.a1_drow[0:1, 128 * h:128 * (h + 1)], start=True, stop=True),
                        waits=([sv.now, self.consts_ready, ctxp_ok, po0_guard]
                               if (j == 0 and par == 0) else [sv.now]),
                        inc=sm.bump())
                    P.op("vector", lambda e, j=j, par=par: e.tensor_tensor(
                        A[64 * par:64 * par + 64, a_rcn + 128 * j:a_rcn + 128 * (j + 1)],
                        self.a1_ctxred[64 * par:64 * par + 64, 128 * j:128 * (j + 1)],
                        self.po[0][64 * par:64 * par + 64, 0:128], ALU.mult),
                        waits=[(sm, sm.n)], inc=sv.bump())
            return sv.now

        row_done_se = se.now
        row_done_sv = sv.now
        row_ctxevac = ctxp_ok

        qp_free = [None, None]
        vw_free = [(sm, vrowmm), (sm, vrowmm)]
        ctx_free = None
        sg_free = None
        sl_free = None
        pt_free = None
        po1_r_free = None
        NPrun = NP if not (isinstance(ATT1_STOP, str) and ATT1_STOP.startswith("p")
                           and ATT1_STOP[1:].isdigit()) else int(ATT1_STOP[1:])
        row_sl0 = self.s_sl[0].now
        pair_order = (list(range(1, NP - 1)) + [0, NP - 1]) if NP > 2 else list(range(NP))
        rcn_ok = None
        for p_idx, p in enumerate(pair_order[:NPrun]):
            pi = p_idx % 2
            if p in (0, NP - 1) and rcn_ok is None:
                rcn_ok = emit_normalize(sg_free if sg_free is not None else ctxp_ok)
                sg_free = rcn_ok
            ww = [qspill_ok]
            ww.append(qp_free[0] if qp_free[0] is not None else row_sl0)
            P.op("sync", lambda e, p=p: e.dma_start(
                out=A[:, a_qp:a_qp + 1024].rearrange("p (j t) -> p j t", t=128),
                in_=self.dq1[:].rearrange("p (j t) -> p j t", t=Tc)[:, :, 128 * p:128 * (p + 1)]),
                waits=ww, inc=self.s_sl[0].bump(16))
            qp_ok = self.s_sl[0].now
            qp_ap = A[:, a_qp:a_qp + 1024]
            vwb = a_vst + 4096 * pi
            ww = [vspill_ok, halo_ok]
            if vw_free[pi] is not None:
                ww.append(vw_free[pi])
            def vdma(dst_lo, dst_n, src_row, first):
                P.op("sync", lambda e, dst_lo=dst_lo, dst_n=dst_n, src_row=src_row: e.dma_start(
                    out=A[dst_lo[0]:dst_lo[0] + dst_n, dst_lo[1]:dst_lo[1] + 1024],
                    in_=self.dv1[src_row:src_row + dst_n, :]),
                    waits=(ww if first else []), inc=self.s_sl[2 + pi].bump(16))
            first = True
            if p == 0:
                P.op("sync", lambda e, vwb=vwb: e.dma_start(
                    out=A[0:64, vwb:vwb + 1024], in_=A[64:128, a_vedge:a_vedge + 1024]),
                    waits=ww, inc=self.s_sl[2 + pi].bump(16))
                first = False
                vdma((64, vwb), 64, 0, first)
            else:
                vdma((0, vwb), 128, 128 * p - 64, first)
                first = False
            vdma((0, vwb + 1024), 64, 128 * p + 64, False)
            vdma((0, vwb + 2048), 128, 128 * p, False)
            if p != NP - 1:
                vdma((0, vwb + 3072), 64, 128 * p + 128, False)
            vw_ok = self.s_sl[2 + pi].now
            tA0 = A[:, vwb:vwb + 1024]
            tA1 = A[0:64, vwb + 1024:vwb + 2048]
            tB0 = A[:, vwb + 2048:vwb + 3072]
            tB1 = (A[0:64, a_vedge:a_vedge + 1024] if p == NP - 1
                   else A[0:64, vwb + 3072:vwb + 3072 + 1024])
            for hg in range(2):
                swl = [qp_ok, k_ok, halo_ok]
                if sl_free is not None:
                    swl.append(sl_free)
                else:
                    swl += [row_done_se, row_done_sv]
                first_sl = True
                for m in range(DC):
                    for bp in range(2):
                        b = 2 * p + bp
                        lhs = qp_ap[64 * hg:64 * hg + 64, 128 * m + 64 * bp:128 * m + 64 * bp + 64]
                        if b == 0:
                            pieces = [(Tc, 64, 0), (0, 64, 64), (64, 64, 128)]
                        elif b == NB - 1:
                            pieces = [(64 * (b - 1), 128, 0), (Tc + 64, 64, 128)]
                        else:
                            pieces = [(64 * (b - 1), 128, 0), (64 * (b - 1) + 128, 64, 128)]
                        for (src, n, dsto) in pieces:
                            if dsto < 128:
                                outap = self.P1[64 * bp:64 * bp + 64,
                                                128 * m + dsto:128 * m + dsto + n]
                            else:
                                outap = self.P1[64 * bp:64 * bp + 64,
                                                1024 + 64 * m:1024 + 64 * m + 64]
                            P.op("tensor", lambda e, src=src, n=n, lhs=lhs, hg=hg,
                                 m=m, outap=outap: e.matmul(
                                     outap, lhs,
                                     A[64 * hg:64 * hg + 64,
                                       a_kh + TCH * m + src:a_kh + TCH * m + src + n],
                                     start=True, stop=True),
                                 waits=(swl if first_sl else []))
                            first_sl = False
                        P.op("tensor", lambda e, m=m, bp=bp, lhs=lhs, hg=hg: e.matmul(
                            self.po[0][64 * bp:64 * bp + 64, 128 * m:128 * (m + 1)],
                            lhs,
                            A[64 * hg:64 * hg + 64, a_gk + 128 * m:a_gk + 128 * (m + 1)],
                            start=True, stop=True),
                            waits=(([gfix_ok, sg_free] if sg_free else
                                    [gfix_ok, ctxp_ok])
                                   if (m == 0 and bp == 0) else []),
                            inc=(sm.bump() if (m == DC - 1 and bp == 1) else None))
                vSg = sm.n
                vSl = sm.n
                if PAIR_STAGE == "sl":
                    sl_free = (sm, vSl); sg_free = (sm, vSg); continue
                ww = [(sm, vSl), self.consts_ready]
                if pt_free is not None:
                    ww.append(pt_free)
                P.op("scalar", lambda e: e.activation(
                    A[:, a_pl:a_pl + 1536], self.P1[:, 0:1536], AF.Exp,
                    bias=self.cpool[:, 3:4], scale=L1_SCALE),
                    waits=ww, inc=se.bump())
                sl_free = se.now
                P.op("scalar", lambda e: e.activation(
                    A[:, a_pgl:a_pgl + 1024], self.po[0][:], AF.Exp,
                    bias=self.cpool[:, 3:4], scale=L1_SCALE),
                    waits=[(sm, vSg)], inc=se.bump())
                sg_free = se.now
                vexp = se.n
                if PAIR_STAGE == "exp":
                    continue
                P.op("vector", lambda e: e.tensor_reduce(
                    self.a1_dl[:], A[:, a_pl:a_pl + 1024].rearrange("p (m t) -> p m t", t=128),
                    AXX, ALU.add),
                    waits=[(se, vexp)], inc=sv.bump())
                P.op("vector", lambda e: e.tensor_reduce(
                    self.gsc[:, 0:8],
                    A[:, a_pl + 1024:a_pl + 1536].rearrange("p (m t) -> p m t", t=64),
                    AXX, ALU.add),
                    waits=[(se, vexp)], inc=sv.bump())
                P.op("vector", lambda e: e.tensor_add(self.a1_dl[:], self.a1_dl[:],
                                                      self.gsc[:, 0:8]),
                     waits=[sv.now], inc=sv.bump())
                P.op("vector", lambda e: e.tensor_reduce(
                    self.a1_dg[:], A[:, a_pgl:a_pgl + 1024].rearrange("p (m t) -> p m t", t=128),
                    AXX, ALU.add),
                    waits=[(se, vexp)], inc=sv.bump())
                P.op("vector", lambda e: e.reciprocal(self.a1_dl[:], self.a1_dl[:]),
                     waits=[sv.now], inc=sv.bump())
                P.op("vector", lambda e: e.reciprocal(self.a1_dg[:], self.a1_dg[:]),
                     waits=[sv.now], inc=sv.bump())
                vrec = sv.n
                for m in range(DC):
                    P.op("vector", lambda e, m=m: e.tensor_scalar(
                        A[:, a_pl + 128 * m:a_pl + 128 * (m + 1)],
                        A[:, a_pl + 128 * m:a_pl + 128 * (m + 1)],
                        self.a1_dl[:, m:m + 1], None, ALU.mult),
                        waits=[(sv, vrec)], inc=sv.bump())
                    P.op("vector", lambda e, m=m: e.tensor_scalar(
                        A[:, a_pl + 1024 + 64 * m:a_pl + 1024 + 64 * (m + 1)],
                        A[:, a_pl + 1024 + 64 * m:a_pl + 1024 + 64 * (m + 1)],
                        self.a1_dl[:, m:m + 1], None, ALU.mult),
                        waits=[(sv, vrec)], inc=sv.bump())
                    P.op("vector", lambda e, m=m: e.tensor_scalar(
                        A[:, a_pgl + 128 * m:a_pgl + 128 * (m + 1)],
                        A[:, a_pgl + 128 * m:a_pgl + 128 * (m + 1)],
                        self.a1_dg[:, m:m + 1], None, ALU.mult),
                        waits=[(sv, vrec)], inc=sv.bump())
                vnorm = sv.n
                if PAIR_STAGE == "norm":
                    continue
                ptw = [(sv, vnorm), self.s_xt.now]
                for rnd, (co, width) in enumerate((((0), 128), ((128), 64), (("g"), 128))):
                    for m in range(DC):
                        if co == "g":
                            src = A[:, a_pgl + 128 * m:a_pgl + 128 * (m + 1)]
                        elif co == 0:
                            src = A[:, a_pl + 128 * m:a_pl + 128 * (m + 1)]
                        else:
                            src = A[:, a_pl + 1024 + 64 * m:a_pl + 1024 + 64 * (m + 1)]
                        P.op("tensor", lambda e, src=src, m=m, width=width: e.transpose(
                            self.pt[0:width, 128 * m:128 * (m + 1)], src,
                            self.ident_bf[:]),
                            waits=(ptw if m == 0 else []),
                            inc=(sm.bump() if m == DC - 1 else None))
                    vtr = sm.n
                    P.op("vector", lambda e, rnd=rnd: e.tensor_copy(
                        A[:, a_pt + 1024 * rnd:a_pt + 1024 * (rnd + 1)], self.pt[:]),
                        waits=[(sm, vtr)], inc=self.s_xt.bump())
                    ptw = [self.s_xt.now]
                pt_free = self.s_xt.now
                ptall = self.s_xt.now
                if PAIR_STAGE == "tr":
                    continue
                cw = [ptall, vw_ok, gfix_ok]
                if hg == 0:
                    cw.append(ctx_free if ctx_free is not None else guard_r)
                    if po1_r_free is not None:
                        cw.append(po1_r_free)
                first_ctx = True
                for m in range(DC):
                    h = 2 * m + hg
                    for bp in range(2):
                        idx = 2 * m + bp
                        t0 = tA0 if bp == 0 else tB0
                        t1 = tA1 if bp == 0 else tB1
                        outap = self.po[1][64 * hg:64 * hg + 64,
                                           128 * m + 64 * bp:128 * m + 64 * bp + 64]
                        P.op("tensor", lambda e, t0=t0, h=h, idx=idx, outap=outap: e.matmul(
                            outap, t0[:, 64 * h:64 * h + 64],
                            A[:, a_pt + 64 * idx:a_pt + 64 * (idx + 1)],
                            start=True, stop=False),
                            waits=(cw if first_ctx else []))
                        first_ctx = False
                        P.op("tensor", lambda e, t1=t1, h=h, idx=idx, outap=outap: e.matmul(
                            outap, t1[0:64, 64 * h:64 * h + 64],
                            A[0:64, a_pt + 1024 + 64 * idx:a_pt + 1024 + 64 * (idx + 1)],
                            start=False, stop=False))
                        P.op("tensor", lambda e, h=h, idx=idx, outap=outap: e.matmul(
                            outap, A[:, a_gv + 64 * h:a_gv + 64 * h + 64],
                            A[:, a_pt + 2048 + 64 * idx:a_pt + 2048 + 64 * (idx + 1)],
                            start=False, stop=True),
                            inc=(sm.bump() if (m == DC - 1 and bp == 1) else None))
                vctx = sm.n
                vw_free[pi] = sm.now
                qp_free[0] = sm.now
            if PAIR_STAGE in ("sl", "exp", "norm", "tr"):
                continue
            P.op("vector", lambda e: e.tensor_copy(
                A[:, a_ctxp:a_ctxp + 1024], self.po[1][:]),
                waits=[(sm, vctx)], inc=sv.bump())
            ctx_free = sv.now
            ctxp_done = sv.n
            if PAIR_STAGE == "ctx":
                continue
            if p == 0 or p == NP - 1:
                qoff = 0 if p == 0 else 64
                mcol = 8 if p == 0 else 9
                mscr = A[:, a_pgl:a_pgl + 64]
                for j in range(DC):
                    P.op("vector", lambda e, j=j, qoff=qoff, mscr=mscr: e.tensor_sub(
                        mscr,
                        A[:, a_rcn + 128 * j + qoff:a_rcn + 128 * j + qoff + 64],
                        A[:, a_ctxp + 128 * j + qoff:a_ctxp + 128 * j + qoff + 64]),
                        waits=[(sv, ctxp_done), rcn_ok, mrep_ok, sv.now, sl_free],
                        inc=sv.bump())
                    P.op("vector", lambda e, j=j, qoff=qoff, mcol=mcol, mscr=mscr: e.scalar_tensor_tensor(
                        A[:, a_ctxp + 128 * j + qoff:a_ctxp + 128 * j + qoff + 64],
                        mscr, self.a1_rsel[:, mcol:mcol + 1],
                        A[:, a_ctxp + 128 * j + qoff:a_ctxp + 128 * j + qoff + 64],
                        ALU.mult, ALU.add),
                        waits=[sv.now, small_ok], inc=sv.bump())
            merge_done = sv.n
            ww = [(sv, merge_done), wocast, bias_ok, self.consts_ready,
                  (sv, ctxp_done)]
            if po1_r_free is not None:
                ww.append(po1_r_free)
            for hh in range(2):
                P.op("tensor", lambda e, hh=hh: e.matmul(
                    self.po[1][:, 512 * hh:512 * (hh + 1)], self.ones_row_bf[:],
                    self.a1_bobf[0:1, 512 * hh:512 * (hh + 1)],
                    start=True, stop=False), waits=(ww if hh == 0 else []))
                for j in range(DC):
                    P.op("tensor", lambda e, j=j, hh=hh: e.matmul(
                        self.po[1][:, 512 * hh:512 * (hh + 1)],
                        A[:, a_ctxp + 128 * j:a_ctxp + 128 * (j + 1)],
                        wsb(j)[:, 512 * hh:512 * (hh + 1)],
                        start=False, stop=(j == DC - 1)),
                        inc=(sm.bump() if (j == DC - 1 and hh == 1) else None))
            vwo = sm.n
            P.op("vector", lambda e, p=p: e.tensor_tensor(
                self.R[:, 1024 * p:1024 * (p + 1)], self.R[:, 1024 * p:1024 * (p + 1)],
                self.po[1][:], ALU.add),
                waits=[(sm, vwo)], inc=self.s_r.bump())
            po1_r_free = self.s_r.now
        self.stage_free[0] = rcn_ok
        self.stage_free[1] = rcn_ok
        return self.s_r.now

    def p_ffn1(self, r_dep, first):
        P, c = self.P, self.cfg
        A = self.ABF
        Tc, DC, NCH = c.Tc, c.DC, c.NCH
        xt_ok = self.emit_ln([r_dep])
        guard_mm = self.s_mm.now
        guard_io = self.s_io.now
        a_qw = 0
        a_qT = 8 * 1024
        a_sgn = a_qT + DC * Tc
        a_cmp = a_sgn + 1024
        a_l1b = a_cmp + 1024
        assert a_l1b + 2048 <= 46592
        l1b = A[0:1, a_l1b:a_l1b + 1024]
        l1bpre = A[0:1, a_l1b + 1024:a_l1b + 2048]
        stg = [self.AFP[:, 0:1408], self.AFP[:, 1408:2816]]
        sv, se = self.s_ve, self.s_ev
        L = self.l1v

        P.op("vector", lambda e: e.reduce_max(L[:, 0:1], self.XT[:, 0:c.Tc], axis=AXX,
                                              apply_absolute_value=True),
             waits=[xt_ok], inc=sv.bump())
        for jj in range(1, DC):
            P.op("vector", lambda e, jj=jj: e.reduce_max(L[:, 1:2], self.XT[:, Tc * jj:Tc * (jj + 1)],
                                                         axis=AXX, apply_absolute_value=True),
                 waits=[sv.now], inc=sv.bump())
            P.op("vector", lambda e: e.tensor_max(L[:, 0:1], L[:, 0:1], L[:, 1:2]),
                 waits=[sv.now], inc=sv.bump())
        mrow = self.AFP[0:1, 2432:2560]
        w = [sv.now] + ([self.tp_free] if self.tp_free else [])
        P.op("sync", lambda e, mrow=mrow: e.dma_start(out=mrow, in_=L[:, 0:1]),
             waits=w, inc=self.s_tp.bump(16))
        P.op("vector", lambda e, mrow=mrow: e.reduce_max(self.l1s[0:1, 3:4], mrow, axis=AXX),
             waits=[self.s_tp.now], inc=sv.bump())
        self.tp_free = sv.now
        P.op("gpsimd", lambda e: e.dma_start(out=self.dmx_in, in_=self.l1s[0:1, 3:4]),
             waits=[sv.now, self.s_io.now], inc=self.s_io.bump(16))
        P.op("gpsimd", lambda e: e.collective_compute(
            "AllReduce", ALU.max, replica_groups=[list(range(8))],
            ins=[self.dmx_in.opt()], outs=[self.dmx_out.opt()]),
            waits=[self.s_io.now], inc=self.s_cc.bump())
        P.op("gpsimd", lambda e: e.dma_start(out=self.l1s[0:1, 0:1], in_=self.dmx_out),
             waits=[self.s_cc.now], inc=self.s_io.bump(16))
        gmax_ok = self.s_io.now

        if first:
            awcols = self.gsc
            passA_done = []
            for j in range(DC):
                si = j % 2
                w = [w for w in [self.stage_free[si]] if w] + [guard_mm]
                P.op("sync", lambda e, j=j, si=si: e.dma_start(
                    out=stg[si][:, 0:1024], in_=self.dl1_wT[128 * j:128 * (j + 1), :]),
                    waits=w, inc=self.s_stg[si].bump(16))
                wd = self.s_stg[si].now
                P.op("vector", lambda e, j=j, si=si: e.reduce_sum(
                    awcols[:, j:j + 1], stg[si][:, 0:1024], axis=AXX,
                    apply_absolute_value=True),
                    waits=[wd, sv.now], inc=sv.bump())
                self.stage_free[si] = sv.now
            P.op("vector", lambda e: e.reduce_sum(L[:, 1:2], awcols[:, 0:DC], axis=AXX),
                 waits=[sv.now], inc=sv.bump())
            vws = sv.n
            P.op("tensor", lambda e: e.matmul(self.po[1][0:1, 512:513], L[:, 1:2],
                                              self.cpool[:, 4:5], start=True, stop=True),
                 waits=[(sv, vws), self.consts_ready], inc=self.s_mm.bump())
            vsum = self.s_mm.n
            P.op("vector", lambda e: e.tensor_scalar(self.l1s[0:1, 4:5], self.po[1][0:1, 512:513],
                                                     1.0 / (1024.0 * 1024.0), None, ALU.mult),
                 waits=[(self.s_mm, vsum)], inc=sv.bump())
            P.op("tensor", lambda e: e.matmul(self.po[1][:, 513:514], self.ones_row_f32[:],
                                              self.l1s[0:1, 4:5], start=True, stop=True),
                 waits=[sv.now], inc=self.s_mm.bump())
            vwb = self.s_mm.n
            P.op("vector", lambda e: e.tensor_copy(L[:, 2:3], self.po[1][:, 513:514]),
                 waits=[(self.s_mm, vwb)], inc=sv.bump())
            P.op("vector", lambda e: e.tensor_scalar(L[:, 3:4], L[:, 2:3],
                                                     0.5, None, ALU.mult),
                 waits=[sv.now], inc=sv.bump())
            ws_ok = sv.now
            absbuf = self.AFP[:, 2816:3840]
            for j in range(DC):
                si = j % 2
                w = [w for w in [self.stage_free[si]] if w]
                P.op("sync", lambda e, j=j, si=si: e.dma_start(
                    out=stg[si][:, 0:1024], in_=self.dl1_wT[128 * j:128 * (j + 1), :]),
                    waits=w, inc=self.s_stg[si].bump(16))
                wd = self.s_stg[si].now
                P.op("scalar", lambda e, si=si: e.activation(
                    A[:, a_sgn:a_sgn + 1024], stg[si][:, 0:1024], AF.Sign, scale=-1.0),
                    waits=[wd, se.now, sv.now, guard_mm, guard_io], inc=se.bump())
                P.op("scalar", lambda e, si=si: e.activation(
                    absbuf, stg[si][:, 0:1024], AF.Abs),
                    waits=[wd, sv.now], inc=se.bump())
                self.stage_free[si] = se.now
                P.op("vector", lambda e: e.tensor_scalar(
                    A[:, a_cmp:a_cmp + 1024], absbuf, L[:, 3:4], None, ALU.is_gt),
                    waits=[se.now, ws_ok], inc=sv.bump())
                P.op("vector", lambda e, j=j: e.tensor_tensor(
                    A[:, a_qw + 1024 * j:a_qw + 1024 * (j + 1)],
                    A[:, a_sgn:a_sgn + 1024], A[:, a_cmp:a_cmp + 1024], ALU.mult),
                    waits=[sv.now, se.now], inc=sv.bump())
            qw_ok = sv.now
            P.op("gpsimd", lambda e: e.dma_start(out=self.dqw, in_=A[:, a_qw:a_qw + 8 * 1024]),
                 waits=[qw_ok], inc=self.s_stgu.bump(16))
            P.op("gpsimd", lambda e: e.dma_start(out=self.dws, in_=L[:, 2:3]),
                 waits=[qw_ok], inc=self.s_stgu.bump(16))
            self.qw_spill_done = self.s_stgu.now
        else:
            P.op("sync", lambda e: e.dma_start(out=A[:, a_qw:a_qw + 8 * 1024], in_=self.dqw),
                 waits=[guard_mm, self.qw_spill_done], inc=self.s_stg[0].bump(16))
            P.op("sync", lambda e: e.dma_start(out=L[:, 2:3], in_=self.dws),
                 inc=self.s_stg[0].bump(16))
            qw_ok = self.s_stg[0].now

        P.op("vector", lambda e: e.reciprocal(self.l1s[0:1, 1:2], self.l1s[0:1, 0:1]),
             waits=[gmax_ok, sv.now], inc=sv.bump())
        P.op("vector", lambda e: e.tensor_scalar(self.l1s[0:1, 1:2], self.l1s[0:1, 1:2],
                                                 127.0, None, ALU.mult),
             waits=[sv.now], inc=sv.bump())
        P.op("vector", lambda e: e.tensor_scalar(self.l1s[0:1, 2:3], self.l1s[0:1, 0:1],
                                                 1.0 / 127.0, None, ALU.mult),
             waits=[sv.now], inc=sv.bump())
        vsc = sv.n
        P.op("tensor", lambda e: e.matmul(self.po[1][:, 0:1], self.ones_row_f32[:],
                                          self.l1s[0:1, 1:2], start=True, stop=True),
             waits=[(sv, vsc), self.consts_ready], inc=self.s_mm.bump())
        P.op("tensor", lambda e: e.matmul(self.po[1][:, 1:2], self.ones_row_f32[:],
                                          self.l1s[0:1, 2:3], start=True, stop=True),
             inc=self.s_mm.bump())
        vbc = self.s_mm.n
        P.op("vector", lambda e: e.tensor_copy(L[:, 4:5], self.po[1][:, 0:1]),
             waits=[(self.s_mm, vbc)], inc=sv.bump())
        P.op("vector", lambda e: e.tensor_copy(L[:, 5:6], self.po[1][:, 1:2]),
             waits=[(self.s_mm, vbc)], inc=sv.bump())
        iscale_ok = sv.now

        w = [x for x in [self.stage_free[1]] if x]
        P.op("sync", lambda e: e.dma_start(out=self.AFP[0:1, 1408:2432], in_=self.dl1_bias),
             waits=w, inc=self.s_stg[1].bump(16))
        P.op("vector", lambda e: e.tensor_copy(l1b, self.AFP[0:1, 1408:2432]),
             waits=[self.s_stg[1].now, guard_mm], inc=self.s_cast.bump())
        self.stage_free[1] = self.s_cast.now
        bias_ok = self.s_cast.now
        P.op("vector", lambda e: e.scalar_tensor_tensor(
            self.l1s[0:1, 3:4], self.l1s[0:1, 2:3], 1.0, L[0:1, 2:3], ALU.mult, ALU.mult),
            waits=[iscale_ok, qw_ok], inc=sv.bump())
        P.op("vector", lambda e: e.reciprocal(self.l1s[0:1, 3:4], self.l1s[0:1, 3:4]),
             waits=[sv.now], inc=sv.bump())
        P.op("vector", lambda e: e.tensor_scalar(l1bpre, l1b,
                                                 self.l1s[0:1, 3:4], None, ALU.mult),
             waits=[sv.now, bias_ok], inc=sv.bump())
        P.op("vector", lambda e: e.scalar_tensor_tensor(
            L[:, 6:7], L[:, 2:3], 0.5, L[:, 5:6], ALU.mult, ALU.mult),
            waits=[sv.now], inc=sv.bump())
        prep_ok = sv.now

        QW = min(512, Tc)
        vb2 = self.AFP[:, 0:QW]
        rb2 = self.AFP[:, 512:512 + QW]
        NCC = Tc // QW
        qx_val = [None] * NCC
        for cc in range(NCC):
            for jj in range(DC):
                P.op("vector", lambda e, jj=jj, cc=cc: e.tensor_scalar(
                    vb2, self.XT[:, Tc * jj + QW * cc:Tc * jj + QW * (cc + 1)],
                    L[:, 4:5], 256.5, ALU.mult, ALU.add),
                    waits=[xt_ok, iscale_ok, sv.now, self.stage_free[0] or sv.now,
                           self.stage_free[1] or sv.now],
                    inc=sv.bump())
                P.op("vector", lambda e: e.tensor_copy(self.ibuf[:, 0:QW], vb2),
                     waits=[sv.now], inc=sv.bump())
                P.op("vector", lambda e: e.tensor_copy(rb2, self.ibuf[:, 0:QW]),
                     waits=[sv.now], inc=sv.bump())
                P.op("vector", lambda e: e.tensor_sub(vb2, vb2, rb2),
                     waits=[sv.now], inc=sv.bump())
                P.op("vector", lambda e: e.tensor_single_scalar(
                    vb2, vb2, 0.0, ALU.is_lt),
                    waits=[sv.now], inc=sv.bump())
                P.op("vector", lambda e: e.tensor_scalar(
                    rb2, rb2, -1.0, 256.0, ALU.mult, ALU.add),
                    waits=[sv.now], inc=sv.bump())
                P.op("vector", lambda e, jj=jj, cc=cc: e.tensor_tensor(
                    A[:, a_qT + Tc * jj + QW * cc:a_qT + Tc * jj + QW * (cc + 1)],
                    rb2, vb2, ALU.add),
                    waits=[sv.now, guard_mm, guard_io], inc=sv.bump())
            qx_val[cc] = sv.n
        qx_ok = sv.now
        self.stage_free[0] = qx_ok
        self.stage_free[1] = qx_ok

        po_free = [None, None]
        for k in range(NCH):
            pb = k % 2
            bank = self.po[pb]
            ww = [(sv, qx_val[k // (NCH // NCC)]), qw_ok, prep_ok, self.consts_ready]
            if po_free[pb] is not None:
                ww.append(po_free[pb])
            for hh in range(2):
                P.op("tensor", lambda e, hh=hh, bank=bank: e.matmul(
                    bank[:, 512 * hh:512 * (hh + 1)], self.ones_row_bf[:],
                    l1bpre[0:1, 512 * hh:512 * (hh + 1)],
                    start=True, stop=False), waits=(ww if hh == 0 else []))
                for j in range(DC):
                    P.op("tensor", lambda e, k=k, j=j, hh=hh, bank=bank: e.matmul(
                        bank[:, 512 * hh:512 * (hh + 1)],
                        A[:, a_qT + Tc * j + 128 * k:a_qT + Tc * j + 128 * (k + 1)],
                        A[:, a_qw + 1024 * j + 512 * hh:a_qw + 1024 * j + 512 * (hh + 1)],
                        start=False, stop=(j == DC - 1)),
                        inc=(self.s_mm.bump() if (j == DC - 1 and hh == 1) else None))
            vmm = self.s_mm.n
            P.op("vector", lambda e, k=k, bank=bank: e.scalar_tensor_tensor(
                self.R[:, 1024 * k:1024 * (k + 1)], bank[:], L[:, 6:7],
                self.R[:, 1024 * k:1024 * (k + 1)], ALU.mult, ALU.add),
                waits=[(self.s_mm, vmm)], inc=self.s_r.bump())
            po_free[pb] = self.s_r.now
            self.r_chunk_done[k] = self.s_r.now
        return self.s_r.now

    def p_store(self, dep):
        P, c = self.P, self.cfg
        if DEBUG_DUMP:
            P.op("gpsimd", lambda e: e.dma_start(
                out=self.ddbg, in_=self.ABF[:, 0:24576]),
                waits=[dep], inc=self.s_st.bump(16))
        for k in range(c.NCH):
            kdep = self.r_chunk_done[k] if self.r_chunk_done[k] is not None else dep
            P.op("gpsimd", lambda e, k=k: e.dma_start(
                out=self.dout[128 * k:128 * (k + 1), :],
                in_=self.R[:, 1024 * k:1024 * (k + 1)]),
                waits=[kdep], inc=self.s_st.bump(16))
        P.op("gpsimd", lambda e: e.engine_nop(), waits=[self.s_st.now])

    def finish(self):
        self.P.emit()
        return self.nc


ALL_PHASES = ("ffn0a", "att0", "ffn0b", "ffn1a", "att1", "ffn1b")


def build(cfg: Cfg, phases=ALL_PHASES):
    b = Builder(cfg)
    b.p_load()
    done = b.x_ready
    if "ffn0a" in phases:
        done = b.p_ffn0(done)
    if "att0" in phases:
        done = b.p_att0(done)
    if "ffn0b" in phases:
        done = b.p_ffn0(done)
    if "ffn1a" in phases:
        done = b.p_ffn1(done, first=True)
    if "att1" in phases:
        done = b.p_att1(done)
    if "ffn1b" in phases:
        done = b.p_ffn1(done, first="ffn1a" not in phases)
    b.p_store(done)
    nc = b.finish()
    return nc, b


def make_const_inputs(cfg: Cfg):
    ident = np.eye(128, dtype=np.float32)
    import ml_dtypes
    cpool = np.zeros((128, 8), np.float32)
    cpool[:, 0] = -PI
    cpool[:, 1] = 1e-5
    cpool[:, 2] = -C_L0
    cpool[:, 3] = -C_L1
    cpool[:, 4] = 1.0
    cpool[:, 5] = 0.5
    cpool[:, 6] = -1.0
    return {
        "ident_bf": ident.astype(ml_dtypes.bfloat16),
        "cpool": cpool,
        "ones_bf": np.ones((128, 1), ml_dtypes.bfloat16),
        "ones_row_bf": np.ones((1, 128), ml_dtypes.bfloat16),
        "ones_row_f32": np.ones((1, 128), np.float32),
    }


def make_shared_inputs(inputs):
    f = lambda a: np.ascontiguousarray(np.asarray(a, np.float32))
    m = {
        "L0_w_gate": f(inputs["L0_ffn_w_gate"]),
        "L0_w_g": f(inputs["L0_ffn_w_g"]),
        "L0_w_u": f(inputs["L0_ffn_w_u"]),
        "L0_w_d": f(inputs["L0_ffn_w_d"]),
        "A0_wqT": f(np.asarray(inputs["L0_attn_Wq"], np.float32).T),
        "A0_wkT": f(np.asarray(inputs["L0_attn_Wk"], np.float32).T),
        "A0_wvT": f(np.asarray(inputs["L0_attn_Wv"], np.float32).T),
        "A0_woT": f(np.asarray(inputs["L0_attn_Wo"], np.float32).T),
        "A0_bo": f(inputs["L0_attn_bo"]).reshape(1, 1024),
        "A0_vecs": np.stack([f(inputs["L0_attn_wavelengths"]),
                             f(inputs["L0_attn_phase_bias"]),
                             f(inputs["L0_attn_bq"]),
                             f(inputs["L0_attn_bk"]),
                             f(inputs["L0_attn_bv"])], axis=1).copy(),
        "A1_wqT": f(np.asarray(inputs["L1_attn_Wq"], np.float32).T),
        "A1_wkT": f(np.asarray(inputs["L1_attn_Wk"], np.float32).T),
        "A1_wvT": f(np.asarray(inputs["L1_attn_Wv"], np.float32).T),
        "A1_woT": f(np.asarray(inputs["L1_attn_Wo"], np.float32).T),
        "A1_bqkv": np.concatenate([
            f(inputs["L1_attn_bq"]).reshape(8, 128).T,
            f(inputs["L1_attn_bk"]).reshape(8, 128).T,
            f(inputs["L1_attn_bv"]).reshape(8, 128).T], axis=1).copy(),
        "A1_bo": f(inputs["L1_attn_bo"]).reshape(1, 1024),
        "L1f_wT": f(np.asarray(inputs["L1_ffn_weight"], np.float32).T),
        "L1f_bias": f(inputs["L1_ffn_bias"]).reshape(1, 1024),
    }
    return m


def make_core_inputs(cfg, core):
    r = core % 4
    rolesel = np.zeros((128, 10), np.float32)
    rolesel[:, (r - 1) % 4] = 1.0
    rolesel[:, 4 + (r + 1) % 4] = 1.0
    if r == 0:
        rolesel[:, 8] = 1.0
    if r == 1:
        rolesel[:, 9] = 1.0
    return {"A1_rolesel": rolesel}


def kernel(**inputs):
    cfg = Cfg(Tc=2048)
    gi = np.asarray(inputs["L1_attn_global_indices"]).ravel()
    assert int(gi[0]) == 0 and int(gi[1]) == 2 * cfg.NBLK - 1, gi
    nc, _ = build(cfg)
    consts = make_const_inputs(cfg)
    shared = make_shared_inputs(inputs)
    x = np.ascontiguousarray(np.asarray(inputs["L0_attn_x"], np.float32).reshape(16384, 1024))
    in_maps = []
    for c in range(8):
        m = dict(consts)
        m.update(shared)
        m.update(make_core_inputs(cfg, c))
        m["x"] = np.ascontiguousarray(x[2048 * c:2048 * (c + 1)])
        in_maps.append(m)
    res = run_bass_kernel_spmd(nc, in_maps, core_ids=list(range(8)))
    out = np.concatenate([res.results[c]["out"] for c in range(8)], axis=0)
    return out.reshape(2, 8192, 1024)


# version 18
# speedup vs baseline: 1.0392x; 1.0355x over previous
import math
from contextlib import ExitStack

import numpy as np

import bass_rust
import concourse.bass as bass
from concourse import library_config as _LC
import concourse.mybir as mybir
from concourse.bass_utils import run_bass_kernel_spmd

F32 = mybir.dt.float32
BF16 = mybir.dt.bfloat16
AF = mybir.ActivationFunctionType
ALU = mybir.AluOpType
AXX = mybir.AxisListType.X

TWO_PI = float(2.0 * math.pi)
PI = float(math.pi)
C_L0 = 12.0
C_L1 = 16.0
INV_SQRT_H0 = 1.0 / math.sqrt(128.0)
L1_SCALE = 0.125


ATT1_STOP = None
PAIR_STAGE = None
DEBUG_DUMP = False


class Cfg:
    def __init__(self, Tc=2048):
        self.Tc = Tc
        self.D = 1024
        self.E = 8
        self.M = 1408
        self.H1 = 16
        self.DH = 64
        self.BS = 64
        self.SEQC = 4
        self.NCH = Tc // 128
        self.DC = self.D // 128
        self.MC = self.M // 128
        self.S = Tc * self.SEQC
        self.NBLK = Tc // self.BS
        self.NPAIR = Tc // 128
        self.TH = Tc // 2
        self.QT = min(512, self.TH)


class Sem:
    __slots__ = ("h", "n", "name")
    def __init__(self, h, name):
        self.h = h; self.n = 0; self.name = name
    def bump(self, k=1):
        self.n += k
        return (self, k)
    @property
    def now(self):
        return (self, self.n)


class Prog:
    ENGINES = ("sync", "tensor", "scalar", "vector", "gpsimd")

    def __init__(self, nc, es):
        self.nc = nc
        self.es = es
        self.ops = {e: [] for e in self.ENGINES}

    def sem(self, name):
        return Sem(self.es.enter_context(self.nc.semaphore(name)), name)

    def op(self, engine, fn, waits=(), inc=None, incs=None):
        ii = list(incs) if incs else []
        if inc is not None:
            ii.append(inc)
        self.ops[engine].append((list(waits), fn, ii))

    def emit(self):
        with self.nc.Block("main") as b:
            def mk(engine):
                def run(eng):
                    seen = {}
                    for waits, fn, incs in self.ops[engine]:
                        for (s, v) in waits:
                            if seen.get(id(s), -1) >= v:
                                continue
                            eng.wait_ge(s.h, v)
                            seen[id(s)] = v
                        ins = fn(eng)
                        for (s, d) in incs:
                            ins = ins.then_inc(s.h, d)
                return run
            b.sync(mk("sync"))
            b.tensor(mk("tensor"))
            b.scalar(mk("scalar"))
            b.vector(mk("vector"))
            b.gpsimd(mk("gpsimd"))


class Builder:
    def __init__(self, cfg: Cfg):
        self.cfg = cfg
        self.nc = bass.Bass("TRN2", target_bir_lowering=False)
        self.es = ExitStack()
        self.P = Prog(self.nc, self.es)
        self._declare_io()
        self._alloc()
        self._sems()

    def _declare_io(self):
        nc, c = self.nc, self.cfg
        D, E, M = c.D, c.E, c.M
        dram = lambda n, sh, dt=F32: nc.dram_tensor(n, sh, dt, kind="ExternalInput").ap()
        self.dx = dram("x", [c.Tc, D])
        self.dout = nc.dram_tensor("out", [c.Tc, D], F32, kind="ExternalOutput").ap()
        self.dident_bf = dram("ident_bf", [128, 128], BF16)
        self.dcpool = dram("cpool", [128, 8])
        self.dones_bf = dram("ones_bf", [128, 1], BF16)
        self.dwgate = dram("L0_w_gate", [D, E])
        self.dwg = dram("L0_w_g", [E, D, M])
        self.dwu = dram("L0_w_u", [E, D, M])
        self.dwd = dram("L0_w_d", [E, M, D])
        self.da0_wqT = dram("A0_wqT", [D, 128])
        self.da0_wkT = dram("A0_wkT", [D, 128])
        self.da0_wvT = dram("A0_wvT", [D, 128])
        self.da0_woT = dram("A0_woT", [128, D])
        self.da0_bo = dram("A0_bo", [1, D])
        self.da0_vecs = dram("A0_vecs", [128, 5])
        self.dcc0_in = nc.dram_tensor("cc0_in", [384, c.Tc], BF16).ap()
        self.dcc0_out = nc.dram_tensor("cc0_out", [4 * 384, c.Tc], BF16).ap()
        self.da1_wqT = dram("A1_wqT", [D, D])
        self.da1_wkT = dram("A1_wkT", [D, D])
        self.da1_wvT = dram("A1_wvT", [D, D])
        self.da1_woT = dram("A1_woT", [D, D])
        self.da1_bqkv = dram("A1_bqkv", [128, 24])
        self.da1_bo = dram("A1_bo", [1, D])
        self.da1_rolesel = dram("A1_rolesel", [128, 10])
        self.dq1 = nc.dram_tensor("q1_spill", [128, 8 * c.Tc], BF16).ap()
        self.dv1 = nc.dram_tensor("v1_spill", [c.Tc, D], BF16).ap()
        self.dcc1_in = nc.dram_tensor("cc1_in", [128, 3072], BF16).ap()
        self.dcc1_out = nc.dram_tensor("cc1_out", [512, 3072], BF16).ap()
        self.dcc1r_in = nc.dram_tensor("cc1r_in", [130, 1024], F32).ap()
        self.dcc1r_out = nc.dram_tensor("cc1r_out", [130, 1024], F32).ap()
        self.dl1_wT = dram("L1f_wT", [D, D])
        self.dl1_bias = dram("L1f_bias", [1, D])
        self.dones_row_bf = dram("ones_row_bf", [1, 128], BF16)
        self.dones_row_f32 = dram("ones_row_f32", [1, 128])
        self.dmx_in = nc.dram_tensor("mx_in", [1, 1], F32).ap()
        self.dmx_out = nc.dram_tensor("mx_out", [1, 1], F32).ap()
        self.dqw = nc.dram_tensor("qw_spill", [128, 8 * D], BF16).ap()
        self.dwcache = nc.dram_tensor("wcache", [8 * 128, 33792], BF16).ap()
        self.dws = nc.dram_tensor("ws_spill", [128, 1], F32).ap()
        if DEBUG_DUMP:
            self.ddbg = nc.dram_tensor("dbg", [128, 24576], BF16,
                                       kind="ExternalOutput").ap()

    def _alloc(self):
        nc, c = self.nc, self.cfg
        sb = lambda n, sh, dt=F32: self.es.enter_context(nc.sbuf_tensor(n, sh, dt))
        psa = lambda n, sh, dt=F32: self.es.enter_context(nc.psum_tensor(n, sh, dt))
        self.R = sb("R", [128, c.NCH * 1024])
        self.XT = sb("XT", [128, c.DC * c.Tc], BF16)
        self.ABF = sb("ABF", [128, 46592], BF16)
        self.AFP = sb("AFP", [128, 4096], F32)
        self.ident_bf = sb("identbf", [128, 128], BF16)
        self.cpool = sb("cpools", [128, 8], F32)
        self.ones_bf = sb("onesbf", [128, 1], BF16)
        self.ln_s6 = sb("ln_s6", [128, 12], F32)
        self.ln_mv = sb("ln_mv", [128, 2], F32)
        self.ln_rs = sb("ln_rs", [128, 1], F32)
        self.ln_b = sb("ln_b", [128, 1], F32)
        self.xstage = None
        self.Gh = sb("Gh", [128, c.NCH * 8], F32)
        self.gsc = sb("gsc", [128, 24], F32)
        self.geq1 = sb("geq1", [128, 8], F32)
        self.geq2 = sb("geq2", [128, 8], F32)
        self.wgate_bf = sb("wgatebf", [128, c.DC * 8], BF16)
        self.a1_bqkv = sb("a1bqkv", [128, 24], F32)
        self.a1_dl = sb("a1dl", [128, 8], F32)
        self.a1_dg = sb("a1dg", [128, 8], F32)
        self.a1_msk = sb("a1msk", [1, 256], BF16)
        self.a1_rsel = sb("a1rsel", [128, 10], F32)
        self.ones_row_bf = sb("onesrowbf", [1, 128], BF16)
        self.ones_row_f32 = sb("onesrowf32", [1, 128], F32)
        self.ibuf = sb("ibuf", [128, 512], mybir.dt.int32)
        self.l1v = sb("l1v", [128, 8], F32)
        self.l1s = sb("l1s", [1, 8], F32)
        self.a0v = sb("a0v", [128, 16], F32)
        self.P1 = psa("P1", [128, 1536])
        self.pg = [self.P1[:, 0:512]]
        self.pu = [self.P1[:, 512:1024]]
        self.po = [psa(f"po{i}", [128, 1024]) for i in range(2)]
        self.pt = psa("pt_bf", [128, 1024], BF16)
        self.xstage = self.ABF[:, 45568:46592]

        A = 0
        def carve(n):
            nonlocal A
            r = A; A += n; return r
        self.o_H = carve(c.MC * c.TH)
        self.o_wg = carve(c.MC * 128 * c.DC)
        self.o_wu = carve(c.MC * 128 * c.DC)
        self.o_wd = carve(c.MC * 1024)
        self.o_sg = carve(2 * c.QT)
        self.o_su = carve(c.QT)
        self.ffn_bf_end = A
        assert A <= 46592, A
        self.of_stage = 0

    def _sems(self):
        P = self.P
        self.s_in = P.sem("s_in")
        self.s_const = P.sem("s_const")
        self.s_stg = [P.sem("s_stg0"), P.sem("s_stg1")]
        self.s_stgu = P.sem("s_stgu")
        self.s_wcast = P.sem("s_wcast")
        self.s_io = P.sem("s_io")
        self.s_cc = P.sem("s_cc")
        self.stage_free = [None, None]
        self.s_sl = [P.sem(f"s_sl{i}") for i in range(4)]
        self.s_msk = P.sem("s_msk")
        self.s_tp = P.sem("s_tp")
        self.tp_free = None
        self.s_l1b = P.sem("s_l1b")
        self.s_cast = P.sem("s_cast")
        self.s_mm = P.sem("s_mm")
        self.s_ev = P.sem("s_ev")
        self.s_ve = P.sem("s_ve")
        self.s_ln = P.sem("s_ln")
        self.s_xt = P.sem("s_xt")
        self.s_g = P.sem("s_g")
        self.s_h = P.sem("s_h")
        self.s_r = P.sem("s_r")
        self.s_st = P.sem("s_st")

    def p_load(self):
        P, c = self.P, self.cfg
        for k in range(c.NCH):
            P.op("sync", lambda e, k=k: e.dma_start(
                out=self.R[:, 1024 * k:1024 * (k + 1)],
                in_=self.dx[128 * k:128 * (k + 1), :]), inc=self.s_in.bump(16))
        P.op("sync", lambda e: e.dma_start(out=self.ident_bf[:], in_=self.dident_bf),
             inc=self.s_const.bump(16))
        P.op("sync", lambda e: e.dma_start(out=self.cpool[:], in_=self.dcpool),
             inc=self.s_const.bump(16))
        P.op("sync", lambda e: e.dma_start(out=self.ones_bf[:], in_=self.dones_bf),
             inc=self.s_const.bump(16))
        P.op("sync", lambda e: e.dma_start(out=self.ones_row_bf[:], in_=self.dones_row_bf),
             inc=self.s_const.bump(16))
        P.op("sync", lambda e: e.dma_start(out=self.ones_row_f32[:], in_=self.dones_row_f32),
             inc=self.s_const.bump(16))
        self.consts_ready = self.s_const.now
        self.x_ready = self.s_in.now

    def emit_ln(self, r_waits):
        P, c = self.P, self.cfg
        for k in range(c.NCH):
            w = list(r_waits) + [self.consts_ready]
            P.op("vector", lambda e, k=k: e.bn_stats(
                self.ln_s6[:, 0:6], self.R[:, 1024 * k:1024 * k + 512]),
                waits=w, inc=self.s_ln.bump())
            P.op("vector", lambda e, k=k: e.bn_stats(
                self.ln_s6[:, 6:12], self.R[:, 1024 * k + 512:1024 * k + 1024]),
                inc=self.s_ln.bump())
            P.op("vector", lambda e: e.bn_aggr(self.ln_mv[:], self.ln_s6[:]),
                 waits=[self.s_ln.now], inc=self.s_ln.bump())
            P.op("scalar", lambda e: e.activation(self.ln_rs[:], self.ln_mv[:, 1:2],
                                                  AF.Sqrt, bias=self.cpool[:, 1:2]),
                 waits=[self.s_ln.now], inc=self.s_ln.bump())
            P.op("vector", lambda e: e.reciprocal(self.ln_rs[:], self.ln_rs[:]),
                 waits=[self.s_ln.now], inc=self.s_ln.bump())
            P.op("vector", lambda e: e.scalar_tensor_tensor(
                self.ln_b[:], self.ln_mv[:, 0:1], -1.0, self.ln_rs[:], ALU.mult, ALU.mult),
                waits=[self.s_ln.now], inc=self.s_ln.bump())
            P.op("vector", lambda e, k=k: e.tensor_scalar(
                self.xstage[:], self.R[:, 1024 * k:1024 * (k + 1)],
                self.ln_rs[:], self.ln_b[:], ALU.mult, ALU.add),
                waits=[self.s_ln.now, self.s_mm.now], inc=self.s_ln.bump())
            vx = self.s_ln.n
            for j in range(c.DC):
                P.op("tensor", lambda e, j=j: e.transpose(
                    self.pt[:, 128 * j:128 * (j + 1)],
                    self.xstage[:, 128 * j:128 * (j + 1)], self.ident_bf[:]),
                    waits=[(self.s_ln, vx), self.s_xt.now], inc=self.s_mm.bump())
            vt = self.s_mm.n

            def evac_xt(e, k=k):
                dst = self.XT[:].rearrange("p (j t) -> p j t", t=c.Tc)[:, :, 128 * k:128 * (k + 1)]
                src = self.pt[:].rearrange("p (j t) -> p j t", t=128)
                return e.tensor_copy(dst, src)
            P.op("vector", evac_xt, waits=[(self.s_mm, vt)], inc=self.s_xt.bump())
        return self.s_xt.now

    def p_ffn0(self, r_dep, cache=None):
        P, c = self.P, self.cfg
        A = self.ABF
        E, MC, DC, TH, QT = c.E, c.MC, c.DC, c.TH, c.QT
        NQ = TH // QT
        xt_ok = self.emit_ln([r_dep])

        P.op("sync", lambda e: e.dma_start(
            out=self.AFP[:, 0:8 * DC].rearrange("p (j e) -> p j e", e=8),
            in_=self.dwgate[:].rearrange("(j p) e -> p j e", p=128)),
             waits=[w for w in [self.stage_free[0]] if w], inc=self.s_stg[0].bump(16))
        P.op("vector", lambda e: e.tensor_copy(self.wgate_bf[:], self.AFP[:, 0:8 * DC]),
             waits=[self.s_stg[0].now, self.s_mm.now], inc=self.s_cast.bump())
        wgate_ok = self.s_cast.now
        self.stage_free[0] = self.s_cast.now
        sg = self.s_g
        gate_bank_free = [None, None]
        for k in range(c.NCH):
            bank = self.pg[0]
            bw = [xt_ok, wgate_ok]
            if gate_bank_free[0] is not None:
                bw.append(gate_bank_free[0])
            else:
                bw += [self.s_ev.now, self.s_ve.now]
            for j in range(DC):
                P.op("tensor", lambda e, k=k, j=j, bank=bank: e.matmul(
                    bank[:, 0:8],
                    self.XT[:, c.Tc * j + 128 * k:c.Tc * j + 128 * (k + 1)],
                    self.wgate_bf[:, 8 * j:8 * (j + 1)],
                    start=(j == 0), stop=(j == DC - 1)),
                    waits=(bw if j == 0 else []),
                    inc=(self.s_mm.bump() if j == DC - 1 else None))
            vmm = self.s_mm.n
            P.op("vector", lambda e, bank=bank: e.reduce_max(self.gsc[:, 0:1], bank[:, 0:8], axis=AXX),
                 waits=[(self.s_mm, vmm)], inc=sg.bump())
            P.op("vector", lambda e, bank=bank: e.tensor_scalar(self.geq1[:], bank[:, 0:8],
                                                                self.gsc[:, 0:1], None, ALU.is_ge),
                 waits=[sg.now], inc=sg.bump())
            P.op("vector", lambda e, bank=bank: e.scalar_tensor_tensor(
                self.gsc[:, 8:16], self.geq1[:], -1e30, bank[:, 0:8], ALU.mult, ALU.add),
                waits=[sg.now], inc=sg.bump())
            gate_bank_free[0] = sg.now
            P.op("vector", lambda e: e.reduce_max(self.gsc[:, 1:2], self.gsc[:, 8:16], axis=AXX),
                 waits=[sg.now], inc=sg.bump())
            P.op("vector", lambda e: e.tensor_scalar(self.geq2[:], self.gsc[:, 8:16],
                                                     self.gsc[:, 1:2], None, ALU.is_ge),
                 waits=[sg.now], inc=sg.bump())
            P.op("scalar", lambda e: e.activation(self.gsc[:, 2:3], self.gsc[:, 1:2],
                                                  AF.Sigmoid, bias=self.gsc[:, 0:1], scale=-1.0),
                 waits=[sg.now], inc=sg.bump())
            P.op("vector", lambda e: e.tensor_scalar(self.gsc[:, 3:4], self.gsc[:, 2:3],
                                                     -0.5, 0.5, ALU.mult, ALU.add),
                 waits=[sg.now], inc=sg.bump())
            P.op("vector", lambda e: e.tensor_scalar(self.gsc[:, 2:3], self.gsc[:, 2:3],
                                                     0.5, None, ALU.mult),
                 waits=[sg.now], inc=sg.bump())
            P.op("vector", lambda e: e.tensor_scalar(self.geq1[:], self.geq1[:],
                                                     self.gsc[:, 2:3], None, ALU.mult),
                 waits=[sg.now], inc=sg.bump())
            P.op("vector", lambda e, k=k: e.scalar_tensor_tensor(
                self.Gh[:, 8 * k:8 * (k + 1)], self.geq2[:], self.gsc[:, 3:4],
                self.geq1[:], ALU.mult, ALU.add),
                waits=[sg.now], inc=sg.bump())
        g_all = self.s_g.now

        o_H, o_wg, o_wu, o_wd = self.o_H, self.o_wg, self.o_wu, self.o_wd
        o_sg, o_su = self.o_sg, self.o_su
        stg = [self.AFP[:, 0:1408], self.AFP[:, 1408:2816]]
        wgwu_free = [self.s_mm.now, self.s_io.now]
        wd_free = [self.s_mm.now, self.s_io.now]
        po_free = [self.s_r.now, self.s_r.now]
        pgpu_free = [[g_all, self.s_ve.now]]
        sgsu_free = [self.s_ve.now, self.s_ve.now]

        wd_cast_ok = [self.s_cast.now]
        wgwu_cast_ok = [self.s_cast.now]
        h_write_guard = self.s_mm.now

        def dma_weight_dchunk(src3, e_, j_, st_i, q_eng):
            sem = self.s_stgu if q_eng == "gpsimd" else self.s_stg[st_i]
            P.op(q_eng, lambda e, e_=e_, j_=j_, st_i=st_i: e.dma_start(
                out=stg[st_i], in_=src3[e_, 128 * j_:128 * (j_ + 1), :]),
                waits=[w for w in [self.stage_free[st_i]] if w], inc=sem.bump(16))
            return sem.now

        def dma_wd_mchunk(e_, m_, st_i, q_eng):
            sem = self.s_stgu if q_eng == "gpsimd" else self.s_stg[st_i]
            P.op(q_eng, lambda e, e_=e_, m_=m_, st_i=st_i: e.dma_start(
                out=stg[st_i][:, 0:1024], in_=self.dwd[e_, 128 * m_:128 * (m_ + 1), :]),
                waits=[w for w in [self.stage_free[st_i]] if w], inc=sem.bump(16))
            return sem.now

        WGUW = MC * 1024 * 2
        WALL = MC * 1024 * 3
        if cache == "read":
            entry_all = [self.s_mm.now, self.s_io.now, self.s_ve.now,
                         self.s_ev.now, self.s_r.now, self.s_cast.now]
        for ei in range(E):
            if cache == "read":
                ww = list(wgwu_free) + (entry_all if ei == 0 else [])
                P.op("sync", lambda e, ei=ei: e.dma_start(
                    out=A[:, o_wg:o_wg + WGUW],
                    in_=self.dwcache[128 * ei:128 * (ei + 1), 0:WGUW]),
                    waits=ww, inc=self.s_stg[0].bump(16))
                wgwu_cast_ok = [self.s_stg[0].now]
                P.op("sync", lambda e, ei=ei: e.dma_start(
                    out=A[:, o_wd:o_wd + MC * 1024],
                    in_=self.dwcache[128 * ei:128 * (ei + 1), WGUW:WALL]),
                    waits=list(wd_free) + (entry_all if ei == 0 else []),
                    inc=self.s_stg[0].bump(16))
                wd_cast_val = [self.s_stg[0].now] * MC
            else:
                for j in range(DC):
                    for (src3, dsto, st_i) in ((self.dwg, o_wg, 0), (self.dwu, o_wu, 1)):
                        wdma = dma_weight_dchunk(src3, ei, j, st_i, "sync")
                        def cast(e, dsto=dsto, j=j, si=st_i):
                            d3 = A[:, dsto:dsto + MC * 1024].rearrange("p (mc q) -> p mc q", q=1024)
                            dd = d3[:, :, 128 * j:128 * (j + 1)]
                            ss = stg[si][:].rearrange("p (mc q) -> p mc q", q=128)
                            return e.activation(dd, ss, AF.Copy)
                        P.op("scalar", cast, waits=[wdma] + wgwu_free, inc=self.s_wcast.bump())
                        self.stage_free[st_i] = self.s_wcast.now
                wgwu_cast_ok = [self.s_wcast.now]
                wd_cast_val = [None] * MC

            for half in range(2):
                t0 = half * TH
                for m in range(MC):
                    for q in range(NQ):
                        pb = 0
                        for j in range(DC):
                            P.op("tensor", lambda e, m=m, q=q, j=j, pb=pb, t0=t0: e.matmul(
                                self.pg[pb][:, 0:QT],
                                A[:, o_wg + 1024 * m + 128 * j:o_wg + 1024 * m + 128 * (j + 1)],
                                self.XT[:, c.Tc * j + t0 + QT * q:c.Tc * j + t0 + QT * (q + 1)],
                                start=(j == 0), stop=(j == DC - 1)),
                                waits=(wgwu_cast_ok + [xt_ok] + ([pgpu_free[pb][0]] if j == 0 else [])),
                                inc=(self.s_mm.bump() if j == DC - 1 else None))
                        vg = self.s_mm.n
                        for j in range(DC):
                            P.op("tensor", lambda e, m=m, q=q, j=j, pb=pb, t0=t0: e.matmul(
                                self.pu[pb][:, 0:QT],
                                A[:, o_wu + 1024 * m + 128 * j:o_wu + 1024 * m + 128 * (j + 1)],
                                self.XT[:, c.Tc * j + t0 + QT * q:c.Tc * j + t0 + QT * (q + 1)],
                                start=(j == 0), stop=(j == DC - 1)),
                                waits=([pgpu_free[pb][1]] if j == 0 else []),
                                inc=(self.s_mm.bump() if j == DC - 1 else None))
                        vu = self.s_mm.n
                        sbuf_i = (m * NQ + q) % 2
                        P.op("scalar", lambda e, pb=pb, sbuf_i=sbuf_i: e.activation(
                            A[:, o_sg + QT * sbuf_i:o_sg + QT * (sbuf_i + 1)],
                            self.pg[pb][:, 0:QT], AF.Silu),
                            waits=[(self.s_mm, vg), sgsu_free[sbuf_i]], inc=self.s_ev.bump())
                        pgpu_free[pb][0] = self.s_ev.now
                        vsilu = self.s_ev.n
                        P.op("vector", lambda e, pb=pb: e.tensor_copy(
                            A[:, o_su:o_su + QT], self.pu[pb][:, 0:QT]),
                            waits=[(self.s_mm, vu), self.s_h.now], inc=self.s_ve.bump())
                        pgpu_free[pb][1] = self.s_ve.now
                        vsu = self.s_ve.n
                        P.op("vector", lambda e, m=m, q=q, sbuf_i=sbuf_i, t0=t0: e.tensor_tensor(
                            A[:, o_H + TH * m + QT * q:o_H + TH * m + QT * (q + 1)],
                            A[:, o_sg + QT * sbuf_i:o_sg + QT * (sbuf_i + 1)],
                            A[:, o_su:o_su + QT], ALU.mult),
                            waits=[(self.s_ev, vsilu), (self.s_ve, vsu), h_write_guard],
                            inc=self.s_h.bump())
                        sgsu_free[sbuf_i] = self.s_h.now
                h_all = self.s_h.n
                if half == 1:
                    hid_end = self.s_mm.n
                if half == 0 and cache != "read":
                    for m in range(MC):
                        st_i = m % 2
                        wdma = dma_wd_mchunk(ei, m, st_i, "sync")
                        def castd(e, m=m, si=st_i):
                            return e.tensor_copy(A[:, o_wd + 1024 * m:o_wd + 1024 * (m + 1)],
                                                 stg[si][:, 0:1024])
                        P.op("vector", castd, waits=[wdma] + wd_free, inc=self.s_cast.bump())
                        self.stage_free[st_i] = self.s_cast.now
                        wd_cast_val[m] = (self.s_cast, self.s_cast.n)
                    if cache == "write":
                        P.op("gpsimd", lambda e, ei=ei: e.dma_start(
                            out=self.dwcache[128 * ei:128 * (ei + 1), :],
                            in_=A[:, o_wg:o_wg + WALL]),
                            waits=wgwu_cast_ok + [wd_cast_val[MC - 1]],
                            inc=self.s_stgu.bump(16))
                        spill_done = self.s_stgu.now
                for kk in range(TH // 128):
                    k = half * (TH // 128) + kk
                    pb = kk % 2
                    for m in range(MC):
                        for hh in range(2):
                            P.op("tensor", lambda e, k=k, m=m, hh=hh, pb=pb, t0=t0: e.matmul(
                                self.po[pb][:, 512 * hh:512 * (hh + 1)],
                                A[:, o_H + TH * m + 128 * (k - (t0 // 128)):o_H + TH * m + 128 * (k - (t0 // 128) + 1)],
                                A[:, o_wd + 1024 * m + 512 * hh:o_wd + 1024 * m + 512 * (hh + 1)],
                                start=(m == 0), stop=(m == MC - 1)),
                                waits=([(self.s_h, h_all), wd_cast_val[m]] +
                                       ([po_free[pb]] if m == 0 and hh == 0 else [])),
                                inc=(self.s_mm.bump() if (m == MC - 1 and hh == 1) else None))
                    vwd = self.s_mm.n
                    P.op("vector", lambda e, k=k, ei=ei, pb=pb: e.scalar_tensor_tensor(
                        self.R[:, 1024 * k:1024 * (k + 1)], self.po[pb][:],
                        self.Gh[:, 8 * k + ei:8 * k + ei + 1],
                        self.R[:, 1024 * k:1024 * (k + 1)], ALU.mult, ALU.add),
                        waits=[(self.s_mm, vwd), g_all], inc=self.s_r.bump())
                    po_free[pb] = self.s_r.now
                h_write_guard = self.s_mm.now
            wgwu_free = [(self.s_mm, hid_end)]
            wd_free = [self.s_mm.now]
            if cache == "write":
                wgwu_free.append(spill_done)
                wd_free.append(spill_done)
        return self.s_r.now

    def p_att0(self, r_dep):
        P, c = self.P, self.cfg
        A = self.ABF
        Tc, S, DC, NCH = c.Tc, c.S, c.DC, c.NCH
        NS = S // 128
        FQ = min(512, Tc)
        NFQ = Tc // FQ
        xt_ok = self.emit_ln([r_dep])
        guard_mm = self.s_mm.now
        guard_r = self.s_r.now
        guard_io = self.s_io.now
        guard_sp = self.s_stgu.now

        a_cosQ, a_sinQ = 0, Tc
        a_cosK, a_sinK = 2 * Tc, 3 * Tc
        a_vtm = 4 * Tc
        a_kallc, a_kalls, a_vall = 5 * Tc, 5 * Tc + S, 5 * Tc + 2 * S
        a_PT = 5 * Tc + 3 * S
        a_OT = a_PT + 2 * 512
        a_vT = a_PT
        a_w = a_OT + Tc
        a_dbf = a_w + 4096
        a_bo0 = a_dbf + Tc
        assert a_bo0 + 1024 <= 46592
        self.a0_Dbf = A[0:1, a_dbf:a_dbf + Tc]
        self.a0_bobf = A[0:1, a_bo0:a_bo0 + 1024]
        self.a0_D = self.AFP[0:1, 0:Tc]
        self.a0_Dpp = self.AFP[:, 3968:3968 + NCH]

        stg = [self.AFP[:, 0:1408], self.AFP[:, 1408:2816]]
        wslots = []
        for i, dsrc in enumerate((self.da0_wqT, self.da0_wkT, self.da0_wvT, self.da0_woT)):
            si = i % 2
            w = [self.stage_free[si]] if self.stage_free[si] else []
            P.op("sync", lambda e, dsrc=dsrc, si=si: e.dma_start(
                out=stg[si][:, 0:1024].rearrange("p (j q) -> p j q", q=128),
                in_=dsrc[:].rearrange("(j p) q -> p j q", p=128)),
                waits=w + [guard_r], inc=self.s_stg[si].bump(16))
            wd = self.s_stg[si].now
            P.op("vector", lambda e, i=i, si=si: e.tensor_copy(
                A[:, a_w + 1024 * i:a_w + 1024 * (i + 1)], stg[si][:, 0:1024]),
                waits=[wd, guard_mm, guard_io, guard_sp], inc=self.s_cast.bump())
            self.stage_free[si] = self.s_cast.now
        w_ok = self.s_cast.now
        wq = lambda j: A[:, a_w + 128 * j:a_w + 128 * (j + 1)]
        wk = lambda j: A[:, a_w + 1024 + 128 * j:a_w + 1024 + 128 * (j + 1)]
        wv = lambda j: A[:, a_w + 2048 + 128 * j:a_w + 2048 + 128 * (j + 1)]
        woT = A[:, a_w + 3072:a_w + 4096]
        si = 0
        w = [self.stage_free[si]] if self.stage_free[si] else []
        P.op("sync", lambda e: e.dma_start(out=self.AFP[:, 0:5], in_=self.da0_vecs[:, 0:5]),
             waits=w, inc=self.s_stg[0].bump(16))
        vd = self.s_stg[0].now
        P.op("vector", lambda e: e.tensor_copy(self.a0v[:, 0:5], self.AFP[:, 0:5]),
             waits=[vd], inc=self.s_cast.bump())
        self.stage_free[0] = self.s_cast.now
        w = [self.stage_free[1]] if self.stage_free[1] else []
        P.op("sync", lambda e: e.dma_start(out=self.AFP[0:1, 1408:2432], in_=self.da0_bo),
             waits=w, inc=self.s_stg[1].bump(16))
        bod = self.s_stg[1].now
        P.op("vector", lambda e: e.tensor_copy(self.a0_bobf[:], self.AFP[0:1, 1408:2432]),
             waits=[bod, guard_sp], inc=self.s_cast.bump())
        self.stage_free[1] = self.s_cast.now
        sv = self.s_cast
        P.op("vector", lambda e: e.tensor_scalar(self.a0v[:, 5:6], self.a0v[:, 0:1],
                                                 1e-8, None, ALU.add),
             waits=[sv.now], inc=sv.bump())
        P.op("vector", lambda e: e.reciprocal(self.a0v[:, 5:6], self.a0v[:, 5:6]),
             waits=[sv.now], inc=sv.bump())
        P.op("vector", lambda e: e.tensor_scalar(self.a0v[:, 6:7], self.a0v[:, 5:6],
                                                 1.0 / TWO_PI, None, ALU.mult),
             waits=[sv.now], inc=sv.bump())
        P.op("vector", lambda e: e.scalar_tensor_tensor(self.a0v[:, 7:8], self.a0v[:, 2:3],
                                                        1.0, self.a0v[:, 5:6], ALU.mult, ALU.mult),
             waits=[sv.now], inc=sv.bump())
        P.op("vector", lambda e: e.tensor_tensor(self.a0v[:, 7:8], self.a0v[:, 7:8],
                                                 self.a0v[:, 1:2], ALU.add),
             waits=[sv.now], inc=sv.bump())
        P.op("vector", lambda e: e.tensor_scalar(self.a0v[:, 7:8], self.a0v[:, 7:8],
                                                 1.0 / TWO_PI, None, ALU.mult),
             waits=[sv.now], inc=sv.bump())
        P.op("vector", lambda e: e.scalar_tensor_tensor(self.a0v[:, 8:9], self.a0v[:, 3:4],
                                                        1.0, self.a0v[:, 5:6], ALU.mult, ALU.mult),
             waits=[sv.now], inc=sv.bump())
        P.op("vector", lambda e: e.tensor_tensor(self.a0v[:, 8:9], self.a0v[:, 8:9],
                                                 self.a0v[:, 1:2], ALU.add),
             waits=[sv.now], inc=sv.bump())
        P.op("vector", lambda e: e.tensor_scalar(self.a0v[:, 8:9], self.a0v[:, 8:9],
                                                 1.0 / TWO_PI, None, ALU.mult),
             waits=[sv.now], inc=sv.bump())
        vecs_ok = sv.now

        NT = Tc // 512 if Tc >= 512 else 1
        PW = min(512, Tc)
        pqk = [self.pg[0], self.pu[0]]
        pq_free = [None, None]
        ubuf = self.AFP[:, 2048:2048 + PW]
        vb = self.AFP[:, 2816:2816 + PW]
        rb = self.AFP[:, 3328:3328 + PW]

        def proj_one(wt, bias_col, kind):
            for tt in range(NT):
                pb = tt % 2
                bank = pqk[pb]
                for j in range(DC):
                    ww = [xt_ok, w_ok] if j == 0 else []
                    if j == 0 and pq_free[pb] is not None:
                        ww.append(pq_free[pb])
                    P.op("tensor", lambda e, wt=wt, j=j, tt=tt, bank=bank: e.matmul(
                        bank[:, 0:PW], wt(j),
                        self.XT[:, Tc * j + PW * tt:Tc * j + PW * (tt + 1)],
                        start=(j == 0), stop=(j == DC - 1)),
                        waits=ww, inc=(self.s_mm.bump() if j == DC - 1 else None))
                vmm = self.s_mm.n
                if kind != "v":
                    a_cos = a_cosQ if kind == "q" else a_cosK
                    a_sin = a_sinQ if kind == "q" else a_sinK
                    P.op("vector", lambda e, bank=bank, bc=bias_col: e.tensor_scalar(
                        ubuf, bank[:, 0:PW],
                        self.a0v[:, 6:7], self.a0v[:, bc:bc + 1], ALU.mult, ALU.add),
                        waits=[(self.s_mm, vmm), vecs_ok, self.s_cast.now,
                               self.s_ve.now, self.s_ev.now],
                        inc=self.s_ve.bump())
                    pq_free[pb] = self.s_ve.now
                    for (shift, dsto) in ((128.5, a_sin), (128.75, a_cos)):
                        P.op("vector", lambda e, shift=shift: e.tensor_scalar(
                            vb, ubuf, shift, None, ALU.add),
                            waits=[self.s_ve.now, self.s_ev.now], inc=self.s_ve.bump())
                        P.op("vector", lambda e: e.tensor_copy(self.ibuf[:, 0:PW], vb),
                             waits=[self.s_ve.now], inc=self.s_ve.bump())
                        P.op("vector", lambda e: e.tensor_copy(rb, self.ibuf[:, 0:PW]),
                             waits=[self.s_ve.now], inc=self.s_ve.bump())
                        P.op("vector", lambda e: e.tensor_sub(vb, vb, rb),
                             waits=[self.s_ve.now], inc=self.s_ve.bump())
                        P.op("vector", lambda e: e.tensor_single_scalar(
                            rb, vb, 0.0, ALU.is_lt),
                            waits=[self.s_ve.now], inc=self.s_ve.bump())
                        P.op("vector", lambda e: e.scalar_tensor_tensor(
                            vb, rb, -0.5, vb, ALU.add, ALU.add),
                            waits=[self.s_ve.now], inc=self.s_ve.bump())
                        P.op("scalar", lambda e, tt=tt, dsto=dsto: e.activation(
                            A[:, dsto + PW * tt:dsto + PW * (tt + 1)],
                            vb, AF.Sin, scale=TWO_PI),
                            waits=[self.s_ve.now], inc=self.s_ev.bump())
                else:
                    P.op("vector", lambda e, tt=tt, bank=bank: e.tensor_scalar(
                        A[:, a_vT + PW * tt:a_vT + PW * (tt + 1)], bank[:, 0:PW],
                        self.a0v[:, 4:5], None, ALU.add),
                        waits=[(self.s_mm, vmm), vecs_ok], inc=self.s_ve.bump())
                    pq_free[pb] = self.s_ve.now

        proj_one(wk, 8, "k")
        ktrig_ok = self.s_ev.now
        proj_one(wv, 4, "v")

        kdone = 0
        while kdone < NCH:
            grp = min(8, NCH - kdone)
            for i in range(grp):
                k = kdone + i
                P.op("tensor", lambda e, k=k, i=i: e.transpose(
                    self.pt[:, 128 * i:128 * (i + 1)],
                    A[:, a_vT + 128 * k:a_vT + 128 * (k + 1)], self.ident_bf[:]),
                    waits=([self.s_ve.now, self.s_xt.now] if i == 0 else []),
                    inc=(self.s_mm.bump() if i == grp - 1 else None))
            vt = self.s_mm.n
            P.op("vector", lambda e, kdone=kdone, grp=grp: e.tensor_copy(
                A[:, a_vtm + 128 * kdone:a_vtm + 128 * (kdone + grp)],
                self.pt[:, 0:128 * grp]),
                waits=[(self.s_mm, vt)], inc=self.s_xt.bump())
            kdone += grp
        vtm_ok = self.s_xt.now

        sio = self.s_io
        P.op("gpsimd", lambda e: e.dma_start(out=self.dcc0_in[0:128, :],
                                             in_=A[:, a_cosK:a_cosK + Tc]),
             waits=[ktrig_ok], inc=sio.bump(16))
        P.op("gpsimd", lambda e: e.dma_start(out=self.dcc0_in[128:256, :],
                                             in_=A[:, a_sinK:a_sinK + Tc]),
             inc=sio.bump(16))
        P.op("gpsimd", lambda e: e.dma_start(out=self.dcc0_in[256:384, :],
                                             in_=A[:, a_vtm:a_vtm + Tc]),
             waits=[vtm_ok], inc=sio.bump(16))
        P.op("gpsimd", lambda e: e.collective_compute(
            "AllGather", ALU.bypass, replica_groups=[[0, 1, 2, 3], [4, 5, 6, 7]],
            ins=[self.dcc0_in.opt()], outs=[self.dcc0_out.opt()]),
            waits=[sio.now], inc=self.s_cc.bump())
        cc_ok = self.s_cc.now

        proj_one(wq, 7, "q")
        trig_ok = self.s_ev.now
        for r in range(4):
            P.op("gpsimd", lambda e, r=r: e.dma_start(
                out=A[:, a_kallc + Tc * r:a_kallc + Tc * (r + 1)],
                in_=self.dcc0_out[384 * r:384 * r + 128, :]),
                waits=[cc_ok], inc=sio.bump(16))
            P.op("gpsimd", lambda e, r=r: e.dma_start(
                out=A[:, a_kalls + Tc * r:a_kalls + Tc * (r + 1)],
                in_=self.dcc0_out[384 * r + 128:384 * r + 256, :]),
                inc=sio.bump(16))
            P.op("gpsimd", lambda e, r=r: e.dma_start(
                out=A[:, a_vall + Tc * r:a_vall + Tc * (r + 1)],
                in_=self.dcc0_out[384 * r + 256:384 * r + 384, :]),
                inc=sio.bump(16))
        kv_ok = sio.now

        sbanks = [self.pg[0], self.pu[0]]
        sbank_free = [None, None]
        pt_free = [None, None]
        for qt in range(NFQ):
            pO = self.po[0][:, 512 * (qt % 2):512 * (qt % 2) + FQ]
            pD = self.po[1][0:1, 512 * (qt % 2):512 * (qt % 2) + FQ]
            for kc in range(NS):
                pb = kc % 2
                sb_ = sbanks[pb]
                ww = [kv_ok, trig_ok]
                if sbank_free[pb] is not None:
                    ww.append(sbank_free[pb])
                P.op("tensor", lambda e, kc=kc, qt=qt, sb_=sb_: e.matmul(
                    sb_[:, 0:FQ], A[:, a_kallc + 128 * kc:a_kallc + 128 * (kc + 1)],
                    A[:, a_cosQ + FQ * qt:a_cosQ + FQ * (qt + 1)],
                    start=True, stop=False), waits=ww)
                P.op("tensor", lambda e, kc=kc, qt=qt, sb_=sb_: e.matmul(
                    sb_[:, 0:FQ], A[:, a_kalls + 128 * kc:a_kalls + 128 * (kc + 1)],
                    A[:, a_sinQ + FQ * qt:a_sinQ + FQ * (qt + 1)],
                    start=False, stop=True), inc=self.s_mm.bump())
                vS = self.s_mm.n
                ptb = a_PT + 512 * (kc % 2)
                ww = [(self.s_mm, vS)]
                if pt_free[kc % 2] is not None:
                    ww.append(pt_free[kc % 2])
                P.op("scalar", lambda e, sb_=sb_, ptb=ptb: e.activation(
                    A[:, ptb:ptb + FQ], sb_[:, 0:FQ], AF.Exp,
                    bias=self.cpool[:, 2:3], scale=INV_SQRT_H0),
                    waits=ww + [self.consts_ready], inc=self.s_ev.bump())
                sbank_free[pb] = self.s_ev.now
                vP = self.s_ev.n
                P.op("tensor", lambda e, kc=kc, ptb=ptb, pO=pO: e.matmul(
                    pO, A[:, a_vall + 128 * kc:a_vall + 128 * (kc + 1)],
                    A[:, ptb:ptb + FQ], start=(kc == 0), stop=(kc == NS - 1)),
                    waits=[(self.s_ev, vP)])
                P.op("tensor", lambda e, kc=kc, ptb=ptb, pD=pD: e.matmul(
                    pD, self.ones_bf[:], A[:, ptb:ptb + FQ],
                    start=(kc == 0), stop=(kc == NS - 1)),
                    waits=([self.consts_ready] + ([self.s_ve.now] if kc == 0 else [])),
                    inc=self.s_mm.bump())
                pt_free[kc % 2] = self.s_mm.now
            vOD = self.s_mm.n
            P.op("scalar", lambda e, qt=qt, pO=pO: e.copy(
                A[:, a_OT + FQ * qt:a_OT + FQ * (qt + 1)], pO),
                waits=[(self.s_mm, vOD)], inc=self.s_ev.bump())
            P.op("vector", lambda e, qt=qt, pD=pD: e.tensor_copy(
                self.a0_D[0:1, FQ * qt:FQ * (qt + 1)], pD),
                waits=[(self.s_mm, vOD)], inc=self.s_ve.bump())
        ot_ok = self.s_ev.now
        d_ok = self.s_ve.now

        P.op("vector", lambda e: e.tensor_copy(self.a0_Dbf[:], self.a0_D[:]),
             waits=[d_ok], inc=self.s_ve.bump())
        P.op("vector", lambda e: e.reciprocal(self.a0_D[:], self.a0_D[:]),
             waits=[d_ok, self.s_ve.now], inc=self.s_ve.bump())
        drec_ok = self.s_ve.n
        for k in range(NCH):
            P.op("tensor", lambda e, k=k: e.matmul(
                self.po[1][:, k:k + 1], self.a0_D[0:1, 128 * k:128 * (k + 1)],
                self.cpool[0:1, 4:5], start=True, stop=True),
                waits=[(self.s_ve, drec_ok), self.consts_ready] if k == 0 else [],
                inc=(self.s_mm.bump() if k == NCH - 1 else None))
        vdt = self.s_mm.n
        P.op("vector", lambda e: e.tensor_copy(self.a0_Dpp[:], self.po[1][:, 0:NCH]),
             waits=[(self.s_mm, vdt)], inc=self.s_ve.bump())
        dpp_ok = self.s_ve.now

        po_free2 = [None, None]
        for k in range(NCH):
            pb = k % 2
            bank = self.po[pb]
            ww = [ot_ok, self.s_ve.now, self.s_cast.now]
            if po_free2[pb] is not None:
                ww.append(po_free2[pb])
            for hh in range(2):
                P.op("tensor", lambda e, k=k, hh=hh, bank=bank: e.matmul(
                    bank[:, 512 * hh:512 * (hh + 1)],
                    self.a0_Dbf[0:1, 128 * k:128 * (k + 1)],
                    self.a0_bobf[0:1, 512 * hh:512 * (hh + 1)],
                    start=True, stop=False), waits=(ww if hh == 0 else []))
                P.op("tensor", lambda e, k=k, hh=hh, bank=bank: e.matmul(
                    bank[:, 512 * hh:512 * (hh + 1)],
                    A[:, a_OT + 128 * k:a_OT + 128 * (k + 1)],
                    woT[:, 512 * hh:512 * (hh + 1)],
                    start=False, stop=True),
                    inc=(self.s_mm.bump() if hh == 1 else None))
            vmm = self.s_mm.n
            P.op("vector", lambda e, k=k, bank=bank: e.scalar_tensor_tensor(
                self.R[:, 1024 * k:1024 * (k + 1)], bank[:],
                self.a0_Dpp[:, k:k + 1], self.R[:, 1024 * k:1024 * (k + 1)],
                ALU.mult, ALU.add),
                waits=[(self.s_mm, vmm), dpp_ok], inc=self.s_r.bump())
            po_free2[pb] = self.s_r.now
        self.stage_free[0] = self.s_r.now
        self.stage_free[1] = self.s_r.now
        return self.s_r.now

    def p_att1(self, r_dep):
        P, c = self.P, self.cfg
        A = self.ABF
        Tc, DC, NCH, NP = c.Tc, c.DC, c.NCH, c.NPAIR
        NB = c.NBLK
        TCH = Tc + 128
        NT = max(1, Tc // 512)
        PW = min(512, Tc)
        SG = [[0, 1, 2, 3], [4, 5, 6, 7]]
        xt_ok = self.emit_ln([r_dep])
        guard_mm = self.s_mm.now
        guard_r = self.s_r.now
        guard_io = self.s_io.now
        guard_sp = self.s_stgu.now
        sv, se, sm = self.s_ve, self.s_ev, self.s_mm

        a_kh = 0
        a_vst = a_kh + 8 * TCH
        a_w = a_vst + 8192
        a_gq = a_w + 8192
        a_gk = a_gq + 1024
        a_gv = a_gk + 1024
        a_vedge = a_gv + 1024
        a_qp = a_vedge + 1024
        a_pl = a_qp + 1024
        a_pgl = a_pl + 1536
        a_pt = a_pgl + 1024
        a_ctxp = a_pt + 3072
        a_rcn = a_ctxp + 1024
        assert a_rcn + 1024 <= 46592, a_rcn + 1024
        a_rpt = a_pt
        a_qst = a_pt

        stg = [self.AFP[:, 0:1408], self.AFP[:, 1408:2816]]
        self.a1_ctxred = self.AFP[:, 2816:3840]
        self.a1_drow = self.AFP[0:1, 0:2048]
        wsb = lambda j: A[:, a_w + 1024 * j:a_w + 1024 * (j + 1)]

        si = 0
        w = [x for x in [self.stage_free[0]] if x]
        P.op("sync", lambda e: e.dma_start(out=self.AFP[:, 0:24], in_=self.da1_bqkv),
             waits=w, inc=self.s_stg[0].bump(16))
        P.op("vector", lambda e: e.tensor_copy(self.a1_bqkv[:], self.AFP[:, 0:24]),
             waits=[self.s_stg[0].now], inc=self.s_cast.bump())
        self.stage_free[0] = self.s_cast.now
        self.a1_bobf = A[0:1, a_gq:a_gq + 1024]
        P.op("sync", lambda e: e.dma_start(out=self.a1_rsel[:], in_=self.da1_rolesel),
             inc=self.s_msk.bump(16))
        small_ok = self.s_msk.now
        bias_ok = self.s_cast.now
        mrep_ok = small_ok

        def load_w(dsrc, ready_evt):
            for j in range(DC):
                sj = j % 2
                w = [x for x in [self.stage_free[sj]] if x]
                P.op("sync", lambda e, j=j, sj=sj, dsrc=dsrc: e.dma_start(
                    out=stg[sj][:, 0:1024], in_=dsrc[128 * j:128 * (j + 1), :]),
                    waits=w, inc=self.s_stg[sj].bump(16))
                wd = self.s_stg[sj].now
                P.op("vector", lambda e, j=j, sj=sj: e.tensor_copy(wsb(j), stg[sj][:, 0:1024]),
                     waits=[wd] + ready_evt, inc=self.s_cast.bump())
                self.stage_free[sj] = self.s_cast.now
            return self.s_cast.now

        pq_free = [None, None]
        pqk = [self.pg[0], self.pu[0]]

        def proj_tiles(cast_ok, bias_col, emit_out):
            for oc in range(DC):
                for tt in range(NT):
                    pb = (oc * NT + tt) % 2
                    bank = pqk[pb]
                    for j in range(DC):
                        ww = [xt_ok, cast_ok] if j == 0 else []
                        if j == 0 and pq_free[pb] is not None:
                            ww.append(pq_free[pb])
                        P.op("tensor", lambda e, oc=oc, tt=tt, j=j, bank=bank: e.matmul(
                            bank[:, 0:PW], wsb(j)[:, 128 * oc:128 * (oc + 1)],
                            self.XT[:, Tc * j + PW * tt:Tc * j + PW * (tt + 1)],
                            start=(j == 0), stop=(j == DC - 1)),
                            waits=ww, inc=(sm.bump() if j == DC - 1 else None))
                    emit_out(oc, tt, bank, sm.n)

        kcast = load_w(self.da1_wkT, [guard_mm, guard_io])
        def k_out(oc, tt, bank, vmm):
            P.op("vector", lambda e, oc=oc, tt=tt, bank=bank: e.tensor_scalar(
                A[:, a_kh + TCH * oc + PW * tt:a_kh + TCH * oc + PW * (tt + 1)],
                bank[:, 0:PW], self.a1_bqkv[:, 8 + oc:9 + oc], None, ALU.add),
                waits=[(sm, vmm), bias_ok, guard_io, guard_sp], inc=sv.bump())
            pq_free[(oc * NT + tt) % 2] = sv.now
        proj_tiles(kcast, 8, k_out)
        k_ok = sv.now

        vcast = load_w(self.da1_wvT, [self.s_mm.now])
        vT = lambda oc: A[:, a_vst + PW * oc:a_vst + PW * (oc + 1)]
        for tt in range(NT):
            for oc in range(DC):
                pb = (tt * DC + oc) % 2
                bank = pqk[pb]
                for j in range(DC):
                    ww = [xt_ok, vcast] if j == 0 else []
                    if j == 0 and pq_free[pb] is not None:
                        ww.append(pq_free[pb])
                    P.op("tensor", lambda e, oc=oc, tt=tt, j=j, bank=bank: e.matmul(
                        bank[:, 0:PW], wsb(j)[:, 128 * oc:128 * (oc + 1)],
                        self.XT[:, Tc * j + PW * tt:Tc * j + PW * (tt + 1)],
                        start=(j == 0), stop=(j == DC - 1)),
                        waits=ww, inc=(sm.bump() if j == DC - 1 else None))
                vmm = sm.n
                P.op("vector", lambda e, oc=oc, bank=bank: e.tensor_scalar(
                    vT(oc), bank[:, 0:PW], self.a1_bqkv[:, 16 + oc:17 + oc], None, ALU.add),
                    waits=[(sm, vmm), bias_ok, sm.now, guard_io], inc=sv.bump())
                pq_free[pb] = sv.now
            vT_ok = sv.n
            for i in range(PW // 128):
                for oc in range(DC):
                    P.op("tensor", lambda e, i=i, oc=oc: e.transpose(
                        self.pt[:, 128 * oc:128 * (oc + 1)],
                        vT(oc)[:, 128 * i:128 * (i + 1)], self.ident_bf[:]),
                        waits=([(sv, vT_ok), self.s_xt.now] if oc == 0 else []),
                        inc=(sm.bump() if oc == DC - 1 else None))
                vt2 = sm.n
                P.op("vector", lambda e: e.tensor_copy(A[:, a_qst:a_qst + 1024], self.pt[:]),
                     waits=[(sm, vt2), self.s_io.now], inc=self.s_xt.bump())
                kchunk = tt * (PW // 128) + i
                P.op("gpsimd", lambda e, kchunk=kchunk: e.dma_start(
                    out=self.dv1[128 * kchunk:128 * (kchunk + 1), :],
                    in_=A[:, a_qst:a_qst + 1024]),
                    waits=[self.s_xt.now], inc=self.s_io.bump(16))
        vspill_ok = self.s_io.now

        qcast = load_w(self.da1_wqT, [self.s_mm.now])
        def q_out(oc, tt, bank, vmm):
            P.op("vector", lambda e, oc=oc, tt=tt, bank=bank: e.tensor_scalar(
                A[:, a_qst + 1024:a_qst + 1024 + PW], bank[:, 0:PW],
                self.a1_bqkv[:, 0 + oc:1 + oc], None, ALU.add),
                waits=[(sm, vmm), bias_ok, self.s_io.now], inc=sv.bump())
            pq_free[(oc * NT + tt) % 2] = sv.now
            P.op("gpsimd", lambda e, oc=oc, tt=tt: e.dma_start(
                out=self.dq1[:, Tc * oc + PW * tt:Tc * oc + PW * (tt + 1)],
                in_=A[:, a_qst + 1024:a_qst + 1024 + PW]),
                waits=[sv.now], inc=self.s_io.bump(16))
        proj_tiles(qcast, 0, q_out)
        qspill_ok = self.s_io.now
        if ATT1_STOP == "proj":
            return self.s_r.now

        kh3 = lambda off: A[:, a_kh + off:a_kh + off + 8 * TCH].rearrange(
            "p (j t) -> p j t", t=TCH)[:, :, 0:64]
        P.op("gpsimd", lambda e: e.dma_start(
            out=self.dcc1_in[:, 0:512].rearrange("p (j t) -> p j t", t=64),
            in_=A[:, a_kh:a_kh + 8 * TCH].rearrange("p (j t) -> p j t", t=TCH)[:, :, 0:64]),
            waits=[k_ok, vspill_ok, qspill_ok], inc=self.s_io.bump(16))
        P.op("gpsimd", lambda e: e.dma_start(
            out=self.dcc1_in[:, 512:1024].rearrange("p (j t) -> p j t", t=64),
            in_=A[:, a_kh:a_kh + 8 * TCH].rearrange("p (j t) -> p j t", t=TCH)[:, :, Tc - 64:Tc]),
            inc=self.s_io.bump(16))
        P.op("gpsimd", lambda e: e.dma_start(
            out=self.dcc1_in[:, 1024:1536].rearrange("p (j t) -> p j t", t=64),
            in_=self.dq1[:].rearrange("p (j t) -> p j t", t=Tc)[:, :, 0:64]),
            waits=[qspill_ok], inc=self.s_io.bump(16))
        P.op("gpsimd", lambda e: e.dma_start(
            out=self.dcc1_in[:, 1536:2048].rearrange("p (j t) -> p j t", t=64),
            in_=self.dq1[:].rearrange("p (j t) -> p j t", t=Tc)[:, :, Tc - 64:Tc]),
            inc=self.s_io.bump(16))
        P.op("gpsimd", lambda e: e.dma_start(
            out=self.dcc1_in[0:64, 2048:3072], in_=self.dv1[0:64, :]),
            waits=[vspill_ok], inc=self.s_io.bump(16))
        P.op("gpsimd", lambda e: e.dma_start(
            out=self.dcc1_in[64:128, 2048:3072], in_=self.dv1[Tc - 64:Tc, :]),
            inc=self.s_io.bump(16))
        P.op("gpsimd", lambda e: e.collective_compute(
            "AllGather", ALU.bypass, replica_groups=SG,
            ins=[self.dcc1_in.opt()], outs=[self.dcc1_out.opt()]),
            waits=[self.s_io.now], inc=self.s_cc.bump())
        ag_ok = self.s_cc.now

        P.op("gpsimd", lambda e: e.dma_start(
            out=A[:, a_gq:a_gq + 1024].rearrange("p (j t) -> p j t", t=128)[:, :, 0:64],
            in_=self.dcc1_out[0:128, 1024:1536].rearrange("p (j t) -> p j t", t=64)),
            waits=[ag_ok], inc=self.s_io.bump(16))
        P.op("gpsimd", lambda e: e.dma_start(
            out=A[:, a_gq:a_gq + 1024].rearrange("p (j t) -> p j t", t=128)[:, :, 64:128],
            in_=self.dcc1_out[128:256, 1536:2048].rearrange("p (j t) -> p j t", t=64)),
            inc=self.s_io.bump(16))
        P.op("gpsimd", lambda e: e.dma_start(
            out=A[:, a_gk:a_gk + 1024].rearrange("p (j t) -> p j t", t=128)[:, :, 0:64],
            in_=self.dcc1_out[0:128, 0:512].rearrange("p (j t) -> p j t", t=64)),
            inc=self.s_io.bump(16))
        P.op("gpsimd", lambda e: e.dma_start(
            out=A[:, a_gk:a_gk + 1024].rearrange("p (j t) -> p j t", t=128)[:, :, 64:128],
            in_=self.dcc1_out[128:256, 512:1024].rearrange("p (j t) -> p j t", t=64)),
            inc=self.s_io.bump(16))
        P.op("gpsimd", lambda e: e.dma_start(
            out=A[0:64, a_gv:a_gv + 1024], in_=self.dcc1_out[0:64, 2048:3072]),
            inc=self.s_io.bump(16))
        P.op("gpsimd", lambda e: e.dma_start(
            out=A[64:128, a_gv:a_gv + 1024], in_=self.dcc1_out[128 + 64:256, 2048:3072]),
            inc=self.s_io.bump(16))
        gfix_ok = self.s_io.now
        if ATT1_STOP == "ag":
            return self.s_r.now

        scr = A[:, a_pt:a_pt + 512]
        scr2 = A[:, a_pt + 512:a_pt + 1024]
        kh_slot = lambda off: A[:, a_kh:a_kh + 8 * TCH].rearrange(
            "p (j t) -> p j t", t=TCH)[:, :, Tc + off:Tc + off + 64]
        def masked_fill(dst3, piece_col, sel_base, name):
            P.op("vector", lambda e, dst3=dst3: e.memset(dst3, 0.0),
                 waits=[k_ok, sv.now], inc=sv.bump())
            for r in range(4):
                P.op("gpsimd", lambda e, r=r, piece_col=piece_col: e.dma_start(
                    out=scr.rearrange("p (j t) -> p j t", t=64),
                    in_=self.dcc1_out[128 * r:128 * (r + 1),
                                      piece_col:piece_col + 512].rearrange(
                                          "p (j t) -> p j t", t=64)),
                    waits=[ag_ok, sv.now, self.s_io.now], inc=self.s_io.bump(16))
                P.op("vector", lambda e, r=r, sel_base=sel_base: e.tensor_scalar(
                    scr2, scr, self.a1_rsel[:, sel_base + r:sel_base + r + 1],
                    None, ALU.mult),
                    waits=[self.s_io.now, small_ok], inc=sv.bump())
                P.op("vector", lambda e, dst3=dst3: e.tensor_tensor(
                    dst3, dst3, scr2.rearrange("p (j t) -> p j t", t=64), ALU.add),
                    waits=[sv.now], inc=sv.bump())
        masked_fill(kh_slot(0), 512, 0, "kprev")
        masked_fill(kh_slot(64), 0, 4, "knext")
        def masked_fill_v(p0, rowr, sel_base):
            dst = A[p0:p0 + 64, a_vedge:a_vedge + 1024]
            P.op("vector", lambda e, dst=dst: e.memset(dst, 0.0),
                 waits=[sv.now], inc=sv.bump())
            for r in range(4):
                P.op("gpsimd", lambda e, r=r, rowr=rowr, p0=p0: e.dma_start(
                    out=scr[p0:p0 + 64, 0:512], in_=self.dcc1_out[
                        128 * r + rowr:128 * r + rowr + 64, 2048:2560]),
                    waits=[ag_ok, sv.now, self.s_io.now], inc=self.s_io.bump(16))
                P.op("gpsimd", lambda e, r=r, rowr=rowr, p0=p0: e.dma_start(
                    out=scr2[p0:p0 + 64, 0:512], in_=self.dcc1_out[
                        128 * r + rowr:128 * r + rowr + 64, 2560:3072]),
                    inc=self.s_io.bump(16))
                P.op("vector", lambda e, r=r, sel_base=sel_base, dst=dst, p0=p0: e.scalar_tensor_tensor(
                    dst[:, 0:512], scr[p0:p0 + 64, 0:512],
                    self.a1_rsel[p0:p0 + 64, sel_base + r:sel_base + r + 1],
                    dst[:, 0:512], ALU.mult, ALU.add),
                    waits=[self.s_io.now, small_ok], inc=sv.bump())
                P.op("vector", lambda e, r=r, sel_base=sel_base, dst=dst, p0=p0: e.scalar_tensor_tensor(
                    dst[:, 512:1024], scr2[p0:p0 + 64, 0:512],
                    self.a1_rsel[p0:p0 + 64, sel_base + r:sel_base + r + 1],
                    dst[:, 512:1024], ALU.mult, ALU.add),
                    waits=[sv.now], inc=sv.bump())
        masked_fill_v(64, 64, 0)
        masked_fill_v(0, 0, 4)
        halo_ok = sv.now

        wocast = load_w(self.da1_woT, [self.s_mm.now])

        P.op("vector", lambda e: e.memset(self.a1_drow[:], 0.0),
             waits=[sv.now, self.s_cast.now], inc=sv.bump())
        vgrp = [A[:, a_vst:a_vst + 4096], A[:, a_vst + 4096:a_vst + 8192]]
        vgrp_free = [None, None]
        rpt_free = [None, None]
        pS_free = [None, None]
        NKG = max(1, NCH // 4)
        KGW = min(4, NCH)
        row_ctx_started = False
        for kcg in range(NKG):
            gi = kcg % 2
            ww = [vspill_ok, self.s_cast.now]
            if vgrp_free[gi] is not None:
                ww.append(vgrp_free[gi])
            for i in range(KGW):
                P.op("sync", lambda e, kcg=kcg, i=i, gi=gi: e.dma_start(
                    out=vgrp[gi][:, 1024 * i:1024 * (i + 1)],
                    in_=self.dv1[128 * (KGW * kcg + i):128 * (KGW * kcg + i + 1), :]),
                    waits=(ww if i == 0 else []), inc=self.s_sl[gi].bump(16))
            vg_ok = self.s_sl[gi].now
            for h in range(16):
                hp, hc = h % 2, h // 2
                pb = (kcg * 16 + h) % 2
                bank = pqk[pb]
                for i in range(KGW):
                    ww = [gfix_ok, vg_ok, k_ok]
                    if i == 0 and pS_free[pb] is not None:
                        ww.append(pS_free[pb])
                    P.op("tensor", lambda e, kcg=kcg, h=h, i=i, hp=hp, hc=hc, bank=bank: e.matmul(
                        bank[:, 128 * i:128 * (i + 1)],
                        A[64 * hp:64 * hp + 64,
                          a_kh + TCH * hc + 128 * (KGW * kcg + i):
                          a_kh + TCH * hc + 128 * (KGW * kcg + i + 1)],
                        A[64 * hp:64 * hp + 64, a_gq + 128 * hc:a_gq + 128 * (hc + 1)],
                        start=True, stop=True),
                        waits=(ww if i == 0 else []),
                        inc=(sm.bump() if i == KGW - 1 else None))
                vS = sm.n
                rp = (kcg * 16 + h) % 2
                ww2 = [(sm, vS), self.consts_ready]
                if rpt_free[rp] is not None:
                    ww2.append(rpt_free[rp])
                P.op("scalar", lambda e, rp=rp, bank=bank: e.activation(
                    A[:, a_rpt + 512 * rp:a_rpt + 512 * rp + 128 * KGW],
                    bank[:, 0:128 * KGW], AF.Exp,
                    bias=self.cpool[:, 3:4], scale=L1_SCALE),
                    waits=ww2, inc=se.bump())
                pS_free[pb] = se.now
                vP = se.n
                for i in range(KGW):
                    P.op("tensor", lambda e, kcg=kcg, h=h, i=i, hp=hp, hc=hc, rp=rp, gi=gi: e.matmul(
                        self.po[0][64 * hp:64 * hp + 64, 128 * hc:128 * (hc + 1)],
                        vgrp[gi][:, 1024 * i + 64 * h:1024 * i + 64 * h + 64],
                        A[:, a_rpt + 512 * rp + 128 * i:a_rpt + 512 * rp + 128 * (i + 1)],
                        start=(kcg == 0 and i == 0), stop=(kcg == NKG - 1 and i == KGW - 1)),
                        waits=[(se, vP)])
                    P.op("tensor", lambda e, i=i, rp=rp: e.matmul(
                        self.P1[0:1, 1024 + 128 * 0:1024 + 128 * 1],
                        self.ones_bf[:],
                        A[:, a_rpt + 512 * rp + 128 * i:a_rpt + 512 * rp + 128 * (i + 1)],
                        start=(i == 0), stop=(i == KGW - 1)),
                        waits=[self.consts_ready] + ([sv.now] if i == 0 else []),
                        inc=(sm.bump() if i == KGW - 1 else None))
                vD = sm.n
                rpt_free[rp] = sm.now
                vgrp_free[gi] = sm.now
                P.op("vector", lambda e, h=h: e.tensor_tensor(
                    self.a1_drow[0:1, 128 * h:128 * (h + 1)],
                    self.a1_drow[0:1, 128 * h:128 * (h + 1)],
                    self.P1[0:1, 1024:1152], ALU.add),
                    waits=[(sm, vD)], inc=sv.bump())
        vrowmm = sm.n
        drow_ok = sv.now
        if ATT1_STOP == "row":
            return self.s_r.now
        P.op("vector", lambda e: e.tensor_copy(self.a1_ctxred[:], self.po[0][:]),
             waits=[(sm, vrowmm)], inc=sv.bump())
        ctxp_ok = sv.now
        P.op("gpsimd", lambda e: e.dma_start(out=self.dcc1r_in[0:128, :],
                                             in_=self.a1_ctxred[:]),
             waits=[ctxp_ok, self.s_io.now], inc=self.s_io.bump(16))
        P.op("gpsimd", lambda e: e.dma_start(
            out=self.dcc1r_in[128:130, :],
            in_=self.a1_drow[0:1, :].rearrange("p (a b) -> p a b", b=1024)),
            waits=[drow_ok], inc=self.s_io.bump(16))
        P.op("gpsimd", lambda e: e.collective_compute(
            "AllReduce", ALU.add, replica_groups=SG,
            ins=[self.dcc1r_in.opt()], outs=[self.dcc1r_out.opt()]),
            waits=[self.s_io.now], inc=self.s_cc.bump())
        rr_ok = self.s_cc.now
        P.op("gpsimd", lambda e: e.dma_start(out=self.a1_ctxred[:],
                                             in_=self.dcc1r_out[0:128, :]),
             waits=[rr_ok], inc=self.s_io.bump(16))
        P.op("gpsimd", lambda e: e.dma_start(
            out=self.a1_drow[0:1, :].rearrange("p (a b) -> p a b", b=1024),
            in_=self.dcc1r_out[128:130, :]),
            inc=self.s_io.bump(16))
        rred_ok = self.s_io.now
        P.op("gpsimd", lambda e: e.dma_start(out=self.a1_bobf, in_=self.da1_bo),
             waits=[(sm, vrowmm)], inc=self.s_stgu.bump(16))
        bias_ok = self.s_stgu.now

        def emit_normalize(po0_guard):
            P.op("vector", lambda e: e.reciprocal(self.a1_drow[:], self.a1_drow[:]),
                 waits=[rred_ok, sv.now], inc=sv.bump())
            for j in range(DC):
                for par in range(2):
                    h = 2 * j + par
                    P.op("tensor", lambda e, h=h: e.matmul(
                        self.po[0][:, 0:128], self.ones_row_f32[:],
                        self.a1_drow[0:1, 128 * h:128 * (h + 1)], start=True, stop=True),
                        waits=([sv.now, self.consts_ready, ctxp_ok, po0_guard]
                               if (j == 0 and par == 0) else [sv.now]),
                        inc=sm.bump())
                    P.op("vector", lambda e, j=j, par=par: e.tensor_tensor(
                        A[64 * par:64 * par + 64, a_rcn + 128 * j:a_rcn + 128 * (j + 1)],
                        self.a1_ctxred[64 * par:64 * par + 64, 128 * j:128 * (j + 1)],
                        self.po[0][64 * par:64 * par + 64, 0:128], ALU.mult),
                        waits=[(sm, sm.n)], inc=sv.bump())
            return sv.now

        row_done_se = se.now
        row_done_sv = sv.now
        row_ctxevac = ctxp_ok

        qp_free = [None, None]
        vw_free = [(sm, vrowmm), (sm, vrowmm)]
        ctx_free = None
        sg_free = None
        sl_free = None
        pt_free = None
        po1_r_free = None
        NPrun = NP if not (isinstance(ATT1_STOP, str) and ATT1_STOP.startswith("p")
                           and ATT1_STOP[1:].isdigit()) else int(ATT1_STOP[1:])
        row_sl0 = self.s_sl[0].now
        pair_order = (list(range(1, NP - 1)) + [0, NP - 1]) if NP > 2 else list(range(NP))
        rcn_ok = None
        for p_idx, p in enumerate(pair_order[:NPrun]):
            pi = p_idx % 2
            if p in (0, NP - 1) and rcn_ok is None:
                rcn_ok = emit_normalize(sg_free if sg_free is not None else ctxp_ok)
                sg_free = rcn_ok
            ww = [qspill_ok]
            ww.append(qp_free[0] if qp_free[0] is not None else row_sl0)
            P.op("sync", lambda e, p=p: e.dma_start(
                out=A[:, a_qp:a_qp + 1024].rearrange("p (j t) -> p j t", t=128),
                in_=self.dq1[:].rearrange("p (j t) -> p j t", t=Tc)[:, :, 128 * p:128 * (p + 1)]),
                waits=ww, inc=self.s_sl[0].bump(16))
            qp_ok = self.s_sl[0].now
            qp_ap = A[:, a_qp:a_qp + 1024]
            vwb = a_vst + 4096 * pi
            ww = [vspill_ok, halo_ok]
            if vw_free[pi] is not None:
                ww.append(vw_free[pi])
            def vdma(dst_lo, dst_n, src_row, first):
                P.op("sync", lambda e, dst_lo=dst_lo, dst_n=dst_n, src_row=src_row: e.dma_start(
                    out=A[dst_lo[0]:dst_lo[0] + dst_n, dst_lo[1]:dst_lo[1] + 1024],
                    in_=self.dv1[src_row:src_row + dst_n, :]),
                    waits=(ww if first else []), inc=self.s_sl[2 + pi].bump(16))
            first = True
            if p == 0:
                P.op("sync", lambda e, vwb=vwb: e.dma_start(
                    out=A[0:64, vwb:vwb + 1024], in_=A[64:128, a_vedge:a_vedge + 1024]),
                    waits=ww, inc=self.s_sl[2 + pi].bump(16))
                first = False
                vdma((64, vwb), 64, 0, first)
            else:
                vdma((0, vwb), 128, 128 * p - 64, first)
                first = False
            vdma((0, vwb + 1024), 64, 128 * p + 64, False)
            vdma((0, vwb + 2048), 128, 128 * p, False)
            if p != NP - 1:
                vdma((0, vwb + 3072), 64, 128 * p + 128, False)
            vw_ok = self.s_sl[2 + pi].now
            tA0 = A[:, vwb:vwb + 1024]
            tA1 = A[0:64, vwb + 1024:vwb + 2048]
            tB0 = A[:, vwb + 2048:vwb + 3072]
            tB1 = (A[0:64, a_vedge:a_vedge + 1024] if p == NP - 1
                   else A[0:64, vwb + 3072:vwb + 3072 + 1024])
            for hg in range(2):
                swl = [qp_ok, k_ok, halo_ok]
                if sl_free is not None:
                    swl.append(sl_free)
                else:
                    swl += [row_done_se, row_done_sv]
                first_sl = True
                for m in range(DC):
                    for bp in range(2):
                        b = 2 * p + bp
                        lhs = qp_ap[64 * hg:64 * hg + 64, 128 * m + 64 * bp:128 * m + 64 * bp + 64]
                        if b == 0:
                            pieces = [(Tc, 64, 0), (0, 64, 64), (64, 64, 128)]
                        elif b == NB - 1:
                            pieces = [(64 * (b - 1), 128, 0), (Tc + 64, 64, 128)]
                        else:
                            pieces = [(64 * (b - 1), 128, 0), (64 * (b - 1) + 128, 64, 128)]
                        for (src, n, dsto) in pieces:
                            if dsto < 128:
                                outap = self.P1[64 * bp:64 * bp + 64,
                                                128 * m + dsto:128 * m + dsto + n]
                            else:
                                outap = self.P1[64 * bp:64 * bp + 64,
                                                1024 + 64 * m:1024 + 64 * m + 64]
                            P.op("tensor", lambda e, src=src, n=n, lhs=lhs, hg=hg,
                                 m=m, outap=outap: e.matmul(
                                     outap, lhs,
                                     A[64 * hg:64 * hg + 64,
                                       a_kh + TCH * m + src:a_kh + TCH * m + src + n],
                                     start=True, stop=True),
                                 waits=(swl if first_sl else []))
                            first_sl = False
                        P.op("tensor", lambda e, m=m, bp=bp, lhs=lhs, hg=hg: e.matmul(
                            self.po[0][64 * bp:64 * bp + 64, 128 * m:128 * (m + 1)],
                            lhs,
                            A[64 * hg:64 * hg + 64, a_gk + 128 * m:a_gk + 128 * (m + 1)],
                            start=True, stop=True),
                            waits=(([gfix_ok, sg_free] if sg_free else
                                    [gfix_ok, ctxp_ok])
                                   if (m == 0 and bp == 0) else []),
                            inc=(sm.bump() if (m == DC - 1 and bp == 1) else None))
                vSg = sm.n
                vSl = sm.n
                if PAIR_STAGE == "sl":
                    sl_free = (sm, vSl); sg_free = (sm, vSg); continue
                ww = [(sm, vSl), self.consts_ready]
                if pt_free is not None:
                    ww.append(pt_free)
                P.op("scalar", lambda e: e.activation(
                    A[:, a_pl:a_pl + 1536], self.P1[:, 0:1536], AF.Exp,
                    bias=self.cpool[:, 3:4], scale=L1_SCALE),
                    waits=ww, inc=se.bump())
                sl_free = se.now
                P.op("scalar", lambda e: e.activation(
                    A[:, a_pgl:a_pgl + 1024], self.po[0][:], AF.Exp,
                    bias=self.cpool[:, 3:4], scale=L1_SCALE),
                    waits=[(sm, vSg)], inc=se.bump())
                sg_free = se.now
                vexp = se.n
                if PAIR_STAGE == "exp":
                    continue
                P.op("vector", lambda e: e.tensor_reduce(
                    self.a1_dl[:], A[:, a_pl:a_pl + 1024].rearrange("p (m t) -> p m t", t=128),
                    AXX, ALU.add),
                    waits=[(se, vexp)], inc=sv.bump())
                P.op("vector", lambda e: e.tensor_reduce(
                    self.gsc[:, 0:8],
                    A[:, a_pl + 1024:a_pl + 1536].rearrange("p (m t) -> p m t", t=64),
                    AXX, ALU.add),
                    waits=[(se, vexp)], inc=sv.bump())
                P.op("vector", lambda e: e.tensor_add(self.a1_dl[:], self.a1_dl[:],
                                                      self.gsc[:, 0:8]),
                     waits=[sv.now], inc=sv.bump())
                P.op("vector", lambda e: e.tensor_reduce(
                    self.a1_dg[:], A[:, a_pgl:a_pgl + 1024].rearrange("p (m t) -> p m t", t=128),
                    AXX, ALU.add),
                    waits=[(se, vexp)], inc=sv.bump())
                P.op("vector", lambda e: e.reciprocal(self.a1_dl[:], self.a1_dl[:]),
                     waits=[sv.now], inc=sv.bump())
                P.op("vector", lambda e: e.reciprocal(self.a1_dg[:], self.a1_dg[:]),
                     waits=[sv.now], inc=sv.bump())
                vrec = sv.n
                for m in range(DC):
                    P.op("vector", lambda e, m=m: e.tensor_scalar(
                        A[:, a_pl + 128 * m:a_pl + 128 * (m + 1)],
                        A[:, a_pl + 128 * m:a_pl + 128 * (m + 1)],
                        self.a1_dl[:, m:m + 1], None, ALU.mult),
                        waits=[(sv, vrec)], inc=sv.bump())
                    P.op("vector", lambda e, m=m: e.tensor_scalar(
                        A[:, a_pl + 1024 + 64 * m:a_pl + 1024 + 64 * (m + 1)],
                        A[:, a_pl + 1024 + 64 * m:a_pl + 1024 + 64 * (m + 1)],
                        self.a1_dl[:, m:m + 1], None, ALU.mult),
                        waits=[(sv, vrec)], inc=sv.bump())
                    P.op("vector", lambda e, m=m: e.tensor_scalar(
                        A[:, a_pgl + 128 * m:a_pgl + 128 * (m + 1)],
                        A[:, a_pgl + 128 * m:a_pgl + 128 * (m + 1)],
                        self.a1_dg[:, m:m + 1], None, ALU.mult),
                        waits=[(sv, vrec)], inc=sv.bump())
                vnorm = sv.n
                if PAIR_STAGE == "norm":
                    continue
                ptw = [(sv, vnorm), self.s_xt.now]
                for rnd, (co, width) in enumerate((((0), 128), ((128), 64), (("g"), 128))):
                    for m in range(DC):
                        if co == "g":
                            src = A[:, a_pgl + 128 * m:a_pgl + 128 * (m + 1)]
                        elif co == 0:
                            src = A[:, a_pl + 128 * m:a_pl + 128 * (m + 1)]
                        else:
                            src = A[:, a_pl + 1024 + 64 * m:a_pl + 1024 + 64 * (m + 1)]
                        P.op("tensor", lambda e, src=src, m=m, width=width: e.transpose(
                            self.pt[0:width, 128 * m:128 * (m + 1)], src,
                            self.ident_bf[:]),
                            waits=(ptw if m == 0 else []),
                            inc=(sm.bump() if m == DC - 1 else None))
                    vtr = sm.n
                    P.op("vector", lambda e, rnd=rnd: e.tensor_copy(
                        A[:, a_pt + 1024 * rnd:a_pt + 1024 * (rnd + 1)], self.pt[:]),
                        waits=[(sm, vtr)], inc=self.s_xt.bump())
                    ptw = [self.s_xt.now]
                pt_free = self.s_xt.now
                ptall = self.s_xt.now
                if PAIR_STAGE == "tr":
                    continue
                cw = [ptall, vw_ok, gfix_ok]
                if hg == 0:
                    cw.append(ctx_free if ctx_free is not None else guard_r)
                    if po1_r_free is not None:
                        cw.append(po1_r_free)
                first_ctx = True
                for m in range(DC):
                    h = 2 * m + hg
                    for bp in range(2):
                        idx = 2 * m + bp
                        t0 = tA0 if bp == 0 else tB0
                        t1 = tA1 if bp == 0 else tB1
                        outap = self.po[1][64 * hg:64 * hg + 64,
                                           128 * m + 64 * bp:128 * m + 64 * bp + 64]
                        P.op("tensor", lambda e, t0=t0, h=h, idx=idx, outap=outap: e.matmul(
                            outap, t0[:, 64 * h:64 * h + 64],
                            A[:, a_pt + 64 * idx:a_pt + 64 * (idx + 1)],
                            start=True, stop=False),
                            waits=(cw if first_ctx else []))
                        first_ctx = False
                        P.op("tensor", lambda e, t1=t1, h=h, idx=idx, outap=outap: e.matmul(
                            outap, t1[0:64, 64 * h:64 * h + 64],
                            A[0:64, a_pt + 1024 + 64 * idx:a_pt + 1024 + 64 * (idx + 1)],
                            start=False, stop=False))
                        P.op("tensor", lambda e, h=h, idx=idx, outap=outap: e.matmul(
                            outap, A[:, a_gv + 64 * h:a_gv + 64 * h + 64],
                            A[:, a_pt + 2048 + 64 * idx:a_pt + 2048 + 64 * (idx + 1)],
                            start=False, stop=True),
                            inc=(sm.bump() if (m == DC - 1 and bp == 1) else None))
                vctx = sm.n
                vw_free[pi] = sm.now
                qp_free[0] = sm.now
            if PAIR_STAGE in ("sl", "exp", "norm", "tr"):
                continue
            P.op("vector", lambda e: e.tensor_copy(
                A[:, a_ctxp:a_ctxp + 1024], self.po[1][:]),
                waits=[(sm, vctx)], inc=sv.bump())
            ctx_free = sv.now
            ctxp_done = sv.n
            if PAIR_STAGE == "ctx":
                continue
            if p == 0 or p == NP - 1:
                qoff = 0 if p == 0 else 64
                mcol = 8 if p == 0 else 9
                mscr = A[:, a_pgl:a_pgl + 64]
                for j in range(DC):
                    P.op("vector", lambda e, j=j, qoff=qoff, mscr=mscr: e.tensor_sub(
                        mscr,
                        A[:, a_rcn + 128 * j + qoff:a_rcn + 128 * j + qoff + 64],
                        A[:, a_ctxp + 128 * j + qoff:a_ctxp + 128 * j + qoff + 64]),
                        waits=[(sv, ctxp_done), rcn_ok, mrep_ok, sv.now, sl_free],
                        inc=sv.bump())
                    P.op("vector", lambda e, j=j, qoff=qoff, mcol=mcol, mscr=mscr: e.scalar_tensor_tensor(
                        A[:, a_ctxp + 128 * j + qoff:a_ctxp + 128 * j + qoff + 64],
                        mscr, self.a1_rsel[:, mcol:mcol + 1],
                        A[:, a_ctxp + 128 * j + qoff:a_ctxp + 128 * j + qoff + 64],
                        ALU.mult, ALU.add),
                        waits=[sv.now, small_ok], inc=sv.bump())
            merge_done = sv.n
            ww = [(sv, merge_done), wocast, bias_ok, self.consts_ready,
                  (sv, ctxp_done)]
            if po1_r_free is not None:
                ww.append(po1_r_free)
            for hh in range(2):
                P.op("tensor", lambda e, hh=hh: e.matmul(
                    self.po[1][:, 512 * hh:512 * (hh + 1)], self.ones_row_bf[:],
                    self.a1_bobf[0:1, 512 * hh:512 * (hh + 1)],
                    start=True, stop=False), waits=(ww if hh == 0 else []))
                for j in range(DC):
                    P.op("tensor", lambda e, j=j, hh=hh: e.matmul(
                        self.po[1][:, 512 * hh:512 * (hh + 1)],
                        A[:, a_ctxp + 128 * j:a_ctxp + 128 * (j + 1)],
                        wsb(j)[:, 512 * hh:512 * (hh + 1)],
                        start=False, stop=(j == DC - 1)),
                        inc=(sm.bump() if (j == DC - 1 and hh == 1) else None))
            vwo = sm.n
            P.op("vector", lambda e, p=p: e.tensor_tensor(
                self.R[:, 1024 * p:1024 * (p + 1)], self.R[:, 1024 * p:1024 * (p + 1)],
                self.po[1][:], ALU.add),
                waits=[(sm, vwo)], inc=self.s_r.bump())
            po1_r_free = self.s_r.now
        self.stage_free[0] = rcn_ok
        self.stage_free[1] = rcn_ok
        return self.s_r.now

    def p_ffn1(self, r_dep, first):
        P, c = self.P, self.cfg
        A = self.ABF
        Tc, DC, NCH = c.Tc, c.DC, c.NCH
        xt_ok = self.emit_ln([r_dep])
        guard_mm = self.s_mm.now
        guard_io = self.s_io.now
        a_qw = 0
        a_qT = 8 * 1024
        a_sgn = a_qT + DC * Tc
        a_cmp = a_sgn + 1024
        a_l1b = a_cmp + 1024
        assert a_l1b + 2048 <= 46592
        l1b = A[0:1, a_l1b:a_l1b + 1024]
        l1bpre = A[0:1, a_l1b + 1024:a_l1b + 2048]
        stg = [self.AFP[:, 0:1408], self.AFP[:, 1408:2816]]
        sv, se = self.s_ve, self.s_ev
        L = self.l1v

        P.op("vector", lambda e: e.reduce_max(L[:, 0:1], self.XT[:, 0:c.Tc], axis=AXX,
                                              apply_absolute_value=True),
             waits=[xt_ok], inc=sv.bump())
        for jj in range(1, DC):
            P.op("vector", lambda e, jj=jj: e.reduce_max(L[:, 1:2], self.XT[:, Tc * jj:Tc * (jj + 1)],
                                                         axis=AXX, apply_absolute_value=True),
                 waits=[sv.now], inc=sv.bump())
            P.op("vector", lambda e: e.tensor_max(L[:, 0:1], L[:, 0:1], L[:, 1:2]),
                 waits=[sv.now], inc=sv.bump())
        mrow = self.AFP[0:1, 2432:2560]
        w = [sv.now] + ([self.tp_free] if self.tp_free else [])
        P.op("sync", lambda e, mrow=mrow: e.dma_start(out=mrow, in_=L[:, 0:1]),
             waits=w, inc=self.s_tp.bump(16))
        P.op("vector", lambda e, mrow=mrow: e.reduce_max(self.l1s[0:1, 3:4], mrow, axis=AXX),
             waits=[self.s_tp.now], inc=sv.bump())
        self.tp_free = sv.now
        P.op("gpsimd", lambda e: e.dma_start(out=self.dmx_in, in_=self.l1s[0:1, 3:4]),
             waits=[sv.now, self.s_io.now], inc=self.s_io.bump(16))
        P.op("gpsimd", lambda e: e.collective_compute(
            "AllReduce", ALU.max, replica_groups=[list(range(8))],
            ins=[self.dmx_in.opt()], outs=[self.dmx_out.opt()]),
            waits=[self.s_io.now], inc=self.s_cc.bump())
        P.op("gpsimd", lambda e: e.dma_start(out=self.l1s[0:1, 0:1], in_=self.dmx_out),
             waits=[self.s_cc.now], inc=self.s_io.bump(16))
        gmax_ok = self.s_io.now

        if first:
            awcols = self.gsc
            passA_done = []
            for j in range(DC):
                si = j % 2
                w = [w for w in [self.stage_free[si]] if w] + [guard_mm]
                P.op("sync", lambda e, j=j, si=si: e.dma_start(
                    out=stg[si][:, 0:1024], in_=self.dl1_wT[128 * j:128 * (j + 1), :]),
                    waits=w, inc=self.s_stg[si].bump(16))
                wd = self.s_stg[si].now
                P.op("vector", lambda e, j=j, si=si: e.reduce_sum(
                    awcols[:, j:j + 1], stg[si][:, 0:1024], axis=AXX,
                    apply_absolute_value=True),
                    waits=[wd, sv.now], inc=sv.bump())
                self.stage_free[si] = sv.now
            P.op("vector", lambda e: e.reduce_sum(L[:, 1:2], awcols[:, 0:DC], axis=AXX),
                 waits=[sv.now], inc=sv.bump())
            vws = sv.n
            P.op("tensor", lambda e: e.matmul(self.po[1][0:1, 512:513], L[:, 1:2],
                                              self.cpool[:, 4:5], start=True, stop=True),
                 waits=[(sv, vws), self.consts_ready], inc=self.s_mm.bump())
            vsum = self.s_mm.n
            P.op("vector", lambda e: e.tensor_scalar(self.l1s[0:1, 4:5], self.po[1][0:1, 512:513],
                                                     1.0 / (1024.0 * 1024.0), None, ALU.mult),
                 waits=[(self.s_mm, vsum)], inc=sv.bump())
            P.op("tensor", lambda e: e.matmul(self.po[1][:, 513:514], self.ones_row_f32[:],
                                              self.l1s[0:1, 4:5], start=True, stop=True),
                 waits=[sv.now], inc=self.s_mm.bump())
            vwb = self.s_mm.n
            P.op("vector", lambda e: e.tensor_copy(L[:, 2:3], self.po[1][:, 513:514]),
                 waits=[(self.s_mm, vwb)], inc=sv.bump())
            P.op("vector", lambda e: e.tensor_scalar(L[:, 3:4], L[:, 2:3],
                                                     0.5, None, ALU.mult),
                 waits=[sv.now], inc=sv.bump())
            ws_ok = sv.now
            absbuf = self.AFP[:, 2816:3840]
            for j in range(DC):
                si = j % 2
                w = [w for w in [self.stage_free[si]] if w]
                P.op("sync", lambda e, j=j, si=si: e.dma_start(
                    out=stg[si][:, 0:1024], in_=self.dl1_wT[128 * j:128 * (j + 1), :]),
                    waits=w, inc=self.s_stg[si].bump(16))
                wd = self.s_stg[si].now
                P.op("scalar", lambda e, si=si: e.activation(
                    A[:, a_sgn:a_sgn + 1024], stg[si][:, 0:1024], AF.Sign, scale=-1.0),
                    waits=[wd, se.now, sv.now, guard_mm, guard_io], inc=se.bump())
                P.op("scalar", lambda e, si=si: e.activation(
                    absbuf, stg[si][:, 0:1024], AF.Abs),
                    waits=[wd, sv.now], inc=se.bump())
                self.stage_free[si] = se.now
                P.op("vector", lambda e: e.tensor_scalar(
                    A[:, a_cmp:a_cmp + 1024], absbuf, L[:, 3:4], None, ALU.is_gt),
                    waits=[se.now, ws_ok], inc=sv.bump())
                P.op("vector", lambda e, j=j: e.tensor_tensor(
                    A[:, a_qw + 1024 * j:a_qw + 1024 * (j + 1)],
                    A[:, a_sgn:a_sgn + 1024], A[:, a_cmp:a_cmp + 1024], ALU.mult),
                    waits=[sv.now, se.now], inc=sv.bump())
            qw_ok = sv.now
            P.op("gpsimd", lambda e: e.dma_start(out=self.dqw, in_=A[:, a_qw:a_qw + 8 * 1024]),
                 waits=[qw_ok], inc=self.s_stgu.bump(16))
            P.op("gpsimd", lambda e: e.dma_start(out=self.dws, in_=L[:, 2:3]),
                 waits=[qw_ok], inc=self.s_stgu.bump(16))
            self.qw_spill_done = self.s_stgu.now
        else:
            P.op("sync", lambda e: e.dma_start(out=A[:, a_qw:a_qw + 8 * 1024], in_=self.dqw),
                 waits=[guard_mm, self.qw_spill_done], inc=self.s_stg[0].bump(16))
            P.op("sync", lambda e: e.dma_start(out=L[:, 2:3], in_=self.dws),
                 inc=self.s_stg[0].bump(16))
            qw_ok = self.s_stg[0].now

        P.op("vector", lambda e: e.reciprocal(self.l1s[0:1, 1:2], self.l1s[0:1, 0:1]),
             waits=[gmax_ok, sv.now], inc=sv.bump())
        P.op("vector", lambda e: e.tensor_scalar(self.l1s[0:1, 1:2], self.l1s[0:1, 1:2],
                                                 127.0, None, ALU.mult),
             waits=[sv.now], inc=sv.bump())
        P.op("vector", lambda e: e.tensor_scalar(self.l1s[0:1, 2:3], self.l1s[0:1, 0:1],
                                                 1.0 / 127.0, None, ALU.mult),
             waits=[sv.now], inc=sv.bump())
        vsc = sv.n
        P.op("tensor", lambda e: e.matmul(self.po[1][:, 0:1], self.ones_row_f32[:],
                                          self.l1s[0:1, 1:2], start=True, stop=True),
             waits=[(sv, vsc), self.consts_ready], inc=self.s_mm.bump())
        P.op("tensor", lambda e: e.matmul(self.po[1][:, 1:2], self.ones_row_f32[:],
                                          self.l1s[0:1, 2:3], start=True, stop=True),
             inc=self.s_mm.bump())
        vbc = self.s_mm.n
        P.op("vector", lambda e: e.tensor_copy(L[:, 4:5], self.po[1][:, 0:1]),
             waits=[(self.s_mm, vbc)], inc=sv.bump())
        P.op("vector", lambda e: e.tensor_copy(L[:, 5:6], self.po[1][:, 1:2]),
             waits=[(self.s_mm, vbc)], inc=sv.bump())
        iscale_ok = sv.now

        w = [x for x in [self.stage_free[1]] if x]
        P.op("sync", lambda e: e.dma_start(out=self.AFP[0:1, 1408:2432], in_=self.dl1_bias),
             waits=w, inc=self.s_stg[1].bump(16))
        P.op("vector", lambda e: e.tensor_copy(l1b, self.AFP[0:1, 1408:2432]),
             waits=[self.s_stg[1].now, guard_mm], inc=self.s_cast.bump())
        self.stage_free[1] = self.s_cast.now
        bias_ok = self.s_cast.now
        P.op("vector", lambda e: e.scalar_tensor_tensor(
            self.l1s[0:1, 3:4], self.l1s[0:1, 2:3], 1.0, L[0:1, 2:3], ALU.mult, ALU.mult),
            waits=[iscale_ok, qw_ok], inc=sv.bump())
        P.op("vector", lambda e: e.reciprocal(self.l1s[0:1, 3:4], self.l1s[0:1, 3:4]),
             waits=[sv.now], inc=sv.bump())
        P.op("vector", lambda e: e.tensor_scalar(l1bpre, l1b,
                                                 self.l1s[0:1, 3:4], None, ALU.mult),
             waits=[sv.now, bias_ok], inc=sv.bump())
        P.op("vector", lambda e: e.scalar_tensor_tensor(
            L[:, 6:7], L[:, 2:3], 0.5, L[:, 5:6], ALU.mult, ALU.mult),
            waits=[sv.now], inc=sv.bump())
        prep_ok = sv.now

        QW = min(512, Tc)
        vb2 = self.AFP[:, 0:QW]
        rb2 = self.AFP[:, 512:512 + QW]
        NCC = Tc // QW
        qx_val = [None] * NCC
        for cc in range(NCC):
            for jj in range(DC):
                P.op("vector", lambda e, jj=jj, cc=cc: e.tensor_scalar(
                    vb2, self.XT[:, Tc * jj + QW * cc:Tc * jj + QW * (cc + 1)],
                    L[:, 4:5], 256.5, ALU.mult, ALU.add),
                    waits=[xt_ok, iscale_ok, sv.now, self.stage_free[0] or sv.now,
                           self.stage_free[1] or sv.now],
                    inc=sv.bump())
                P.op("vector", lambda e: e.tensor_copy(self.ibuf[:, 0:QW], vb2),
                     waits=[sv.now], inc=sv.bump())
                P.op("vector", lambda e: e.tensor_copy(rb2, self.ibuf[:, 0:QW]),
                     waits=[sv.now], inc=sv.bump())
                P.op("vector", lambda e: e.tensor_sub(vb2, vb2, rb2),
                     waits=[sv.now], inc=sv.bump())
                P.op("vector", lambda e: e.tensor_single_scalar(
                    vb2, vb2, 0.0, ALU.is_lt),
                    waits=[sv.now], inc=sv.bump())
                P.op("vector", lambda e: e.tensor_scalar(
                    rb2, rb2, -1.0, 256.0, ALU.mult, ALU.add),
                    waits=[sv.now], inc=sv.bump())
                P.op("vector", lambda e, jj=jj, cc=cc: e.tensor_tensor(
                    A[:, a_qT + Tc * jj + QW * cc:a_qT + Tc * jj + QW * (cc + 1)],
                    rb2, vb2, ALU.add),
                    waits=[sv.now, guard_mm, guard_io], inc=sv.bump())
            qx_val[cc] = sv.n
        qx_ok = sv.now
        self.stage_free[0] = qx_ok
        self.stage_free[1] = qx_ok

        po_free = [None, None]
        for k in range(NCH):
            pb = k % 2
            bank = self.po[pb]
            ww = [(sv, qx_val[k // (NCH // NCC)]), qw_ok, prep_ok, self.consts_ready]
            if po_free[pb] is not None:
                ww.append(po_free[pb])
            for hh in range(2):
                P.op("tensor", lambda e, hh=hh, bank=bank: e.matmul(
                    bank[:, 512 * hh:512 * (hh + 1)], self.ones_row_bf[:],
                    l1bpre[0:1, 512 * hh:512 * (hh + 1)],
                    start=True, stop=False), waits=(ww if hh == 0 else []))
                for j in range(DC):
                    P.op("tensor", lambda e, k=k, j=j, hh=hh, bank=bank: e.matmul(
                        bank[:, 512 * hh:512 * (hh + 1)],
                        A[:, a_qT + Tc * j + 128 * k:a_qT + Tc * j + 128 * (k + 1)],
                        A[:, a_qw + 1024 * j + 512 * hh:a_qw + 1024 * j + 512 * (hh + 1)],
                        start=False, stop=(j == DC - 1)),
                        inc=(self.s_mm.bump() if (j == DC - 1 and hh == 1) else None))
            vmm = self.s_mm.n
            P.op("vector", lambda e, k=k, bank=bank: e.scalar_tensor_tensor(
                self.R[:, 1024 * k:1024 * (k + 1)], bank[:], L[:, 6:7],
                self.R[:, 1024 * k:1024 * (k + 1)], ALU.mult, ALU.add),
                waits=[(self.s_mm, vmm)], inc=self.s_r.bump())
            po_free[pb] = self.s_r.now
        return self.s_r.now

    def p_store(self, dep):
        P, c = self.P, self.cfg
        if DEBUG_DUMP:
            P.op("gpsimd", lambda e: e.dma_start(
                out=self.ddbg, in_=self.ABF[:, 0:24576]),
                waits=[dep], inc=self.s_st.bump(16))
        for k in range(c.NCH):
            P.op("gpsimd", lambda e, k=k: e.dma_start(
                out=self.dout[128 * k:128 * (k + 1), :],
                in_=self.R[:, 1024 * k:1024 * (k + 1)]),
                waits=[dep], inc=self.s_st.bump(16))
        P.op("gpsimd", lambda e: e.engine_nop(), waits=[self.s_st.now])

    def finish(self):
        self.P.emit()
        return self.nc


ALL_PHASES = ("ffn0a", "att0", "ffn0b", "ffn1a", "att1", "ffn1b")


def build(cfg: Cfg, phases=ALL_PHASES):
    b = Builder(cfg)
    b.p_load()
    done = b.x_ready
    if "ffn0a" in phases:
        done = b.p_ffn0(done, cache="write" if "ffn0b" in phases else None)
    if "att0" in phases:
        done = b.p_att0(done)
    if "ffn0b" in phases:
        done = b.p_ffn0(done, cache="read" if "ffn0a" in phases else None)
    if "ffn1a" in phases:
        done = b.p_ffn1(done, first=True)
    if "att1" in phases:
        done = b.p_att1(done)
    if "ffn1b" in phases:
        done = b.p_ffn1(done, first="ffn1a" not in phases)
    b.p_store(done)
    nc = b.finish()
    return nc, b


def make_const_inputs(cfg: Cfg):
    ident = np.eye(128, dtype=np.float32)
    import ml_dtypes
    cpool = np.zeros((128, 8), np.float32)
    cpool[:, 0] = -PI
    cpool[:, 1] = 1e-5
    cpool[:, 2] = -C_L0
    cpool[:, 3] = -C_L1
    cpool[:, 4] = 1.0
    cpool[:, 5] = 0.5
    cpool[:, 6] = -1.0
    return {
        "ident_bf": ident.astype(ml_dtypes.bfloat16),
        "cpool": cpool,
        "ones_bf": np.ones((128, 1), ml_dtypes.bfloat16),
        "ones_row_bf": np.ones((1, 128), ml_dtypes.bfloat16),
        "ones_row_f32": np.ones((1, 128), np.float32),
    }


def make_shared_inputs(inputs):
    f = lambda a: np.ascontiguousarray(np.asarray(a, np.float32))
    m = {
        "L0_w_gate": f(inputs["L0_ffn_w_gate"]),
        "L0_w_g": f(inputs["L0_ffn_w_g"]),
        "L0_w_u": f(inputs["L0_ffn_w_u"]),
        "L0_w_d": f(inputs["L0_ffn_w_d"]),
        "A0_wqT": f(np.asarray(inputs["L0_attn_Wq"], np.float32).T),
        "A0_wkT": f(np.asarray(inputs["L0_attn_Wk"], np.float32).T),
        "A0_wvT": f(np.asarray(inputs["L0_attn_Wv"], np.float32).T),
        "A0_woT": f(np.asarray(inputs["L0_attn_Wo"], np.float32).T),
        "A0_bo": f(inputs["L0_attn_bo"]).reshape(1, 1024),
        "A0_vecs": np.stack([f(inputs["L0_attn_wavelengths"]),
                             f(inputs["L0_attn_phase_bias"]),
                             f(inputs["L0_attn_bq"]),
                             f(inputs["L0_attn_bk"]),
                             f(inputs["L0_attn_bv"])], axis=1).copy(),
        "A1_wqT": f(np.asarray(inputs["L1_attn_Wq"], np.float32).T),
        "A1_wkT": f(np.asarray(inputs["L1_attn_Wk"], np.float32).T),
        "A1_wvT": f(np.asarray(inputs["L1_attn_Wv"], np.float32).T),
        "A1_woT": f(np.asarray(inputs["L1_attn_Wo"], np.float32).T),
        "A1_bqkv": np.concatenate([
            f(inputs["L1_attn_bq"]).reshape(8, 128).T,
            f(inputs["L1_attn_bk"]).reshape(8, 128).T,
            f(inputs["L1_attn_bv"]).reshape(8, 128).T], axis=1).copy(),
        "A1_bo": f(inputs["L1_attn_bo"]).reshape(1, 1024),
        "L1f_wT": f(np.asarray(inputs["L1_ffn_weight"], np.float32).T),
        "L1f_bias": f(inputs["L1_ffn_bias"]).reshape(1, 1024),
    }
    return m


def make_core_inputs(cfg, core):
    r = core % 4
    rolesel = np.zeros((128, 10), np.float32)
    rolesel[:, (r - 1) % 4] = 1.0
    rolesel[:, 4 + (r + 1) % 4] = 1.0
    if r == 0:
        rolesel[:, 8] = 1.0
    if r == 1:
        rolesel[:, 9] = 1.0
    return {"A1_rolesel": rolesel}


def kernel(**inputs):
    cfg = Cfg(Tc=2048)
    gi = np.asarray(inputs["L1_attn_global_indices"]).ravel()
    assert int(gi[0]) == 0 and int(gi[1]) == 2 * cfg.NBLK - 1, gi
    nc, _ = build(cfg)
    consts = make_const_inputs(cfg)
    shared = make_shared_inputs(inputs)
    x = np.ascontiguousarray(np.asarray(inputs["L0_attn_x"], np.float32).reshape(16384, 1024))
    in_maps = []
    for c in range(8):
        m = dict(consts)
        m.update(shared)
        m.update(make_core_inputs(cfg, c))
        m["x"] = np.ascontiguousarray(x[2048 * c:2048 * (c + 1)])
        in_maps.append(m)
    res = run_bass_kernel_spmd(nc, in_maps, core_ids=list(range(8)))
    out = np.concatenate([res.results[c]["out"] for c in range(8)], axis=0)
    return out.reshape(2, 8192, 1024)


# version 19 (final)
# speedup vs baseline: 1.0434x; 1.0041x over previous
import math
from contextlib import ExitStack

import numpy as np

import bass_rust
import concourse.bass as bass
from concourse import library_config as _LC
import concourse.mybir as mybir
from concourse.bass_utils import run_bass_kernel_spmd

F32 = mybir.dt.float32
BF16 = mybir.dt.bfloat16
AF = mybir.ActivationFunctionType
ALU = mybir.AluOpType
AXX = mybir.AxisListType.X

TWO_PI = float(2.0 * math.pi)
PI = float(math.pi)
C_L0 = 12.0
C_L1 = 16.0
INV_SQRT_H0 = 1.0 / math.sqrt(128.0)
L1_SCALE = 0.125


ATT1_STOP = None
PAIR_STAGE = None
DEBUG_DUMP = False


class Cfg:
    def __init__(self, Tc=2048):
        self.Tc = Tc
        self.D = 1024
        self.E = 8
        self.M = 1408
        self.H1 = 16
        self.DH = 64
        self.BS = 64
        self.SEQC = 4
        self.NCH = Tc // 128
        self.DC = self.D // 128
        self.MC = self.M // 128
        self.S = Tc * self.SEQC
        self.NBLK = Tc // self.BS
        self.NPAIR = Tc // 128
        self.TH = Tc // 2
        self.QT = min(512, self.TH)


class Sem:
    __slots__ = ("h", "n", "name")
    def __init__(self, h, name):
        self.h = h; self.n = 0; self.name = name
    def bump(self, k=1):
        self.n += k
        return (self, k)
    @property
    def now(self):
        return (self, self.n)


class Prog:
    ENGINES = ("sync", "tensor", "scalar", "vector", "gpsimd")

    def __init__(self, nc, es):
        self.nc = nc
        self.es = es
        self.ops = {e: [] for e in self.ENGINES}

    def sem(self, name):
        return Sem(self.es.enter_context(self.nc.semaphore(name)), name)

    def op(self, engine, fn, waits=(), inc=None, incs=None):
        ii = list(incs) if incs else []
        if inc is not None:
            ii.append(inc)
        self.ops[engine].append((list(waits), fn, ii))

    def emit(self):
        with self.nc.Block("main") as b:
            def mk(engine):
                def run(eng):
                    seen = {}
                    for waits, fn, incs in self.ops[engine]:
                        for (s, v) in waits:
                            if seen.get(id(s), -1) >= v:
                                continue
                            eng.wait_ge(s.h, v)
                            seen[id(s)] = v
                        ins = fn(eng)
                        for (s, d) in incs:
                            ins = ins.then_inc(s.h, d)
                return run
            b.sync(mk("sync"))
            b.tensor(mk("tensor"))
            b.scalar(mk("scalar"))
            b.vector(mk("vector"))
            b.gpsimd(mk("gpsimd"))


class Builder:
    def __init__(self, cfg: Cfg):
        self.cfg = cfg
        self.nc = bass.Bass("TRN2", target_bir_lowering=False)
        self.es = ExitStack()
        self.P = Prog(self.nc, self.es)
        self._declare_io()
        self._alloc()
        self._sems()

    def _declare_io(self):
        nc, c = self.nc, self.cfg
        D, E, M = c.D, c.E, c.M
        dram = lambda n, sh, dt=F32: nc.dram_tensor(n, sh, dt, kind="ExternalInput").ap()
        self.dx = dram("x", [c.Tc, D])
        self.dout = nc.dram_tensor("out", [c.Tc, D], F32, kind="ExternalOutput").ap()
        self.dident_bf = dram("ident_bf", [128, 128], BF16)
        self.dcpool = dram("cpool", [128, 8])
        self.dones_bf = dram("ones_bf", [128, 1], BF16)
        self.dwgate = dram("L0_w_gate", [D, E])
        self.dwg = dram("L0_w_g", [E, D, M])
        self.dwu = dram("L0_w_u", [E, D, M])
        self.dwd = dram("L0_w_d", [E, M, D])
        self.da0_wqT = dram("A0_wqT", [D, 128])
        self.da0_wkT = dram("A0_wkT", [D, 128])
        self.da0_wvT = dram("A0_wvT", [D, 128])
        self.da0_woT = dram("A0_woT", [128, D])
        self.da0_bo = dram("A0_bo", [1, D])
        self.da0_vecs = dram("A0_vecs", [128, 5])
        self.dcc0_in = nc.dram_tensor("cc0_in", [384, c.Tc], BF16).ap()
        self.dcc0_out = nc.dram_tensor("cc0_out", [4 * 384, c.Tc], BF16).ap()
        self.da1_wqT = dram("A1_wqT", [D, D])
        self.da1_wkT = dram("A1_wkT", [D, D])
        self.da1_wvT = dram("A1_wvT", [D, D])
        self.da1_woT = dram("A1_woT", [D, D])
        self.da1_bqkv = dram("A1_bqkv", [128, 24])
        self.da1_bo = dram("A1_bo", [1, D])
        self.da1_rolesel = dram("A1_rolesel", [128, 10])
        self.dq1 = nc.dram_tensor("q1_spill", [128, 8 * c.Tc], BF16).ap()
        self.dv1 = nc.dram_tensor("v1_spill", [c.Tc, D], BF16).ap()
        self.dcc1_in = nc.dram_tensor("cc1_in", [128, 3072], BF16).ap()
        self.dcc1_out = nc.dram_tensor("cc1_out", [512, 3072], BF16).ap()
        self.dcc1r_in = nc.dram_tensor("cc1r_in", [130, 1024], F32).ap()
        self.dcc1r_out = nc.dram_tensor("cc1r_out", [130, 1024], F32).ap()
        self.dl1_wT = dram("L1f_wT", [D, D])
        self.dl1_bias = dram("L1f_bias", [1, D])
        self.dones_row_bf = dram("ones_row_bf", [1, 128], BF16)
        self.dones_row_f32 = dram("ones_row_f32", [1, 128])
        self.dmx_in = nc.dram_tensor("mx_in", [1, 1], F32).ap()
        self.dmx_out = nc.dram_tensor("mx_out", [1, 1], F32).ap()
        self.dqw = nc.dram_tensor("qw_spill", [128, 8 * D], BF16).ap()
        self.dwcache = nc.dram_tensor("wcache", [8 * 128, 33792], BF16).ap()
        self.dws = nc.dram_tensor("ws_spill", [128, 1], F32).ap()
        if DEBUG_DUMP:
            self.ddbg = nc.dram_tensor("dbg", [128, 24576], BF16,
                                       kind="ExternalOutput").ap()

    def _alloc(self):
        nc, c = self.nc, self.cfg
        sb = lambda n, sh, dt=F32: self.es.enter_context(nc.sbuf_tensor(n, sh, dt))
        psa = lambda n, sh, dt=F32: self.es.enter_context(nc.psum_tensor(n, sh, dt))
        self.R = sb("R", [128, c.NCH * 1024])
        self.XT = sb("XT", [128, c.DC * c.Tc], BF16)
        self.ABF = sb("ABF", [128, 46592], BF16)
        self.AFP = sb("AFP", [128, 4096], F32)
        self.ident_bf = sb("identbf", [128, 128], BF16)
        self.cpool = sb("cpools", [128, 8], F32)
        self.ones_bf = sb("onesbf", [128, 1], BF16)
        self.ln_s6 = sb("ln_s6", [128, 12], F32)
        self.ln_mv = sb("ln_mv", [128, 2], F32)
        self.ln_rs = sb("ln_rs", [128, 1], F32)
        self.ln_b = sb("ln_b", [128, 1], F32)
        self.xstage = None
        self.Gh = sb("Gh", [128, c.NCH * 8], F32)
        self.gsc = sb("gsc", [128, 24], F32)
        self.geq1 = sb("geq1", [128, 8], F32)
        self.geq2 = sb("geq2", [128, 8], F32)
        self.wgate_bf = sb("wgatebf", [128, c.DC * 8], BF16)
        self.a1_bqkv = sb("a1bqkv", [128, 24], F32)
        self.a1_dl = sb("a1dl", [128, 8], F32)
        self.a1_dg = sb("a1dg", [128, 8], F32)
        self.a1_msk = sb("a1msk", [1, 256], BF16)
        self.a1_rsel = sb("a1rsel", [128, 10], F32)
        self.ones_row_bf = sb("onesrowbf", [1, 128], BF16)
        self.ones_row_f32 = sb("onesrowf32", [1, 128], F32)
        self.ibuf = sb("ibuf", [128, 512], mybir.dt.int32)
        self.l1v = sb("l1v", [128, 8], F32)
        self.l1s = sb("l1s", [1, 8], F32)
        self.a0v = sb("a0v", [128, 16], F32)
        self.P1 = psa("P1", [128, 1536])
        self.pg = [self.P1[:, 0:512]]
        self.pu = [self.P1[:, 512:1024]]
        self.po = [psa(f"po{i}", [128, 1024]) for i in range(2)]
        self.pt = psa("pt_bf", [128, 1024], BF16)
        self.xstage = self.ABF[:, 45568:46592]

        A = 0
        def carve(n):
            nonlocal A
            r = A; A += n; return r
        self.o_H = carve(c.MC * c.TH)
        self.o_wg = carve(c.MC * 128 * c.DC)
        self.o_wu = carve(c.MC * 128 * c.DC)
        self.o_wd = carve(c.MC * 1024)
        self.o_sg = carve(2 * c.QT)
        self.o_su = carve(c.QT)
        self.ffn_bf_end = A
        assert A <= 46592, A
        self.of_stage = 0

    def _sems(self):
        P = self.P
        self.s_in = P.sem("s_in")
        self.s_const = P.sem("s_const")
        self.s_stg = [P.sem("s_stg0"), P.sem("s_stg1")]
        self.s_stgu = P.sem("s_stgu")
        self.s_wcast = P.sem("s_wcast")
        self.s_io = P.sem("s_io")
        self.s_cc = P.sem("s_cc")
        self.stage_free = [None, None]
        self.s_sl = [P.sem(f"s_sl{i}") for i in range(4)]
        self.s_msk = P.sem("s_msk")
        self.s_tp = P.sem("s_tp")
        self.tp_free = None
        self.s_l1b = P.sem("s_l1b")
        self.s_cast = P.sem("s_cast")
        self.s_mm = P.sem("s_mm")
        self.s_ev = P.sem("s_ev")
        self.s_ve = P.sem("s_ve")
        self.s_ln = P.sem("s_ln")
        self.s_xt = P.sem("s_xt")
        self.s_g = P.sem("s_g")
        self.s_h = P.sem("s_h")
        self.s_r = P.sem("s_r")
        self.s_st = P.sem("s_st")
        self.r_chunk_done = [None] * self.cfg.NCH

    def p_load(self):
        P, c = self.P, self.cfg
        for k in range(c.NCH):
            P.op("sync", lambda e, k=k: e.dma_start(
                out=self.R[:, 1024 * k:1024 * (k + 1)],
                in_=self.dx[128 * k:128 * (k + 1), :]), inc=self.s_in.bump(16))
        P.op("sync", lambda e: e.dma_start(out=self.ident_bf[:], in_=self.dident_bf),
             inc=self.s_const.bump(16))
        P.op("sync", lambda e: e.dma_start(out=self.cpool[:], in_=self.dcpool),
             inc=self.s_const.bump(16))
        P.op("sync", lambda e: e.dma_start(out=self.ones_bf[:], in_=self.dones_bf),
             inc=self.s_const.bump(16))
        P.op("sync", lambda e: e.dma_start(out=self.ones_row_bf[:], in_=self.dones_row_bf),
             inc=self.s_const.bump(16))
        P.op("sync", lambda e: e.dma_start(out=self.ones_row_f32[:], in_=self.dones_row_f32),
             inc=self.s_const.bump(16))
        self.consts_ready = self.s_const.now
        self.x_ready = self.s_in.now

    def emit_ln(self, r_waits):
        P, c = self.P, self.cfg
        for k in range(c.NCH):
            w = list(r_waits) + [self.consts_ready]
            P.op("vector", lambda e, k=k: e.bn_stats(
                self.ln_s6[:, 0:6], self.R[:, 1024 * k:1024 * k + 512]),
                waits=w, inc=self.s_ln.bump())
            P.op("vector", lambda e, k=k: e.bn_stats(
                self.ln_s6[:, 6:12], self.R[:, 1024 * k + 512:1024 * k + 1024]),
                inc=self.s_ln.bump())
            P.op("vector", lambda e: e.bn_aggr(self.ln_mv[:], self.ln_s6[:]),
                 waits=[self.s_ln.now], inc=self.s_ln.bump())
            P.op("scalar", lambda e: e.activation(self.ln_rs[:], self.ln_mv[:, 1:2],
                                                  AF.Sqrt, bias=self.cpool[:, 1:2]),
                 waits=[self.s_ln.now], inc=self.s_ln.bump())
            P.op("vector", lambda e: e.reciprocal(self.ln_rs[:], self.ln_rs[:]),
                 waits=[self.s_ln.now], inc=self.s_ln.bump())
            P.op("vector", lambda e: e.scalar_tensor_tensor(
                self.ln_b[:], self.ln_mv[:, 0:1], -1.0, self.ln_rs[:], ALU.mult, ALU.mult),
                waits=[self.s_ln.now], inc=self.s_ln.bump())
            P.op("vector", lambda e, k=k: e.tensor_scalar(
                self.xstage[:], self.R[:, 1024 * k:1024 * (k + 1)],
                self.ln_rs[:], self.ln_b[:], ALU.mult, ALU.add),
                waits=[self.s_ln.now, self.s_mm.now], inc=self.s_ln.bump())
            vx = self.s_ln.n
            for j in range(c.DC):
                P.op("tensor", lambda e, j=j: e.transpose(
                    self.pt[:, 128 * j:128 * (j + 1)],
                    self.xstage[:, 128 * j:128 * (j + 1)], self.ident_bf[:]),
                    waits=[(self.s_ln, vx), self.s_xt.now], inc=self.s_mm.bump())
            vt = self.s_mm.n

            def evac_xt(e, k=k):
                dst = self.XT[:].rearrange("p (j t) -> p j t", t=c.Tc)[:, :, 128 * k:128 * (k + 1)]
                src = self.pt[:].rearrange("p (j t) -> p j t", t=128)
                return e.tensor_copy(dst, src)
            P.op("vector", evac_xt, waits=[(self.s_mm, vt)], inc=self.s_xt.bump())
        return self.s_xt.now

    def p_ffn0(self, r_dep, cache=None):
        P, c = self.P, self.cfg
        self.r_chunk_done = [None] * c.NCH
        A = self.ABF
        E, MC, DC, TH, QT = c.E, c.MC, c.DC, c.TH, c.QT
        NQ = TH // QT
        xt_ok = self.emit_ln([r_dep])

        P.op("sync", lambda e: e.dma_start(
            out=self.AFP[:, 0:8 * DC].rearrange("p (j e) -> p j e", e=8),
            in_=self.dwgate[:].rearrange("(j p) e -> p j e", p=128)),
             waits=[w for w in [self.stage_free[0]] if w], inc=self.s_stg[0].bump(16))
        P.op("vector", lambda e: e.tensor_copy(self.wgate_bf[:], self.AFP[:, 0:8 * DC]),
             waits=[self.s_stg[0].now, self.s_mm.now], inc=self.s_cast.bump())
        wgate_ok = self.s_cast.now
        self.stage_free[0] = self.s_cast.now
        sg = self.s_g
        gate_bank_free = [None, None]
        for k in range(c.NCH):
            bank = self.pg[0]
            bw = [xt_ok, wgate_ok]
            if gate_bank_free[0] is not None:
                bw.append(gate_bank_free[0])
            else:
                bw += [self.s_ev.now, self.s_ve.now]
            for j in range(DC):
                P.op("tensor", lambda e, k=k, j=j, bank=bank: e.matmul(
                    bank[:, 0:8],
                    self.XT[:, c.Tc * j + 128 * k:c.Tc * j + 128 * (k + 1)],
                    self.wgate_bf[:, 8 * j:8 * (j + 1)],
                    start=(j == 0), stop=(j == DC - 1)),
                    waits=(bw if j == 0 else []),
                    inc=(self.s_mm.bump() if j == DC - 1 else None))
            vmm = self.s_mm.n
            P.op("vector", lambda e, bank=bank: e.reduce_max(self.gsc[:, 0:1], bank[:, 0:8], axis=AXX),
                 waits=[(self.s_mm, vmm)], inc=sg.bump())
            P.op("vector", lambda e, bank=bank: e.tensor_scalar(self.geq1[:], bank[:, 0:8],
                                                                self.gsc[:, 0:1], None, ALU.is_ge),
                 waits=[sg.now], inc=sg.bump())
            P.op("vector", lambda e, bank=bank: e.scalar_tensor_tensor(
                self.gsc[:, 8:16], self.geq1[:], -1e30, bank[:, 0:8], ALU.mult, ALU.add),
                waits=[sg.now], inc=sg.bump())
            gate_bank_free[0] = sg.now
            P.op("vector", lambda e: e.reduce_max(self.gsc[:, 1:2], self.gsc[:, 8:16], axis=AXX),
                 waits=[sg.now], inc=sg.bump())
            P.op("vector", lambda e: e.tensor_scalar(self.geq2[:], self.gsc[:, 8:16],
                                                     self.gsc[:, 1:2], None, ALU.is_ge),
                 waits=[sg.now], inc=sg.bump())
            P.op("scalar", lambda e: e.activation(self.gsc[:, 2:3], self.gsc[:, 1:2],
                                                  AF.Sigmoid, bias=self.gsc[:, 0:1], scale=-1.0),
                 waits=[sg.now], inc=sg.bump())
            P.op("vector", lambda e: e.tensor_scalar(self.gsc[:, 3:4], self.gsc[:, 2:3],
                                                     -0.5, 0.5, ALU.mult, ALU.add),
                 waits=[sg.now], inc=sg.bump())
            P.op("vector", lambda e: e.tensor_scalar(self.gsc[:, 2:3], self.gsc[:, 2:3],
                                                     0.5, None, ALU.mult),
                 waits=[sg.now], inc=sg.bump())
            P.op("vector", lambda e: e.tensor_scalar(self.geq1[:], self.geq1[:],
                                                     self.gsc[:, 2:3], None, ALU.mult),
                 waits=[sg.now], inc=sg.bump())
            P.op("vector", lambda e, k=k: e.scalar_tensor_tensor(
                self.Gh[:, 8 * k:8 * (k + 1)], self.geq2[:], self.gsc[:, 3:4],
                self.geq1[:], ALU.mult, ALU.add),
                waits=[sg.now], inc=sg.bump())
        g_all = self.s_g.now

        o_H, o_wg, o_wu, o_wd = self.o_H, self.o_wg, self.o_wu, self.o_wd
        o_sg, o_su = self.o_sg, self.o_su
        stg = [self.AFP[:, 0:1408], self.AFP[:, 1408:2816]]
        wgwu_free = [self.s_mm.now, self.s_io.now]
        wd_free = [self.s_mm.now, self.s_io.now]
        po_free = [self.s_r.now, self.s_r.now]
        pgpu_free = [[g_all, self.s_ve.now]]
        sgsu_free = [self.s_ve.now, self.s_ve.now]

        wd_cast_ok = [self.s_cast.now]
        wgwu_cast_ok = [self.s_cast.now]
        h_write_guard = self.s_mm.now

        def dma_weight_dchunk(src3, e_, j_, st_i, q_eng):
            sem = self.s_stgu if q_eng == "gpsimd" else self.s_stg[st_i]
            P.op(q_eng, lambda e, e_=e_, j_=j_, st_i=st_i: e.dma_start(
                out=stg[st_i], in_=src3[e_, 128 * j_:128 * (j_ + 1), :]),
                waits=[w for w in [self.stage_free[st_i]] if w], inc=sem.bump(16))
            return sem.now

        def dma_wd_mchunk(e_, m_, st_i, q_eng):
            sem = self.s_stgu if q_eng == "gpsimd" else self.s_stg[st_i]
            P.op(q_eng, lambda e, e_=e_, m_=m_, st_i=st_i: e.dma_start(
                out=stg[st_i][:, 0:1024], in_=self.dwd[e_, 128 * m_:128 * (m_ + 1), :]),
                waits=[w for w in [self.stage_free[st_i]] if w], inc=sem.bump(16))
            return sem.now

        WGUW = MC * 1024 * 2
        WALL = MC * 1024 * 3
        if cache == "read":
            entry_all = [self.s_mm.now, self.s_io.now, self.s_ve.now,
                         self.s_ev.now, self.s_r.now, self.s_cast.now]
        for ei in range(E):
            if cache == "read":
                ww = list(wgwu_free) + (entry_all if ei == 0 else [])
                P.op("sync", lambda e, ei=ei: e.dma_start(
                    out=A[:, o_wg:o_wg + WGUW],
                    in_=self.dwcache[128 * ei:128 * (ei + 1), 0:WGUW]),
                    waits=ww, inc=self.s_stg[0].bump(16))
                wgwu_cast_ok = [self.s_stg[0].now]
                P.op("sync", lambda e, ei=ei: e.dma_start(
                    out=A[:, o_wd:o_wd + MC * 1024],
                    in_=self.dwcache[128 * ei:128 * (ei + 1), WGUW:WALL]),
                    waits=list(wd_free) + (entry_all if ei == 0 else []),
                    inc=self.s_stg[0].bump(16))
                wd_cast_val = [self.s_stg[0].now] * MC
            else:
                for j in range(DC):
                    for (src3, dsto, st_i) in ((self.dwg, o_wg, 0), (self.dwu, o_wu, 1)):
                        wdma = dma_weight_dchunk(src3, ei, j, st_i, "sync")
                        def cast(e, dsto=dsto, j=j, si=st_i):
                            d3 = A[:, dsto:dsto + MC * 1024].rearrange("p (mc q) -> p mc q", q=1024)
                            dd = d3[:, :, 128 * j:128 * (j + 1)]
                            ss = stg[si][:].rearrange("p (mc q) -> p mc q", q=128)
                            return e.activation(dd, ss, AF.Copy)
                        P.op("scalar", cast, waits=[wdma] + wgwu_free, inc=self.s_wcast.bump())
                        self.stage_free[st_i] = self.s_wcast.now
                wgwu_cast_ok = [self.s_wcast.now]
                wd_cast_val = [None] * MC

            for half in range(2):
                t0 = half * TH
                for m in range(MC):
                    for q in range(NQ):
                        pb = 0
                        for j in range(DC):
                            P.op("tensor", lambda e, m=m, q=q, j=j, pb=pb, t0=t0: e.matmul(
                                self.pg[pb][:, 0:QT],
                                A[:, o_wg + 1024 * m + 128 * j:o_wg + 1024 * m + 128 * (j + 1)],
                                self.XT[:, c.Tc * j + t0 + QT * q:c.Tc * j + t0 + QT * (q + 1)],
                                start=(j == 0), stop=(j == DC - 1)),
                                waits=(wgwu_cast_ok + [xt_ok] + ([pgpu_free[pb][0]] if j == 0 else [])),
                                inc=(self.s_mm.bump() if j == DC - 1 else None))
                        vg = self.s_mm.n
                        for j in range(DC):
                            P.op("tensor", lambda e, m=m, q=q, j=j, pb=pb, t0=t0: e.matmul(
                                self.pu[pb][:, 0:QT],
                                A[:, o_wu + 1024 * m + 128 * j:o_wu + 1024 * m + 128 * (j + 1)],
                                self.XT[:, c.Tc * j + t0 + QT * q:c.Tc * j + t0 + QT * (q + 1)],
                                start=(j == 0), stop=(j == DC - 1)),
                                waits=([pgpu_free[pb][1]] if j == 0 else []),
                                inc=(self.s_mm.bump() if j == DC - 1 else None))
                        vu = self.s_mm.n
                        sbuf_i = (m * NQ + q) % 2
                        P.op("scalar", lambda e, pb=pb, sbuf_i=sbuf_i: e.activation(
                            A[:, o_sg + QT * sbuf_i:o_sg + QT * (sbuf_i + 1)],
                            self.pg[pb][:, 0:QT], AF.Silu),
                            waits=[(self.s_mm, vg), sgsu_free[sbuf_i]], inc=self.s_ev.bump())
                        pgpu_free[pb][0] = self.s_ev.now
                        vsilu = self.s_ev.n
                        P.op("vector", lambda e, pb=pb: e.tensor_copy(
                            A[:, o_su:o_su + QT], self.pu[pb][:, 0:QT]),
                            waits=[(self.s_mm, vu), self.s_h.now], inc=self.s_ve.bump())
                        pgpu_free[pb][1] = self.s_ve.now
                        vsu = self.s_ve.n
                        P.op("vector", lambda e, m=m, q=q, sbuf_i=sbuf_i, t0=t0: e.tensor_tensor(
                            A[:, o_H + TH * m + QT * q:o_H + TH * m + QT * (q + 1)],
                            A[:, o_sg + QT * sbuf_i:o_sg + QT * (sbuf_i + 1)],
                            A[:, o_su:o_su + QT], ALU.mult),
                            waits=[(self.s_ev, vsilu), (self.s_ve, vsu), h_write_guard],
                            inc=self.s_h.bump())
                        sgsu_free[sbuf_i] = self.s_h.now
                h_all = self.s_h.n
                if half == 1:
                    hid_end = self.s_mm.n
                if half == 0 and cache != "read":
                    for m in range(MC):
                        st_i = m % 2
                        wdma = dma_wd_mchunk(ei, m, st_i, "sync")
                        def castd(e, m=m, si=st_i):
                            return e.tensor_copy(A[:, o_wd + 1024 * m:o_wd + 1024 * (m + 1)],
                                                 stg[si][:, 0:1024])
                        P.op("vector", castd, waits=[wdma] + wd_free, inc=self.s_cast.bump())
                        self.stage_free[st_i] = self.s_cast.now
                        wd_cast_val[m] = (self.s_cast, self.s_cast.n)
                    if cache == "write":
                        P.op("gpsimd", lambda e, ei=ei: e.dma_start(
                            out=self.dwcache[128 * ei:128 * (ei + 1), :],
                            in_=A[:, o_wg:o_wg + WALL]),
                            waits=wgwu_cast_ok + [wd_cast_val[MC - 1]],
                            inc=self.s_stgu.bump(16))
                        spill_done = self.s_stgu.now
                for kk in range(TH // 128):
                    k = half * (TH // 128) + kk
                    pb = kk % 2
                    for m in range(MC):
                        for hh in range(2):
                            P.op("tensor", lambda e, k=k, m=m, hh=hh, pb=pb, t0=t0: e.matmul(
                                self.po[pb][:, 512 * hh:512 * (hh + 1)],
                                A[:, o_H + TH * m + 128 * (k - (t0 // 128)):o_H + TH * m + 128 * (k - (t0 // 128) + 1)],
                                A[:, o_wd + 1024 * m + 512 * hh:o_wd + 1024 * m + 512 * (hh + 1)],
                                start=(m == 0), stop=(m == MC - 1)),
                                waits=([(self.s_h, h_all), wd_cast_val[m]] +
                                       ([po_free[pb]] if m == 0 and hh == 0 else [])),
                                inc=(self.s_mm.bump() if (m == MC - 1 and hh == 1) else None))
                    vwd = self.s_mm.n
                    P.op("vector", lambda e, k=k, ei=ei, pb=pb: e.scalar_tensor_tensor(
                        self.R[:, 1024 * k:1024 * (k + 1)], self.po[pb][:],
                        self.Gh[:, 8 * k + ei:8 * k + ei + 1],
                        self.R[:, 1024 * k:1024 * (k + 1)], ALU.mult, ALU.add),
                        waits=[(self.s_mm, vwd), g_all], inc=self.s_r.bump())
                    po_free[pb] = self.s_r.now
                h_write_guard = self.s_mm.now
            wgwu_free = [(self.s_mm, hid_end)]
            wd_free = [self.s_mm.now]
            if cache == "write":
                wgwu_free.append(spill_done)
                wd_free.append(spill_done)
        return self.s_r.now

    def p_att0(self, r_dep):
        P, c = self.P, self.cfg
        self.r_chunk_done = [None] * c.NCH
        A = self.ABF
        Tc, S, DC, NCH = c.Tc, c.S, c.DC, c.NCH
        NS = S // 128
        FQ = min(512, Tc)
        NFQ = Tc // FQ
        xt_ok = self.emit_ln([r_dep])
        guard_mm = self.s_mm.now
        guard_r = self.s_r.now
        guard_io = self.s_io.now
        guard_sp = self.s_stgu.now

        a_cosQ, a_sinQ = 0, Tc
        a_cosK, a_sinK = 2 * Tc, 3 * Tc
        a_vtm = 4 * Tc
        a_kallc, a_kalls, a_vall = 5 * Tc, 5 * Tc + S, 5 * Tc + 2 * S
        a_PT = 5 * Tc + 3 * S
        a_OT = a_PT + 2 * 512
        a_vT = a_PT
        a_w = a_OT + Tc
        a_dbf = a_w + 4096
        a_bo0 = a_dbf + Tc
        assert a_bo0 + 1024 <= 46592
        self.a0_Dbf = A[0:1, a_dbf:a_dbf + Tc]
        self.a0_bobf = A[0:1, a_bo0:a_bo0 + 1024]
        self.a0_D = self.AFP[0:1, 0:Tc]
        self.a0_Dpp = self.AFP[:, 3968:3968 + NCH]

        stg = [self.AFP[:, 0:1408], self.AFP[:, 1408:2816]]
        wslots = []
        for i, dsrc in enumerate((self.da0_wqT, self.da0_wkT, self.da0_wvT, self.da0_woT)):
            si = i % 2
            w = [self.stage_free[si]] if self.stage_free[si] else []
            P.op("sync", lambda e, dsrc=dsrc, si=si: e.dma_start(
                out=stg[si][:, 0:1024].rearrange("p (j q) -> p j q", q=128),
                in_=dsrc[:].rearrange("(j p) q -> p j q", p=128)),
                waits=w + [guard_r], inc=self.s_stg[si].bump(16))
            wd = self.s_stg[si].now
            P.op("vector", lambda e, i=i, si=si: e.tensor_copy(
                A[:, a_w + 1024 * i:a_w + 1024 * (i + 1)], stg[si][:, 0:1024]),
                waits=[wd, guard_mm, guard_io, guard_sp], inc=self.s_cast.bump())
            self.stage_free[si] = self.s_cast.now
        w_ok = self.s_cast.now
        wq = lambda j: A[:, a_w + 128 * j:a_w + 128 * (j + 1)]
        wk = lambda j: A[:, a_w + 1024 + 128 * j:a_w + 1024 + 128 * (j + 1)]
        wv = lambda j: A[:, a_w + 2048 + 128 * j:a_w + 2048 + 128 * (j + 1)]
        woT = A[:, a_w + 3072:a_w + 4096]
        si = 0
        w = [self.stage_free[si]] if self.stage_free[si] else []
        P.op("sync", lambda e: e.dma_start(out=self.AFP[:, 0:5], in_=self.da0_vecs[:, 0:5]),
             waits=w, inc=self.s_stg[0].bump(16))
        vd = self.s_stg[0].now
        P.op("vector", lambda e: e.tensor_copy(self.a0v[:, 0:5], self.AFP[:, 0:5]),
             waits=[vd], inc=self.s_cast.bump())
        self.stage_free[0] = self.s_cast.now
        w = [self.stage_free[1]] if self.stage_free[1] else []
        P.op("sync", lambda e: e.dma_start(out=self.AFP[0:1, 1408:2432], in_=self.da0_bo),
             waits=w, inc=self.s_stg[1].bump(16))
        bod = self.s_stg[1].now
        P.op("vector", lambda e: e.tensor_copy(self.a0_bobf[:], self.AFP[0:1, 1408:2432]),
             waits=[bod, guard_sp], inc=self.s_cast.bump())
        self.stage_free[1] = self.s_cast.now
        sv = self.s_cast
        P.op("vector", lambda e: e.tensor_scalar(self.a0v[:, 5:6], self.a0v[:, 0:1],
                                                 1e-8, None, ALU.add),
             waits=[sv.now], inc=sv.bump())
        P.op("vector", lambda e: e.reciprocal(self.a0v[:, 5:6], self.a0v[:, 5:6]),
             waits=[sv.now], inc=sv.bump())
        P.op("vector", lambda e: e.tensor_scalar(self.a0v[:, 6:7], self.a0v[:, 5:6],
                                                 1.0 / TWO_PI, None, ALU.mult),
             waits=[sv.now], inc=sv.bump())
        P.op("vector", lambda e: e.scalar_tensor_tensor(self.a0v[:, 7:8], self.a0v[:, 2:3],
                                                        1.0, self.a0v[:, 5:6], ALU.mult, ALU.mult),
             waits=[sv.now], inc=sv.bump())
        P.op("vector", lambda e: e.tensor_tensor(self.a0v[:, 7:8], self.a0v[:, 7:8],
                                                 self.a0v[:, 1:2], ALU.add),
             waits=[sv.now], inc=sv.bump())
        P.op("vector", lambda e: e.tensor_scalar(self.a0v[:, 7:8], self.a0v[:, 7:8],
                                                 1.0 / TWO_PI, None, ALU.mult),
             waits=[sv.now], inc=sv.bump())
        P.op("vector", lambda e: e.scalar_tensor_tensor(self.a0v[:, 8:9], self.a0v[:, 3:4],
                                                        1.0, self.a0v[:, 5:6], ALU.mult, ALU.mult),
             waits=[sv.now], inc=sv.bump())
        P.op("vector", lambda e: e.tensor_tensor(self.a0v[:, 8:9], self.a0v[:, 8:9],
                                                 self.a0v[:, 1:2], ALU.add),
             waits=[sv.now], inc=sv.bump())
        P.op("vector", lambda e: e.tensor_scalar(self.a0v[:, 8:9], self.a0v[:, 8:9],
                                                 1.0 / TWO_PI, None, ALU.mult),
             waits=[sv.now], inc=sv.bump())
        vecs_ok = sv.now

        NT = Tc // 512 if Tc >= 512 else 1
        PW = min(512, Tc)
        pqk = [self.pg[0], self.pu[0]]
        pq_free = [None, None]
        ubuf = self.AFP[:, 2048:2048 + PW]
        vb = self.AFP[:, 2816:2816 + PW]
        rb = self.AFP[:, 3328:3328 + PW]

        def proj_one(wt, bias_col, kind):
            for tt in range(NT):
                pb = tt % 2
                bank = pqk[pb]
                for j in range(DC):
                    ww = [xt_ok, w_ok] if j == 0 else []
                    if j == 0 and pq_free[pb] is not None:
                        ww.append(pq_free[pb])
                    P.op("tensor", lambda e, wt=wt, j=j, tt=tt, bank=bank: e.matmul(
                        bank[:, 0:PW], wt(j),
                        self.XT[:, Tc * j + PW * tt:Tc * j + PW * (tt + 1)],
                        start=(j == 0), stop=(j == DC - 1)),
                        waits=ww, inc=(self.s_mm.bump() if j == DC - 1 else None))
                vmm = self.s_mm.n
                if kind != "v":
                    a_cos = a_cosQ if kind == "q" else a_cosK
                    a_sin = a_sinQ if kind == "q" else a_sinK
                    P.op("vector", lambda e, bank=bank, bc=bias_col: e.tensor_scalar(
                        ubuf, bank[:, 0:PW],
                        self.a0v[:, 6:7], self.a0v[:, bc:bc + 1], ALU.mult, ALU.add),
                        waits=[(self.s_mm, vmm), vecs_ok, self.s_cast.now,
                               self.s_ve.now, self.s_ev.now],
                        inc=self.s_ve.bump())
                    pq_free[pb] = self.s_ve.now
                    for (shift, dsto) in ((128.5, a_sin), (128.75, a_cos)):
                        P.op("vector", lambda e, shift=shift: e.tensor_scalar(
                            vb, ubuf, shift, None, ALU.add),
                            waits=[self.s_ve.now, self.s_ev.now], inc=self.s_ve.bump())
                        P.op("vector", lambda e: e.tensor_copy(self.ibuf[:, 0:PW], vb),
                             waits=[self.s_ve.now], inc=self.s_ve.bump())
                        P.op("vector", lambda e: e.tensor_copy(rb, self.ibuf[:, 0:PW]),
                             waits=[self.s_ve.now], inc=self.s_ve.bump())
                        P.op("vector", lambda e: e.tensor_sub(vb, vb, rb),
                             waits=[self.s_ve.now], inc=self.s_ve.bump())
                        P.op("vector", lambda e: e.tensor_single_scalar(
                            rb, vb, 0.0, ALU.is_lt),
                            waits=[self.s_ve.now], inc=self.s_ve.bump())
                        P.op("vector", lambda e: e.scalar_tensor_tensor(
                            vb, rb, -0.5, vb, ALU.add, ALU.add),
                            waits=[self.s_ve.now], inc=self.s_ve.bump())
                        P.op("scalar", lambda e, tt=tt, dsto=dsto: e.activation(
                            A[:, dsto + PW * tt:dsto + PW * (tt + 1)],
                            vb, AF.Sin, scale=TWO_PI),
                            waits=[self.s_ve.now], inc=self.s_ev.bump())
                else:
                    P.op("vector", lambda e, tt=tt, bank=bank: e.tensor_scalar(
                        A[:, a_vT + PW * tt:a_vT + PW * (tt + 1)], bank[:, 0:PW],
                        self.a0v[:, 4:5], None, ALU.add),
                        waits=[(self.s_mm, vmm), vecs_ok], inc=self.s_ve.bump())
                    pq_free[pb] = self.s_ve.now

        proj_one(wk, 8, "k")
        ktrig_ok = self.s_ev.now
        proj_one(wv, 4, "v")

        kdone = 0
        while kdone < NCH:
            grp = min(8, NCH - kdone)
            for i in range(grp):
                k = kdone + i
                P.op("tensor", lambda e, k=k, i=i: e.transpose(
                    self.pt[:, 128 * i:128 * (i + 1)],
                    A[:, a_vT + 128 * k:a_vT + 128 * (k + 1)], self.ident_bf[:]),
                    waits=([self.s_ve.now, self.s_xt.now] if i == 0 else []),
                    inc=(self.s_mm.bump() if i == grp - 1 else None))
            vt = self.s_mm.n
            P.op("vector", lambda e, kdone=kdone, grp=grp: e.tensor_copy(
                A[:, a_vtm + 128 * kdone:a_vtm + 128 * (kdone + grp)],
                self.pt[:, 0:128 * grp]),
                waits=[(self.s_mm, vt)], inc=self.s_xt.bump())
            kdone += grp
        vtm_ok = self.s_xt.now

        sio = self.s_io
        P.op("gpsimd", lambda e: e.dma_start(out=self.dcc0_in[0:128, :],
                                             in_=A[:, a_cosK:a_cosK + Tc]),
             waits=[ktrig_ok], inc=sio.bump(16))
        P.op("gpsimd", lambda e: e.dma_start(out=self.dcc0_in[128:256, :],
                                             in_=A[:, a_sinK:a_sinK + Tc]),
             inc=sio.bump(16))
        P.op("gpsimd", lambda e: e.dma_start(out=self.dcc0_in[256:384, :],
                                             in_=A[:, a_vtm:a_vtm + Tc]),
             waits=[vtm_ok], inc=sio.bump(16))
        P.op("gpsimd", lambda e: e.collective_compute(
            "AllGather", ALU.bypass, replica_groups=[[0, 1, 2, 3], [4, 5, 6, 7]],
            ins=[self.dcc0_in.opt()], outs=[self.dcc0_out.opt()]),
            waits=[sio.now], inc=self.s_cc.bump())
        cc_ok = self.s_cc.now

        proj_one(wq, 7, "q")
        trig_ok = self.s_ev.now
        for r in range(4):
            P.op("gpsimd", lambda e, r=r: e.dma_start(
                out=A[:, a_kallc + Tc * r:a_kallc + Tc * (r + 1)],
                in_=self.dcc0_out[384 * r:384 * r + 128, :]),
                waits=[cc_ok], inc=sio.bump(16))
            P.op("gpsimd", lambda e, r=r: e.dma_start(
                out=A[:, a_kalls + Tc * r:a_kalls + Tc * (r + 1)],
                in_=self.dcc0_out[384 * r + 128:384 * r + 256, :]),
                inc=sio.bump(16))
            P.op("gpsimd", lambda e, r=r: e.dma_start(
                out=A[:, a_vall + Tc * r:a_vall + Tc * (r + 1)],
                in_=self.dcc0_out[384 * r + 256:384 * r + 384, :]),
                inc=sio.bump(16))
        kv_ok = sio.now

        sbanks = [self.pg[0], self.pu[0]]
        sbank_free = [None, None]
        pt_free = [None, None]
        for qt in range(NFQ):
            pO = self.po[0][:, 512 * (qt % 2):512 * (qt % 2) + FQ]
            pD = self.po[1][0:1, 512 * (qt % 2):512 * (qt % 2) + FQ]
            for kc in range(NS):
                pb = kc % 2
                sb_ = sbanks[pb]
                ww = [kv_ok, trig_ok]
                if sbank_free[pb] is not None:
                    ww.append(sbank_free[pb])
                P.op("tensor", lambda e, kc=kc, qt=qt, sb_=sb_: e.matmul(
                    sb_[:, 0:FQ], A[:, a_kallc + 128 * kc:a_kallc + 128 * (kc + 1)],
                    A[:, a_cosQ + FQ * qt:a_cosQ + FQ * (qt + 1)],
                    start=True, stop=False), waits=ww)
                P.op("tensor", lambda e, kc=kc, qt=qt, sb_=sb_: e.matmul(
                    sb_[:, 0:FQ], A[:, a_kalls + 128 * kc:a_kalls + 128 * (kc + 1)],
                    A[:, a_sinQ + FQ * qt:a_sinQ + FQ * (qt + 1)],
                    start=False, stop=True), inc=self.s_mm.bump())
                vS = self.s_mm.n
                ptb = a_PT + 512 * (kc % 2)
                ww = [(self.s_mm, vS)]
                if pt_free[kc % 2] is not None:
                    ww.append(pt_free[kc % 2])
                P.op("scalar", lambda e, sb_=sb_, ptb=ptb: e.activation(
                    A[:, ptb:ptb + FQ], sb_[:, 0:FQ], AF.Exp,
                    bias=self.cpool[:, 2:3], scale=INV_SQRT_H0),
                    waits=ww + [self.consts_ready], inc=self.s_ev.bump())
                sbank_free[pb] = self.s_ev.now
                vP = self.s_ev.n
                P.op("tensor", lambda e, kc=kc, ptb=ptb, pO=pO: e.matmul(
                    pO, A[:, a_vall + 128 * kc:a_vall + 128 * (kc + 1)],
                    A[:, ptb:ptb + FQ], start=(kc == 0), stop=(kc == NS - 1)),
                    waits=[(self.s_ev, vP)])
                P.op("tensor", lambda e, kc=kc, ptb=ptb, pD=pD: e.matmul(
                    pD, self.ones_bf[:], A[:, ptb:ptb + FQ],
                    start=(kc == 0), stop=(kc == NS - 1)),
                    waits=([self.consts_ready] + ([self.s_ve.now] if kc == 0 else [])),
                    inc=self.s_mm.bump())
                pt_free[kc % 2] = self.s_mm.now
            vOD = self.s_mm.n
            P.op("scalar", lambda e, qt=qt, pO=pO: e.copy(
                A[:, a_OT + FQ * qt:a_OT + FQ * (qt + 1)], pO),
                waits=[(self.s_mm, vOD)], inc=self.s_ev.bump())
            P.op("vector", lambda e, qt=qt, pD=pD: e.tensor_copy(
                self.a0_D[0:1, FQ * qt:FQ * (qt + 1)], pD),
                waits=[(self.s_mm, vOD)], inc=self.s_ve.bump())
        ot_ok = self.s_ev.now
        d_ok = self.s_ve.now

        P.op("vector", lambda e: e.tensor_copy(self.a0_Dbf[:], self.a0_D[:]),
             waits=[d_ok], inc=self.s_ve.bump())
        P.op("vector", lambda e: e.reciprocal(self.a0_D[:], self.a0_D[:]),
             waits=[d_ok, self.s_ve.now], inc=self.s_ve.bump())
        drec_ok = self.s_ve.n
        for k in range(NCH):
            P.op("tensor", lambda e, k=k: e.matmul(
                self.po[1][:, k:k + 1], self.a0_D[0:1, 128 * k:128 * (k + 1)],
                self.cpool[0:1, 4:5], start=True, stop=True),
                waits=[(self.s_ve, drec_ok), self.consts_ready] if k == 0 else [],
                inc=(self.s_mm.bump() if k == NCH - 1 else None))
        vdt = self.s_mm.n
        P.op("vector", lambda e: e.tensor_copy(self.a0_Dpp[:], self.po[1][:, 0:NCH]),
             waits=[(self.s_mm, vdt)], inc=self.s_ve.bump())
        dpp_ok = self.s_ve.now

        po_free2 = [None, None]
        for k in range(NCH):
            pb = k % 2
            bank = self.po[pb]
            ww = [ot_ok, self.s_ve.now, self.s_cast.now]
            if po_free2[pb] is not None:
                ww.append(po_free2[pb])
            for hh in range(2):
                P.op("tensor", lambda e, k=k, hh=hh, bank=bank: e.matmul(
                    bank[:, 512 * hh:512 * (hh + 1)],
                    self.a0_Dbf[0:1, 128 * k:128 * (k + 1)],
                    self.a0_bobf[0:1, 512 * hh:512 * (hh + 1)],
                    start=True, stop=False), waits=(ww if hh == 0 else []))
                P.op("tensor", lambda e, k=k, hh=hh, bank=bank: e.matmul(
                    bank[:, 512 * hh:512 * (hh + 1)],
                    A[:, a_OT + 128 * k:a_OT + 128 * (k + 1)],
                    woT[:, 512 * hh:512 * (hh + 1)],
                    start=False, stop=True),
                    inc=(self.s_mm.bump() if hh == 1 else None))
            vmm = self.s_mm.n
            P.op("vector", lambda e, k=k, bank=bank: e.scalar_tensor_tensor(
                self.R[:, 1024 * k:1024 * (k + 1)], bank[:],
                self.a0_Dpp[:, k:k + 1], self.R[:, 1024 * k:1024 * (k + 1)],
                ALU.mult, ALU.add),
                waits=[(self.s_mm, vmm), dpp_ok], inc=self.s_r.bump())
            po_free2[pb] = self.s_r.now
        self.stage_free[0] = self.s_r.now
        self.stage_free[1] = self.s_r.now
        return self.s_r.now

    def p_att1(self, r_dep):
        P, c = self.P, self.cfg
        self.r_chunk_done = [None] * c.NCH
        A = self.ABF
        Tc, DC, NCH, NP = c.Tc, c.DC, c.NCH, c.NPAIR
        NB = c.NBLK
        TCH = Tc + 128
        NT = max(1, Tc // 512)
        PW = min(512, Tc)
        SG = [[0, 1, 2, 3], [4, 5, 6, 7]]
        xt_ok = self.emit_ln([r_dep])
        guard_mm = self.s_mm.now
        guard_r = self.s_r.now
        guard_io = self.s_io.now
        guard_sp = self.s_stgu.now
        sv, se, sm = self.s_ve, self.s_ev, self.s_mm

        a_kh = 0
        a_vst = a_kh + 8 * TCH
        a_w = a_vst + 8192
        a_gq = a_w + 8192
        a_gk = a_gq + 1024
        a_gv = a_gk + 1024
        a_vedge = a_gv + 1024
        a_qp = a_vedge + 1024
        a_pl = a_qp + 1024
        a_pgl = a_pl + 1536
        a_pt = a_pgl + 1024
        a_ctxp = a_pt + 3072
        a_rcn = a_ctxp + 1024
        assert a_rcn + 1024 <= 46592, a_rcn + 1024
        a_rpt = a_pt
        a_qst = a_pt

        stg = [self.AFP[:, 0:1408], self.AFP[:, 1408:2816]]
        self.a1_ctxred = self.AFP[:, 2816:3840]
        self.a1_drow = self.AFP[0:1, 0:2048]
        wsb = lambda j: A[:, a_w + 1024 * j:a_w + 1024 * (j + 1)]

        si = 0
        w = [x for x in [self.stage_free[0]] if x]
        P.op("sync", lambda e: e.dma_start(out=self.AFP[:, 0:24], in_=self.da1_bqkv),
             waits=w, inc=self.s_stg[0].bump(16))
        P.op("vector", lambda e: e.tensor_copy(self.a1_bqkv[:], self.AFP[:, 0:24]),
             waits=[self.s_stg[0].now], inc=self.s_cast.bump())
        self.stage_free[0] = self.s_cast.now
        self.a1_bobf = A[0:1, a_gq:a_gq + 1024]
        P.op("sync", lambda e: e.dma_start(out=self.a1_rsel[:], in_=self.da1_rolesel),
             inc=self.s_msk.bump(16))
        small_ok = self.s_msk.now
        bias_ok = self.s_cast.now
        mrep_ok = small_ok

        def load_w(dsrc, ready_evt):
            for j in range(DC):
                sj = j % 2
                w = [x for x in [self.stage_free[sj]] if x]
                P.op("sync", lambda e, j=j, sj=sj, dsrc=dsrc: e.dma_start(
                    out=stg[sj][:, 0:1024], in_=dsrc[128 * j:128 * (j + 1), :]),
                    waits=w, inc=self.s_stg[sj].bump(16))
                wd = self.s_stg[sj].now
                P.op("vector", lambda e, j=j, sj=sj: e.tensor_copy(wsb(j), stg[sj][:, 0:1024]),
                     waits=[wd] + ready_evt, inc=self.s_cast.bump())
                self.stage_free[sj] = self.s_cast.now
            return self.s_cast.now

        pq_free = [None, None]
        pqk = [self.pg[0], self.pu[0]]

        def proj_tiles(cast_ok, bias_col, emit_out):
            for oc in range(DC):
                for tt in range(NT):
                    pb = (oc * NT + tt) % 2
                    bank = pqk[pb]
                    for j in range(DC):
                        ww = [xt_ok, cast_ok] if j == 0 else []
                        if j == 0 and pq_free[pb] is not None:
                            ww.append(pq_free[pb])
                        P.op("tensor", lambda e, oc=oc, tt=tt, j=j, bank=bank: e.matmul(
                            bank[:, 0:PW], wsb(j)[:, 128 * oc:128 * (oc + 1)],
                            self.XT[:, Tc * j + PW * tt:Tc * j + PW * (tt + 1)],
                            start=(j == 0), stop=(j == DC - 1)),
                            waits=ww, inc=(sm.bump() if j == DC - 1 else None))
                    emit_out(oc, tt, bank, sm.n)

        kcast = load_w(self.da1_wkT, [guard_mm, guard_io])
        def k_out(oc, tt, bank, vmm):
            P.op("vector", lambda e, oc=oc, tt=tt, bank=bank: e.tensor_scalar(
                A[:, a_kh + TCH * oc + PW * tt:a_kh + TCH * oc + PW * (tt + 1)],
                bank[:, 0:PW], self.a1_bqkv[:, 8 + oc:9 + oc], None, ALU.add),
                waits=[(sm, vmm), bias_ok, guard_io, guard_sp], inc=sv.bump())
            pq_free[(oc * NT + tt) % 2] = sv.now
        proj_tiles(kcast, 8, k_out)
        k_ok = sv.now

        vcast = load_w(self.da1_wvT, [self.s_mm.now])
        vT = lambda oc: A[:, a_vst + PW * oc:a_vst + PW * (oc + 1)]
        for tt in range(NT):
            for oc in range(DC):
                pb = (tt * DC + oc) % 2
                bank = pqk[pb]
                for j in range(DC):
                    ww = [xt_ok, vcast] if j == 0 else []
                    if j == 0 and pq_free[pb] is not None:
                        ww.append(pq_free[pb])
                    P.op("tensor", lambda e, oc=oc, tt=tt, j=j, bank=bank: e.matmul(
                        bank[:, 0:PW], wsb(j)[:, 128 * oc:128 * (oc + 1)],
                        self.XT[:, Tc * j + PW * tt:Tc * j + PW * (tt + 1)],
                        start=(j == 0), stop=(j == DC - 1)),
                        waits=ww, inc=(sm.bump() if j == DC - 1 else None))
                vmm = sm.n
                P.op("vector", lambda e, oc=oc, bank=bank: e.tensor_scalar(
                    vT(oc), bank[:, 0:PW], self.a1_bqkv[:, 16 + oc:17 + oc], None, ALU.add),
                    waits=[(sm, vmm), bias_ok, sm.now, guard_io], inc=sv.bump())
                pq_free[pb] = sv.now
            vT_ok = sv.n
            for i in range(PW // 128):
                for oc in range(DC):
                    P.op("tensor", lambda e, i=i, oc=oc: e.transpose(
                        self.pt[:, 128 * oc:128 * (oc + 1)],
                        vT(oc)[:, 128 * i:128 * (i + 1)], self.ident_bf[:]),
                        waits=([(sv, vT_ok), self.s_xt.now] if oc == 0 else []),
                        inc=(sm.bump() if oc == DC - 1 else None))
                vt2 = sm.n
                P.op("vector", lambda e: e.tensor_copy(A[:, a_qst:a_qst + 1024], self.pt[:]),
                     waits=[(sm, vt2), self.s_io.now], inc=self.s_xt.bump())
                kchunk = tt * (PW // 128) + i
                P.op("gpsimd", lambda e, kchunk=kchunk: e.dma_start(
                    out=self.dv1[128 * kchunk:128 * (kchunk + 1), :],
                    in_=A[:, a_qst:a_qst + 1024]),
                    waits=[self.s_xt.now], inc=self.s_io.bump(16))
        vspill_ok = self.s_io.now

        qcast = load_w(self.da1_wqT, [self.s_mm.now])
        def q_out(oc, tt, bank, vmm):
            P.op("vector", lambda e, oc=oc, tt=tt, bank=bank: e.tensor_scalar(
                A[:, a_qst + 1024:a_qst + 1024 + PW], bank[:, 0:PW],
                self.a1_bqkv[:, 0 + oc:1 + oc], None, ALU.add),
                waits=[(sm, vmm), bias_ok, self.s_io.now], inc=sv.bump())
            pq_free[(oc * NT + tt) % 2] = sv.now
            P.op("gpsimd", lambda e, oc=oc, tt=tt: e.dma_start(
                out=self.dq1[:, Tc * oc + PW * tt:Tc * oc + PW * (tt + 1)],
                in_=A[:, a_qst + 1024:a_qst + 1024 + PW]),
                waits=[sv.now], inc=self.s_io.bump(16))
        proj_tiles(qcast, 0, q_out)
        qspill_ok = self.s_io.now
        if ATT1_STOP == "proj":
            return self.s_r.now

        kh3 = lambda off: A[:, a_kh + off:a_kh + off + 8 * TCH].rearrange(
            "p (j t) -> p j t", t=TCH)[:, :, 0:64]
        P.op("gpsimd", lambda e: e.dma_start(
            out=self.dcc1_in[:, 0:512].rearrange("p (j t) -> p j t", t=64),
            in_=A[:, a_kh:a_kh + 8 * TCH].rearrange("p (j t) -> p j t", t=TCH)[:, :, 0:64]),
            waits=[k_ok, vspill_ok, qspill_ok], inc=self.s_io.bump(16))
        P.op("gpsimd", lambda e: e.dma_start(
            out=self.dcc1_in[:, 512:1024].rearrange("p (j t) -> p j t", t=64),
            in_=A[:, a_kh:a_kh + 8 * TCH].rearrange("p (j t) -> p j t", t=TCH)[:, :, Tc - 64:Tc]),
            inc=self.s_io.bump(16))
        P.op("gpsimd", lambda e: e.dma_start(
            out=self.dcc1_in[:, 1024:1536].rearrange("p (j t) -> p j t", t=64),
            in_=self.dq1[:].rearrange("p (j t) -> p j t", t=Tc)[:, :, 0:64]),
            waits=[qspill_ok], inc=self.s_io.bump(16))
        P.op("gpsimd", lambda e: e.dma_start(
            out=self.dcc1_in[:, 1536:2048].rearrange("p (j t) -> p j t", t=64),
            in_=self.dq1[:].rearrange("p (j t) -> p j t", t=Tc)[:, :, Tc - 64:Tc]),
            inc=self.s_io.bump(16))
        P.op("gpsimd", lambda e: e.dma_start(
            out=self.dcc1_in[0:64, 2048:3072], in_=self.dv1[0:64, :]),
            waits=[vspill_ok], inc=self.s_io.bump(16))
        P.op("gpsimd", lambda e: e.dma_start(
            out=self.dcc1_in[64:128, 2048:3072], in_=self.dv1[Tc - 64:Tc, :]),
            inc=self.s_io.bump(16))
        P.op("gpsimd", lambda e: e.collective_compute(
            "AllGather", ALU.bypass, replica_groups=SG,
            ins=[self.dcc1_in.opt()], outs=[self.dcc1_out.opt()]),
            waits=[self.s_io.now], inc=self.s_cc.bump())
        ag_ok = self.s_cc.now

        P.op("gpsimd", lambda e: e.dma_start(
            out=A[:, a_gq:a_gq + 1024].rearrange("p (j t) -> p j t", t=128)[:, :, 0:64],
            in_=self.dcc1_out[0:128, 1024:1536].rearrange("p (j t) -> p j t", t=64)),
            waits=[ag_ok], inc=self.s_io.bump(16))
        P.op("gpsimd", lambda e: e.dma_start(
            out=A[:, a_gq:a_gq + 1024].rearrange("p (j t) -> p j t", t=128)[:, :, 64:128],
            in_=self.dcc1_out[128:256, 1536:2048].rearrange("p (j t) -> p j t", t=64)),
            inc=self.s_io.bump(16))
        P.op("gpsimd", lambda e: e.dma_start(
            out=A[:, a_gk:a_gk + 1024].rearrange("p (j t) -> p j t", t=128)[:, :, 0:64],
            in_=self.dcc1_out[0:128, 0:512].rearrange("p (j t) -> p j t", t=64)),
            inc=self.s_io.bump(16))
        P.op("gpsimd", lambda e: e.dma_start(
            out=A[:, a_gk:a_gk + 1024].rearrange("p (j t) -> p j t", t=128)[:, :, 64:128],
            in_=self.dcc1_out[128:256, 512:1024].rearrange("p (j t) -> p j t", t=64)),
            inc=self.s_io.bump(16))
        P.op("gpsimd", lambda e: e.dma_start(
            out=A[0:64, a_gv:a_gv + 1024], in_=self.dcc1_out[0:64, 2048:3072]),
            inc=self.s_io.bump(16))
        P.op("gpsimd", lambda e: e.dma_start(
            out=A[64:128, a_gv:a_gv + 1024], in_=self.dcc1_out[128 + 64:256, 2048:3072]),
            inc=self.s_io.bump(16))
        gfix_ok = self.s_io.now
        if ATT1_STOP == "ag":
            return self.s_r.now

        scr = A[:, a_pt:a_pt + 512]
        scr2 = A[:, a_pt + 512:a_pt + 1024]
        kh_slot = lambda off: A[:, a_kh:a_kh + 8 * TCH].rearrange(
            "p (j t) -> p j t", t=TCH)[:, :, Tc + off:Tc + off + 64]
        def masked_fill(dst3, piece_col, sel_base, name):
            P.op("vector", lambda e, dst3=dst3: e.memset(dst3, 0.0),
                 waits=[k_ok, sv.now], inc=sv.bump())
            for r in range(4):
                P.op("gpsimd", lambda e, r=r, piece_col=piece_col: e.dma_start(
                    out=scr.rearrange("p (j t) -> p j t", t=64),
                    in_=self.dcc1_out[128 * r:128 * (r + 1),
                                      piece_col:piece_col + 512].rearrange(
                                          "p (j t) -> p j t", t=64)),
                    waits=[ag_ok, sv.now, self.s_io.now], inc=self.s_io.bump(16))
                P.op("vector", lambda e, r=r, sel_base=sel_base: e.tensor_scalar(
                    scr2, scr, self.a1_rsel[:, sel_base + r:sel_base + r + 1],
                    None, ALU.mult),
                    waits=[self.s_io.now, small_ok], inc=sv.bump())
                P.op("vector", lambda e, dst3=dst3: e.tensor_tensor(
                    dst3, dst3, scr2.rearrange("p (j t) -> p j t", t=64), ALU.add),
                    waits=[sv.now], inc=sv.bump())
        masked_fill(kh_slot(0), 512, 0, "kprev")
        masked_fill(kh_slot(64), 0, 4, "knext")
        def masked_fill_v(p0, rowr, sel_base):
            dst = A[p0:p0 + 64, a_vedge:a_vedge + 1024]
            P.op("vector", lambda e, dst=dst: e.memset(dst, 0.0),
                 waits=[sv.now], inc=sv.bump())
            for r in range(4):
                P.op("gpsimd", lambda e, r=r, rowr=rowr, p0=p0: e.dma_start(
                    out=scr[p0:p0 + 64, 0:512], in_=self.dcc1_out[
                        128 * r + rowr:128 * r + rowr + 64, 2048:2560]),
                    waits=[ag_ok, sv.now, self.s_io.now], inc=self.s_io.bump(16))
                P.op("gpsimd", lambda e, r=r, rowr=rowr, p0=p0: e.dma_start(
                    out=scr2[p0:p0 + 64, 0:512], in_=self.dcc1_out[
                        128 * r + rowr:128 * r + rowr + 64, 2560:3072]),
                    inc=self.s_io.bump(16))
                P.op("vector", lambda e, r=r, sel_base=sel_base, dst=dst, p0=p0: e.scalar_tensor_tensor(
                    dst[:, 0:512], scr[p0:p0 + 64, 0:512],
                    self.a1_rsel[p0:p0 + 64, sel_base + r:sel_base + r + 1],
                    dst[:, 0:512], ALU.mult, ALU.add),
                    waits=[self.s_io.now, small_ok], inc=sv.bump())
                P.op("vector", lambda e, r=r, sel_base=sel_base, dst=dst, p0=p0: e.scalar_tensor_tensor(
                    dst[:, 512:1024], scr2[p0:p0 + 64, 0:512],
                    self.a1_rsel[p0:p0 + 64, sel_base + r:sel_base + r + 1],
                    dst[:, 512:1024], ALU.mult, ALU.add),
                    waits=[sv.now], inc=sv.bump())
        masked_fill_v(64, 64, 0)
        masked_fill_v(0, 0, 4)
        halo_ok = sv.now

        wocast = load_w(self.da1_woT, [self.s_mm.now])

        P.op("vector", lambda e: e.memset(self.a1_drow[:], 0.0),
             waits=[sv.now, self.s_cast.now], inc=sv.bump())
        vgrp = [A[:, a_vst:a_vst + 4096], A[:, a_vst + 4096:a_vst + 8192]]
        vgrp_free = [None, None]
        rpt_free = [None, None]
        pS_free = [None, None]
        NKG = max(1, NCH // 4)
        KGW = min(4, NCH)
        row_ctx_started = False
        for kcg in range(NKG):
            gi = kcg % 2
            ww = [vspill_ok, self.s_cast.now]
            if vgrp_free[gi] is not None:
                ww.append(vgrp_free[gi])
            for i in range(KGW):
                P.op("sync", lambda e, kcg=kcg, i=i, gi=gi: e.dma_start(
                    out=vgrp[gi][:, 1024 * i:1024 * (i + 1)],
                    in_=self.dv1[128 * (KGW * kcg + i):128 * (KGW * kcg + i + 1), :]),
                    waits=(ww if i == 0 else []), inc=self.s_sl[gi].bump(16))
            vg_ok = self.s_sl[gi].now
            for h in range(16):
                hp, hc = h % 2, h // 2
                pb = (kcg * 16 + h) % 2
                bank = pqk[pb]
                for i in range(KGW):
                    ww = [gfix_ok, vg_ok, k_ok]
                    if i == 0 and pS_free[pb] is not None:
                        ww.append(pS_free[pb])
                    P.op("tensor", lambda e, kcg=kcg, h=h, i=i, hp=hp, hc=hc, bank=bank: e.matmul(
                        bank[:, 128 * i:128 * (i + 1)],
                        A[64 * hp:64 * hp + 64,
                          a_kh + TCH * hc + 128 * (KGW * kcg + i):
                          a_kh + TCH * hc + 128 * (KGW * kcg + i + 1)],
                        A[64 * hp:64 * hp + 64, a_gq + 128 * hc:a_gq + 128 * (hc + 1)],
                        start=True, stop=True),
                        waits=(ww if i == 0 else []),
                        inc=(sm.bump() if i == KGW - 1 else None))
                vS = sm.n
                rp = (kcg * 16 + h) % 2
                ww2 = [(sm, vS), self.consts_ready]
                if rpt_free[rp] is not None:
                    ww2.append(rpt_free[rp])
                P.op("scalar", lambda e, rp=rp, bank=bank: e.activation(
                    A[:, a_rpt + 512 * rp:a_rpt + 512 * rp + 128 * KGW],
                    bank[:, 0:128 * KGW], AF.Exp,
                    bias=self.cpool[:, 3:4], scale=L1_SCALE),
                    waits=ww2, inc=se.bump())
                pS_free[pb] = se.now
                vP = se.n
                for i in range(KGW):
                    P.op("tensor", lambda e, kcg=kcg, h=h, i=i, hp=hp, hc=hc, rp=rp, gi=gi: e.matmul(
                        self.po[0][64 * hp:64 * hp + 64, 128 * hc:128 * (hc + 1)],
                        vgrp[gi][:, 1024 * i + 64 * h:1024 * i + 64 * h + 64],
                        A[:, a_rpt + 512 * rp + 128 * i:a_rpt + 512 * rp + 128 * (i + 1)],
                        start=(kcg == 0 and i == 0), stop=(kcg == NKG - 1 and i == KGW - 1)),
                        waits=[(se, vP)])
                    P.op("tensor", lambda e, i=i, rp=rp: e.matmul(
                        self.P1[0:1, 1024 + 128 * 0:1024 + 128 * 1],
                        self.ones_bf[:],
                        A[:, a_rpt + 512 * rp + 128 * i:a_rpt + 512 * rp + 128 * (i + 1)],
                        start=(i == 0), stop=(i == KGW - 1)),
                        waits=[self.consts_ready] + ([sv.now] if i == 0 else []),
                        inc=(sm.bump() if i == KGW - 1 else None))
                vD = sm.n
                rpt_free[rp] = sm.now
                vgrp_free[gi] = sm.now
                P.op("vector", lambda e, h=h: e.tensor_tensor(
                    self.a1_drow[0:1, 128 * h:128 * (h + 1)],
                    self.a1_drow[0:1, 128 * h:128 * (h + 1)],
                    self.P1[0:1, 1024:1152], ALU.add),
                    waits=[(sm, vD)], inc=sv.bump())
        vrowmm = sm.n
        drow_ok = sv.now
        if ATT1_STOP == "row":
            return self.s_r.now
        P.op("vector", lambda e: e.tensor_copy(self.a1_ctxred[:], self.po[0][:]),
             waits=[(sm, vrowmm)], inc=sv.bump())
        ctxp_ok = sv.now
        P.op("gpsimd", lambda e: e.dma_start(out=self.dcc1r_in[0:128, :],
                                             in_=self.a1_ctxred[:]),
             waits=[ctxp_ok, self.s_io.now], inc=self.s_io.bump(16))
        P.op("gpsimd", lambda e: e.dma_start(
            out=self.dcc1r_in[128:130, :],
            in_=self.a1_drow[0:1, :].rearrange("p (a b) -> p a b", b=1024)),
            waits=[drow_ok], inc=self.s_io.bump(16))
        P.op("gpsimd", lambda e: e.collective_compute(
            "AllReduce", ALU.add, replica_groups=SG,
            ins=[self.dcc1r_in.opt()], outs=[self.dcc1r_out.opt()]),
            waits=[self.s_io.now], inc=self.s_cc.bump())
        rr_ok = self.s_cc.now
        P.op("gpsimd", lambda e: e.dma_start(out=self.a1_ctxred[:],
                                             in_=self.dcc1r_out[0:128, :]),
             waits=[rr_ok], inc=self.s_io.bump(16))
        P.op("gpsimd", lambda e: e.dma_start(
            out=self.a1_drow[0:1, :].rearrange("p (a b) -> p a b", b=1024),
            in_=self.dcc1r_out[128:130, :]),
            inc=self.s_io.bump(16))
        rred_ok = self.s_io.now
        P.op("gpsimd", lambda e: e.dma_start(out=self.a1_bobf, in_=self.da1_bo),
             waits=[(sm, vrowmm)], inc=self.s_stgu.bump(16))
        bias_ok = self.s_stgu.now

        def emit_normalize(po0_guard):
            P.op("vector", lambda e: e.reciprocal(self.a1_drow[:], self.a1_drow[:]),
                 waits=[rred_ok, sv.now], inc=sv.bump())
            for j in range(DC):
                for par in range(2):
                    h = 2 * j + par
                    P.op("tensor", lambda e, h=h: e.matmul(
                        self.po[0][:, 0:128], self.ones_row_f32[:],
                        self.a1_drow[0:1, 128 * h:128 * (h + 1)], start=True, stop=True),
                        waits=([sv.now, self.consts_ready, ctxp_ok, po0_guard]
                               if (j == 0 and par == 0) else [sv.now]),
                        inc=sm.bump())
                    P.op("vector", lambda e, j=j, par=par: e.tensor_tensor(
                        A[64 * par:64 * par + 64, a_rcn + 128 * j:a_rcn + 128 * (j + 1)],
                        self.a1_ctxred[64 * par:64 * par + 64, 128 * j:128 * (j + 1)],
                        self.po[0][64 * par:64 * par + 64, 0:128], ALU.mult),
                        waits=[(sm, sm.n)], inc=sv.bump())
            return sv.now

        row_done_se = se.now
        row_done_sv = sv.now
        row_ctxevac = ctxp_ok

        qp_free = [None, None]
        vw_free = [(sm, vrowmm), (sm, vrowmm)]
        ctx_free = None
        sg_free = None
        sl_free = None
        pt_free = None
        po1_r_free = None
        NPrun = NP if not (isinstance(ATT1_STOP, str) and ATT1_STOP.startswith("p")
                           and ATT1_STOP[1:].isdigit()) else int(ATT1_STOP[1:])
        row_sl0 = self.s_sl[0].now
        pair_order = (list(range(1, NP - 1)) + [0, NP - 1]) if NP > 2 else list(range(NP))
        rcn_ok = None
        for p_idx, p in enumerate(pair_order[:NPrun]):
            pi = p_idx % 2
            if p in (0, NP - 1) and rcn_ok is None:
                rcn_ok = emit_normalize(sg_free if sg_free is not None else ctxp_ok)
                sg_free = rcn_ok
            ww = [qspill_ok]
            ww.append(qp_free[0] if qp_free[0] is not None else row_sl0)
            P.op("sync", lambda e, p=p: e.dma_start(
                out=A[:, a_qp:a_qp + 1024].rearrange("p (j t) -> p j t", t=128),
                in_=self.dq1[:].rearrange("p (j t) -> p j t", t=Tc)[:, :, 128 * p:128 * (p + 1)]),
                waits=ww, inc=self.s_sl[0].bump(16))
            qp_ok = self.s_sl[0].now
            qp_ap = A[:, a_qp:a_qp + 1024]
            vwb = a_vst + 4096 * pi
            ww = [vspill_ok, halo_ok]
            if vw_free[pi] is not None:
                ww.append(vw_free[pi])
            def vdma(dst_lo, dst_n, src_row, first):
                P.op("sync", lambda e, dst_lo=dst_lo, dst_n=dst_n, src_row=src_row: e.dma_start(
                    out=A[dst_lo[0]:dst_lo[0] + dst_n, dst_lo[1]:dst_lo[1] + 1024],
                    in_=self.dv1[src_row:src_row + dst_n, :]),
                    waits=(ww if first else []), inc=self.s_sl[2 + pi].bump(16))
            first = True
            if p == 0:
                P.op("sync", lambda e, vwb=vwb: e.dma_start(
                    out=A[0:64, vwb:vwb + 1024], in_=A[64:128, a_vedge:a_vedge + 1024]),
                    waits=ww, inc=self.s_sl[2 + pi].bump(16))
                first = False
                vdma((64, vwb), 64, 0, first)
            else:
                vdma((0, vwb), 128, 128 * p - 64, first)
                first = False
            vdma((0, vwb + 1024), 64, 128 * p + 64, False)
            vdma((0, vwb + 2048), 128, 128 * p, False)
            if p != NP - 1:
                vdma((0, vwb + 3072), 64, 128 * p + 128, False)
            vw_ok = self.s_sl[2 + pi].now
            tA0 = A[:, vwb:vwb + 1024]
            tA1 = A[0:64, vwb + 1024:vwb + 2048]
            tB0 = A[:, vwb + 2048:vwb + 3072]
            tB1 = (A[0:64, a_vedge:a_vedge + 1024] if p == NP - 1
                   else A[0:64, vwb + 3072:vwb + 3072 + 1024])
            for hg in range(2):
                swl = [qp_ok, k_ok, halo_ok]
                if sl_free is not None:
                    swl.append(sl_free)
                else:
                    swl += [row_done_se, row_done_sv]
                first_sl = True
                for m in range(DC):
                    for bp in range(2):
                        b = 2 * p + bp
                        lhs = qp_ap[64 * hg:64 * hg + 64, 128 * m + 64 * bp:128 * m + 64 * bp + 64]
                        if b == 0:
                            pieces = [(Tc, 64, 0), (0, 64, 64), (64, 64, 128)]
                        elif b == NB - 1:
                            pieces = [(64 * (b - 1), 128, 0), (Tc + 64, 64, 128)]
                        else:
                            pieces = [(64 * (b - 1), 128, 0), (64 * (b - 1) + 128, 64, 128)]
                        for (src, n, dsto) in pieces:
                            if dsto < 128:
                                outap = self.P1[64 * bp:64 * bp + 64,
                                                128 * m + dsto:128 * m + dsto + n]
                            else:
                                outap = self.P1[64 * bp:64 * bp + 64,
                                                1024 + 64 * m:1024 + 64 * m + 64]
                            P.op("tensor", lambda e, src=src, n=n, lhs=lhs, hg=hg,
                                 m=m, outap=outap: e.matmul(
                                     outap, lhs,
                                     A[64 * hg:64 * hg + 64,
                                       a_kh + TCH * m + src:a_kh + TCH * m + src + n],
                                     start=True, stop=True),
                                 waits=(swl if first_sl else []))
                            first_sl = False
                        P.op("tensor", lambda e, m=m, bp=bp, lhs=lhs, hg=hg: e.matmul(
                            self.po[0][64 * bp:64 * bp + 64, 128 * m:128 * (m + 1)],
                            lhs,
                            A[64 * hg:64 * hg + 64, a_gk + 128 * m:a_gk + 128 * (m + 1)],
                            start=True, stop=True),
                            waits=(([gfix_ok, sg_free] if sg_free else
                                    [gfix_ok, ctxp_ok])
                                   if (m == 0 and bp == 0) else []),
                            inc=(sm.bump() if (m == DC - 1 and bp == 1) else None))
                vSg = sm.n
                vSl = sm.n
                if PAIR_STAGE == "sl":
                    sl_free = (sm, vSl); sg_free = (sm, vSg); continue
                ww = [(sm, vSl), self.consts_ready]
                if pt_free is not None:
                    ww.append(pt_free)
                P.op("scalar", lambda e: e.activation(
                    A[:, a_pl:a_pl + 1536], self.P1[:, 0:1536], AF.Exp,
                    bias=self.cpool[:, 3:4], scale=L1_SCALE),
                    waits=ww, inc=se.bump())
                sl_free = se.now
                P.op("scalar", lambda e: e.activation(
                    A[:, a_pgl:a_pgl + 1024], self.po[0][:], AF.Exp,
                    bias=self.cpool[:, 3:4], scale=L1_SCALE),
                    waits=[(sm, vSg)], inc=se.bump())
                sg_free = se.now
                vexp = se.n
                if PAIR_STAGE == "exp":
                    continue
                P.op("vector", lambda e: e.tensor_reduce(
                    self.a1_dl[:], A[:, a_pl:a_pl + 1024].rearrange("p (m t) -> p m t", t=128),
                    AXX, ALU.add),
                    waits=[(se, vexp)], inc=sv.bump())
                P.op("vector", lambda e: e.tensor_reduce(
                    self.gsc[:, 0:8],
                    A[:, a_pl + 1024:a_pl + 1536].rearrange("p (m t) -> p m t", t=64),
                    AXX, ALU.add),
                    waits=[(se, vexp)], inc=sv.bump())
                P.op("vector", lambda e: e.tensor_add(self.a1_dl[:], self.a1_dl[:],
                                                      self.gsc[:, 0:8]),
                     waits=[sv.now], inc=sv.bump())
                P.op("vector", lambda e: e.tensor_reduce(
                    self.a1_dg[:], A[:, a_pgl:a_pgl + 1024].rearrange("p (m t) -> p m t", t=128),
                    AXX, ALU.add),
                    waits=[(se, vexp)], inc=sv.bump())
                P.op("vector", lambda e: e.reciprocal(self.a1_dl[:], self.a1_dl[:]),
                     waits=[sv.now], inc=sv.bump())
                P.op("vector", lambda e: e.reciprocal(self.a1_dg[:], self.a1_dg[:]),
                     waits=[sv.now], inc=sv.bump())
                vrec = sv.n
                for m in range(DC):
                    P.op("vector", lambda e, m=m: e.tensor_scalar(
                        A[:, a_pl + 128 * m:a_pl + 128 * (m + 1)],
                        A[:, a_pl + 128 * m:a_pl + 128 * (m + 1)],
                        self.a1_dl[:, m:m + 1], None, ALU.mult),
                        waits=[(sv, vrec)], inc=sv.bump())
                    P.op("vector", lambda e, m=m: e.tensor_scalar(
                        A[:, a_pl + 1024 + 64 * m:a_pl + 1024 + 64 * (m + 1)],
                        A[:, a_pl + 1024 + 64 * m:a_pl + 1024 + 64 * (m + 1)],
                        self.a1_dl[:, m:m + 1], None, ALU.mult),
                        waits=[(sv, vrec)], inc=sv.bump())
                    P.op("vector", lambda e, m=m: e.tensor_scalar(
                        A[:, a_pgl + 128 * m:a_pgl + 128 * (m + 1)],
                        A[:, a_pgl + 128 * m:a_pgl + 128 * (m + 1)],
                        self.a1_dg[:, m:m + 1], None, ALU.mult),
                        waits=[(sv, vrec)], inc=sv.bump())
                vnorm = sv.n
                if PAIR_STAGE == "norm":
                    continue
                ptw = [(sv, vnorm), self.s_xt.now]
                for rnd, (co, width) in enumerate((((0), 128), ((128), 64), (("g"), 128))):
                    for m in range(DC):
                        if co == "g":
                            src = A[:, a_pgl + 128 * m:a_pgl + 128 * (m + 1)]
                        elif co == 0:
                            src = A[:, a_pl + 128 * m:a_pl + 128 * (m + 1)]
                        else:
                            src = A[:, a_pl + 1024 + 64 * m:a_pl + 1024 + 64 * (m + 1)]
                        P.op("tensor", lambda e, src=src, m=m, width=width: e.transpose(
                            self.pt[0:width, 128 * m:128 * (m + 1)], src,
                            self.ident_bf[:]),
                            waits=(ptw if m == 0 else []),
                            inc=(sm.bump() if m == DC - 1 else None))
                    vtr = sm.n
                    P.op("vector", lambda e, rnd=rnd: e.tensor_copy(
                        A[:, a_pt + 1024 * rnd:a_pt + 1024 * (rnd + 1)], self.pt[:]),
                        waits=[(sm, vtr)], inc=self.s_xt.bump())
                    ptw = [self.s_xt.now]
                pt_free = self.s_xt.now
                ptall = self.s_xt.now
                if PAIR_STAGE == "tr":
                    continue
                cw = [ptall, vw_ok, gfix_ok]
                if hg == 0:
                    cw.append(ctx_free if ctx_free is not None else guard_r)
                    if po1_r_free is not None:
                        cw.append(po1_r_free)
                first_ctx = True
                for m in range(DC):
                    h = 2 * m + hg
                    for bp in range(2):
                        idx = 2 * m + bp
                        t0 = tA0 if bp == 0 else tB0
                        t1 = tA1 if bp == 0 else tB1
                        outap = self.po[1][64 * hg:64 * hg + 64,
                                           128 * m + 64 * bp:128 * m + 64 * bp + 64]
                        P.op("tensor", lambda e, t0=t0, h=h, idx=idx, outap=outap: e.matmul(
                            outap, t0[:, 64 * h:64 * h + 64],
                            A[:, a_pt + 64 * idx:a_pt + 64 * (idx + 1)],
                            start=True, stop=False),
                            waits=(cw if first_ctx else []))
                        first_ctx = False
                        P.op("tensor", lambda e, t1=t1, h=h, idx=idx, outap=outap: e.matmul(
                            outap, t1[0:64, 64 * h:64 * h + 64],
                            A[0:64, a_pt + 1024 + 64 * idx:a_pt + 1024 + 64 * (idx + 1)],
                            start=False, stop=False))
                        P.op("tensor", lambda e, h=h, idx=idx, outap=outap: e.matmul(
                            outap, A[:, a_gv + 64 * h:a_gv + 64 * h + 64],
                            A[:, a_pt + 2048 + 64 * idx:a_pt + 2048 + 64 * (idx + 1)],
                            start=False, stop=True),
                            inc=(sm.bump() if (m == DC - 1 and bp == 1) else None))
                vctx = sm.n
                vw_free[pi] = sm.now
                qp_free[0] = sm.now
            if PAIR_STAGE in ("sl", "exp", "norm", "tr"):
                continue
            P.op("vector", lambda e: e.tensor_copy(
                A[:, a_ctxp:a_ctxp + 1024], self.po[1][:]),
                waits=[(sm, vctx)], inc=sv.bump())
            ctx_free = sv.now
            ctxp_done = sv.n
            if PAIR_STAGE == "ctx":
                continue
            if p == 0 or p == NP - 1:
                qoff = 0 if p == 0 else 64
                mcol = 8 if p == 0 else 9
                mscr = A[:, a_pgl:a_pgl + 64]
                for j in range(DC):
                    P.op("vector", lambda e, j=j, qoff=qoff, mscr=mscr: e.tensor_sub(
                        mscr,
                        A[:, a_rcn + 128 * j + qoff:a_rcn + 128 * j + qoff + 64],
                        A[:, a_ctxp + 128 * j + qoff:a_ctxp + 128 * j + qoff + 64]),
                        waits=[(sv, ctxp_done), rcn_ok, mrep_ok, sv.now, sl_free],
                        inc=sv.bump())
                    P.op("vector", lambda e, j=j, qoff=qoff, mcol=mcol, mscr=mscr: e.scalar_tensor_tensor(
                        A[:, a_ctxp + 128 * j + qoff:a_ctxp + 128 * j + qoff + 64],
                        mscr, self.a1_rsel[:, mcol:mcol + 1],
                        A[:, a_ctxp + 128 * j + qoff:a_ctxp + 128 * j + qoff + 64],
                        ALU.mult, ALU.add),
                        waits=[sv.now, small_ok], inc=sv.bump())
            merge_done = sv.n
            ww = [(sv, merge_done), wocast, bias_ok, self.consts_ready,
                  (sv, ctxp_done)]
            if po1_r_free is not None:
                ww.append(po1_r_free)
            for hh in range(2):
                P.op("tensor", lambda e, hh=hh: e.matmul(
                    self.po[1][:, 512 * hh:512 * (hh + 1)], self.ones_row_bf[:],
                    self.a1_bobf[0:1, 512 * hh:512 * (hh + 1)],
                    start=True, stop=False), waits=(ww if hh == 0 else []))
                for j in range(DC):
                    P.op("tensor", lambda e, j=j, hh=hh: e.matmul(
                        self.po[1][:, 512 * hh:512 * (hh + 1)],
                        A[:, a_ctxp + 128 * j:a_ctxp + 128 * (j + 1)],
                        wsb(j)[:, 512 * hh:512 * (hh + 1)],
                        start=False, stop=(j == DC - 1)),
                        inc=(sm.bump() if (j == DC - 1 and hh == 1) else None))
            vwo = sm.n
            P.op("vector", lambda e, p=p: e.tensor_tensor(
                self.R[:, 1024 * p:1024 * (p + 1)], self.R[:, 1024 * p:1024 * (p + 1)],
                self.po[1][:], ALU.add),
                waits=[(sm, vwo)], inc=self.s_r.bump())
            po1_r_free = self.s_r.now
        self.stage_free[0] = rcn_ok
        self.stage_free[1] = rcn_ok
        return self.s_r.now

    def p_ffn1(self, r_dep, first):
        P, c = self.P, self.cfg
        A = self.ABF
        Tc, DC, NCH = c.Tc, c.DC, c.NCH
        xt_ok = self.emit_ln([r_dep])
        guard_mm = self.s_mm.now
        guard_io = self.s_io.now
        a_qw = 0
        a_qT = 8 * 1024
        a_sgn = a_qT + DC * Tc
        a_cmp = a_sgn + 1024
        a_l1b = a_cmp + 1024
        assert a_l1b + 2048 <= 46592
        l1b = A[0:1, a_l1b:a_l1b + 1024]
        l1bpre = A[0:1, a_l1b + 1024:a_l1b + 2048]
        stg = [self.AFP[:, 0:1408], self.AFP[:, 1408:2816]]
        sv, se = self.s_ve, self.s_ev
        L = self.l1v

        P.op("vector", lambda e: e.reduce_max(L[:, 0:1], self.XT[:, 0:c.Tc], axis=AXX,
                                              apply_absolute_value=True),
             waits=[xt_ok], inc=sv.bump())
        for jj in range(1, DC):
            P.op("vector", lambda e, jj=jj: e.reduce_max(L[:, 1:2], self.XT[:, Tc * jj:Tc * (jj + 1)],
                                                         axis=AXX, apply_absolute_value=True),
                 waits=[sv.now], inc=sv.bump())
            P.op("vector", lambda e: e.tensor_max(L[:, 0:1], L[:, 0:1], L[:, 1:2]),
                 waits=[sv.now], inc=sv.bump())
        mrow = self.AFP[0:1, 2432:2560]
        w = [sv.now] + ([self.tp_free] if self.tp_free else [])
        P.op("sync", lambda e, mrow=mrow: e.dma_start(out=mrow, in_=L[:, 0:1]),
             waits=w, inc=self.s_tp.bump(16))
        P.op("vector", lambda e, mrow=mrow: e.reduce_max(self.l1s[0:1, 3:4], mrow, axis=AXX),
             waits=[self.s_tp.now], inc=sv.bump())
        self.tp_free = sv.now
        P.op("gpsimd", lambda e: e.dma_start(out=self.dmx_in, in_=self.l1s[0:1, 3:4]),
             waits=[sv.now, self.s_io.now], inc=self.s_io.bump(16))
        P.op("gpsimd", lambda e: e.collective_compute(
            "AllReduce", ALU.max, replica_groups=[list(range(8))],
            ins=[self.dmx_in.opt()], outs=[self.dmx_out.opt()]),
            waits=[self.s_io.now], inc=self.s_cc.bump())
        P.op("gpsimd", lambda e: e.dma_start(out=self.l1s[0:1, 0:1], in_=self.dmx_out),
             waits=[self.s_cc.now], inc=self.s_io.bump(16))
        gmax_ok = self.s_io.now

        if first:
            awcols = self.gsc
            passA_done = []
            for j in range(DC):
                si = j % 2
                w = [w for w in [self.stage_free[si]] if w] + [guard_mm]
                P.op("sync", lambda e, j=j, si=si: e.dma_start(
                    out=stg[si][:, 0:1024], in_=self.dl1_wT[128 * j:128 * (j + 1), :]),
                    waits=w, inc=self.s_stg[si].bump(16))
                wd = self.s_stg[si].now
                P.op("vector", lambda e, j=j, si=si: e.reduce_sum(
                    awcols[:, j:j + 1], stg[si][:, 0:1024], axis=AXX,
                    apply_absolute_value=True),
                    waits=[wd, sv.now], inc=sv.bump())
                self.stage_free[si] = sv.now
            P.op("vector", lambda e: e.reduce_sum(L[:, 1:2], awcols[:, 0:DC], axis=AXX),
                 waits=[sv.now], inc=sv.bump())
            vws = sv.n
            P.op("tensor", lambda e: e.matmul(self.po[1][0:1, 512:513], L[:, 1:2],
                                              self.cpool[:, 4:5], start=True, stop=True),
                 waits=[(sv, vws), self.consts_ready], inc=self.s_mm.bump())
            vsum = self.s_mm.n
            P.op("vector", lambda e: e.tensor_scalar(self.l1s[0:1, 4:5], self.po[1][0:1, 512:513],
                                                     1.0 / (1024.0 * 1024.0), None, ALU.mult),
                 waits=[(self.s_mm, vsum)], inc=sv.bump())
            P.op("tensor", lambda e: e.matmul(self.po[1][:, 513:514], self.ones_row_f32[:],
                                              self.l1s[0:1, 4:5], start=True, stop=True),
                 waits=[sv.now], inc=self.s_mm.bump())
            vwb = self.s_mm.n
            P.op("vector", lambda e: e.tensor_copy(L[:, 2:3], self.po[1][:, 513:514]),
                 waits=[(self.s_mm, vwb)], inc=sv.bump())
            P.op("vector", lambda e: e.tensor_scalar(L[:, 3:4], L[:, 2:3],
                                                     0.5, None, ALU.mult),
                 waits=[sv.now], inc=sv.bump())
            ws_ok = sv.now
            absbuf = self.AFP[:, 2816:3840]
            for j in range(DC):
                si = j % 2
                w = [w for w in [self.stage_free[si]] if w]
                P.op("sync", lambda e, j=j, si=si: e.dma_start(
                    out=stg[si][:, 0:1024], in_=self.dl1_wT[128 * j:128 * (j + 1), :]),
                    waits=w, inc=self.s_stg[si].bump(16))
                wd = self.s_stg[si].now
                P.op("scalar", lambda e, si=si: e.activation(
                    A[:, a_sgn:a_sgn + 1024], stg[si][:, 0:1024], AF.Sign, scale=-1.0),
                    waits=[wd, se.now, sv.now, guard_mm, guard_io], inc=se.bump())
                P.op("scalar", lambda e, si=si: e.activation(
                    absbuf, stg[si][:, 0:1024], AF.Abs),
                    waits=[wd, sv.now], inc=se.bump())
                self.stage_free[si] = se.now
                P.op("vector", lambda e: e.tensor_scalar(
                    A[:, a_cmp:a_cmp + 1024], absbuf, L[:, 3:4], None, ALU.is_gt),
                    waits=[se.now, ws_ok], inc=sv.bump())
                P.op("vector", lambda e, j=j: e.tensor_tensor(
                    A[:, a_qw + 1024 * j:a_qw + 1024 * (j + 1)],
                    A[:, a_sgn:a_sgn + 1024], A[:, a_cmp:a_cmp + 1024], ALU.mult),
                    waits=[sv.now, se.now], inc=sv.bump())
            qw_ok = sv.now
            P.op("gpsimd", lambda e: e.dma_start(out=self.dqw, in_=A[:, a_qw:a_qw + 8 * 1024]),
                 waits=[qw_ok], inc=self.s_stgu.bump(16))
            P.op("gpsimd", lambda e: e.dma_start(out=self.dws, in_=L[:, 2:3]),
                 waits=[qw_ok], inc=self.s_stgu.bump(16))
            self.qw_spill_done = self.s_stgu.now
        else:
            P.op("sync", lambda e: e.dma_start(out=A[:, a_qw:a_qw + 8 * 1024], in_=self.dqw),
                 waits=[guard_mm, self.qw_spill_done], inc=self.s_stg[0].bump(16))
            P.op("sync", lambda e: e.dma_start(out=L[:, 2:3], in_=self.dws),
                 inc=self.s_stg[0].bump(16))
            qw_ok = self.s_stg[0].now

        P.op("vector", lambda e: e.reciprocal(self.l1s[0:1, 1:2], self.l1s[0:1, 0:1]),
             waits=[gmax_ok, sv.now], inc=sv.bump())
        P.op("vector", lambda e: e.tensor_scalar(self.l1s[0:1, 1:2], self.l1s[0:1, 1:2],
                                                 127.0, None, ALU.mult),
             waits=[sv.now], inc=sv.bump())
        P.op("vector", lambda e: e.tensor_scalar(self.l1s[0:1, 2:3], self.l1s[0:1, 0:1],
                                                 1.0 / 127.0, None, ALU.mult),
             waits=[sv.now], inc=sv.bump())
        vsc = sv.n
        P.op("tensor", lambda e: e.matmul(self.po[1][:, 0:1], self.ones_row_f32[:],
                                          self.l1s[0:1, 1:2], start=True, stop=True),
             waits=[(sv, vsc), self.consts_ready], inc=self.s_mm.bump())
        P.op("tensor", lambda e: e.matmul(self.po[1][:, 1:2], self.ones_row_f32[:],
                                          self.l1s[0:1, 2:3], start=True, stop=True),
             inc=self.s_mm.bump())
        vbc = self.s_mm.n
        P.op("vector", lambda e: e.tensor_copy(L[:, 4:5], self.po[1][:, 0:1]),
             waits=[(self.s_mm, vbc)], inc=sv.bump())
        P.op("vector", lambda e: e.tensor_copy(L[:, 5:6], self.po[1][:, 1:2]),
             waits=[(self.s_mm, vbc)], inc=sv.bump())
        iscale_ok = sv.now

        w = [x for x in [self.stage_free[1]] if x]
        P.op("sync", lambda e: e.dma_start(out=self.AFP[0:1, 1408:2432], in_=self.dl1_bias),
             waits=w, inc=self.s_stg[1].bump(16))
        P.op("vector", lambda e: e.tensor_copy(l1b, self.AFP[0:1, 1408:2432]),
             waits=[self.s_stg[1].now, guard_mm], inc=self.s_cast.bump())
        self.stage_free[1] = self.s_cast.now
        bias_ok = self.s_cast.now
        P.op("vector", lambda e: e.scalar_tensor_tensor(
            self.l1s[0:1, 3:4], self.l1s[0:1, 2:3], 1.0, L[0:1, 2:3], ALU.mult, ALU.mult),
            waits=[iscale_ok, qw_ok], inc=sv.bump())
        P.op("vector", lambda e: e.reciprocal(self.l1s[0:1, 3:4], self.l1s[0:1, 3:4]),
             waits=[sv.now], inc=sv.bump())
        P.op("vector", lambda e: e.tensor_scalar(l1bpre, l1b,
                                                 self.l1s[0:1, 3:4], None, ALU.mult),
             waits=[sv.now, bias_ok], inc=sv.bump())
        P.op("vector", lambda e: e.scalar_tensor_tensor(
            L[:, 6:7], L[:, 2:3], 0.5, L[:, 5:6], ALU.mult, ALU.mult),
            waits=[sv.now], inc=sv.bump())
        prep_ok = sv.now

        QW = min(512, Tc)
        vb2 = self.AFP[:, 0:QW]
        rb2 = self.AFP[:, 512:512 + QW]
        NCC = Tc // QW
        qx_val = [None] * NCC
        for cc in range(NCC):
            for jj in range(DC):
                P.op("vector", lambda e, jj=jj, cc=cc: e.tensor_scalar(
                    vb2, self.XT[:, Tc * jj + QW * cc:Tc * jj + QW * (cc + 1)],
                    L[:, 4:5], 256.5, ALU.mult, ALU.add),
                    waits=[xt_ok, iscale_ok, sv.now, self.stage_free[0] or sv.now,
                           self.stage_free[1] or sv.now],
                    inc=sv.bump())
                P.op("vector", lambda e: e.tensor_copy(self.ibuf[:, 0:QW], vb2),
                     waits=[sv.now], inc=sv.bump())
                P.op("vector", lambda e: e.tensor_copy(rb2, self.ibuf[:, 0:QW]),
                     waits=[sv.now], inc=sv.bump())
                P.op("vector", lambda e: e.tensor_sub(vb2, vb2, rb2),
                     waits=[sv.now], inc=sv.bump())
                P.op("vector", lambda e: e.tensor_single_scalar(
                    vb2, vb2, 0.0, ALU.is_lt),
                    waits=[sv.now], inc=sv.bump())
                P.op("vector", lambda e: e.tensor_scalar(
                    rb2, rb2, -1.0, 256.0, ALU.mult, ALU.add),
                    waits=[sv.now], inc=sv.bump())
                P.op("vector", lambda e, jj=jj, cc=cc: e.tensor_tensor(
                    A[:, a_qT + Tc * jj + QW * cc:a_qT + Tc * jj + QW * (cc + 1)],
                    rb2, vb2, ALU.add),
                    waits=[sv.now, guard_mm, guard_io], inc=sv.bump())
            qx_val[cc] = sv.n
        qx_ok = sv.now
        self.stage_free[0] = qx_ok
        self.stage_free[1] = qx_ok

        po_free = [None, None]
        for k in range(NCH):
            pb = k % 2
            bank = self.po[pb]
            ww = [(sv, qx_val[k // (NCH // NCC)]), qw_ok, prep_ok, self.consts_ready]
            if po_free[pb] is not None:
                ww.append(po_free[pb])
            for hh in range(2):
                P.op("tensor", lambda e, hh=hh, bank=bank: e.matmul(
                    bank[:, 512 * hh:512 * (hh + 1)], self.ones_row_bf[:],
                    l1bpre[0:1, 512 * hh:512 * (hh + 1)],
                    start=True, stop=False), waits=(ww if hh == 0 else []))
                for j in range(DC):
                    P.op("tensor", lambda e, k=k, j=j, hh=hh, bank=bank: e.matmul(
                        bank[:, 512 * hh:512 * (hh + 1)],
                        A[:, a_qT + Tc * j + 128 * k:a_qT + Tc * j + 128 * (k + 1)],
                        A[:, a_qw + 1024 * j + 512 * hh:a_qw + 1024 * j + 512 * (hh + 1)],
                        start=False, stop=(j == DC - 1)),
                        inc=(self.s_mm.bump() if (j == DC - 1 and hh == 1) else None))
            vmm = self.s_mm.n
            P.op("vector", lambda e, k=k, bank=bank: e.scalar_tensor_tensor(
                self.R[:, 1024 * k:1024 * (k + 1)], bank[:], L[:, 6:7],
                self.R[:, 1024 * k:1024 * (k + 1)], ALU.mult, ALU.add),
                waits=[(self.s_mm, vmm)], inc=self.s_r.bump())
            po_free[pb] = self.s_r.now
            self.r_chunk_done[k] = self.s_r.now
        return self.s_r.now

    def p_store(self, dep):
        P, c = self.P, self.cfg
        if DEBUG_DUMP:
            P.op("gpsimd", lambda e: e.dma_start(
                out=self.ddbg, in_=self.ABF[:, 0:24576]),
                waits=[dep], inc=self.s_st.bump(16))
        for k in range(c.NCH):
            kdep = self.r_chunk_done[k] if self.r_chunk_done[k] is not None else dep
            P.op("gpsimd", lambda e, k=k: e.dma_start(
                out=self.dout[128 * k:128 * (k + 1), :],
                in_=self.R[:, 1024 * k:1024 * (k + 1)]),
                waits=[kdep], inc=self.s_st.bump(16))
        P.op("gpsimd", lambda e: e.engine_nop(), waits=[self.s_st.now])

    def finish(self):
        self.P.emit()
        return self.nc


ALL_PHASES = ("ffn0a", "att0", "ffn0b", "ffn1a", "att1", "ffn1b")


def build(cfg: Cfg, phases=ALL_PHASES):
    b = Builder(cfg)
    b.p_load()
    done = b.x_ready
    if "ffn0a" in phases:
        done = b.p_ffn0(done, cache="write" if "ffn0b" in phases else None)
    if "att0" in phases:
        done = b.p_att0(done)
    if "ffn0b" in phases:
        done = b.p_ffn0(done, cache="read" if "ffn0a" in phases else None)
    if "ffn1a" in phases:
        done = b.p_ffn1(done, first=True)
    if "att1" in phases:
        done = b.p_att1(done)
    if "ffn1b" in phases:
        done = b.p_ffn1(done, first="ffn1a" not in phases)
    b.p_store(done)
    nc = b.finish()
    return nc, b


def make_const_inputs(cfg: Cfg):
    ident = np.eye(128, dtype=np.float32)
    import ml_dtypes
    cpool = np.zeros((128, 8), np.float32)
    cpool[:, 0] = -PI
    cpool[:, 1] = 1e-5
    cpool[:, 2] = -C_L0
    cpool[:, 3] = -C_L1
    cpool[:, 4] = 1.0
    cpool[:, 5] = 0.5
    cpool[:, 6] = -1.0
    return {
        "ident_bf": ident.astype(ml_dtypes.bfloat16),
        "cpool": cpool,
        "ones_bf": np.ones((128, 1), ml_dtypes.bfloat16),
        "ones_row_bf": np.ones((1, 128), ml_dtypes.bfloat16),
        "ones_row_f32": np.ones((1, 128), np.float32),
    }


def make_shared_inputs(inputs):
    f = lambda a: np.ascontiguousarray(np.asarray(a, np.float32))
    m = {
        "L0_w_gate": f(inputs["L0_ffn_w_gate"]),
        "L0_w_g": f(inputs["L0_ffn_w_g"]),
        "L0_w_u": f(inputs["L0_ffn_w_u"]),
        "L0_w_d": f(inputs["L0_ffn_w_d"]),
        "A0_wqT": f(np.asarray(inputs["L0_attn_Wq"], np.float32).T),
        "A0_wkT": f(np.asarray(inputs["L0_attn_Wk"], np.float32).T),
        "A0_wvT": f(np.asarray(inputs["L0_attn_Wv"], np.float32).T),
        "A0_woT": f(np.asarray(inputs["L0_attn_Wo"], np.float32).T),
        "A0_bo": f(inputs["L0_attn_bo"]).reshape(1, 1024),
        "A0_vecs": np.stack([f(inputs["L0_attn_wavelengths"]),
                             f(inputs["L0_attn_phase_bias"]),
                             f(inputs["L0_attn_bq"]),
                             f(inputs["L0_attn_bk"]),
                             f(inputs["L0_attn_bv"])], axis=1).copy(),
        "A1_wqT": f(np.asarray(inputs["L1_attn_Wq"], np.float32).T),
        "A1_wkT": f(np.asarray(inputs["L1_attn_Wk"], np.float32).T),
        "A1_wvT": f(np.asarray(inputs["L1_attn_Wv"], np.float32).T),
        "A1_woT": f(np.asarray(inputs["L1_attn_Wo"], np.float32).T),
        "A1_bqkv": np.concatenate([
            f(inputs["L1_attn_bq"]).reshape(8, 128).T,
            f(inputs["L1_attn_bk"]).reshape(8, 128).T,
            f(inputs["L1_attn_bv"]).reshape(8, 128).T], axis=1).copy(),
        "A1_bo": f(inputs["L1_attn_bo"]).reshape(1, 1024),
        "L1f_wT": f(np.asarray(inputs["L1_ffn_weight"], np.float32).T),
        "L1f_bias": f(inputs["L1_ffn_bias"]).reshape(1, 1024),
    }
    return m


def make_core_inputs(cfg, core):
    r = core % 4
    rolesel = np.zeros((128, 10), np.float32)
    rolesel[:, (r - 1) % 4] = 1.0
    rolesel[:, 4 + (r + 1) % 4] = 1.0
    if r == 0:
        rolesel[:, 8] = 1.0
    if r == 1:
        rolesel[:, 9] = 1.0
    return {"A1_rolesel": rolesel}


def kernel(**inputs):
    cfg = Cfg(Tc=2048)
    gi = np.asarray(inputs["L1_attn_global_indices"]).ravel()
    assert int(gi[0]) == 0 and int(gi[1]) == 2 * cfg.NBLK - 1, gi
    nc, _ = build(cfg)
    consts = make_const_inputs(cfg)
    shared = make_shared_inputs(inputs)
    x = np.ascontiguousarray(np.asarray(inputs["L0_attn_x"], np.float32).reshape(16384, 1024))
    in_maps = []
    for c in range(8):
        m = dict(consts)
        m.update(shared)
        m.update(make_core_inputs(cfg, c))
        m["x"] = np.ascontiguousarray(x[2048 * c:2048 * (c + 1)])
        in_maps.append(m)
    res = run_bass_kernel_spmd(nc, in_maps, core_ids=list(range(8)))
    out = np.concatenate([res.results[c]["out"] for c in range(8)], axis=0)
    return out.reshape(2, 8192, 1024)
